# version 02
# baseline (speedup 1.0000x reference)
import numpy as np
import ml_dtypes
import concourse.bass as bass
import concourse.mybir as mybir
from concourse.bass_utils import run_bass_kernel_spmd

F32 = mybir.dt.float32
BF16 = mybir.dt.bfloat16
AF = mybir.ActivationFunctionType
ALU = mybir.AluOpType
AX = mybir.AxisListType

NCORES = 8
SERIAL = False
XSERIAL = None
T_CORE = 4096
DM = 1024
NSTATE = 256
BF = ml_dtypes.bfloat16

LN_EPS = 1e-5
RMS_EPS = 1e-6


ENGINES = ("sync", "gpsimd", "tensor", "vector", "scalar")


class Prog:

    AUTO_CHAIN = ("vector", "scalar")

    def __init__(self, nc, serial=False):
        self.nc = nc
        self.ops = {e: [] for e in ENGINES}
        self._sems = {}
        self._cnt = {}
        self._stack = None
        self.serial = serial
        self._last_edge = {e: None for e in ENGINES}

    def sem(self, name):
        if name not in self._sems:
            self._sems[name] = self._stack.enter_context(self.nc.semaphore(name))
            self._cnt[name] = 0
        return self._sems[name]

    def cnt(self, name):
        return self._cnt.get(name, 0)

    def on(self, eng, fn, wait=None, inc=None):
        if XSERIAL and not self.serial:
            prod, cons = XSERIAL
            if eng == prod:
                tk = self.sem("_XT")
                self._cnt["_XT"] += 1

                def fn2(e, fn=fn, tk=tk):
                    ins = fn(e)
                    e.nop().then_inc(tk, 1)
                    return ins
                fn = fn2
            elif eng == cons:
                tk = self.sem("_XT")
                tgt = self._cnt["_XT"]
                wait = list(wait) if wait else []
                wait.append(("_XT", tgt))
        if self.serial:
            step = 16 if eng in ("sync", "gpsimd") else 1
            tk = self.sem("_TK")
            start = self._cnt["_TK"]
            self._cnt["_TK"] = start + step
            if inc is not None:
                self._cnt[inc[0]] = self._cnt.get(inc[0], 0) + inc[1]

            def run_s(e, fn=fn, tk=tk, start=start, step=step):
                if start > 0:
                    e.wait_ge(tk, start)
                fn(e).then_inc(tk, step)
            self.ops[eng].append(run_s)
            if inc is not None:
                return self._cnt[inc[0]]
            return None
        if eng in self.AUTO_CHAIN:
            if inc is None:
                inc = ("_T" + eng, 1)
            prev = self._last_edge[eng]
            if prev is not None:
                wait = list(wait) if wait else []
                wait.append(prev)
        sem_h = None
        inc_amt = 0
        if inc is not None:
            name, amt = inc
            sem_h = self.sem(name)
            self._cnt[name] += amt
            inc_amt = amt
            if eng in self.AUTO_CHAIN:
                self._last_edge[eng] = (name, self._cnt[name])
        waits = []
        if wait:
            for wname, wval in wait:
                if wval > 0:
                    waits.append((self.sem(wname), wval))

        def run(e, fn=fn, waits=waits, sem_h=sem_h, inc_amt=inc_amt):
            for s, v in waits:
                e.wait_ge(s, v)
            ins = fn(e)
            if sem_h is not None:
                ins.then_inc(sem_h, inc_amt)
        self.ops[eng].append(run)
        if inc is not None:
            return self._cnt[inc[0]]
        return None

    def wait(self, eng, wait):
        if self.serial:
            tk = self.sem("_TK")
            target = self._cnt["_TK"]

            def run_s(e, tk=tk, target=target):
                e.wait_ge(tk, target)
            self.ops[eng].append(run_s)
            return
        waits = [(self.sem(n), v) for n, v in wait]

        def run(e, waits=waits):
            for s, v in waits:
                e.wait_ge(s, v)
        self.ops[eng].append(run)

    def barrier(self, name):
        h = self.sem(name)
        self._cnt[name] += len(ENGINES)
        target = self._cnt[name]
        for eng in ENGINES:
            def run(e, h=h, target=target):
                e.nop().then_inc(h, 1)
                e.wait_ge(h, target)
            self.ops[eng].append(run)

    def emit(self, block):
        for eng in ENGINES:
            ops = self.ops[eng]

            def run_all(e, ops=ops):
                for fn in ops:
                    fn(e)
            getattr(block, eng)(run_all)


def build_phase_a():
    nc = bass.Bass("TRN2", target_bir_lowering=False)
    u = nc.dram_tensor("u", [T_CORE, DM], F32, kind="ExternalInput").ap()
    uT = nc.dram_tensor("uT", [DM, T_CORE], F32, kind="ExternalInput").ap()
    uTprev = nc.dram_tensor("uTprev", [DM, T_CORE], F32, kind="ExternalInput").ap()
    BwT = nc.dram_tensor("BwT", [DM, NSTATE], F32, kind="ExternalInput").ap()
    CwT = nc.dram_tensor("CwT", [NSTATE, DM], F32, kind="ExternalInput").ap()
    ll2 = nc.dram_tensor("ll2", [128, 2], F32, kind="ExternalInput").ap()
    Drow = nc.dram_tensor("Drow", [1, DM], F32, kind="ExternalInput").ap()
    z1 = nc.dram_tensor("z1", [T_CORE, DM], F32, kind="ExternalOutput").ap()

    st = _STACK
    al = lambda *a: st.enter_context(nc.sbuf_tensor(*a))
    alp = lambda *a: st.enter_context(nc.psum_tensor(*a))
    bwt = al("bwt", [128, 8 * NSTATE], BF16)
    cwt = al("cwt", [128, 2 * DM], BF16)
    ll = al("ll", [128, 2], F32)
    lamc = al("lamc", [128, 2], F32)
    lamb = al("lamb", [128, 2 * 512], F32)
    drow_s = al("drow_s", [1, DM], F32)
    onesr = al("onesr", [1, 128], F32)
    d1b = al("d1b", [128, DM], F32)
    zcol = al("zcol", [128, 1], F32)
    utb = al("utb", [128, 2 * 8 * 512], BF16)
    uf = al("uf", [128, 2 * 4 * DM], F32)
    hf = al("hf", [128, 4 * 512], F32)
    hbf = al("hbf", [128, 2 * 2 * 512], BF16)
    ep0 = al("ep0", [128, 2 * DM], F32)
    ep1 = al("ep1", [128, 2 * DM], F32)
    sqb = al("sqb", [128, 2 * DM], F32)
    z1t = al("z1t", [128, 2 * DM], F32)
    mcol = al("mcol", [128, 8], F32)
    konst = al("konst", [128, 4], F32)
    psb0 = alp("psb0", [128, 512], F32)
    psb1 = alp("psb1", [128, 512], F32)
    psy0 = alp("psy0", [128, DM], F32)
    psy1 = alp("psy1", [128, DM], F32)
    psd = alp("psd", [128, DM], F32)
    with nc.Block() as block:
        p = Prog(nc, serial=SERIAL)
        p._stack = _STACK
        psb = [psb0, psb1]
        psy = [psy0, psy1]

        uT3 = uT.rearrange("(c p) l -> p c l", p=128)
        uTp3 = uTprev.rearrange("(c p) l -> p c l", p=128)
        u4 = u.rearrange("(ch g p) m -> ch p g m", g=4, p=128)
        z14 = z1.rearrange("(ch g p) m -> ch g p m", g=4, p=128)

        p.on("gpsimd", lambda e: e.dma_start(
            out=bwt[:, :].rearrange("p (c n) -> p c n", c=8),
            in_=BwT.rearrange("(c p) n -> p c n", p=128)), inc=("LDWG", 16))
        p.on("gpsimd", lambda e: e.dma_start(
            out=cwt[:, :].rearrange("p (c m) -> p c m", c=2),
            in_=CwT.rearrange("(c p) m -> p c m", p=128)), inc=("LDWG", 16))
        p.on("sync", lambda e: e.dma_start(out=ll[:, :], in_=ll2), inc=("LDWS", 16))
        p.on("sync", lambda e: e.dma_start(out=drow_s[:, :], in_=Drow), inc=("LDWS", 16))
        p.on("scalar", lambda e: e.activation(lamc[:, :], ll[:, :], AF.Sigmoid),
             wait=[("LDWS", 32)], inc=("SIG", 1))
        p.on("vector", lambda e: e.memset(lamb[:, :], 1.0))
        p.on("vector", lambda e: e.memset(zcol[:, :], 0.0))
        p.on("vector", lambda e: e.memset(konst[:, 0:1], LN_EPS))
        p.on("vector", lambda e: e.memset(konst[:, 1:2], -0.5))
        p.on("vector", lambda e: e.memset(konst[:, 2:3], RMS_EPS))
        p.on("vector", lambda e: e.memset(onesr[:, :], 1.0), inc=("MST", 1))
        for nh in range(2):
            p.on("vector", lambda e, nh=nh: e.tensor_scalar(
                lamb[:, nh * 512:(nh + 1) * 512], lamb[:, nh * 512:(nh + 1) * 512],
                lamc[:, nh:nh + 1], None, ALU.mult),
                wait=[("SIG", 1)], inc=("PREP", 1))
        for f in range(2):
            p.on("tensor", lambda e, f=f: e.matmul(
                psd[:, f * 512:(f + 1) * 512], onesr[0:1, :], drow_s[0:1, f * 512:(f + 1) * 512],
                start=True, stop=True), wait=[("MST", 1), ("LDWS", 32)], inc=("MMD", 1))
        p.on("vector", lambda e: e.tensor_scalar(d1b[:, :], psd[:, :], 1.0, None, ALU.add),
             wait=[("MMD", 2)], inc=("PREP", 1))

        def emit_bu_mms(chunk, src_ut_ap, bufsel, sem_ld, ld_target, extra_wait):
            for nh in range(2):
                for c in range(8):
                    last = (c == 7)
                    w = [(sem_ld, ld_target), ("LDWG", 32)] if c == 0 else None
                    if c == 0 and extra_wait is not None:
                        w = [(sem_ld, ld_target), ("LDWG", 32), extra_wait]
                    p.on("tensor", lambda e, nh=nh, c=c, bufsel=bufsel, last=last: e.matmul(
                        psb[nh][:, :],
                        bwt[:, c * NSTATE + nh * 128: c * NSTATE + (nh + 1) * 128],
                        utb[:, bufsel * 4096 + c * 512:(bufsel) * 4096 + (c + 1) * 512],
                        start=(c == 0), stop=last),
                        wait=w, inc=(("MMB", 1) if last else None))

        for ch in range(8):
            bufsel = ch % 2
            wld = None
            if ch >= 2:
                wld = [("MMB", 2 * (ch - 1))]
            ldt_tgt = p.on("gpsimd", lambda e, ch=ch, bufsel=bufsel: e.dma_start(
                out=utb[:, bufsel * 4096:(bufsel + 1) * 4096].rearrange("p (c l) -> p c l", c=8),
                in_=uTp3[:, :, ch * 512:(ch + 1) * 512]),
                wait=wld, inc=("LDT%d" % bufsel, 16))
            extra = ("SCN", 2 * ch) if ch >= 1 else None
            emit_bu_mms(ch, uTp3, bufsel, "LDT%d" % bufsel, ldt_tgt, extra)
            for nh in range(2):
                init_ap = zcol[:, 0:1] if ch == 0 else hf[:, ((ch - 1) % 2) * 1024 + nh * 512 + 511: ((ch - 1) % 2) * 1024 + nh * 512 + 512]
                p.on("vector", lambda e, ch=ch, nh=nh, init_ap=init_ap: e.tensor_tensor_scan(
                    hf[:, (ch % 2) * 1024 + nh * 512:(ch % 2) * 1024 + nh * 512 + 512],
                    lamb[:, nh * 512:(nh + 1) * 512],
                    psb[nh][:, :],
                    init_ap, ALU.mult, ALU.add),
                    wait=[("MMB", 2 * ch + nh + 1), ("PREP", 2)],
                    inc=("SCN", 1))
        h0_ap = [hf[:, 1024 + nh * 512 + 511:1024 + nh * 512 + 512] for nh in range(2)]

        MMB0 = p.cnt("MMB")
        SCN0 = p.cnt("SCN")
        for ch in range(8):
            bufsel = ch % 2
            wld = [("MMB", MMB0 + 2 * (ch - 1))] if ch >= 2 else [("MMB", MMB0)]
            if ch == 0:
                wld = [("SCN", 16)]
            ldt_tgt = p.on("gpsimd", lambda e, ch=ch, bufsel=bufsel: e.dma_start(
                out=utb[:, bufsel * 4096:(bufsel + 1) * 4096].rearrange("p (c l) -> p c l", c=8),
                in_=uT3[:, :, ch * 512:(ch + 1) * 512]),
                wait=wld, inc=("LDT%d" % bufsel, 16))
            wlu = [("EPI", 4 * (ch - 1))] if ch >= 2 else None
            ldu_tgt = p.on("sync", lambda e, ch=ch, bufsel=bufsel: e.dma_start(
                out=uf[:, bufsel * 4096:(bufsel + 1) * 4096].rearrange("p (g m) -> p g m", g=4),
                in_=u4[ch]),
                wait=wlu, inc=("LDU%d" % bufsel, 16))
            extra = ("SCN", SCN0 + 2 * ch) if ch >= 1 else None
            emit_bu_mms(ch, uT3, bufsel, "LDT%d" % bufsel, ldt_tgt, extra)
            for nh in range(2):
                init_ap = h0_ap[nh] if ch == 0 else hf[:, ((ch - 1) % 2) * 1024 + nh * 512 + 511: ((ch - 1) % 2) * 1024 + nh * 512 + 512]
                p.on("vector", lambda e, ch=ch, nh=nh, init_ap=init_ap: e.tensor_tensor_scan(
                    hf[:, (ch % 2) * 1024 + nh * 512:(ch % 2) * 1024 + nh * 512 + 512],
                    lamb[:, nh * 512:(nh + 1) * 512],
                    psb[nh][:, :],
                    init_ap, ALU.mult, ALU.add),
                    wait=[("MMB", MMB0 + 2 * ch + nh + 1)],
                    inc=("SCN", 1))
                wcast = [("SCN", SCN0 + 2 * ch + nh + 1)]
                if ch >= 2:
                    wcast.append(("MMY", 4 * (ch - 1)))
                p.on("vector", lambda e, ch=ch, nh=nh: e.tensor_copy(
                    hbf[:, (ch % 2) * 1024 + nh * 512:(ch % 2) * 1024 + nh * 512 + 512],
                    hf[:, (ch % 2) * 1024 + nh * 512:(ch % 2) * 1024 + nh * 512 + 512]),
                    wait=wcast, inc=("HBF", 1))
            for g in range(4):
                gi = 4 * ch + g
                ps = psy[g % 2]
                k = 0
                for nh in range(2):
                    for f in range(2):
                        k += 1
                        w = []
                        if k == 1:
                            w = [("HBF", 2 * (ch + 1)), ("LDWG", 32)]
                            if gi >= 2:
                                w.append(("PSR", gi - 1))
                        p.on("tensor", lambda e, ch=ch, g=g, nh=nh, f=f, ps=ps: e.matmul(
                            ps[:, f * 512:(f + 1) * 512],
                            hbf[:, (ch % 2) * 1024 + nh * 512 + g * 128:(ch % 2) * 1024 + nh * 512 + (g + 1) * 128],
                            cwt[:, nh * DM + f * 512:nh * DM + (f + 1) * 512],
                            start=(nh == 0), stop=(nh == 1)),
                            wait=(w if w else None), inc=(("MMY", 1) if k == 4 else None))
                b2 = gi % 2
                uap = uf[:, (ch % 2) * 4096 + g * DM:(ch % 2) * 4096 + (g + 1) * DM]
                p.on("vector", lambda e, uap=uap, b2=b2: e.tensor_mul(
                    ep0[:, b2 * DM:(b2 + 1) * DM], uap, d1b[:, :]),
                    wait=[("LDU%d" % (ch % 2), ldu_tgt)], inc=None)
                p.on("vector", lambda e, b2=b2, ps=ps: e.tensor_add(
                    ep0[:, b2 * DM:(b2 + 1) * DM], ep0[:, b2 * DM:(b2 + 1) * DM], ps[:, :]),
                    wait=[("MMY", gi + 1)], inc=("PSR", 1))
                p.on("vector", lambda e, b2=b2: e.reduce_sum(
                    mcol[:, b2 * 4:b2 * 4 + 1], ep0[:, b2 * DM:(b2 + 1) * DM], AX.X), inc=None)
                p.on("vector", lambda e, b2=b2: e.tensor_scalar(
                    mcol[:, b2 * 4 + 1:b2 * 4 + 2], mcol[:, b2 * 4:b2 * 4 + 1], 1.0 / DM, None, ALU.mult), inc=None)
                p.on("vector", lambda e, b2=b2: e.tensor_scalar(
                    ep1[:, b2 * DM:(b2 + 1) * DM], ep0[:, b2 * DM:(b2 + 1) * DM],
                    mcol[:, b2 * 4 + 1:b2 * 4 + 2], None, ALU.subtract),
                    inc=("EZC", 1))
                p.on("scalar", lambda e, b2=b2: e.activation(
                    sqb[:, b2 * DM:(b2 + 1) * DM], ep1[:, b2 * DM:(b2 + 1) * DM], AF.Square),
                    wait=[("EZC", gi + 1)], inc=("SQD", 1))
                p.on("vector", lambda e, b2=b2: e.reduce_sum(
                    mcol[:, b2 * 4 + 2:b2 * 4 + 3], sqb[:, b2 * DM:(b2 + 1) * DM], AX.X),
                    wait=[("SQD", gi + 1)], inc=None)
                p.on("vector", lambda e, b2=b2: e.tensor_scalar(
                    mcol[:, b2 * 4 + 2:b2 * 4 + 3], mcol[:, b2 * 4 + 2:b2 * 4 + 3], 1.0 / DM, None, ALU.mult),
                    inc=("EVV", 1))
                p.on("scalar", lambda e, b2=b2: e.activation(
                    mcol[:, b2 * 4 + 3:b2 * 4 + 4], mcol[:, b2 * 4 + 2:b2 * 4 + 3], AF.Ln, bias=konst[:, 0:1]),
                    wait=[("EVV", gi + 1)], inc=None)
                p.on("scalar", lambda e, b2=b2: e.activation(
                    mcol[:, b2 * 4 + 3:b2 * 4 + 4], mcol[:, b2 * 4 + 3:b2 * 4 + 4], AF.Exp, scale=konst[:, 1:2]),
                    inc=("RSD", 1))
                wz = [("RSD", gi + 1)]
                if gi >= 2:
                    wz.append(("ZS%d" % b2, p.cnt("ZS%d" % b2)))
                p.on("vector", lambda e, b2=b2: e.tensor_scalar(
                    z1t[:, b2 * DM:(b2 + 1) * DM], ep1[:, b2 * DM:(b2 + 1) * DM],
                    mcol[:, b2 * 4 + 3:b2 * 4 + 4], None, ALU.mult),
                    wait=wz, inc=("EPI", 1))
                p.on("sync", lambda e, ch=ch, g=g, b2=b2: e.dma_start(
                    out=z14[ch, g], in_=z1t[:, b2 * DM:(b2 + 1) * DM]),
                    wait=[("EPI", gi + 1)], inc=("ZS%d" % b2, 16))
        p.wait("sync", [("ZS0", p.cnt("ZS0")), ("ZS1", p.cnt("ZS1"))])
        p.emit(block)
    return nc


NE = 8
DE = 2816
NJ = DE // 128


def build_phase_b():
    nc = bass.Bass("TRN2", target_bir_lowering=False)
    z1 = nc.dram_tensor("z1", [T_CORE, DM], F32, kind="ExternalInput").ap()
    gwT = nc.dram_tensor("gwT", [DM, NE], F32, kind="ExternalInput").ap()
    rmsw = nc.dram_tensor("rmsw", [1, DM], F32, kind="ExternalInput").ap()
    W1T = nc.dram_tensor("W1T", [NE, DM, DE], F32, kind="ExternalInput").ap()
    W2T = nc.dram_tensor("W2T", [NE, DM, DE], F32, kind="ExternalInput").ap()
    W3T = nc.dram_tensor("W3T", [NE, DE, DM], F32, kind="ExternalInput").ap()
    identd = nc.dram_tensor("identd", [128, 128], F32, kind="ExternalInput")
    ident_d = identd
    z2 = nc.dram_tensor("z2", [T_CORE, DM], F32, kind="ExternalOutput").ap()
    comb = nc.dram_tensor("comb", [T_CORE, DM], F32).ap()

    st = _STACK
    al = lambda *a: st.enter_context(nc.sbuf_tensor(*a))
    alp = lambda *a: st.enter_context(nc.psum_tensor(*a))
    rmsb = al("rmsb", [128, DM], F32)
    rmsr = al("rmsr", [1, DM], F32)
    onesr = al("onesr", [1, 128], F32)
    konst = al("konst", [128, 4], F32)
    identf = al("identf_s", [128, 128], F32)
    gw = al("gw", [128, 8 * NE], F32)
    z1b = al("z1b", [128, 2 * DM], F32)
    sqb = al("sqb", [128, DM], F32)
    rcol = al("rcol", [128, 2 * 4], F32)
    yf = al("yf", [128, DM], F32)
    yT32 = al("yT32", [128, DM], F32)
    ytb = al("ytb", [128, 8 * 1024], BF16)
    lgb = al("lgb", [128, 8 * 8], F32)
    gcol = al("gcol", [128, 8], F32)
    wf = al("wf", [128, 8 * NE], F32)
    w1j = al("w1j", [128, 2 * 8 * 128], BF16)
    w2j = al("w2j", [128, 2 * 8 * 128], BF16)
    w3b = al("w3b", [128, NJ * DM], BF16)
    hb = al("hb", [128, NJ * 1024], BF16)
    silb = al("silb", [128, 2 * 512], F32)
    outs = al("outs", [128, 4 * DM], F32)
    combb = al("combb", [128, 2 * DM], F32)
    z2t = al("z2t", [128, 2 * DM], F32)
    ep0 = al("ep0b", [128, 2 * DM], F32)
    ep1 = al("ep1b", [128, 2 * DM], F32)
    mcol = al("mcolb", [128, 2 * 8], F32)
    pu0 = alp("pu0", [128, 512], F32)
    pu1 = alp("pu1", [128, 512], F32)
    pg0 = alp("pg0", [128, 512], F32)
    pg1 = alp("pg1", [128, 512], F32)
    po0 = alp("po0", [128, DM], F32)
    po1 = alp("po1", [128, DM], F32)
    pu = [pu0, pu1]
    pg = [pg0, pg1]
    po = [po0, po1]
    ps4 = [pu0, pu1, pg0, pg1]

    with nc.Block() as block:
        p = Prog(nc, serial=SERIAL)
        p._stack = _STACK

        z1t4 = z1.rearrange("(tq p) m -> tq p m", p=128)
        z2t4 = z2.rearrange("(tq p) m -> tq p m", p=128)
        comb4 = comb.rearrange("(tq p) m -> tq p m", p=128)
        W1T4 = W1T.rearrange("e (k p) n -> e p k n", p=128)
        W2T4 = W2T.rearrange("e (k p) n -> e p k n", p=128)
        W3T4 = W3T.rearrange("e (j p) m -> e p j m", p=128)

        p.on("sync", lambda e: e.dma_start(out=gw[:, :].rearrange("p (k e) -> p k e", k=8),
                                           in_=gwT.rearrange("(k p) e -> p k e", p=128)), inc=("LG", 16))
        p.on("sync", lambda e: e.dma_start(out=rmsr[:, :], in_=rmsw), inc=("LG", 16))
        p.on("sync", lambda e: e.dma_start(out=identf[:, :], in_=ident_d.ap()), inc=("LG", 16))
        p.on("vector", lambda e: e.memset(konst[:, 0:1], RMS_EPS))
        p.on("vector", lambda e: e.memset(konst[:, 1:2], -0.5))
        p.on("vector", lambda e: e.memset(konst[:, 2:3], LN_EPS))
        p.on("vector", lambda e: e.memset(onesr[:, :], 1.0), inc=("MST", 1))
        for f in range(2):
            p.on("tensor", lambda e, f=f: e.matmul(
                po0[:, f * 512:(f + 1) * 512], onesr[0:1, :], rmsr[0:1, f * 512:(f + 1) * 512],
                start=True, stop=True), wait=[("MST", 1), ("LG", 48)], inc=("MMD", 1))
        p.on("vector", lambda e: e.tensor_copy(rmsb[:, :], po0[:, :]), wait=[("MMD", 2)], inc=("RMB", 1))

        NQ = 4
        TQ = 8
        vst_slot = [0, 0]
        sil_slot = [0, 0]
        outs_slot = [0, 0, 0, 0]

        def evac_src(k):
            return ps4[k // 2][:, (k % 2) * 128:(k % 2) * 128 + 128]

        def emit_b1(q):
            for T in range(TQ):
                tq = q * TQ + T
                b2 = tq % 2
                wld = [("YR", p.cnt("YR") - 1)] if tq >= 2 else None
                lz_tgt = p.on("sync", lambda e, tq=tq, b2=b2: e.dma_start(
                    out=z1b[:, b2 * DM:(b2 + 1) * DM], in_=z1t4[tq]),
                    wait=wld, inc=("LZ%d" % b2, 16))
                wsq = [("LZ%d" % b2, lz_tgt), ("EVV", p.cnt("EVV"))]
                p.on("scalar", lambda e, b2=b2: e.activation(
                    sqb[:, :], z1b[:, b2 * DM:(b2 + 1) * DM], AF.Square),
                    wait=wsq, inc=("SQD", 1))
                p.on("vector", lambda e, b2=b2: e.reduce_sum(
                    rcol[:, b2 * 4:b2 * 4 + 1], sqb[:, :], AX.X),
                    wait=[("SQD", p.cnt("SQD"))])
                p.on("vector", lambda e, b2=b2: e.tensor_scalar(
                    rcol[:, b2 * 4:b2 * 4 + 1], rcol[:, b2 * 4:b2 * 4 + 1], 1.0 / DM, None, ALU.mult),
                    inc=("EVV", 1))
                p.on("scalar", lambda e, b2=b2: e.activation(
                    rcol[:, b2 * 4 + 1:b2 * 4 + 2], rcol[:, b2 * 4:b2 * 4 + 1], AF.Ln, bias=konst[:, 0:1]),
                    wait=[("EVV", p.cnt("EVV"))])
                p.on("scalar", lambda e, b2=b2: e.activation(
                    rcol[:, b2 * 4 + 1:b2 * 4 + 2], rcol[:, b2 * 4 + 1:b2 * 4 + 2], AF.Exp, scale=konst[:, 1:2]),
                    inc=("RSD", 1))
                wy = [("RSD", p.cnt("RSD")), ("RMB", 1)]
                if tq >= 1:
                    wy.append(("MTT", p.cnt("MTT")))
                p.on("vector", lambda e, b2=b2: e.tensor_scalar(
                    yf[:, :], z1b[:, b2 * DM:(b2 + 1) * DM], rcol[:, b2 * 4 + 1:b2 * 4 + 2], None, ALU.mult),
                    wait=wy)
                yr_tgt = p.on("vector", lambda e: e.tensor_mul(yf[:, :], yf[:, :], rmsb[:, :]),
                              inc=("YR", 1))
                wtr = [("YR", yr_tgt)]
                if tq >= 1:
                    wtr.append(("EVF", p.cnt("EVF")))
                    wtr.append(("YQB", p.cnt("YQB")))
                if q >= 1 and T == 0:
                    wtr.append(("VST", p.cnt("VST")))
                for k in range(8):
                    p.on("tensor", lambda e, k=k: e.transpose(
                        evac_src(k), yf[:, k * 128:(k + 1) * 128], identf[:, :]),
                        wait=(wtr if k == 0 else None),
                        inc=(("MTT", 1) if k == 7 else None))
                mtt_tgt = p.cnt("MTT")
                for k in range(8):
                    p.on("vector", lambda e, k=k: e.tensor_copy(
                        yT32[:, k * 128:(k + 1) * 128], evac_src(k)),
                        wait=([("MTT", mtt_tgt)] if k == 0 else None),
                        inc=(("EVF", 1) if k == 7 else None))
                wyq = [("MTT", mtt_tgt), ("EVF", p.cnt("EVF"))]
                if q >= 1 and T == 0:
                    wyq.append(("MMUG", p.cnt("MMUG")))
                for k in range(8):
                    p.on("scalar", lambda e, k=k, T=T: e.activation(
                        ytb[:, k * 1024 + T * 128:k * 1024 + (T + 1) * 128], evac_src(k), AF.Copy),
                        wait=(wyq if k == 0 else None),
                        inc=(("YQB", 1) if k == 7 else None))
                evf_tgt = p.cnt("EVF")
                wlg = [("EVF", evf_tgt), ("LGC", p.cnt("LGC"))]
                if q >= 1 and T == 0:
                    wlg.append(("VEV", p.cnt("VEV")))
                for k in range(8):
                    p.on("tensor", lambda e, k=k: e.matmul(
                        po1[:, 0:8], yT32[:, k * 128:(k + 1) * 128], gw[:, k * NE:(k + 1) * NE],
                        start=(k == 0), stop=(k == 7)),
                        wait=(wlg if k == 0 else None),
                        inc=(("MML", 1) if k == 7 else None))
                p.on("vector", lambda e: e.tensor_copy(lgb[:, 0:8], po1[:, 0:8]),
                     wait=[("MML", p.cnt("MML"))], inc=("LGC", 1))
                p.on("vector", lambda e: e.reduce_max(gcol[:, 0:1], lgb[:, 0:8], AX.X))
                p.on("vector", lambda e: e.tensor_scalar(
                    lgb[:, 8:16], lgb[:, 0:8], gcol[:, 0:1], None, ALU.is_ge))
                p.on("vector", lambda e: e.scalar_tensor_tensor(
                    lgb[:, 16:24], lgb[:, 8:16], -1e30, lgb[:, 0:8], ALU.mult, ALU.add))
                p.on("vector", lambda e: e.reduce_max(gcol[:, 1:2], lgb[:, 16:24], AX.X))
                p.on("vector", lambda e: e.tensor_scalar(
                    lgb[:, 24:32], lgb[:, 0:8], gcol[:, 1:2], None, ALU.is_ge))
                p.on("vector", lambda e: e.tensor_scalar(
                    gcol[:, 2:3], gcol[:, 0:1], -1.0, None, ALU.mult), inc=("GX1", 1))
                p.on("scalar", lambda e: e.activation(
                    lgb[:, 32:40], lgb[:, 0:8], AF.Exp, bias=gcol[:, 2:3]),
                    wait=[("GX1", p.cnt("GX1"))], inc=("GX2", 1))
                p.on("vector", lambda e: e.tensor_mul(
                    lgb[:, 40:48], lgb[:, 32:40], lgb[:, 24:32]),
                    wait=[("GX2", p.cnt("GX2"))])
                p.on("vector", lambda e: e.reduce_sum(gcol[:, 3:4], lgb[:, 40:48], AX.X))
                p.on("vector", lambda e: e.reciprocal(gcol[:, 4:5], gcol[:, 3:4]))
                p.on("vector", lambda e, T=T: e.tensor_scalar(
                    wf[:, T * 8:(T + 1) * 8], lgb[:, 40:48], gcol[:, 4:5], None, ALU.mult),
                    inc=("WFD", 1))

        def emit_experts(q, yqb_q, evf_q):
            for ex in range(NE):
                for j in range(NJ):
                    sl = j % 2
                    wlw = [("MMUG", mmug_slot[sl])] if mmug_slot[sl] > 0 else None
                    lw1_tgt = p.on("gpsimd", lambda e, ex=ex, j=j, sl=sl: e.dma_start(
                        out=w1j[:, sl * 1024:(sl + 1) * 1024].rearrange("p (k n) -> p k n", k=8),
                        in_=W1T4[ex, :, :, j * 128:(j + 1) * 128]),
                        wait=wlw, inc=("LW1%d" % sl, 16))
                    lw2_tgt = p.on("gpsimd", lambda e, ex=ex, j=j, sl=sl: e.dma_start(
                        out=w2j[:, sl * 1024:(sl + 1) * 1024].rearrange("p (k n) -> p k n", k=8),
                        in_=W2T4[ex, :, :, j * 128:(j + 1) * 128]),
                        inc=("LW2%d" % sl, 16))
                    wmm = [("LW1%d" % sl, lw1_tgt), ("LW2%d" % sl, lw2_tgt),
                           ("YQB", yqb_q), ("EVF", evf_q)]
                    if vst_slot[j % 2] > 0:
                        wmm.append(("VST", vst_slot[j % 2]))
                    for f in range(2):
                        for k in range(8):
                            p.on("tensor", lambda e, j=j, sl=sl, f=f, k=k: e.matmul(
                                pu[j % 2][:, :], w1j[:, sl * 1024 + k * 128:sl * 1024 + (k + 1) * 128],
                                ytb[:, k * 1024 + f * 512:k * 1024 + (f + 1) * 512],
                                start=(k == 0), stop=(k == 7)),
                                wait=(wmm if (f == 0 and k == 0) else None))
                        for k in range(8):
                            p.on("tensor", lambda e, j=j, sl=sl, f=f, k=k: e.matmul(
                                pg[j % 2][:, :], w2j[:, sl * 1024 + k * 128:sl * 1024 + (k + 1) * 128],
                                ytb[:, k * 1024 + f * 512:k * 1024 + (f + 1) * 512],
                                start=(k == 0), stop=(k == 7)),
                                inc=(("MMUG", 1) if (f == 1 and k == 7) else None))
                    mmug_slot[sl] = p.cnt("MMUG")
                    mmug_j = p.cnt("MMUG")
                    for f in range(2):
                        wsil = [("MMUG", mmug_j)]
                        if sil_slot[f] > 0:
                            wsil.append(("VST", sil_slot[f]))
                        p.on("scalar", lambda e, j=j, f=f: e.activation(
                            silb[:, f * 512:(f + 1) * 512], pu[j % 2][:, :], AF.Silu),
                            wait=wsil, inc=("SIL", 1))
                        vst_t = p.on("vector", lambda e, j=j, f=f: e.tensor_mul(
                            hb[:, j * 1024 + f * 512:j * 1024 + (f + 1) * 512],
                            silb[:, f * 512:(f + 1) * 512], pg[j % 2][:, :]),
                            wait=[("SIL", p.cnt("SIL"))], inc=("VST", 1))
                        sil_slot[f] = vst_t
                    vst_slot[j % 2] = p.cnt("VST")
                vst_eq = p.cnt("VST")
                wl3 = [("MMO", p.cnt("MMO"))] if p.cnt("MMO") > 0 else None
                lw3_tgt = p.on("gpsimd", lambda e, ex=ex: e.dma_start(
                    out=w3b[:, :].rearrange("p (j m) -> p j m", j=NJ),
                    in_=W3T4[ex]),
                    wait=wl3, inc=("LW3", 16))
                for ob in range(4):
                    wob = [("VST", vst_eq), ("LW3", lw3_tgt)]
                    if p.cnt("VEV") > 0:
                        wob.append(("VEV", p.cnt("VEV")))
                    first = True
                    for j in range(NJ):
                        for ts in range(2):
                            for f in range(2):
                                p.on("tensor", lambda e, j=j, ob=ob, ts=ts, f=f: e.matmul(
                                    po[ts][:, f * 512:(f + 1) * 512],
                                    hb[:, j * 1024 + ob * 256 + ts * 128:j * 1024 + ob * 256 + (ts + 1) * 128],
                                    w3b[:, j * 1024 + f * 512:j * 1024 + (f + 1) * 512],
                                    start=(j == 0), stop=(j == NJ - 1)),
                                    wait=(wob if first else None),
                                    inc=(("MMO", 1) if (j == NJ - 1 and ts == 1 and f == 1) else None))
                                first = False
                    mmo_ob = p.cnt("MMO")
                    for ts in range(2):
                        osl = (ob * 2 + ts) % 4
                        T = ob * 2 + ts
                        wev = [("MMO", mmo_ob), ("WFD", wfd_q[q])]
                        if outs_slot[osl] > 0:
                            wev.append(("CMB%d" % osl, outs_slot[osl]))
                        p.on("vector", lambda e, ts=ts, osl=osl, T=T, ex=ex: e.tensor_scalar(
                            outs[:, osl * DM:(osl + 1) * DM], po[ts][:, :],
                            wf[:, T * 8 + ex:T * 8 + ex + 1], None, ALU.mult),
                            wait=wev, inc=("VEV", 1))
                        vev_now = p.cnt("VEV")
                        cmb_t = p.on("gpsimd", lambda e, q=q, ob=ob, ts=ts, osl=osl, ex=ex: e.dma_start(
                            out=comb4[q * 8 + ob * 2 + ts],
                            in_=outs[:, osl * DM:(osl + 1) * DM],
                            accum_op=(ALU.bypass if ex == 0 else ALU.add)),
                            wait=[("VEV", vev_now)], inc=("CMB%d" % osl, 16))
                        outs_slot[osl] = cmb_t

        def emit_b3(q, cmb_q):
            for T in range(TQ):
                tq = q * TQ + T
                b2 = tq % 2
                i3 = p.cnt("EP3")
                war3 = [("EP3", i3 - 1)] if i3 >= 2 else None
                wlc = [("CMB%d" % s, cmb_q[s]) for s in range(4)] + (war3 or [])
                lc_tgt = p.on("sync", lambda e, tq=tq, b2=b2: e.dma_start(
                    out=combb[:, b2 * DM:(b2 + 1) * DM], in_=comb4[tq]),
                    wait=wlc, inc=("LC%d" % b2, 16))
                lzr_tgt = p.on("sync", lambda e, tq=tq, b2=b2: e.dma_start(
                    out=z2t[:, b2 * DM:(b2 + 1) * DM], in_=z1t4[tq]),
                    wait=war3, inc=("LZR%d" % b2, 16))
                wpre = [("LC%d" % b2, lc_tgt), ("LZR%d" % b2, lzr_tgt)]
                if zs2_slot[b2] > 0:
                    wpre.append(("ZS2%d" % b2, zs2_slot[b2]))
                p.on("vector", lambda e, b2=b2: e.tensor_add(
                    ep0[:, b2 * DM:(b2 + 1) * DM], z2t[:, b2 * DM:(b2 + 1) * DM],
                    combb[:, b2 * DM:(b2 + 1) * DM]),
                    wait=wpre)
                p.on("vector", lambda e, b2=b2: e.reduce_sum(
                    mcol[:, b2 * 8:b2 * 8 + 1], ep0[:, b2 * DM:(b2 + 1) * DM], AX.X))
                p.on("vector", lambda e, b2=b2: e.tensor_scalar(
                    mcol[:, b2 * 8 + 1:b2 * 8 + 2], mcol[:, b2 * 8:b2 * 8 + 1], 1.0 / DM, None, ALU.mult))
                p.on("vector", lambda e, b2=b2: e.tensor_scalar(
                    ep1[:, b2 * DM:(b2 + 1) * DM], ep0[:, b2 * DM:(b2 + 1) * DM],
                    mcol[:, b2 * 8 + 1:b2 * 8 + 2], None, ALU.subtract),
                    inc=("EZC", 1))
                p.on("scalar", lambda e, b2=b2: e.activation(
                    ep0[:, b2 * DM:(b2 + 1) * DM], ep1[:, b2 * DM:(b2 + 1) * DM], AF.Square),
                    wait=[("EZC", p.cnt("EZC"))], inc=("SQ3", 1))
                p.on("vector", lambda e, b2=b2: e.reduce_sum(
                    mcol[:, b2 * 8 + 2:b2 * 8 + 3], ep0[:, b2 * DM:(b2 + 1) * DM], AX.X),
                    wait=[("SQ3", p.cnt("SQ3"))])
                p.on("vector", lambda e, b2=b2: e.tensor_scalar(
                    mcol[:, b2 * 8 + 2:b2 * 8 + 3], mcol[:, b2 * 8 + 2:b2 * 8 + 3], 1.0 / DM, None, ALU.mult),
                    inc=("EV3", 1))
                p.on("scalar", lambda e, b2=b2: e.activation(
                    mcol[:, b2 * 8 + 3:b2 * 8 + 4], mcol[:, b2 * 8 + 2:b2 * 8 + 3], AF.Ln, bias=konst[:, 2:3]),
                    wait=[("EV3", p.cnt("EV3"))])
                p.on("scalar", lambda e, b2=b2: e.activation(
                    mcol[:, b2 * 8 + 3:b2 * 8 + 4], mcol[:, b2 * 8 + 3:b2 * 8 + 4], AF.Exp, scale=konst[:, 1:2]),
                    inc=("RS3", 1))
                p.on("vector", lambda e, b2=b2: e.tensor_scalar(
                    ep1[:, b2 * DM:(b2 + 1) * DM], ep1[:, b2 * DM:(b2 + 1) * DM],
                    mcol[:, b2 * 8 + 3:b2 * 8 + 4], None, ALU.mult),
                    wait=[("RS3", p.cnt("RS3"))], inc=("EP3", 1))
                zs2_t = p.on("sync", lambda e, tq=tq, b2=b2: e.dma_start(
                    out=z2t4[tq], in_=ep1[:, b2 * DM:(b2 + 1) * DM]),
                    wait=[("EP3", p.cnt("EP3"))], inc=("ZS2%d" % b2, 16))
                zs2_slot[b2] = zs2_t

        mmug_slot = [0, 0]
        zs2_slot = [0, 0]
        wfd_q = {}
        yqb_all = {}
        evf_all = {}
        emit_b1(0)
        wfd_q[0] = p.cnt("WFD"); yqb_all[0] = p.cnt("YQB"); evf_all[0] = p.cnt("EVF")
        for q in range(NQ):
            emit_experts(q, yqb_all[q], evf_all[q])
            cmb_q = [p.cnt("CMB%d" % s) for s in range(4)]
            if q + 1 < NQ:
                emit_b1(q + 1)
                wfd_q[q + 1] = p.cnt("WFD"); yqb_all[q + 1] = p.cnt("YQB"); evf_all[q + 1] = p.cnt("EVF")
            emit_b3(q, cmb_q)
        p.wait("sync", [("ZS20", p.cnt("ZS20")), ("ZS21", p.cnt("ZS21"))])
        p.emit(block)
    return nc


def host_inputs_phase_b(inputs, z1_full):
    gwT = np.ascontiguousarray(np.asarray(inputs["L0_ffn_gate_w"], np.float32).T)
    rmsw = np.ascontiguousarray(np.asarray(inputs["L0_ffn_rms_w"], np.float32)[None, :])
    W1 = np.asarray(inputs["L0_ffn_W1"], np.float32)
    W2 = np.asarray(inputs["L0_ffn_W2"], np.float32)
    W3 = np.asarray(inputs["L0_ffn_W3"], np.float32)
    W1T = np.ascontiguousarray(W1.transpose(0, 2, 1))
    W2T = np.ascontiguousarray(W2.transpose(0, 2, 1))
    W3T = np.ascontiguousarray(W3.transpose(0, 2, 1))
    maps = []
    for c in range(NCORES):
        b, h = c // 2, c % 2
        maps.append({"z1": np.ascontiguousarray(z1_full[b, h * T_CORE:(h + 1) * T_CORE]),
                     "gwT": gwT, "rmsw": rmsw, "W1T": W1T, "W2T": W2T, "W3T": W3T,
                     "identd": np.eye(128, dtype=np.float32)})
    return maps


def run_phase_b(inputs, z1_full, trace=False):
    nc = _run_with_stack(build_phase_b)
    maps = host_inputs_phase_b(inputs, z1_full)
    res = run_bass_kernel_spmd(nc, maps, core_ids=list(range(NCORES)), trace=trace)
    z2 = np.empty((4, 8192, DM), np.float32)
    for c in range(NCORES):
        b, h = c // 2, c % 2
        z2[b, h * T_CORE:(h + 1) * T_CORE] = res.results[c]["z2"]
    return z2, res


DH = 4096
NJD = DH // 128


def build_phase_d():
    nc = bass.Bass("TRN2", target_bir_lowering=False)
    z3 = nc.dram_tensor("z3", [T_CORE, DM], F32, kind="ExternalInput").ap()
    w1T = nc.dram_tensor("w1T", [DM, DH], F32, kind="ExternalInput").ap()
    w2T = nc.dram_tensor("w2T", [DM, DH], F32, kind="ExternalInput").ap()
    h1T = nc.dram_tensor("h1T", [DH, DH], F32, kind="ExternalInput").ap()
    h2T = nc.dram_tensor("h2T", [DH, DH], F32, kind="ExternalInput").ap()
    w3T = nc.dram_tensor("w3T", [DH, DM], F32, kind="ExternalInput").ap()
    s1c = nc.dram_tensor("s1c", [128, NJD], F32, kind="ExternalInput").ap()
    s2c = nc.dram_tensor("s2c", [128, NJD], F32, kind="ExternalInput").ap()
    s1hc = nc.dram_tensor("s1hc", [128, NJD], F32, kind="ExternalInput").ap()
    s2hc = nc.dram_tensor("s2hc", [128, NJD], F32, kind="ExternalInput").ap()
    s3r = nc.dram_tensor("s3r", [1, DM], F32, kind="ExternalInput").ap()
    identd = nc.dram_tensor("identd", [128, 128], F32, kind="ExternalInput")
    z4 = nc.dram_tensor("z4", [T_CORE, DM], F32, kind="ExternalOutput").ap()

    st = _STACK
    al = lambda *a: st.enter_context(nc.sbuf_tensor(*a))
    alp = lambda *a: st.enter_context(nc.psum_tensor(*a))
    identf = al("identfd", [128, 128], F32)
    onesr = al("onesrd", [1, 128], F32)
    konst = al("konstd", [128, 2], F32)
    s3b = al("s3b", [128, DM], F32)
    s3s = al("s3s", [1, DM], F32)
    s1s = al("s1s", [128, NJD], F32)
    s2s = al("s2s", [128, NJD], F32)
    s1hs = al("s1hs", [128, NJD], F32)
    s2hs = al("s2hs", [128, NJD], F32)
    z3b = al("z3b", [128, 2 * DM], F32)
    z3T = al("z3T", [128, 8 * 1024], BF16)
    w12j = al("w12j", [128, 2 * 2 * 8 * 128], BF16)
    hjb = al("hjb", [128, 2 * 32 * 128], BF16)
    h1b = al("h1b", [128, NJD * 1024], BF16)
    h2b = al("h2b", [128, NJD * 1024], BF16)
    silb = al("silbd", [128, 1024], BF16)
    z4b = z3b
    ep0 = al("ep0d", [128, 2 * DM], F32)
    ep1 = al("ep1d", [128, 2 * DM], F32)
    mcol = al("mcold", [128, 2 * 8], F32)
    p1 = alp("pd1", [128, DM], F32)
    p2 = alp("pd2", [128, DM], F32)
    p3 = alp("pd3", [128, DM], F32)
    p4 = alp("pd4", [128, DM], F32)
    pab = [(p1, p2), (p3, p4)]
    pc = [p1, p2]
    pd_ = [p1, p2]

    with nc.Block() as block:
        p = Prog(nc, serial=SERIAL)
        p._stack = _STACK

        z3t4 = z3.rearrange("(tq p) m -> tq p m", p=128)
        z4t4 = z4.rearrange("(tq p) m -> tq p m", p=128)
        w1T4 = w1T.rearrange("(k p) n -> p k n", p=128)
        w2T4 = w2T.rearrange("(k p) n -> p k n", p=128)
        h1T4 = h1T.rearrange("(k p) n -> p k n", p=128)
        h2T4 = h2T.rearrange("(k p) n -> p k n", p=128)
        w3T4 = w3T.rearrange("(k p) m -> p k m", p=128)

        p.on("sync", lambda e: e.dma_start(out=identf[:, :], in_=identd.ap()), inc=("LG", 16))
        p.on("sync", lambda e: e.dma_start(out=s3s[:, :], in_=s3r), inc=("LG", 16))
        p.on("sync", lambda e: e.dma_start(out=s1s[:, :], in_=s1c), inc=("LG", 16))
        p.on("sync", lambda e: e.dma_start(out=s2s[:, :], in_=s2c), inc=("LG", 16))
        p.on("sync", lambda e: e.dma_start(out=s1hs[:, :], in_=s1hc), inc=("LG", 16))
        p.on("sync", lambda e: e.dma_start(out=s2hs[:, :], in_=s2hc), inc=("LG", 16))
        p.on("vector", lambda e: e.memset(konst[:, 0:1], LN_EPS))
        p.on("vector", lambda e: e.memset(konst[:, 1:2], -0.5))
        p.on("vector", lambda e: e.memset(onesr[:, :], 1.0), inc=("MST", 1))
        for f in range(2):
            p.on("tensor", lambda e, f=f: e.matmul(
                p2[:, f * 512:(f + 1) * 512], onesr[0:1, :], s3s[0:1, f * 512:(f + 1) * 512],
                start=True, stop=True), wait=[("MST", 1), ("LG", 96)], inc=("MMD", 1))
        p.on("vector", lambda e: e.tensor_copy(s3b[:, :], p2[:, :]), wait=[("MMD", 2)], inc=("S3B", 1))

        NQ = 4
        TQ = 8
        w12_slot = [0, 0]
        hj_slot = [None, None]
        z4s_slot = [0, 0]
        vstd_slot = [None, None]
        ge2_slot = [0, 0]
        ge3_slot = [0, 0]
        mm4_slot = [0, 0]
        prevq = {"MM1": 0, "MM2": 0, "MM3": 0, "MM4": 0, "GE2": 0, "EP4": 0,
                 "SIL1": 0, "VST1": 0}

        for q in range(NQ):
            for T in range(TQ):
                tq = q * TQ + T
                b2 = tq % 2
                wld = [("YQ0", p.cnt("YQ0") - 1)] if p.cnt("YQ0") >= 2 else []
                if q >= 1 and T <= 1:
                    wld = (wld or []) + [("EP4", prevq["EP4"])]
                lz_tgt = p.on("sync", lambda e, tq=tq, b2=b2: e.dma_start(
                    out=z3b[:, b2 * DM:(b2 + 1) * DM], in_=z3t4[tq]),
                    wait=(wld or None), inc=("LZ3%d" % b2, 16))
                wtr = [("LZ3%d" % b2, lz_tgt)]
                if T == 0:
                    wtr.append(("MM4", prevq["MM4"]))
                    wtr.append(("EP4", prevq["EP4"]))
                if p.cnt("YQ0") > 0:
                    wtr.append(("YQ0", p.cnt("YQ0")))
                for k in range(8):
                    p.on("tensor", lambda e, k=k, b2=b2: e.transpose(
                        p1[:, k * 128:(k + 1) * 128], z3b[:, b2 * DM + k * 128:b2 * DM + (k + 1) * 128],
                        identf[:, :]),
                        wait=(wtr if k == 0 else None),
                        inc=(("MT0", 1) if k == 7 else None))
                wev = [("MT0", p.cnt("MT0"))]
                if q >= 1 and T == 0:
                    wev.append(("MM1", prevq["MM1"]))
                for k in range(8):
                    p.on("scalar", lambda e, k=k, T=T: e.activation(
                        z3T[:, k * 1024 + T * 128:k * 1024 + (T + 1) * 128],
                        p1[:, k * 128:(k + 1) * 128], AF.Copy),
                        wait=(wev if k == 0 else None),
                        inc=(("YQ0", 1) if k == 7 else None))
            yq0_q = p.cnt("YQ0")
            mt0_q = p.cnt("MT0")

            for j in range(NJD):
                sl = j % 2
                wlw = [("MM1", w12_slot[sl])] if w12_slot[sl] > 0 else None
                lw_tgt = p.on("gpsimd", lambda e, j=j, sl=sl: e.dma_start(
                    out=w12j[:, sl * 2048:sl * 2048 + 1024].rearrange("p (k n) -> p k n", k=8),
                    in_=w1T4[:, :, j * 128:(j + 1) * 128]),
                    wait=wlw, inc=("LW%d" % sl, 16))
                lw_tgt = p.on("gpsimd", lambda e, j=j, sl=sl: e.dma_start(
                    out=w12j[:, sl * 2048 + 1024:sl * 2048 + 2048].rearrange("p (k n) -> p k n", k=8),
                    in_=w2T4[:, :, j * 128:(j + 1) * 128]),
                    inc=("LW%d" % sl, 16))
                pa, pb = pab[j % 2]
                wmm = [("LW%d" % sl, lw_tgt), ("YQ0", yq0_q)]
                if vstd_slot[j % 2] is not None:
                    wmm.append(("VST1", vstd_slot[j % 2][0]))
                    wmm.append(("SIL1", vstd_slot[j % 2][1]))
                if q == 0 and j <= 1:
                    wmm.append(("S3B", 1))
                first = True
                for f in range(2):
                    for k in range(8):
                        p.on("tensor", lambda e, j=j, sl=sl, f=f, k=k, pa=pa: e.matmul(
                            pa[:, f * 512:(f + 1) * 512],
                            w12j[:, sl * 2048 + k * 128:sl * 2048 + (k + 1) * 128],
                            z3T[:, k * 1024 + f * 512:k * 1024 + (f + 1) * 512],
                            start=(k == 0), stop=(k == 7)),
                            wait=(wmm if first else None))
                        first = False
                    for k in range(8):
                        p.on("tensor", lambda e, j=j, sl=sl, f=f, k=k, pb=pb: e.matmul(
                            pb[:, f * 512:(f + 1) * 512],
                            w12j[:, sl * 2048 + 1024 + k * 128:sl * 2048 + 1024 + (k + 1) * 128],
                            z3T[:, k * 1024 + f * 512:k * 1024 + (f + 1) * 512],
                            start=(k == 0), stop=(k == 7)),
                            inc=(("MM1", 1) if (f == 1 and k == 7) else None))
                mm1_j = p.cnt("MM1")
                p.on("scalar", lambda e, j=j, pa=pa: e.activation(
                    silb[:, :], pa[:, :], AF.Silu, scale=s1s[:, j:j + 1]),
                    wait=[("MM1", mm1_j), ("VST1", p.cnt("VST1"))], inc=("SIL1", 1))
                vst_t = p.on("vector", lambda e, j=j, pb=pb: e.scalar_tensor_tensor(
                    h1b[:, j * 1024:(j + 1) * 1024], pb[:, :], s2s[:, j:j + 1], silb[:, :],
                    ALU.mult, ALU.mult),
                    wait=[("SIL1", p.cnt("SIL1")), ("MM4", prevq["MM4"])], inc=("VST1", 1))
                vstd_slot[j % 2] = (vst_t, p.cnt("SIL1"))
                w12_slot[sl] = p.cnt("MM1")
            vst1_q = p.cnt("VST1")
            sil1_q = p.cnt("SIL1")

            for j2 in range(NJD):
                sl = j2 % 2
                wlh = [hj_slot[sl]] if hj_slot[sl] is not None else None
                lh_tgt = p.on("gpsimd", lambda e, j2=j2, sl=sl: e.dma_start(
                    out=hjb[:, sl * 4096:(sl + 1) * 4096].rearrange("p (k n) -> p k n", k=32),
                    in_=h1T4[:, :, j2 * 128:(j2 + 1) * 128]),
                    wait=wlh, inc=("LH%d" % sl, 16))
                pcx = pc[j2 % 2]
                wmm = [("LH%d" % sl, lh_tgt), ("VST1", vst1_q), ("SIL1", sil1_q)]
                if ge2_slot[j2 % 2] > 0:
                    wmm.append(("GE2", ge2_slot[j2 % 2]))
                first = True
                for f in range(2):
                    for k in range(32):
                        p.on("tensor", lambda e, j2=j2, sl=sl, f=f, k=k, pcx=pcx: e.matmul(
                            pcx[:, f * 512:(f + 1) * 512],
                            hjb[:, sl * 4096 + k * 128:sl * 4096 + (k + 1) * 128],
                            h1b[:, k * 1024 + f * 512:k * 1024 + (f + 1) * 512],
                            start=(k == 0), stop=(k == 31)),
                            wait=(wmm if first else None),
                            inc=(("MM2", 1) if (f == 1 and k == 31) else None))
                        first = False
                wge = [("MM2", p.cnt("MM2"))]
                if q >= 1 and j2 == 0:
                    wge.append(("MM3", prevq["MM3"]))
                    wge.append(("MM4", prevq["MM4"]))
                ge_t = p.on("scalar", lambda e, j2=j2, pcx=pcx: e.activation(
                    h2b[:, j2 * 1024:(j2 + 1) * 1024], pcx[:, :], AF.Gelu, scale=s1hs[:, j2:j2 + 1]),
                    wait=wge, inc=("GE2", 1))
                ge2_slot[j2 % 2] = ge_t
                hj_slot[sl] = ("MM2", p.cnt("MM2"))
            ge2_q = p.cnt("GE2")
            mm2_q = p.cnt("MM2")

            for j3 in range(NJD):
                sl = j3 % 2
                wlh = [hj_slot[sl]] if hj_slot[sl] is not None else None
                lh_tgt = p.on("gpsimd", lambda e, j3=j3, sl=sl: e.dma_start(
                    out=hjb[:, sl * 4096:(sl + 1) * 4096].rearrange("p (k n) -> p k n", k=32),
                    in_=h2T4[:, :, j3 * 128:(j3 + 1) * 128]),
                    wait=wlh, inc=("LH%d" % sl, 16))
                pcx = [p3, p4][j3 % 2]
                wmm = [("LH%d" % sl, lh_tgt), ("GE2", ge2_q)]
                if ge3_slot[j3 % 2] > 0:
                    wmm.append(("GE3", ge3_slot[j3 % 2]))
                first = True
                for f in range(2):
                    for k in range(32):
                        p.on("tensor", lambda e, j3=j3, sl=sl, f=f, k=k, pcx=pcx: e.matmul(
                            pcx[:, f * 512:(f + 1) * 512],
                            hjb[:, sl * 4096 + k * 128:sl * 4096 + (k + 1) * 128],
                            h2b[:, k * 1024 + f * 512:k * 1024 + (f + 1) * 512],
                            start=(k == 0), stop=(k == 31)),
                            wait=(wmm if first else None),
                            inc=(("MM3", 1) if (f == 1 and k == 31) else None))
                        first = False
                wge = [("MM3", p.cnt("MM3")), ("MM2", mm2_q)]
                ge_t = p.on("scalar", lambda e, j3=j3, pcx=pcx: e.activation(
                    h1b[:, j3 * 1024:(j3 + 1) * 1024], pcx[:, :], AF.Gelu, scale=s2hs[:, j3:j3 + 1]),
                    wait=wge, inc=("GE3", 1))
                ge3_slot[j3 % 2] = ge_t
                hj_slot[sl] = ("MM3", p.cnt("MM3"))
            ge3_q = p.cnt("GE3")

            wl3 = [("MM3", p.cnt("MM3"))]
            lw3_tgt = p.on("gpsimd", lambda e: e.dma_start(
                out=h2b[:, :].rearrange("p (k m) -> p k m", k=NJD),
                in_=w3T4),
                wait=wl3, inc=("LW3D", 16))
            for T in range(TQ):
                tq = q * TQ + T
                b2 = tq % 2
                wzr = [("MT0", mt0_q)]
                if p.cnt("EP4") >= 2:
                    wzr.append(("EP4", p.cnt("EP4") - 1))
                lz4_tgt = p.on("sync", lambda e, tq=tq, b2=b2: e.dma_start(
                    out=z4b[:, b2 * DM:(b2 + 1) * DM], in_=z3t4[tq]),
                    wait=wzr, inc=("LZ4%d" % b2, 16))
                pdx = pd_[T % 2]
                wmm = [("GE3", ge3_q), ("LW3D", lw3_tgt)]
                if mm4_slot[T % 2] > 0:
                    wmm.append(("EV4", mm4_slot[T % 2]))
                first = True
                for f in range(2):
                    for k in range(32):
                        p.on("tensor", lambda e, T=T, f=f, k=k, pdx=pdx: e.matmul(
                            pdx[:, f * 512:(f + 1) * 512],
                            h1b[:, k * 1024 + T * 128:k * 1024 + (T + 1) * 128],
                            h2b[:, k * 1024 + f * 512:k * 1024 + (f + 1) * 512],
                            start=(k == 0), stop=(k == 31)),
                            wait=(wmm if first else None),
                            inc=(("MM4", 1) if (f == 1 and k == 31) else None))
                        first = False
                mm4_now = p.cnt("MM4")
                ev_t = p.on("vector", lambda e, b2=b2, pdx=pdx: e.tensor_mul(
                    ep0[:, b2 * DM:(b2 + 1) * DM], pdx[:, :], s3b[:, :]),
                    wait=[("MM4", mm4_now), ("LZ4%d" % b2, lz4_tgt)], inc=("EV4", 1))
                mm4_slot[T % 2] = ev_t
                p.on("vector", lambda e, b2=b2: e.tensor_add(
                    ep0[:, b2 * DM:(b2 + 1) * DM], ep0[:, b2 * DM:(b2 + 1) * DM],
                    z4b[:, b2 * DM:(b2 + 1) * DM]))
                p.on("vector", lambda e, b2=b2: e.reduce_sum(
                    mcol[:, b2 * 8:b2 * 8 + 1], ep0[:, b2 * DM:(b2 + 1) * DM], AX.X))
                p.on("vector", lambda e, b2=b2: e.tensor_scalar(
                    mcol[:, b2 * 8 + 1:b2 * 8 + 2], mcol[:, b2 * 8:b2 * 8 + 1], 1.0 / DM, None, ALU.mult))
                wsub = [("ZS4%d" % b2, z4s_slot[b2])] if z4s_slot[b2] > 0 else None
                p.on("vector", lambda e, b2=b2: e.tensor_scalar(
                    ep1[:, b2 * DM:(b2 + 1) * DM], ep0[:, b2 * DM:(b2 + 1) * DM],
                    mcol[:, b2 * 8 + 1:b2 * 8 + 2], None, ALU.subtract),
                    wait=wsub, inc=("EZ4", 1))
                p.on("scalar", lambda e, b2=b2: e.activation(
                    ep0[:, b2 * DM:(b2 + 1) * DM], ep1[:, b2 * DM:(b2 + 1) * DM], AF.Square),
                    wait=[("EZ4", p.cnt("EZ4"))], inc=("SQ4", 1))
                p.on("vector", lambda e, b2=b2: e.reduce_sum(
                    mcol[:, b2 * 8 + 2:b2 * 8 + 3], ep0[:, b2 * DM:(b2 + 1) * DM], AX.X),
                    wait=[("SQ4", p.cnt("SQ4"))])
                p.on("vector", lambda e, b2=b2: e.tensor_scalar(
                    mcol[:, b2 * 8 + 2:b2 * 8 + 3], mcol[:, b2 * 8 + 2:b2 * 8 + 3], 1.0 / DM, None, ALU.mult),
                    inc=("EVV4", 1))
                p.on("scalar", lambda e, b2=b2: e.activation(
                    mcol[:, b2 * 8 + 3:b2 * 8 + 4], mcol[:, b2 * 8 + 2:b2 * 8 + 3], AF.Ln, bias=konst[:, 0:1]),
                    wait=[("EVV4", p.cnt("EVV4"))])
                p.on("scalar", lambda e, b2=b2: e.activation(
                    mcol[:, b2 * 8 + 3:b2 * 8 + 4], mcol[:, b2 * 8 + 3:b2 * 8 + 4], AF.Exp, scale=konst[:, 1:2]),
                    inc=("RS4", 1))
                p.on("vector", lambda e, b2=b2: e.tensor_scalar(
                    ep1[:, b2 * DM:(b2 + 1) * DM], ep1[:, b2 * DM:(b2 + 1) * DM],
                    mcol[:, b2 * 8 + 3:b2 * 8 + 4], None, ALU.mult),
                    wait=[("RS4", p.cnt("RS4"))], inc=("EP4", 1))
                zs4_t = p.on("sync", lambda e, tq=tq, b2=b2: e.dma_start(
                    out=z4t4[tq], in_=ep1[:, b2 * DM:(b2 + 1) * DM]),
                    wait=[("EP4", p.cnt("EP4"))], inc=("ZS4%d" % b2, 16))
                z4s_slot[b2] = zs4_t
            prevq = {"MM1": p.cnt("MM1"), "MM2": p.cnt("MM2"), "MM3": p.cnt("MM3"),
                     "MM4": p.cnt("MM4"), "GE2": p.cnt("GE2"), "EP4": p.cnt("EP4"),
                     "SIL1": p.cnt("SIL1"), "VST1": p.cnt("VST1")}
        p.wait("sync", [("ZS40", p.cnt("ZS40")), ("ZS41", p.cnt("ZS41"))])
        p.emit(block)
    return nc


def host_inputs_phase_d(inputs, z3_full):
    w1 = np.asarray(inputs["L1_ffn_w1_w"], np.float32)
    w2 = np.asarray(inputs["L1_ffn_w2_w"], np.float32)
    h1 = np.asarray(inputs["L1_ffn_h1_w"], np.float32)
    h2 = np.asarray(inputs["L1_ffn_h2_w"], np.float32)
    w3 = np.asarray(inputs["L1_ffn_w3_w"], np.float32)
    s1 = np.asarray(inputs["L1_ffn_w1_s"], np.float32)
    s2 = np.asarray(inputs["L1_ffn_w2_s"], np.float32)
    s1h = np.asarray(inputs["L1_ffn_h1_s"], np.float32)
    s2h = np.asarray(inputs["L1_ffn_h2_s"], np.float32)
    s3 = np.asarray(inputs["L1_ffn_w3_s"], np.float32)
    common = {
        "w1T": np.ascontiguousarray(w1.T), "w2T": np.ascontiguousarray(w2.T),
        "h1T": np.ascontiguousarray(h1.T), "h2T": np.ascontiguousarray(h2.T),
        "w3T": np.ascontiguousarray(w3.T),
        "s1c": np.ascontiguousarray(s1.reshape(NJD, 128).T),
        "s2c": np.ascontiguousarray(s2.reshape(NJD, 128).T),
        "s1hc": np.ascontiguousarray(s1h.reshape(NJD, 128).T),
        "s2hc": np.ascontiguousarray(s2h.reshape(NJD, 128).T),
        "s3r": np.ascontiguousarray(s3[None, :]),
        "identd": np.eye(128, dtype=np.float32),
    }
    maps = []
    for c in range(NCORES):
        b, h = c // 2, c % 2
        m = dict(common)
        m["z3"] = np.ascontiguousarray(z3_full[b, h * T_CORE:(h + 1) * T_CORE])
        maps.append(m)
    return maps


def run_phase_d(inputs, z3_full, trace=False):
    nc = _run_with_stack(build_phase_d)
    maps = host_inputs_phase_d(inputs, z3_full)
    res = run_bass_kernel_spmd(nc, maps, core_ids=list(range(NCORES)), trace=trace)
    z4 = np.empty((4, 8192, DM), np.float32)
    for c in range(NCORES):
        b, h = c // 2, c % 2
        z4[b, h * T_CORE:(h + 1) * T_CORE] = res.results[c]["z4"]
    return z4, res


DI = 2048
NJC = DI // 128
DST = 16
NCH = 32
NGR = 8


def build_phase_c0():
    nc = bass.Bass("TRN2", target_bir_lowering=False)
    ztt = nc.dram_tensor("ztt", [DM, 4], F32, kind="ExternalInput").ap()
    W_in = nc.dram_tensor("W_in", [DM, 2 * DI], F32, kind="ExternalInput").ap()
    tail = nc.dram_tensor("tail", [128, NJC * 4], F32, kind="ExternalOutput").ap()
    st = _STACK
    al = lambda *a: st.enter_context(nc.sbuf_tensor(*a))
    alp = lambda *a: st.enter_context(nc.psum_tensor(*a))
    wb = al("wbc0", [128, 8 * DI], BF16)
    zb = al("zbc0", [128, 8 * 4], BF16)
    ob = al("obc0", [128, NJC * 4], F32)
    ps = alp("psc0", [128, 1024], F32)
    with nc.Block() as block:
        p = Prog(nc, serial=SERIAL)
        p._stack = _STACK
        p.on("gpsimd", lambda e: e.dma_start(
            out=wb[:, :].rearrange("p (k c) -> p k c", k=8),
            in_=W_in.rearrange("(k p) c -> p k c", p=128)[:, :, 0:DI]), inc=("LW", 16))
        p.on("gpsimd", lambda e: e.dma_start(
            out=zb[:, :].rearrange("p (k c) -> p k c", k=8),
            in_=ztt.rearrange("(k p) c -> p k c", p=128)), inc=("LW", 16))
        for j in range(NJC):
            sl = (j % 2) * 512
            for k in range(8):
                p.on("tensor", lambda e, j=j, k=k, sl=sl: e.matmul(
                    ps[:, sl:sl + 4],
                    wb[:, k * DI + j * 128:k * DI + (j + 1) * 128],
                    zb[:, k * 4:(k + 1) * 4],
                    start=(k == 0), stop=(k == 7)),
                    wait=([("LW", 32)] if (j == 0 and k == 0) else
                          ([("EVC", p.cnt("EVC") - 1)] if (k == 0 and j >= 2) else None)),
                    inc=(("MMC", 1) if k == 7 else None))
            p.on("vector", lambda e, j=j, sl=sl: e.tensor_copy(
                ob[:, j * 4:(j + 1) * 4], ps[:, sl:sl + 4]),
                wait=[("MMC", p.cnt("MMC"))], inc=("EVC", 1))
        p.on("sync", lambda e: e.dma_start(out=tail, in_=ob[:, :]),
             wait=[("EVC", p.cnt("EVC"))], inc=("TS", 16))
        p.wait("sync", [("TS", 16)])
        p.emit(block)
    return nc


def build_phase_c1():
    nc = bass.Bass("TRN2", target_bir_lowering=False)
    z2 = nc.dram_tensor("z2", [T_CORE, DM], F32, kind="ExternalInput").ap()
    W_in = nc.dram_tensor("W_in", [DM, 2 * DI], F32, kind="ExternalInput").ap()
    W_x = nc.dram_tensor("W_x", [DI, 48], F32, kind="ExternalInput").ap()
    cwc_d = nc.dram_tensor("cwc", [128, NJC * 4], F32, kind="ExternalInput").ap()
    cbc_d = nc.dram_tensor("cbc", [128, NJC], F32, kind="ExternalInput").ap()
    dpc_d = nc.dram_tensor("dpc", [128, NJC], F32, kind="ExternalInput").ap()
    alr_d = nc.dram_tensor("alr", [1, DST], F32, kind="ExternalInput").ap()
    alc_d = nc.dram_tensor("alc", [DST, 1], F32, kind="ExternalInput").ap()
    Wout_d = nc.dram_tensor("Wout", [DI, DM], F32, kind="ExternalInput").ap()
    tail_d = nc.dram_tensor("tail_in", [128, NJC * 4], F32, kind="ExternalInput").ap()
    cumin_d = nc.dram_tensor("cumin", [128, 128], F32, kind="ExternalInput").ap()
    sufst_d = nc.dram_tensor("sufst", [128, 128], F32, kind="ExternalInput").ap()
    masktl_d = nc.dram_tensor("masktl", [128, 128], F32, kind="ExternalInput").ap()
    ones128_d = nc.dram_tensor("ones128", [128, 128], F32, kind="ExternalInput").ap()
    identd = nc.dram_tensor("identd", [128, 128], F32, kind="ExternalInput")

    preln = nc.dram_tensor("preln", [T_CORE, DM], F32, kind="ExternalOutput").ap()
    szsp = nc.dram_tensor("szsp", [DI, T_CORE], BF16, kind="ExternalOutput").ap()
    hend = nc.dram_tensor("hend", [DST, DI], F32, kind="ExternalOutput").ap()
    cst_o = nc.dram_tensor("cst", [128, NCH * DST], F32, kind="ExternalOutput").ap()
    sgst_o = nc.dram_tensor("sgst", [128, NCH], F32, kind="ExternalOutput").ap()

    st = _STACK
    al = lambda *a: st.enter_context(nc.sbuf_tensor(*a))
    alp = lambda *a: st.enter_context(nc.psum_tensor(*a))
    wis = al("wis", [128, 2 * 8 * 128], BF16)
    wob = al("wobc", [128, NJC * DM], BF16)
    wxb = al("wxb", [128, NJC * 48], BF16)
    cwc = al("cwcs", [128, NJC * 4], F32)
    cbc = al("cbcs", [128, NJC], F32)
    dpb = al("dpbs", [128, NJC * 128], BF16)
    dpc_s = al("dpcs", [128, NJC], F32)
    onesb = al("onesb", [1, 128], BF16)
    rab = al("rabc", [128, DST], F32)
    cumin = al("cumins", [128, 128], F32)
    sufst = al("sufsts", [128, 128], F32)
    masktl = al("masktls", [128, 128], F32)
    ones128 = al("ones128s", [128, 128], F32)
    identf = al("identfc", [128, 128], F32)
    arow = al("arow", [1, DST], F32)
    acol = al("acolc", [16, 1], F32)
    abc = al("abc", [128, DST], F32)
    onesr = al("onesrc", [1, 128], F32)
    negs = al("negs", [1, 128], F32)
    konst = al("konstc", [128, 4], F32)
    z2g = al("z2g", [128, 4 * DM], F32)
    z2tb = al("z2tb", [128, 8 * 512], BF16)
    xg = al("xg", [128, NJC * 520], BF16)
    zTg = al("zTg", [128, NJC * 512], BF16)
    xcT = al("xcT", [128, NJC * 128], F32)
    xcTb = al("xcTb", [128, NJC * 128], BF16)
    xcb = al("xcb", [128, NJC * 128], BF16)
    dcol = al("dcol", [128, 8], F32)
    bbm = al("bbm", [128, DST], F32)
    bbm2 = al("bbm2", [128, DST], BF16)
    srow = al("srowc", [1, 128], F32)
    srow_loc = al("srowlc", [1, 128], F32)
    slc = al("slcc", [128, 1], F32)
    sgc = al("sgc", [128, 1], F32)
    spv = al("spv", [128, 1], F32)
    dmt = al("dmtc", [128, 128], F32)
    escr = al("escr", [128, 128], BF16)
    g1s = al("g1s", [128, 128], BF16)
    grb = al("grbc", [16, 4], F32)
    accT = al("accTc", [128, 128], F32)
    acbf = al("acbf", [128, 128], BF16)
    ctsb = al("ctsb", [16, 128], F32)
    ctbb = al("ctbb", [16, 128], BF16)
    ctrow = al("ctrowc", [1, DST * 128], BF16)
    cgb = al("cgbc", [16, 128], BF16)
    ebuf = al("ebufc", [16, 128], F32)
    hst = al("hstc", [16, DI], F32)
    hbf = al("hbfc", [16, DI], BF16)
    ysc = al("yscc", [128, NJC * 128], BF16)
    preb = al("prebc", [128, DM], F32)
    cstb = al("cstbc", [128, NCH * DST], F32)
    sgst = al("sgstc", [128, NCH], F32)
    pbig = alp("pbigc", [128, 2048], F32)
    pmid = alp("pmidc", [128, 1024], F32)
    psml0 = alp("psml0c", [128, 512], F32)
    psml1 = alp("psml1c", [128, 512], F32)

    with nc.Block() as block:
        p = Prog(nc, serial=SERIAL)
        p._stack = _STACK

        z2t4 = z2.rearrange("(g f p) m -> g p f m", f=4, p=128)
        pre4 = preln.rearrange("(ch p) m -> ch p m", p=128)
        szsp4 = szsp.rearrange("(j p) t -> p j t", p=128)

        W_in4 = W_in.rearrange("(k p) c -> p k c", p=128)
        p.on("gpsimd", lambda e: e.dma_start(
            out=wob[:, :].rearrange("p (j m) -> p j m", j=NJC),
            in_=Wout_d.rearrange("(j p) m -> p j m", p=128)), inc=("LWC", 16))
        p.on("gpsimd", lambda e: e.dma_start(
            out=wxb[:, :].rearrange("p (j c) -> p j c", j=NJC),
            in_=W_x.rearrange("(j p) c -> p j c", p=128)), inc=("LWC", 16))
        for dst, srcd in ((cwc, cwc_d), (cbc, cbc_d), (cumin, cumin_d),
                          (sufst, sufst_d), (masktl, masktl_d), (ones128, ones128_d),
                          (identf, identd.ap()), (arow, alr_d), (acol, alc_d),
                          (dpc_s, dpc_d)):
            p.on("sync", lambda e, dst=dst, srcd=srcd: e.dma_start(out=dst[:, :], in_=srcd),
                 inc=("LDC", 16))
        p.on("gpsimd", lambda e: e.dma_start(
            out=xg[:, :].rearrange("p (j c) -> p j c", j=NJC)[:, :, 0:4],
            in_=tail_d.rearrange("p (j c) -> p j c", j=NJC)), inc=("LWC", 16))
        ldc_tot = p.cnt("LDC")
        p.on("vector", lambda e: e.memset(onesr[:, :], 1.0))
        p.on("vector", lambda e: e.memset(konst[:, 0:1], -1.0))
        p.on("vector", lambda e: e.memset(spv[:, :], 0.0))
        p.on("vector", lambda e: e.memset(hst[:, :], 0.0), inc=("MST", 1))
        p.on("vector", lambda e: e.tensor_copy(hbf[:, :], hst[:, :]), inc=("HBF", 1))
        p.on("scalar", lambda e: e.activation(arow[:, :], arow[:, :], AF.Exp),
             wait=[("LDC", ldc_tot)], inc=("ARP", 1))
        p.on("vector", lambda e: e.tensor_scalar(arow[:, :], arow[:, :], -1.0, None, ALU.mult),
             wait=[("ARP", 1), ("MST", 1)], inc=("ARN", 1))
        p.on("tensor", lambda e: e.matmul(
            psml0[:, 0:DST], onesr[0:1, :], arow[0:1, :], start=True, stop=True),
            wait=[("ARN", 1)], inc=("MMA", 1))
        p.on("vector", lambda e: e.tensor_copy(abc[:, :], psml0[:, 0:DST]),
             wait=[("MMA", 1)], inc=("ABC", 1))
        p.on("scalar", lambda e: e.activation(acol[:, :], acol[:, :], AF.Exp),
             wait=[("LDC", ldc_tot)], inc=("ACX", 1))
        p.on("vector", lambda e: e.tensor_scalar(acol[:, :], acol[:, :], -1.0, None, ALU.mult),
             wait=[("ACX", 1)], inc=("ACL", 1))
        for j in range(NJC):
            p.on("vector", lambda e, j=j: e.tensor_scalar(
                dpb[:, j * 128:(j + 1) * 128], ones128[:, :], dpc_s[:, j:j + 1], None, ALU.mult),
                wait=([("LDC", ldc_tot)] if j == 0 else None),
                inc=(("DPB", 1) if j == NJC - 1 else None))
        p.on("vector", lambda e: e.tensor_copy(onesb[:, :], onesr[:, :]), inc=("ONB", 1))

        lwc_tot = 48
        spl_slot = [0, 0]
        wis_slot = [0, 0]

        for g in range(NGR):
            lz_w = []
            if g >= 1:
                lz_w = [("TRP", p.cnt("TRP")), ("PRE", p.cnt("PRE"))]
            lz_tgt = p.on("sync", lambda e, g=g: e.dma_start(
                out=z2g[:, :].rearrange("p (f m) -> p f m", f=4), in_=z2t4[g]),
                wait=(lz_w or None), inc=("LZ2", 16))
            for f4 in range(4):
                for k in range(8):
                    i8 = f4 * 8 + k
                    sl = i8 % 2
                    ps = [psml0, psml1][sl]
                    wtr = None
                    if i8 == 0:
                        wtr = [("LZ2", lz_tgt)]
                        if g == 0:
                            wtr.append(("ABC", 1))
                        if g >= 1:
                            wtr += [("EVZ", p.cnt("EVZ")), ("CTS", p.cnt("CTS")),
                                    ("DMB", p.cnt("DMB")), ("GS2", p.cnt("GS2")),
                                    ("EBF", p.cnt("EBF")), ("SRW", p.cnt("SRW"))]
                    elif i8 >= 2:
                        wtr = [("EVZ", p.cnt("EVZ") - 1)]
                    p.on("tensor", lambda e, f4=f4, k=k, ps=ps: e.transpose(
                        ps[:, 128:256],
                        z2g[:, f4 * DM + k * 128:f4 * DM + (k + 1) * 128], identf[:, :]),
                        wait=wtr, inc=("TRP", 1))
                    p.on("scalar", lambda e, f4=f4, k=k, ps=ps: e.activation(
                        z2tb[:, k * 512 + f4 * 128:k * 512 + (f4 + 1) * 128],
                        ps[:, 128:256], AF.Copy),
                        wait=[("TRP", p.cnt("TRP"))], inc=("EVZ", 1))
            evz_g = p.cnt("EVZ")
            if g >= 1:
                p.on("vector", lambda e: e.tensor_copy(
                    xg[:, :].rearrange("p (j c) -> p j c", j=NJC)[:, :, 1:4],
                    xg[:, :].rearrange("p (j c) -> p j c", j=NJC)[:, :, 513:516]),
                    wait=[("CNV", p.cnt("CNV"))], inc=("XGC", 1))
            for o in range(32):
                sl = o % 2
                wlw = [("MMX", wis_slot[sl])] if wis_slot[sl] > 0 else None
                lwi_tgt = p.on("gpsimd", lambda e, o=o, sl=sl: e.dma_start(
                    out=wis[:, sl * 1024:(sl + 1) * 1024].rearrange("p (k c) -> p k c", k=8),
                    in_=W_in4[:, :, o * 128:(o + 1) * 128]),
                    wait=wlw, inc=("LWI%d" % sl, 16))
                wmm = [("LWI%d" % sl, lwi_tgt)]
                if o == 0:
                    wmm += [("EVZ", evz_g), ("LWC", lwc_tot)]
                elif o >= 2:
                    wmm += [("EVX", p.cnt("EVX") - 1)]
                for k in range(8):
                    p.on("tensor", lambda e, o=o, k=k, sl=sl: e.matmul(
                        pmid[:, sl * 512:(sl + 1) * 512],
                        wis[:, sl * 1024 + k * 128:sl * 1024 + (k + 1) * 128],
                        z2tb[:, k * 512:(k + 1) * 512],
                        start=(k == 0), stop=(k == 7)),
                        wait=(wmm if k == 0 else None),
                        inc=(("MMX", 1) if k == 7 else None))
                wis_slot[sl] = p.cnt("MMX")
                wev = [("MMX", p.cnt("MMX"))]
                if o < 16:
                    if g >= 1 and o == 0:
                        wev.append(("XGC", p.cnt("XGC")))
                        wev.append(("CNV", p.cnt("CNV")))
                    p.on("scalar", lambda e, o=o, sl=sl: e.activation(
                        xg[:, o * 520 + 4:o * 520 + 4 + 512], pmid[:, sl * 512:(sl + 1) * 512], AF.Copy),
                        wait=wev, inc=("EVX", 1))
                else:
                    if o == 16:
                        wev.append(("YP3", p.cnt("YP3")))
                        wev.append(("SPL", p.cnt("SPL")))
                    p.on("scalar", lambda e, o=o, sl=sl: e.activation(
                        zTg[:, (o - 16) * 512:(o - 15) * 512], pmid[:, sl * 512:(sl + 1) * 512], AF.Copy),
                        wait=wev, inc=("EVX", 1))
            evx_g = p.cnt("EVX")
            p.on("scalar", lambda e: e.activation(zTg[:, :], zTg[:, :], AF.Silu),
                 wait=[("EVX", evx_g)], inc=("SZD", 1))
            szd_g = p.cnt("SZD")
            p.on("sync", lambda e, g=g: e.dma_start(
                out=szsp4[:, :, g * 512:(g + 1) * 512],
                in_=zTg[:, :].rearrange("p (j t) -> p j t", j=NJC)),
                wait=[("SZD", szd_g)], inc=("SPL", 16))

            for cc in range(4):
                ch = g * 4 + cc
                wcv = [("EVX", evx_g), ("LDC", ldc_tot)]
                if p.cnt("XDB") > 0:
                    wcv.append(("XDB", p.cnt("XDB")))
                    wcv.append(("TRX", p.cnt("TRX")))
                    wcv.append(("YP1", p.cnt("YP1")))
                for j in range(NJC):
                    base = j * 520 + cc * 128
                    p.on("scalar", lambda e, j=j, base=base: e.activation(
                        xcT[:, j * 128:(j + 1) * 128], xg[:, base + 1:base + 129],
                        AF.Identity, bias=cbc[:, j:j + 1], scale=cwc[:, j * 4 + 0:j * 4 + 1]),
                        wait=(wcv if j == 0 else None), inc=("CV0", 1))
                    for k in range(1, 4):
                        p.on("vector", lambda e, j=j, k=k, base=base: e.scalar_tensor_tensor(
                            xcT[:, j * 128:(j + 1) * 128], xg[:, base + 1 + k:base + 129 + k],
                            cwc[:, j * 4 + k:j * 4 + k + 1], xcT[:, j * 128:(j + 1) * 128],
                            ALU.mult, ALU.add),
                            wait=[("CV0", p.cnt("CV0"))] if k == 1 else [("CNV", p.cnt("CNV"))],
                            inc=("CNV", 1))
                p.on("scalar", lambda e: e.activation(xcT[:, :], xcT[:, :], AF.Silu),
                     wait=[("CNV", p.cnt("CNV"))], inc=("SCV", 1))
                p.on("vector", lambda e: e.tensor_copy(xcTb[:, :], xcT[:, :]),
                     wait=[("SCV", p.cnt("SCV")), ("DPB", 1)], inc=("XCB", 1))
                wxd = [("XCB", p.cnt("XCB"))]
                if p.cnt("CSD") > 0:
                    wxd += [("DLT", p.cnt("DLT")), ("BBD", p.cnt("BBD")), ("CSD", p.cnt("CSD"))]
                for j in range(NJC):
                    p.on("tensor", lambda e, j=j: e.matmul(
                        psml0[:, 0:48], xcTb[:, j * 128:(j + 1) * 128],
                        wxb[:, j * 48:(j + 1) * 48],
                        start=(j == 0), stop=(j == NJC - 1)),
                        wait=(wxd if j == 0 else None),
                        inc=(("XDB", 1) if j == NJC - 1 else None))
                xdb_c = p.cnt("XDB")
                p.on("scalar", lambda e: e.activation(
                    dcol[:, 0:1], psml0[:, 0:1], AF.Abs),
                    wait=[("XDB", xdb_c)], inc=("DAB", 1))
                p.on("scalar", lambda e: e.activation(
                    dcol[:, 1:2], dcol[:, 0:1], AF.Exp, scale=konst[:, 0:1]),
                    wait=[("DAB", p.cnt("DAB"))], inc=("DXP", 1))
                p.on("scalar", lambda e: e.activation(
                    dcol[:, 2:3], dcol[:, 1:2], AF.Ln, bias=1.0), inc=("DLN", 1))
                p.on("scalar", lambda e: e.activation(
                    dcol[:, 3:4], psml0[:, 0:1], AF.Relu), inc=("DRL", 1))
                p.on("vector", lambda e: e.tensor_add(
                    dcol[:, 4:5], dcol[:, 3:4], dcol[:, 2:3]),
                    wait=[("DRL", p.cnt("DRL"))], inc=("DLT", 1))
                dlt_c = p.cnt("DLT")
                p.on("vector", lambda e: e.tensor_scalar(
                    bbm[:, :], psml0[:, 1:17], dcol[:, 4:5], None, ALU.mult), inc=("BBD", 1))
                p.on("vector", lambda e, ch=ch: e.tensor_copy(
                    cstb[:, ch * DST:(ch + 1) * DST], psml0[:, 17:33]),
                    inc=("CSD", 1))
                wcs = [("DLT", dlt_c), ("BBD", p.cnt("BBD")), ("CSD", p.cnt("CSD"))]
                if p.cnt("SGD") > 0:
                    wcs += [("SGD", p.cnt("SGD")), ("RRD", p.cnt("RRD")), ("GDC", p.cnt("GDC"))]
                p.on("tensor", lambda e: e.matmul(
                    psml0[:, 64:65], cumin[:, :], dcol[:, 4:5], start=True, stop=True),
                    wait=wcs, inc=("MMS", 1))
                p.on("tensor", lambda e: e.matmul(
                    psml0[:, 65:66], sufst[:, :], dcol[:, 4:5], start=True, stop=True),
                    inc=("MMS", 1))
                p.on("tensor", lambda e: e.matmul(
                    psml0[:, 66:67], ones128[:, :], dcol[:, 4:5], start=True, stop=True),
                    inc=("MMS", 1))
                mms_c = p.cnt("MMS")
                p.on("vector", lambda e: e.tensor_copy(slc[:, :], psml0[:, 64:65]),
                     wait=[("MMS", mms_c)], inc=("SGW", 1))
                p.on("vector", lambda e: e.tensor_tensor(
                    sgc[:, :], psml0[:, 64:65], spv[:, :], ALU.add),
                    inc=("SGW", 1))
                p.on("vector", lambda e, ch=ch: e.tensor_copy(sgst[:, ch:ch + 1], sgc[:, :]))
                p.on("vector", lambda e: e.tensor_tensor(
                    spv[:, :], spv[:, :], psml0[:, 66:67], ALU.add), inc=("SGD", 1))
                p.on("vector", lambda e: e.tensor_scalar(
                    rab[:, :], abc[:, :], psml0[:, 65:66], None, ALU.mult),
                    wait=[("MMS", mms_c), ("BB2", p.cnt("BB2"))], inc=("RRD", 1))
                p.on("scalar", lambda e: e.activation(
                    rab[:, :], rab[:, :], AF.Exp),
                    wait=[("RRD", p.cnt("RRD"))], inc=("ERB", 1))
                p.on("vector", lambda e: e.tensor_tensor(
                    bbm2[:, :], rab[:, :], bbm[:, :], ALU.mult),
                    wait=[("ERB", p.cnt("ERB"))], inc=("BB2", 1))
                wtx = [("MMS", mms_c), ("CSD", p.cnt("CSD")), ("SGW", p.cnt("SGW"))]
                if p.cnt("PRE") > 0:
                    wtx.append(("PRE", p.cnt("PRE")))
                if p.cnt("SRW") > 0:
                    wtx += [("SRW", p.cnt("SRW")), ("CTS", p.cnt("CTS"))]
                p.on("tensor", lambda e: e.transpose(
                    psml1[0:1, 0:128], sgc[:, :], identf[:, :]),
                    wait=wtx, inc=("TRX", 1))
                p.on("tensor", lambda e: e.transpose(
                    psml1[0:1, 128:256], slc[:, :], identf[:, :]),
                    inc=("TRX", 1))
                p.on("tensor", lambda e, ch=ch: e.transpose(
                    psml1[0:16, 256:384], cstb[:, ch * DST:(ch + 1) * DST], identf[:, :]),
                    inc=("TRX", 1))
                for j in range(NJC):
                    p.on("tensor", lambda e, j=j: e.transpose(
                        pbig[:, j * 128:(j + 1) * 128], xcT[:, j * 128:(j + 1) * 128],
                        identf[:, :]),
                        inc=(("TRX", 1) if j == NJC - 1 else None))
                trx_c = p.cnt("TRX")
                p.on("vector", lambda e: e.tensor_copy(srow[:, :], psml1[0:1, 0:128]),
                     wait=[("TRX", trx_c)], inc=("SRW", 1))
                p.on("vector", lambda e: e.tensor_copy(srow_loc[:, :], psml1[0:1, 128:256]),
                     inc=("SRW", 1))
                p.on("vector", lambda e: e.tensor_copy(ctsb[:, :], psml1[0:16, 256:384]),
                     inc=("CTS", 1))
                cts_c = p.cnt("CTS")
                p.on("scalar", lambda e: e.activation(xcb[:, :], pbig[:, :], AF.Copy),
                     wait=[("TRX", trx_c)], inc=("XCC", 1))
                p.on("vector", lambda e: e.tensor_copy(ctbb[:, :], ctsb[:, :]), inc=("CTB", 1))
                p.on("sync", lambda e: e.dma_start(out=ctrow[0:1, :], in_=ctbb[:, :]),
                     wait=[("CTB", p.cnt("CTB")), ("PNM", p.cnt("PNM"))], inc=("CTR", 16))
                ctr_c = p.cnt("CTR")
                p.on("vector", lambda e: e.tensor_scalar(
                    negs[:, :], srow[:, :], -1.0, None, ALU.mult),
                    wait=[("SRW", p.cnt("SRW")), ("DMB", p.cnt("DMB"))], inc=("LRT", 1))
                p.on("tensor", lambda e: e.matmul(
                    psml1[:, 256:384], onesr[0:1, :], srow[0:1, :], start=True, stop=False),
                    wait=[("LRT", p.cnt("LRT")), ("CTS", cts_c)])
                p.on("tensor", lambda e: e.matmul(
                    psml1[:, 256:384], negs[0:1, :], ones128[0:1, :], start=False, stop=True),
                    inc=("MMT", 1))
                p.on("vector", lambda e: e.tensor_tensor(
                    dmt[:, :], psml1[:, 256:384], masktl[:, :], ALU.add),
                    wait=[("MMT", p.cnt("MMT"))], inc=("DMB", 1))
                dmb_c = p.cnt("DMB")
                for n in range(DST):
                    wex = [("GS1", p.cnt("GS1"))]
                    if n == 0:
                        wex = [("DMB", dmb_c), ("ABC", 1), ("SCB", p.cnt("SCB"))]
                    p.on("scalar", lambda e, n=n: e.activation(
                        escr[:, :], dmt[:, :], AF.Exp, scale=abc[:, n:n + 1]),
                        wait=wex, inc=("EXN", 1))
                    wg1 = [("EXN", p.cnt("EXN"))]
                    if n == 0:
                        wg1.append(("BBD", p.cnt("BBD")))
                    p.on("vector", lambda e, n=n: e.tensor_scalar(
                        g1s[:, :], escr[:, :], bbm[:, n:n + 1], None, ALU.mult),
                        wait=wg1, inc=("GS1", 1))
                    psl = psml0 if n % 2 == 0 else psml1
                    p.on("tensor", lambda e, n=n, psl=psl: e.matmul(
                        psl[:, 384:512], onesb[0:1, :], ctrow[0:1, n * 128:(n + 1) * 128],
                        start=True, stop=True),
                        wait=([("CTR", ctr_c), ("ONB", 1), ("DMB", dmb_c)] if n <= 1
                              else [("GS2", p.cnt("GS2") - 1)]),
                        inc=("PNM", 1))
                    p.on("vector", lambda e, n=n, psl=psl: e.tensor_tensor(
                        g1s[:, :], g1s[:, :], psl[:, 384:512], ALU.mult),
                        wait=[("PNM", p.cnt("PNM"))], inc=("GS2", 1))
                    if n == 0:
                        p.on("vector", lambda e: e.tensor_copy(accT[:, :], g1s[:, :]))
                    else:
                        p.on("vector", lambda e: e.tensor_tensor(
                            accT[:, :], accT[:, :], g1s[:, :], ALU.add))
                p.on("vector", lambda e: e.tensor_copy(acbf[:, :], accT[:, :]), inc=("SCB", 1))
                scb_c = p.cnt("SCB")
                p.on("tensor", lambda e: e.matmul(
                    psml0[0:16, 256:384], arow[0:1, :], srow_loc[0:1, :], start=True, stop=True),
                    wait=[("SRW", p.cnt("SRW")), ("CGB", p.cnt("CGB")),
                          ("SCB", scb_c), ("RRD", p.cnt("RRD"))], inc=("PAM", 1))
                p.on("scalar", lambda e: e.activation(
                    ebuf[:, :], psml0[0:16, 256:384], AF.Exp),
                    wait=[("PAM", p.cnt("PAM"))], inc=("EBF", 1))
                p.on("vector", lambda e: e.tensor_tensor(
                    cgb[:, :], ebuf[:, :], ctsb[:, :], ALU.mult),
                    wait=[("EBF", p.cnt("EBF"))], inc=("CGB", 1))
                cgb_c = p.cnt("CGB")
                wym = [("SCB", scb_c), ("CGB", cgb_c), ("XCC", p.cnt("XCC")),
                       ("HBF", p.cnt("HBF")), ("YPE", p.cnt("YPE"))]
                for dj in range(NJC):
                    p.on("tensor", lambda e, dj=dj: e.matmul(
                        pbig[:, dj * 128:(dj + 1) * 128], xcb[:, dj * 128:(dj + 1) * 128],
                        acbf[:, :], start=True, stop=False),
                        wait=(wym if dj == 0 else None))
                    p.on("tensor", lambda e, dj=dj: e.matmul(
                        pbig[:, dj * 128:(dj + 1) * 128], hbf[:, dj * 128:(dj + 1) * 128],
                        cgb[:, :], start=False, stop=True),
                        inc=(("YMM", 1) if dj == NJC - 1 else None))
                ymm_c = p.cnt("YMM")
                p.on("scalar", lambda e: e.activation(ysc[:, :], pbig[:, :], AF.Copy),
                     wait=[("YMM", ymm_c), ("WOM", p.cnt("WOM"))], inc=("YPE", 1))
                p.on("vector", lambda e: e.tensor_tensor(
                    xcTb[:, :], xcTb[:, :], dpb[:, :], ALU.mult),
                    wait=[("YPE", p.cnt("YPE")), ("XDB", xdb_c)], inc=("YP1", 1))
                p.on("vector", lambda e: e.tensor_tensor(
                    ysc[:, :], ysc[:, :], xcTb[:, :], ALU.add), inc=("YP2", 1))
                p.on("vector", lambda e, cc=cc: e.tensor_tensor(
                    ysc[:, :], ysc[:, :],
                    zTg[:, :].rearrange("p (j t) -> p j t", j=NJC)[:, :, cc * 128:(cc + 1) * 128],
                    ALU.mult),
                    wait=[("SZD", szd_g)], inc=("YP3", 1))
                yp3_c = p.cnt("YP3")
                wum = [("BB2", p.cnt("BB2")), ("YPE", p.cnt("YPE")), ("XCC", p.cnt("XCC"))]
                for f in range(4):
                    p.on("tensor", lambda e, f=f: e.matmul(
                        pbig[0:16, f * 512:(f + 1) * 512], bbm2[:, :],
                        xcb[:, f * 512:(f + 1) * 512], start=True, stop=True),
                        wait=(wum if f == 0 else None),
                        inc=(("UMM", 1) if f == 3 else None))
                umm_c = p.cnt("UMM")
                p.on("vector", lambda e: e.tensor_scalar(
                    grb[:, 0:1], acol[:, :], psml0[0:16, 66:67], None, ALU.mult),
                    wait=[("ACL", 1)], inc=("GRD", 1))
                p.on("scalar", lambda e: e.activation(
                    grb[:, 1:2], grb[:, 0:1], AF.Exp),
                    wait=[("GRD", p.cnt("GRD"))], inc=("GDC", 1))
                p.on("vector", lambda e: e.tensor_scalar(
                    hst[:, :], hst[:, :], grb[:, 1:2], None, ALU.mult),
                    wait=[("GDC", p.cnt("GDC")), ("UMM", umm_c)])
                p.on("vector", lambda e: e.tensor_tensor(
                    hst[:, :], hst[:, :], pbig[0:16, :], ALU.add), inc=("HUP", 1))
                p.on("vector", lambda e: e.tensor_copy(hbf[:, :], hst[:, :]),
                     wait=[("YMM", ymm_c)], inc=("HBF", 1))
                wwm = [("YP3", yp3_c), ("UMM", umm_c), ("HUP", p.cnt("HUP"))]
                if p.cnt("PRE") > 0:
                    wwm.append(("PRE", p.cnt("PRE")))
                for dj in range(NJC):
                    for f in range(2):
                        p.on("tensor", lambda e, dj=dj, f=f: e.matmul(
                            pbig[:, f * 512:(f + 1) * 512], ysc[:, dj * 128:(dj + 1) * 128],
                            wob[:, dj * 1024 + f * 512:dj * 1024 + (f + 1) * 512],
                            start=(dj == 0), stop=(dj == NJC - 1)),
                            wait=(wwm if (dj == 0 and f == 0) else None),
                            inc=(("WOM", 1) if (dj == NJC - 1 and f == 1) else None))
                wom_c = p.cnt("WOM")
                wpr = [("WOM", wom_c)]
                if spl_slot[0] > 0:
                    wpr.append(("PSP", spl_slot[0]))
                p.on("vector", lambda e, cc=cc: e.tensor_tensor(
                    preb[:, :], z2g[:, cc * DM:(cc + 1) * DM],
                    pbig[:, 0:1024], ALU.add),
                    wait=wpr, inc=("PRE", 1))
                psp_t = p.on("sync", lambda e, ch=ch: e.dma_start(
                    out=pre4[ch], in_=preb[:, :]),
                    wait=[("PRE", p.cnt("PRE"))], inc=("PSP", 16))
                spl_slot[0] = psp_t
        p.on("sync", lambda e: e.dma_start(out=hend, in_=hst[:, :]),
             wait=[("HUP", p.cnt("HUP"))], inc=("HSP", 16))
        p.on("sync", lambda e: e.dma_start(out=cst_o, in_=cstb[:, :]),
             wait=[("CSD", p.cnt("CSD"))], inc=("HSP", 16))
        p.on("sync", lambda e: e.dma_start(out=sgst_o, in_=sgst[:, :]),
             wait=[("SGD", p.cnt("SGD"))], inc=("HSP", 16))
        p.wait("sync", [("PSP", p.cnt("PSP")), ("SPL", p.cnt("SPL")), ("HSP", p.cnt("HSP"))])
        p.emit(block)
    return nc


def build_phase_c2():
    nc = bass.Bass("TRN2", target_bir_lowering=False)
    preln = nc.dram_tensor("preln", [T_CORE, DM], F32, kind="ExternalInput").ap()
    szsp = nc.dram_tensor("szsp", [DI, T_CORE], BF16, kind="ExternalInput").ap()
    hsel_d = nc.dram_tensor("hsel", [DST, DI], F32, kind="ExternalInput").ap()
    cst_d = nc.dram_tensor("cst", [128, NCH * DST], F32, kind="ExternalInput").ap()
    sgst_d = nc.dram_tensor("sgst", [128, NCH], F32, kind="ExternalInput").ap()
    alr_d = nc.dram_tensor("alr", [1, DST], F32, kind="ExternalInput").ap()
    Wout_d = nc.dram_tensor("Wout", [DI, DM], F32, kind="ExternalInput").ap()
    identd = nc.dram_tensor("identd", [128, 128], F32, kind="ExternalInput")
    z3 = nc.dram_tensor("z3", [T_CORE, DM], F32, kind="ExternalOutput").ap()

    st = _STACK
    al = lambda *a: st.enter_context(nc.sbuf_tensor(*a))
    alp = lambda *a: st.enter_context(nc.psum_tensor(*a))
    wob = al("wob2", [128, NJC * DM], BF16)
    identf = al("identf2", [128, 128], F32)
    arow = al("arow2", [1, DST], F32)
    hselb = al("hselb2", [16, DI], BF16)
    hself = al("hself2", [16, DI], F32)
    cstb = al("cstb2", [128, NCH * DST], F32)
    sgstb = al("sgst2", [128, NCH], F32)
    sgrow = al("sgrow2", [1, 128], F32)
    ctsb = al("ctsb2", [16, 128], F32)
    ebuf = al("ebuf2", [16, 128], F32)
    cgb = al("cgb2", [16, 128], BF16)
    szc = al("szc2", [128, NJC * 128], BF16)
    dyb = al("dyb2", [128, NJC * 128], BF16)
    prb = al("prb2", [128, 2 * DM], F32)
    ep1 = al("ep1c2", [128, 2 * DM], F32)
    mcol = al("mcol2", [128, 2 * 8], F32)
    konst = al("konst2", [128, 2], F32)
    pbig = alp("pbig2", [128, 2048], F32)
    pmid = alp("pmid2", [128, 1024], F32)
    psml = alp("psml2", [128, 512], F32)

    with nc.Block() as block:
        p = Prog(nc, serial=SERIAL)
        p._stack = _STACK
        pre4 = preln.rearrange("(ch p) m -> ch p m", p=128)
        z34 = z3.rearrange("(ch p) m -> ch p m", p=128)
        szsp4 = szsp.rearrange("(j p) t -> p j t", p=128)

        p.on("gpsimd", lambda e: e.dma_start(
            out=wob[:, :].rearrange("p (j m) -> p j m", j=NJC),
            in_=Wout_d.rearrange("(j p) m -> p j m", p=128)), inc=("LW", 16))
        p.on("gpsimd", lambda e: e.dma_start(out=hself[:, :], in_=hsel_d), inc=("LW", 16))
        for dst, srcd in ((cstb, cst_d), (sgstb, sgst_d), (arow, alr_d),
                          (identf, identd.ap())):
            p.on("sync", lambda e, dst=dst, srcd=srcd: e.dma_start(out=dst[:, :], in_=srcd),
                 inc=("LD", 16))
        ld_tot = p.cnt("LD")
        p.on("vector", lambda e: e.memset(konst[:, 0:1], LN_EPS))
        p.on("vector", lambda e: e.memset(konst[:, 1:2], -0.5), inc=("MST", 1))
        p.on("vector", lambda e: e.tensor_copy(hselb[:, :], hself[:, :]),
             wait=[("LW", 32)], inc=("HSB", 1))
        p.on("scalar", lambda e: e.activation(arow[:, :], arow[:, :], AF.Exp),
             wait=[("LD", ld_tot)], inc=("ARP", 1))
        p.on("vector", lambda e: e.tensor_scalar(arow[:, :], arow[:, :], -1.0, None, ALU.mult),
             wait=[("ARP", 1)], inc=("ARN", 1))

        z3s_slot = [0, 0]
        prb_slot = [0, 0]
        for ch in range(NCH):
            b2 = ch % 2
            wlz = [("PRA", prb_slot[b2])] if prb_slot[b2] > 0 else None
            lpr_tgt = p.on("sync", lambda e, ch=ch, b2=b2: e.dma_start(
                out=prb[:, b2 * DM:(b2 + 1) * DM], in_=pre4[ch]),
                wait=wlz, inc=("LP%d" % b2, 16))
            wsz = [("DY3", p.cnt("DY3"))] if p.cnt("DY3") > 0 else None
            lsz_tgt = p.on("gpsimd", lambda e, ch=ch: e.dma_start(
                out=szc[:, :].rearrange("p (j t) -> p j t", j=NJC),
                in_=szsp4[:, :, ch * 128:(ch + 1) * 128]),
                wait=wsz, inc=("LSZ", 16))
            wtr = [("LD", ld_tot)]
            if p.cnt("SRW") > 0:
                wtr += [("SRW", p.cnt("SRW")), ("CTS", p.cnt("CTS"))]
            p.on("tensor", lambda e, ch=ch: e.transpose(
                psml[0:1, 0:128], sgstb[:, ch:ch + 1], identf[:, :]),
                wait=wtr, inc=("TRX", 1))
            p.on("tensor", lambda e, ch=ch: e.transpose(
                psml[0:16, 128:256], cstb[:, ch * DST:(ch + 1) * DST], identf[:, :]),
                inc=("TRX", 1))
            trx_c = p.cnt("TRX")
            p.on("vector", lambda e: e.tensor_copy(sgrow[:, :], psml[0:1, 0:128]),
                 wait=[("TRX", trx_c), ("PAM", p.cnt("PAM"))], inc=("SRW", 1))
            p.on("vector", lambda e: e.tensor_copy(ctsb[:, :], psml[0:16, 128:256]),
                 inc=("CTS", 1))
            p.on("tensor", lambda e: e.matmul(
                psml[0:16, 256:384], arow[0:1, :], sgrow[0:1, :], start=True, stop=True),
                wait=[("SRW", p.cnt("SRW")), ("CTS", p.cnt("CTS")),
                      ("EBF", p.cnt("EBF"))], inc=("PAM", 1))
            p.on("scalar", lambda e: e.activation(
                ebuf[:, :], psml[0:16, 256:384], AF.Exp),
                wait=[("PAM", p.cnt("PAM"))], inc=("EBF", 1))
            p.on("vector", lambda e: e.tensor_tensor(
                cgb[:, :], ebuf[:, :], ctsb[:, :], ALU.mult),
                wait=[("EBF", p.cnt("EBF"))], inc=("CGB", 1))
            cgb_c = p.cnt("CGB")
            wym = [("CGB", cgb_c), ("HSB", 1)]
            if p.cnt("DYE") > 0:
                wym.append(("DYE", p.cnt("DYE")))
            for dj in range(NJC):
                p.on("tensor", lambda e, dj=dj: e.matmul(
                    pbig[:, dj * 128:(dj + 1) * 128], hselb[:, dj * 128:(dj + 1) * 128],
                    cgb[:, :], start=True, stop=True),
                    wait=(wym if dj == 0 else None),
                    inc=(("DYM", 1) if dj == NJC - 1 else None))
            dym_c = p.cnt("DYM")
            p.on("scalar", lambda e: e.activation(dyb[:, :], pbig[:, :], AF.Copy),
                 wait=[("DYM", dym_c), ("WOM", p.cnt("WOM"))], inc=("DYE", 1))
            p.on("vector", lambda e: e.tensor_tensor(
                dyb[:, :], dyb[:, :], szc[:, :], ALU.mult),
                wait=[("DYE", p.cnt("DYE")), ("LSZ", lsz_tgt)], inc=("DY3", 1))
            dy3_c = p.cnt("DY3")
            wwm = [("DY3", dy3_c)]
            if p.cnt("PRA") > 0:
                wwm.append(("PRA", p.cnt("PRA")))
            for dj in range(NJC):
                for f in range(2):
                    p.on("tensor", lambda e, dj=dj, f=f: e.matmul(
                        pmid[:, f * 512:(f + 1) * 512], dyb[:, dj * 128:(dj + 1) * 128],
                        wob[:, dj * 1024 + f * 512:dj * 1024 + (f + 1) * 512],
                        start=(dj == 0), stop=(dj == NJC - 1)),
                        wait=(wwm if (dj == 0 and f == 0) else None),
                        inc=(("WOM", 1) if (dj == NJC - 1 and f == 1) else None))
            wom_c = p.cnt("WOM")
            p.on("vector", lambda e, b2=b2: e.tensor_tensor(
                prb[:, b2 * DM:(b2 + 1) * DM], prb[:, b2 * DM:(b2 + 1) * DM],
                pmid[:, :], ALU.add),
                wait=[("WOM", wom_c), ("LP%d" % b2, lpr_tgt)], inc=("PRA", 1))
            p.on("vector", lambda e, b2=b2: e.reduce_sum(
                mcol[:, b2 * 8:b2 * 8 + 1], prb[:, b2 * DM:(b2 + 1) * DM], AX.X))
            p.on("vector", lambda e, b2=b2: e.tensor_scalar(
                mcol[:, b2 * 8 + 1:b2 * 8 + 2], mcol[:, b2 * 8:b2 * 8 + 1], 1.0 / DM, None, ALU.mult))
            wsub = [("ZS3%d" % b2, z3s_slot[b2])] if z3s_slot[b2] > 0 else None
            p.on("vector", lambda e, b2=b2: e.tensor_scalar(
                ep1[:, b2 * DM:(b2 + 1) * DM], prb[:, b2 * DM:(b2 + 1) * DM],
                mcol[:, b2 * 8 + 1:b2 * 8 + 2], None, ALU.subtract),
                wait=wsub, inc=("EZC", 1))
            p.on("scalar", lambda e, b2=b2: e.activation(
                prb[:, b2 * DM:(b2 + 1) * DM], ep1[:, b2 * DM:(b2 + 1) * DM], AF.Square),
                wait=[("EZC", p.cnt("EZC")), ("MST", 1)], inc=("SQ2", 1))
            p.on("vector", lambda e, b2=b2: e.reduce_sum(
                mcol[:, b2 * 8 + 2:b2 * 8 + 3], prb[:, b2 * DM:(b2 + 1) * DM], AX.X),
                wait=[("SQ2", p.cnt("SQ2"))])
            p.on("vector", lambda e, b2=b2: e.tensor_scalar(
                mcol[:, b2 * 8 + 2:b2 * 8 + 3], mcol[:, b2 * 8 + 2:b2 * 8 + 3], 1.0 / DM, None, ALU.mult),
                inc=("EV2", 1))
            p.on("scalar", lambda e, b2=b2: e.activation(
                mcol[:, b2 * 8 + 3:b2 * 8 + 4], mcol[:, b2 * 8 + 2:b2 * 8 + 3], AF.Ln, bias=konst[:, 0:1]),
                wait=[("EV2", p.cnt("EV2"))])
            p.on("scalar", lambda e, b2=b2: e.activation(
                mcol[:, b2 * 8 + 3:b2 * 8 + 4], mcol[:, b2 * 8 + 3:b2 * 8 + 4], AF.Exp, scale=konst[:, 1:2]),
                inc=("RS2", 1))
            p.on("vector", lambda e, b2=b2: e.tensor_scalar(
                ep1[:, b2 * DM:(b2 + 1) * DM], ep1[:, b2 * DM:(b2 + 1) * DM],
                mcol[:, b2 * 8 + 3:b2 * 8 + 4], None, ALU.mult),
                wait=[("RS2", p.cnt("RS2"))], inc=("EPF", 1))
            zs3_t = p.on("sync", lambda e, ch=ch, b2=b2: e.dma_start(
                out=z34[ch], in_=ep1[:, b2 * DM:(b2 + 1) * DM]),
                wait=[("EPF", p.cnt("EPF"))], inc=("ZS3%d" % b2, 16))
            z3s_slot[b2] = zs3_t
        p.wait("sync", [("ZS30", p.cnt("ZS30")), ("ZS31", p.cnt("ZS31"))])
        p.emit(block)
    return nc


def host_inputs_phase_c(inputs, z2_full):
    W_in = np.asarray(inputs["L1_attn_W_in"], np.float32)
    W_x = np.asarray(inputs["L1_attn_W_x"], np.float32)
    W_x48 = np.zeros((DI, 48), np.float32)
    W_x48[:, 0:33] = W_x
    conv_w = np.asarray(inputs["L1_attn_conv_w"], np.float32)
    conv_b = np.asarray(inputs["L1_attn_conv_b"], np.float32)
    D_param = np.asarray(inputs["L1_attn_D_param"], np.float32)
    A_log = np.asarray(inputs["L1_attn_A_log"], np.float32)
    W_out = np.asarray(inputs["L1_attn_W_out"], np.float32)
    cwc = np.ascontiguousarray(conv_w.reshape(NJC, 128, 4).transpose(1, 0, 2).reshape(128, NJC * 4))
    cbc = np.ascontiguousarray(conv_b.reshape(NJC, 128).T)
    dpc = np.ascontiguousarray(D_param.reshape(NJC, 128).T)
    alr = np.ascontiguousarray(A_log[None, :])
    t_ = np.ones((128, 128), np.float32)
    cumin = np.triu(t_)
    sufst = np.tril(t_, -1)
    masktl = np.ascontiguousarray(np.tril(t_, -1) * 1e30)
    ones128 = t_
    common = {"W_in": W_in, "W_x": W_x48, "cwc": cwc, "cbc": cbc, "dpc": dpc,
              "alr": alr, "alc": np.ascontiguousarray(A_log[:, None]),
              "Wout": W_out, "cumin": np.ascontiguousarray(cumin),
              "sufst": np.ascontiguousarray(sufst), "masktl": masktl,
              "ones128": ones128, "identd": np.eye(128, dtype=np.float32)}
    maps = []
    for c in range(NCORES):
        b, h = c // 2, c % 2
        m = dict(common)
        m["z2"] = np.ascontiguousarray(z2_full[b, h * T_CORE:(h + 1) * T_CORE])
        maps.append(m)
    return maps


def run_phase_c(inputs, z2_full, trace=False):
    nc0 = _run_with_stack(build_phase_c0)
    W_in = np.asarray(inputs["L1_attn_W_in"], np.float32)
    maps0 = []
    for c in range(NCORES):
        b, h = c // 2, c % 2
        zt = np.zeros((DM, 4), np.float32)
        zt[:, 1:4] = z2_full[b, h * T_CORE + T_CORE - 3:h * T_CORE + T_CORE].T
        maps0.append({"ztt": zt, "W_in": W_in})
    res0 = run_bass_kernel_spmd(nc0, maps0, core_ids=list(range(NCORES)))
    tails = [res0.results[c]["tail"] for c in range(NCORES)]
    nc1 = _run_with_stack(build_phase_c1)
    maps1 = host_inputs_phase_c(inputs, z2_full)
    for c in range(NCORES):
        if c % 2 == 1:
            maps1[c]["tail_in"] = tails[c - 1]
        else:
            maps1[c]["tail_in"] = np.zeros((128, NJC * 4), np.float32)
    res1 = run_bass_kernel_spmd(nc1, maps1, core_ids=list(range(NCORES)), trace=trace)
    nc2 = _run_with_stack(build_phase_c2)
    maps2 = []
    for c in range(NCORES):
        m1 = res1.results[c]
        m = {"preln": m1["preln"], "szsp": m1["szsp"], "cst": m1["cst"],
             "sgst": m1["sgst"], "alr": maps1[c]["alr"], "Wout": maps1[c]["Wout"],
             "identd": maps1[c]["identd"]}
        if c % 2 == 1:
            m["hsel"] = res1.results[c - 1]["hend"]
        else:
            m["hsel"] = np.zeros((DST, DI), np.float32)
        maps2.append(m)
    res2 = run_bass_kernel_spmd(nc2, maps2, core_ids=list(range(NCORES)), trace=trace)
    z3 = np.empty((4, 8192, DM), np.float32)
    for c in range(NCORES):
        b, h = c // 2, c % 2
        z3[b, h * T_CORE:(h + 1) * T_CORE] = res2.results[c]["z3"]
    return z3, (res0, res1, res2)


_STACK = None


def _run_with_stack(builder):
    global _STACK
    from contextlib import ExitStack
    with ExitStack() as st:
        _STACK = st
        nc = builder()
        _STACK = None
    return nc


def host_inputs_phase_a(inputs):
    U = np.asarray(inputs["L0_attn_u"], np.float32)
    Bw = np.asarray(inputs["L0_attn_B_w"], np.float32)
    Cw = np.asarray(inputs["L0_attn_C_w"], np.float32)
    ll = np.asarray(inputs["L0_attn_log_lambda"], np.float32)
    D = np.asarray(inputs["L0_attn_D"], np.float32)
    BwT = np.ascontiguousarray(Bw.T)
    CwT = np.ascontiguousarray(Cw.T)
    ll2 = np.ascontiguousarray(ll.reshape(2, 128).T)
    Drow = np.ascontiguousarray(D[None, :])
    maps = []
    for c in range(NCORES):
        b, h = c // 2, c % 2
        u_c = np.ascontiguousarray(U[b, h * T_CORE:(h + 1) * T_CORE])
        uT_c = np.ascontiguousarray(u_c.T)
        if h == 1:
            uTp = np.ascontiguousarray(U[b, 0:T_CORE].T)
        else:
            uTp = np.zeros((DM, T_CORE), np.float32)
        maps.append({"u": u_c, "uT": uT_c, "uTprev": uTp,
                     "BwT": BwT, "CwT": CwT, "ll2": ll2, "Drow": Drow})
    return maps


def run_phase_a(inputs, trace=False):
    nc = _run_with_stack(build_phase_a)
    maps = host_inputs_phase_a(inputs)
    res = run_bass_kernel_spmd(nc, maps, core_ids=list(range(NCORES)), trace=trace)
    z1 = np.empty((4, 8192, DM), np.float32)
    for c in range(NCORES):
        b, h = c // 2, c % 2
        z1[b, h * T_CORE:(h + 1) * T_CORE] = res.results[c]["z1"]
    return z1, res


def kernel(**inputs):
    z1, _ = run_phase_a(inputs)
    z2, _ = run_phase_b(inputs, z1)
    z3, _ = run_phase_c(inputs, z2)
    z4, _ = run_phase_d(inputs, z3)
    return z4


# version 03
# speedup vs baseline: 1.0704x; 1.0704x over previous
import numpy as np
import concourse.bass as bass
import concourse.mybir as mybir
from concourse.bass_utils import run_bass_kernel_spmd

F32 = mybir.dt.float32
BF16 = mybir.dt.bfloat16
AF = mybir.ActivationFunctionType
ALU = mybir.AluOpType
AX = mybir.AxisListType

NCORES = 8
T_CORE = 4096
DM = 1024
NSTATE = 256
LN_EPS = 1e-5
RMS_EPS = 1e-6
NE = 8
DE = 2816
NJ = DE // 128
DI = 2048
NJC = DI // 128
DST = 16
NCH = 32
NGR = 8
DH = 4096
NJD = DH // 128
PAIRS = [[0, 1], [2, 3], [4, 5], [6, 7]]

ENGINES = ("sync", "gpsimd", "tensor", "vector", "scalar")


class Pool:

    def __init__(self, nc):
        self.nc = nc
        self.handles = []
        self.totals = []
        self.free = []
        self.map = {}
        self.counts = {}

    def get(self, name):
        if name not in self.map:
            if self.free:
                idx = self.free.pop(0)
            else:
                idx = len(self.handles)
                self.handles.append(self.nc.alloc_semaphore("ps%d" % idx))
                self.totals.append(0)
            self.map[name] = (idx, self.totals[idx])
            self.counts[name] = 0
        idx, base = self.map[name]
        return self.handles[idx], base

    def cnt(self, name):
        return self.counts.get(name, 0)

    def rebase(self):
        for name, (idx, base) in self.map.items():
            self.totals[idx] = base + self.counts[name]
        self.free = sorted(set(range(len(self.handles))))
        self.map = {}
        self.counts = {}


class Arena:

    def __init__(self, nc, prefix):
        self.nc = nc
        self.prefix = prefix
        self.addr = 0
        self.tensors = []

    def al(self, name, shape, dtype):
        per_part = 1
        for s in shape[1:]:
            per_part *= s
        per_part *= mybir.dt.size(dtype)
        pre = self.addr
        self.addr = (self.addr + per_part + 31) // 32 * 32
        h = self.nc.virtual_sbuf_tensor(self.prefix + name, shape, dtype)
        self.tensors.append((h, pre))
        return h

    def place(self, backing_addr):
        for h, off in self.tensors:
            self.nc._tensor(h.name, h.shape, h.dtype, type="SB")
            mloc = self.nc.lookup_mloc(h)
            mloc.allocated = not self.nc.target_bir_lowering
            mloc.addr = backing_addr + off


class PV:

    def __init__(self, h, c0, w):
        self.h, self.c0, self.w = h, c0, w

    def __getitem__(self, idx):
        ps, cs = idx
        a = self.c0 + (cs.start or 0)
        b = self.c0 + (cs.stop if cs.stop is not None else self.w)
        return self.h[ps, a:b]


class Prog:

    AUTO_CHAIN = ("vector", "scalar")

    def __init__(self, nc, pool):
        self.nc = nc
        self.pool = pool
        self.ops = {e: [] for e in ENGINES}
        self._last_edge = {e: None for e in ENGINES}
        self._nbar = 0

    def cnt(self, name):
        return self.pool.cnt(name)

    def on(self, eng, fn, wait=None, inc=None):
        if eng in self.AUTO_CHAIN:
            if inc is None:
                inc = ("_T" + eng, 1)
            prev = self._last_edge[eng]
            if prev is not None:
                wait = list(wait) if wait else []
                wait.append(prev)
        sem_h = None
        inc_amt = 0
        if inc is not None:
            name, amt = inc
            sem_h, _ = self.pool.get(name)
            self.pool.counts[name] += amt
            inc_amt = amt
            if eng in self.AUTO_CHAIN:
                self._last_edge[eng] = (name, self.pool.counts[name])
        waits = []
        if wait:
            for wname, wval in wait:
                if wval > 0:
                    h, b = self.pool.get(wname)
                    waits.append((h, b + wval))

        def run(e, fn=fn, waits=waits, sem_h=sem_h, inc_amt=inc_amt):
            for s, v in waits:
                e.wait_ge(s, v)
            ins = fn(e)
            if sem_h is not None:
                ins.then_inc(sem_h, inc_amt)
        self.ops[eng].append(run)
        if inc is not None:
            return self.pool.counts[inc[0]]
        return None

    def wait(self, eng, wait):
        waits = []
        for wname, wval in wait:
            if wval > 0:
                h, b = self.pool.get(wname)
                waits.append((h, b + wval))

        def run(e, waits=waits):
            for s, v in waits:
                e.wait_ge(s, v)
        self.ops[eng].append(run)

    def phase_end(self):
        self._nbar += 1
        name = "PBAR%d" % self._nbar
        h, base = self.pool.get(name)
        self.pool.counts[name] += len(ENGINES)
        target = base + len(ENGINES)
        for eng in ENGINES:
            def mark(e, h=h):
                e.nop().then_inc(h, 1)
            self.ops[eng].append(mark)
        self.pool.rebase()
        self._last_edge = {e: None for e in ENGINES}
        for eng in ENGINES:
            def barw(e, h=h, t=target):
                e.wait_ge(h, t)
            self.ops[eng].append(barw)

    def emit(self, block):
        for eng in ENGINES:
            ops = self.ops[eng]

            def run_all(e, ops=ops):
                for fn in ops:
                    fn(e)
            getattr(block, eng)(run_all)


def emit_phase_a(nc, p, al, d, P0, P1):
    u = d["u"]
    uT = d["uT"]
    uTprev = d["uTprev"]
    BwT = d["BwT"]
    CwT = d["CwT"]
    ll2 = d["ll2"]
    Drow = d["Drow"]
    z1 = d["z1"]

    bwt = al("bwt", [128, 8 * NSTATE], BF16)
    cwt = al("cwt", [128, 2 * DM], BF16)
    ll = al("ll", [128, 2], F32)
    lamc = al("lamc", [128, 2], F32)
    lamb = al("lamb", [128, 2 * 512], F32)
    drow_s = al("drow_s", [1, DM], F32)
    onesr = al("onesr", [1, 128], F32)
    d1b = al("d1b", [128, DM], F32)
    zcol = al("zcol", [128, 1], F32)
    utb = al("utb", [128, 2 * 8 * 512], BF16)
    uf = al("uf", [128, 2 * 4 * DM], F32)
    hf = al("hf", [128, 4 * 512], F32)
    hbf = al("hbf", [128, 2 * 2 * 512], BF16)
    ep0 = al("ep0", [128, 2 * DM], F32)
    ep1 = al("ep1", [128, 2 * DM], F32)
    sqb = al("sqb", [128, 2 * DM], F32)
    z1t = al("z1t", [128, 2 * DM], F32)
    mcol = al("mcol", [128, 8], F32)
    konst = al("konst", [128, 4], F32)
    psb0 = PV(P0, 0, 512)
    psb1 = PV(P0, 512, 512)
    psy0 = PV(P0, 1024, 1024)
    psy1 = PV(P1, 0, 1024)
    psd = PV(P1, 1024, 1024)

    psb = [psb0, psb1]
    psy = [psy0, psy1]

    uT3 = uT.rearrange("(c p) l -> p c l", p=128)
    uTp3 = uTprev.rearrange("(c p) l -> p c l", p=128)
    u4 = u.rearrange("(ch g p) m -> ch p g m", g=4, p=128)
    z14 = z1.rearrange("(ch g p) m -> ch g p m", g=4, p=128)

    p.on("gpsimd", lambda e: e.dma_start(
        out=bwt[:, :].rearrange("p (c n) -> p c n", c=8),
        in_=BwT.rearrange("(c p) n -> p c n", p=128)), inc=("LDWG", 16))
    p.on("gpsimd", lambda e: e.dma_start(
        out=cwt[:, :].rearrange("p (c m) -> p c m", c=2),
        in_=CwT.rearrange("(c p) m -> p c m", p=128)), inc=("LDWG", 16))
    p.on("sync", lambda e: e.dma_start(out=ll[:, :], in_=ll2), inc=("LDWS", 16))
    p.on("sync", lambda e: e.dma_start(out=drow_s[:, :], in_=Drow), inc=("LDWS", 16))
    p.on("scalar", lambda e: e.activation(lamc[:, :], ll[:, :], AF.Sigmoid),
         wait=[("LDWS", 32)], inc=("SIG", 1))
    p.on("vector", lambda e: e.memset(lamb[:, :], 1.0))
    p.on("vector", lambda e: e.memset(zcol[:, :], 0.0))
    p.on("vector", lambda e: e.memset(konst[:, 0:1], LN_EPS))
    p.on("vector", lambda e: e.memset(konst[:, 1:2], -0.5))
    p.on("vector", lambda e: e.memset(konst[:, 2:3], RMS_EPS))
    p.on("vector", lambda e: e.memset(onesr[:, :], 1.0), inc=("MST", 1))
    for nh in range(2):
        p.on("vector", lambda e, nh=nh: e.tensor_scalar(
            lamb[:, nh * 512:(nh + 1) * 512], lamb[:, nh * 512:(nh + 1) * 512],
            lamc[:, nh:nh + 1], None, ALU.mult),
            wait=[("SIG", 1)], inc=("PREP", 1))
    for f in range(2):
        p.on("tensor", lambda e, f=f: e.matmul(
            psd[:, f * 512:(f + 1) * 512], onesr[0:1, :], drow_s[0:1, f * 512:(f + 1) * 512],
            start=True, stop=True), wait=[("MST", 1), ("LDWS", 32)], inc=("MMD", 1))
    p.on("vector", lambda e: e.tensor_scalar(d1b[:, :], psd[:, :], 1.0, None, ALU.add),
         wait=[("MMD", 2)], inc=("PREP", 1))

    def emit_bu_mms(chunk, bufsel, sem_ld, ld_target, extra_wait):
        for nh in range(2):
            for c in range(8):
                last = (c == 7)
                w = [(sem_ld, ld_target), ("LDWG", 32)] if c == 0 else None
                if c == 0 and extra_wait is not None:
                    w = [(sem_ld, ld_target), ("LDWG", 32), extra_wait]
                p.on("tensor", lambda e, nh=nh, c=c, bufsel=bufsel, last=last: e.matmul(
                    psb[nh][:, :],
                    bwt[:, c * NSTATE + nh * 128: c * NSTATE + (nh + 1) * 128],
                    utb[:, bufsel * 4096 + c * 512:(bufsel) * 4096 + (c + 1) * 512],
                    start=(c == 0), stop=last),
                    wait=w, inc=(("MMB", 1) if last else None))

    for ch in range(8):
        bufsel = ch % 2
        wld = None
        if ch >= 2:
            wld = [("MMB", 2 * (ch - 1))]
        ldt_tgt = p.on("gpsimd", lambda e, ch=ch, bufsel=bufsel: e.dma_start(
            out=utb[:, bufsel * 4096:(bufsel + 1) * 4096].rearrange("p (c l) -> p c l", c=8),
            in_=uTp3[:, :, ch * 512:(ch + 1) * 512]),
            wait=wld, inc=("LDT%d" % bufsel, 16))
        extra = ("SCN", 2 * ch) if ch >= 1 else None
        emit_bu_mms(ch, bufsel, "LDT%d" % bufsel, ldt_tgt, extra)
        for nh in range(2):
            init_ap = zcol[:, 0:1] if ch == 0 else hf[:, ((ch - 1) % 2) * 1024 + nh * 512 + 511: ((ch - 1) % 2) * 1024 + nh * 512 + 512]
            p.on("vector", lambda e, ch=ch, nh=nh, init_ap=init_ap: e.tensor_tensor_scan(
                hf[:, (ch % 2) * 1024 + nh * 512:(ch % 2) * 1024 + nh * 512 + 512],
                lamb[:, nh * 512:(nh + 1) * 512],
                psb[nh][:, :],
                init_ap, ALU.mult, ALU.add),
                wait=[("MMB", 2 * ch + nh + 1), ("PREP", 2)],
                inc=("SCN", 1))
    h0_ap = [hf[:, 1024 + nh * 512 + 511:1024 + nh * 512 + 512] for nh in range(2)]

    MMB0 = p.cnt("MMB")
    SCN0 = p.cnt("SCN")
    for ch in range(8):
        bufsel = ch % 2
        wld = [("MMB", MMB0 + 2 * (ch - 1))] if ch >= 2 else [("MMB", MMB0)]
        if ch == 0:
            wld = [("SCN", 16)]
        ldt_tgt = p.on("gpsimd", lambda e, ch=ch, bufsel=bufsel: e.dma_start(
            out=utb[:, bufsel * 4096:(bufsel + 1) * 4096].rearrange("p (c l) -> p c l", c=8),
            in_=uT3[:, :, ch * 512:(ch + 1) * 512]),
            wait=wld, inc=("LDT%d" % bufsel, 16))
        wlu = [("EPI", 4 * (ch - 1))] if ch >= 2 else None
        ldu_tgt = p.on("sync", lambda e, ch=ch, bufsel=bufsel: e.dma_start(
            out=uf[:, bufsel * 4096:(bufsel + 1) * 4096].rearrange("p (g m) -> p g m", g=4),
            in_=u4[ch]),
            wait=wlu, inc=("LDU%d" % bufsel, 16))
        extra = ("SCN", SCN0 + 2 * ch) if ch >= 1 else None
        emit_bu_mms(ch, bufsel, "LDT%d" % bufsel, ldt_tgt, extra)
        for nh in range(2):
            init_ap = h0_ap[nh] if ch == 0 else hf[:, ((ch - 1) % 2) * 1024 + nh * 512 + 511: ((ch - 1) % 2) * 1024 + nh * 512 + 512]
            p.on("vector", lambda e, ch=ch, nh=nh, init_ap=init_ap: e.tensor_tensor_scan(
                hf[:, (ch % 2) * 1024 + nh * 512:(ch % 2) * 1024 + nh * 512 + 512],
                lamb[:, nh * 512:(nh + 1) * 512],
                psb[nh][:, :],
                init_ap, ALU.mult, ALU.add),
                wait=[("MMB", MMB0 + 2 * ch + nh + 1)],
                inc=("SCN", 1))
            wcast = [("SCN", SCN0 + 2 * ch + nh + 1)]
            if ch >= 2:
                wcast.append(("MMY", 4 * (ch - 1)))
            p.on("vector", lambda e, ch=ch, nh=nh: e.tensor_copy(
                hbf[:, (ch % 2) * 1024 + nh * 512:(ch % 2) * 1024 + nh * 512 + 512],
                hf[:, (ch % 2) * 1024 + nh * 512:(ch % 2) * 1024 + nh * 512 + 512]),
                wait=wcast, inc=("HBF", 1))
        for g in range(4):
            gi = 4 * ch + g
            ps = psy[g % 2]
            k = 0
            for nh in range(2):
                for f in range(2):
                    k += 1
                    w = []
                    if k == 1:
                        w = [("HBF", 2 * (ch + 1)), ("LDWG", 32)]
                        if gi >= 2:
                            w.append(("PSR", gi - 1))
                    p.on("tensor", lambda e, ch=ch, g=g, nh=nh, f=f, ps=ps: e.matmul(
                        ps[:, f * 512:(f + 1) * 512],
                        hbf[:, (ch % 2) * 1024 + nh * 512 + g * 128:(ch % 2) * 1024 + nh * 512 + (g + 1) * 128],
                        cwt[:, nh * DM + f * 512:nh * DM + (f + 1) * 512],
                        start=(nh == 0), stop=(nh == 1)),
                        wait=(w if w else None), inc=(("MMY", 1) if k == 4 else None))
            b2 = gi % 2
            uap = uf[:, (ch % 2) * 4096 + g * DM:(ch % 2) * 4096 + (g + 1) * DM]
            p.on("vector", lambda e, uap=uap, b2=b2: e.tensor_mul(
                ep0[:, b2 * DM:(b2 + 1) * DM], uap, d1b[:, :]),
                wait=[("LDU%d" % (ch % 2), ldu_tgt)], inc=None)
            p.on("vector", lambda e, b2=b2, ps=ps: e.tensor_add(
                ep0[:, b2 * DM:(b2 + 1) * DM], ep0[:, b2 * DM:(b2 + 1) * DM], ps[:, :]),
                wait=[("MMY", gi + 1)], inc=("PSR", 1))
            p.on("vector", lambda e, b2=b2: e.reduce_sum(
                mcol[:, b2 * 4:b2 * 4 + 1], ep0[:, b2 * DM:(b2 + 1) * DM], AX.X), inc=None)
            p.on("vector", lambda e, b2=b2: e.tensor_scalar(
                mcol[:, b2 * 4 + 1:b2 * 4 + 2], mcol[:, b2 * 4:b2 * 4 + 1], 1.0 / DM, None, ALU.mult), inc=None)
            p.on("vector", lambda e, b2=b2: e.tensor_scalar(
                ep1[:, b2 * DM:(b2 + 1) * DM], ep0[:, b2 * DM:(b2 + 1) * DM],
                mcol[:, b2 * 4 + 1:b2 * 4 + 2], None, ALU.subtract),
                inc=("EZC", 1))
            p.on("scalar", lambda e, b2=b2: e.activation(
                sqb[:, b2 * DM:(b2 + 1) * DM], ep1[:, b2 * DM:(b2 + 1) * DM], AF.Square),
                wait=[("EZC", gi + 1)], inc=("SQD", 1))
            p.on("vector", lambda e, b2=b2: e.reduce_sum(
                mcol[:, b2 * 4 + 2:b2 * 4 + 3], sqb[:, b2 * DM:(b2 + 1) * DM], AX.X),
                wait=[("SQD", gi + 1)], inc=None)
            p.on("vector", lambda e, b2=b2: e.tensor_scalar(
                mcol[:, b2 * 4 + 2:b2 * 4 + 3], mcol[:, b2 * 4 + 2:b2 * 4 + 3], 1.0 / DM, None, ALU.mult),
                inc=("EVV", 1))
            p.on("scalar", lambda e, b2=b2: e.activation(
                mcol[:, b2 * 4 + 3:b2 * 4 + 4], mcol[:, b2 * 4 + 2:b2 * 4 + 3], AF.Ln, bias=konst[:, 0:1]),
                wait=[("EVV", gi + 1)], inc=None)
            p.on("scalar", lambda e, b2=b2: e.activation(
                mcol[:, b2 * 4 + 3:b2 * 4 + 4], mcol[:, b2 * 4 + 3:b2 * 4 + 4], AF.Exp, scale=konst[:, 1:2]),
                inc=("RSD", 1))
            wz = [("RSD", gi + 1)]
            if gi >= 2:
                wz.append(("ZS%d" % b2, p.cnt("ZS%d" % b2)))
            p.on("vector", lambda e, b2=b2: e.tensor_scalar(
                z1t[:, b2 * DM:(b2 + 1) * DM], ep1[:, b2 * DM:(b2 + 1) * DM],
                mcol[:, b2 * 4 + 3:b2 * 4 + 4], None, ALU.mult),
                wait=wz, inc=("EPI", 1))
            p.on("sync", lambda e, ch=ch, g=g, b2=b2: e.dma_start(
                out=z14[ch, g], in_=z1t[:, b2 * DM:(b2 + 1) * DM]),
                wait=[("EPI", gi + 1)], inc=("ZS%d" % b2, 16))
    p.wait("sync", [("ZS0", p.cnt("ZS0")), ("ZS1", p.cnt("ZS1"))])


def emit_phase_b(nc, p, al, d, P0, P1):
    z1 = d["z1"]
    gwT = d["gwT"]
    rmsw = d["rmsw"]
    W1T = d["W1T"]
    W2T = d["W2T"]
    W3T = d["W3T"]
    ident_ap = d["identd"]
    z2 = d["z2"]
    comb = d["comb"]

    rmsb = al("rmsb", [128, DM], F32)
    rmsr = al("rmsr", [1, DM], F32)
    onesr = al("onesr", [1, 128], F32)
    konst = al("konst", [128, 4], F32)
    identf = al("identf_s", [128, 128], F32)
    gw = al("gw", [128, 8 * NE], F32)
    z1b = al("z1b", [128, 2 * DM], F32)
    sqb = al("sqb", [128, DM], F32)
    rcol = al("rcol", [128, 2 * 4], F32)
    yf = al("yf", [128, DM], F32)
    yT32 = al("yT32", [128, DM], F32)
    ytb = al("ytb", [128, 8 * 1024], BF16)
    lgb = al("lgb", [128, 8 * 8], F32)
    gcol = al("gcol", [128, 8], F32)
    wf = al("wf", [128, 8 * NE], F32)
    w1j = al("w1j", [128, 2 * 8 * 128], BF16)
    w2j = al("w2j", [128, 2 * 8 * 128], BF16)
    w3b = al("w3b", [128, NJ * DM], BF16)
    hb = al("hb", [128, NJ * 1024], BF16)
    silb = al("silb", [128, 2 * 512], F32)
    outs = al("outs", [128, 4 * DM], F32)
    combb = al("combb", [128, 2 * DM], F32)
    z2t = al("z2t", [128, 2 * DM], F32)
    ep0 = al("ep0b", [128, 2 * DM], F32)
    ep1 = al("ep1b", [128, 2 * DM], F32)
    mcol = al("mcolb", [128, 2 * 8], F32)
    pu0 = PV(P0, 0, 512)
    pu1 = PV(P0, 512, 512)
    pg0 = PV(P0, 1024, 512)
    pg1 = PV(P0, 1536, 512)
    po0 = PV(P1, 0, 1024)
    po1 = PV(P1, 1024, 1024)
    pu = [pu0, pu1]
    pg = [pg0, pg1]
    po = [po0, po1]
    ps4 = [pu0, pu1, pg0, pg1]

    z1t4 = z1.rearrange("(tq p) m -> tq p m", p=128)
    z2t4 = z2.rearrange("(tq p) m -> tq p m", p=128)
    comb4 = comb.rearrange("(tq p) m -> tq p m", p=128)
    W1T4 = W1T.rearrange("e (k p) n -> e p k n", p=128)
    W2T4 = W2T.rearrange("e (k p) n -> e p k n", p=128)
    W3T4 = W3T.rearrange("e (j p) m -> e p j m", p=128)

    p.on("sync", lambda e: e.dma_start(out=gw[:, :].rearrange("p (k e) -> p k e", k=8),
                                       in_=gwT.rearrange("(k p) e -> p k e", p=128)), inc=("LG", 16))
    p.on("sync", lambda e: e.dma_start(out=rmsr[:, :], in_=rmsw), inc=("LG", 16))
    p.on("sync", lambda e: e.dma_start(out=identf[:, :], in_=ident_ap), inc=("LG", 16))
    p.on("vector", lambda e: e.memset(konst[:, 0:1], RMS_EPS))
    p.on("vector", lambda e: e.memset(konst[:, 1:2], -0.5))
    p.on("vector", lambda e: e.memset(konst[:, 2:3], LN_EPS))
    p.on("vector", lambda e: e.memset(onesr[:, :], 1.0), inc=("MST", 1))
    for f in range(2):
        p.on("tensor", lambda e, f=f: e.matmul(
            po0[:, f * 512:(f + 1) * 512], onesr[0:1, :], rmsr[0:1, f * 512:(f + 1) * 512],
            start=True, stop=True), wait=[("MST", 1), ("LG", 48)], inc=("MMD", 1))
    p.on("vector", lambda e: e.tensor_copy(rmsb[:, :], po0[:, :]), wait=[("MMD", 2)], inc=("RMB", 1))

    NQ = 4
    TQ = 8
    vst_slot = [0, 0]
    sil_slot = [0, 0]
    outs_slot = [0, 0, 0, 0]

    def evac_src(k):
        return ps4[k // 2][:, (k % 2) * 128:(k % 2) * 128 + 128]

    def emit_b1(q):
        for T in range(TQ):
            tq = q * TQ + T
            b2 = tq % 2
            wld = [("YR", p.cnt("YR") - 1)] if tq >= 2 else None
            lz_tgt = p.on("sync", lambda e, tq=tq, b2=b2: e.dma_start(
                out=z1b[:, b2 * DM:(b2 + 1) * DM], in_=z1t4[tq]),
                wait=wld, inc=("LZ%d" % b2, 16))
            wsq = [("LZ%d" % b2, lz_tgt), ("EVV", p.cnt("EVV"))]
            p.on("scalar", lambda e, b2=b2: e.activation(
                sqb[:, :], z1b[:, b2 * DM:(b2 + 1) * DM], AF.Square),
                wait=wsq, inc=("SQD", 1))
            p.on("vector", lambda e, b2=b2: e.reduce_sum(
                rcol[:, b2 * 4:b2 * 4 + 1], sqb[:, :], AX.X),
                wait=[("SQD", p.cnt("SQD"))])
            p.on("vector", lambda e, b2=b2: e.tensor_scalar(
                rcol[:, b2 * 4:b2 * 4 + 1], rcol[:, b2 * 4:b2 * 4 + 1], 1.0 / DM, None, ALU.mult),
                inc=("EVV", 1))
            p.on("scalar", lambda e, b2=b2: e.activation(
                rcol[:, b2 * 4 + 1:b2 * 4 + 2], rcol[:, b2 * 4:b2 * 4 + 1], AF.Ln, bias=konst[:, 0:1]),
                wait=[("EVV", p.cnt("EVV"))])
            p.on("scalar", lambda e, b2=b2: e.activation(
                rcol[:, b2 * 4 + 1:b2 * 4 + 2], rcol[:, b2 * 4 + 1:b2 * 4 + 2], AF.Exp, scale=konst[:, 1:2]),
                inc=("RSD", 1))
            wy = [("RSD", p.cnt("RSD")), ("RMB", 1)]
            if tq >= 1:
                wy.append(("MTT", p.cnt("MTT")))
            p.on("vector", lambda e, b2=b2: e.tensor_scalar(
                yf[:, :], z1b[:, b2 * DM:(b2 + 1) * DM], rcol[:, b2 * 4 + 1:b2 * 4 + 2], None, ALU.mult),
                wait=wy)
            yr_tgt = p.on("vector", lambda e: e.tensor_mul(yf[:, :], yf[:, :], rmsb[:, :]),
                          inc=("YR", 1))
            wtr = [("YR", yr_tgt)]
            if tq >= 1:
                wtr.append(("EVF", p.cnt("EVF")))
                wtr.append(("YQB", p.cnt("YQB")))
            if q >= 1 and T == 0:
                wtr.append(("VST", p.cnt("VST")))
            for k in range(8):
                p.on("tensor", lambda e, k=k: e.transpose(
                    evac_src(k), yf[:, k * 128:(k + 1) * 128], identf[:, :]),
                    wait=(wtr if k == 0 else None),
                    inc=(("MTT", 1) if k == 7 else None))
            mtt_tgt = p.cnt("MTT")
            for k in range(8):
                p.on("vector", lambda e, k=k: e.tensor_copy(
                    yT32[:, k * 128:(k + 1) * 128], evac_src(k)),
                    wait=([("MTT", mtt_tgt)] if k == 0 else None),
                    inc=(("EVF", 1) if k == 7 else None))
            wyq = [("MTT", mtt_tgt), ("EVF", p.cnt("EVF"))]
            if q >= 1 and T == 0:
                wyq.append(("MMUG", p.cnt("MMUG")))
            for k in range(8):
                p.on("scalar", lambda e, k=k, T=T: e.activation(
                    ytb[:, k * 1024 + T * 128:k * 1024 + (T + 1) * 128], evac_src(k), AF.Copy),
                    wait=(wyq if k == 0 else None),
                    inc=(("YQB", 1) if k == 7 else None))
            evf_tgt = p.cnt("EVF")
            wlg = [("EVF", evf_tgt), ("LGC", p.cnt("LGC"))]
            if q >= 1 and T == 0:
                wlg.append(("VEV", p.cnt("VEV")))
            for k in range(8):
                p.on("tensor", lambda e, k=k: e.matmul(
                    po1[:, 0:8], yT32[:, k * 128:(k + 1) * 128], gw[:, k * NE:(k + 1) * NE],
                    start=(k == 0), stop=(k == 7)),
                    wait=(wlg if k == 0 else None),
                    inc=(("MML", 1) if k == 7 else None))
            p.on("vector", lambda e: e.tensor_copy(lgb[:, 0:8], po1[:, 0:8]),
                 wait=[("MML", p.cnt("MML"))], inc=("LGC", 1))
            p.on("vector", lambda e: e.reduce_max(gcol[:, 0:1], lgb[:, 0:8], AX.X))
            p.on("vector", lambda e: e.tensor_scalar(
                lgb[:, 8:16], lgb[:, 0:8], gcol[:, 0:1], None, ALU.is_ge))
            p.on("vector", lambda e: e.scalar_tensor_tensor(
                lgb[:, 16:24], lgb[:, 8:16], -1e30, lgb[:, 0:8], ALU.mult, ALU.add))
            p.on("vector", lambda e: e.reduce_max(gcol[:, 1:2], lgb[:, 16:24], AX.X))
            p.on("vector", lambda e: e.tensor_scalar(
                lgb[:, 24:32], lgb[:, 0:8], gcol[:, 1:2], None, ALU.is_ge))
            p.on("vector", lambda e: e.tensor_scalar(
                gcol[:, 2:3], gcol[:, 0:1], -1.0, None, ALU.mult), inc=("GX1", 1))
            p.on("scalar", lambda e: e.activation(
                lgb[:, 32:40], lgb[:, 0:8], AF.Exp, bias=gcol[:, 2:3]),
                wait=[("GX1", p.cnt("GX1"))], inc=("GX2", 1))
            p.on("vector", lambda e: e.tensor_mul(
                lgb[:, 40:48], lgb[:, 32:40], lgb[:, 24:32]),
                wait=[("GX2", p.cnt("GX2"))])
            p.on("vector", lambda e: e.reduce_sum(gcol[:, 3:4], lgb[:, 40:48], AX.X))
            p.on("vector", lambda e: e.reciprocal(gcol[:, 4:5], gcol[:, 3:4]))
            p.on("vector", lambda e, T=T: e.tensor_scalar(
                wf[:, T * 8:(T + 1) * 8], lgb[:, 40:48], gcol[:, 4:5], None, ALU.mult),
                inc=("WFD", 1))

    def emit_experts(q, yqb_q, evf_q):
        for ex in range(NE):
            for j in range(NJ):
                sl = j % 2
                wlw = [("MMUG", mmug_slot[sl])] if mmug_slot[sl] > 0 else None
                lw1_tgt = p.on("gpsimd", lambda e, ex=ex, j=j, sl=sl: e.dma_start(
                    out=w1j[:, sl * 1024:(sl + 1) * 1024].rearrange("p (k n) -> p k n", k=8),
                    in_=W1T4[ex, :, :, j * 128:(j + 1) * 128]),
                    wait=wlw, inc=("LW1%d" % sl, 16))
                lw2_tgt = p.on("gpsimd", lambda e, ex=ex, j=j, sl=sl: e.dma_start(
                    out=w2j[:, sl * 1024:(sl + 1) * 1024].rearrange("p (k n) -> p k n", k=8),
                    in_=W2T4[ex, :, :, j * 128:(j + 1) * 128]),
                    inc=("LW2%d" % sl, 16))
                wmm = [("LW1%d" % sl, lw1_tgt), ("LW2%d" % sl, lw2_tgt),
                       ("YQB", yqb_q), ("EVF", evf_q)]
                if vst_slot[j % 2] > 0:
                    wmm.append(("VST", vst_slot[j % 2]))
                for f in range(2):
                    for k in range(8):
                        p.on("tensor", lambda e, j=j, sl=sl, f=f, k=k: e.matmul(
                            pu[j % 2][:, :], w1j[:, sl * 1024 + k * 128:sl * 1024 + (k + 1) * 128],
                            ytb[:, k * 1024 + f * 512:k * 1024 + (f + 1) * 512],
                            start=(k == 0), stop=(k == 7)),
                            wait=(wmm if (f == 0 and k == 0) else None))
                    for k in range(8):
                        p.on("tensor", lambda e, j=j, sl=sl, f=f, k=k: e.matmul(
                            pg[j % 2][:, :], w2j[:, sl * 1024 + k * 128:sl * 1024 + (k + 1) * 128],
                            ytb[:, k * 1024 + f * 512:k * 1024 + (f + 1) * 512],
                            start=(k == 0), stop=(k == 7)),
                            inc=(("MMUG", 1) if (f == 1 and k == 7) else None))
                mmug_slot[sl] = p.cnt("MMUG")
                mmug_j = p.cnt("MMUG")
                for f in range(2):
                    wsil = [("MMUG", mmug_j)]
                    if sil_slot[f] > 0:
                        wsil.append(("VST", sil_slot[f]))
                    p.on("scalar", lambda e, j=j, f=f: e.activation(
                        silb[:, f * 512:(f + 1) * 512], pu[j % 2][:, :], AF.Silu),
                        wait=wsil, inc=("SIL", 1))
                    vst_t = p.on("vector", lambda e, j=j, f=f: e.tensor_mul(
                        hb[:, j * 1024 + f * 512:j * 1024 + (f + 1) * 512],
                        silb[:, f * 512:(f + 1) * 512], pg[j % 2][:, :]),
                        wait=[("SIL", p.cnt("SIL"))], inc=("VST", 1))
                    sil_slot[f] = vst_t
                vst_slot[j % 2] = p.cnt("VST")
            vst_eq = p.cnt("VST")
            wl3 = [("MMO", p.cnt("MMO"))] if p.cnt("MMO") > 0 else None
            lw3_tgt = p.on("gpsimd", lambda e, ex=ex: e.dma_start(
                out=w3b[:, :].rearrange("p (j m) -> p j m", j=NJ),
                in_=W3T4[ex]),
                wait=wl3, inc=("LW3", 16))
            for ob in range(4):
                wob = [("VST", vst_eq), ("LW3", lw3_tgt)]
                if p.cnt("VEV") > 0:
                    wob.append(("VEV", p.cnt("VEV")))
                first = True
                for j in range(NJ):
                    for ts in range(2):
                        for f in range(2):
                            p.on("tensor", lambda e, j=j, ob=ob, ts=ts, f=f: e.matmul(
                                po[ts][:, f * 512:(f + 1) * 512],
                                hb[:, j * 1024 + ob * 256 + ts * 128:j * 1024 + ob * 256 + (ts + 1) * 128],
                                w3b[:, j * 1024 + f * 512:j * 1024 + (f + 1) * 512],
                                start=(j == 0), stop=(j == NJ - 1)),
                                wait=(wob if first else None),
                                inc=(("MMO", 1) if (j == NJ - 1 and ts == 1 and f == 1) else None))
                            first = False
                mmo_ob = p.cnt("MMO")
                for ts in range(2):
                    osl = (ob * 2 + ts) % 4
                    T = ob * 2 + ts
                    wev = [("MMO", mmo_ob), ("WFD", wfd_q[q])]
                    if outs_slot[osl] > 0:
                        wev.append(("CMB%d" % osl, outs_slot[osl]))
                    p.on("vector", lambda e, ts=ts, osl=osl, T=T, ex=ex: e.tensor_scalar(
                        outs[:, osl * DM:(osl + 1) * DM], po[ts][:, :],
                        wf[:, T * 8 + ex:T * 8 + ex + 1], None, ALU.mult),
                        wait=wev, inc=("VEV", 1))
                    vev_now = p.cnt("VEV")
                    cmb_t = p.on("gpsimd", lambda e, q=q, ob=ob, ts=ts, osl=osl, ex=ex: e.dma_start(
                        out=comb4[q * 8 + ob * 2 + ts],
                        in_=outs[:, osl * DM:(osl + 1) * DM],
                        accum_op=(ALU.bypass if ex == 0 else ALU.add)),
                        wait=[("VEV", vev_now)], inc=("CMB%d" % osl, 16))
                    outs_slot[osl] = cmb_t

    def emit_b3(q, cmb_q):
        for T in range(TQ):
            tq = q * TQ + T
            b2 = tq % 2
            i3 = p.cnt("EP3")
            war3 = [("EP3", i3 - 1)] if i3 >= 2 else None
            wlc = [("CMB%d" % s, cmb_q[s]) for s in range(4)] + (war3 or [])
            lc_tgt = p.on("sync", lambda e, tq=tq, b2=b2: e.dma_start(
                out=combb[:, b2 * DM:(b2 + 1) * DM], in_=comb4[tq]),
                wait=wlc, inc=("LC%d" % b2, 16))
            lzr_tgt = p.on("sync", lambda e, tq=tq, b2=b2: e.dma_start(
                out=z2t[:, b2 * DM:(b2 + 1) * DM], in_=z1t4[tq]),
                wait=war3, inc=("LZR%d" % b2, 16))
            wpre = [("LC%d" % b2, lc_tgt), ("LZR%d" % b2, lzr_tgt)]
            if zs2_slot[b2] > 0:
                wpre.append(("ZS2%d" % b2, zs2_slot[b2]))
            p.on("vector", lambda e, b2=b2: e.tensor_add(
                ep0[:, b2 * DM:(b2 + 1) * DM], z2t[:, b2 * DM:(b2 + 1) * DM],
                combb[:, b2 * DM:(b2 + 1) * DM]),
                wait=wpre)
            p.on("vector", lambda e, b2=b2: e.reduce_sum(
                mcol[:, b2 * 8:b2 * 8 + 1], ep0[:, b2 * DM:(b2 + 1) * DM], AX.X))
            p.on("vector", lambda e, b2=b2: e.tensor_scalar(
                mcol[:, b2 * 8 + 1:b2 * 8 + 2], mcol[:, b2 * 8:b2 * 8 + 1], 1.0 / DM, None, ALU.mult))
            p.on("vector", lambda e, b2=b2: e.tensor_scalar(
                ep1[:, b2 * DM:(b2 + 1) * DM], ep0[:, b2 * DM:(b2 + 1) * DM],
                mcol[:, b2 * 8 + 1:b2 * 8 + 2], None, ALU.subtract),
                inc=("EZC", 1))
            p.on("scalar", lambda e, b2=b2: e.activation(
                ep0[:, b2 * DM:(b2 + 1) * DM], ep1[:, b2 * DM:(b2 + 1) * DM], AF.Square),
                wait=[("EZC", p.cnt("EZC"))], inc=("SQ3", 1))
            p.on("vector", lambda e, b2=b2: e.reduce_sum(
                mcol[:, b2 * 8 + 2:b2 * 8 + 3], ep0[:, b2 * DM:(b2 + 1) * DM], AX.X),
                wait=[("SQ3", p.cnt("SQ3"))])
            p.on("vector", lambda e, b2=b2: e.tensor_scalar(
                mcol[:, b2 * 8 + 2:b2 * 8 + 3], mcol[:, b2 * 8 + 2:b2 * 8 + 3], 1.0 / DM, None, ALU.mult),
                inc=("EV3", 1))
            p.on("scalar", lambda e, b2=b2: e.activation(
                mcol[:, b2 * 8 + 3:b2 * 8 + 4], mcol[:, b2 * 8 + 2:b2 * 8 + 3], AF.Ln, bias=konst[:, 2:3]),
                wait=[("EV3", p.cnt("EV3"))])
            p.on("scalar", lambda e, b2=b2: e.activation(
                mcol[:, b2 * 8 + 3:b2 * 8 + 4], mcol[:, b2 * 8 + 3:b2 * 8 + 4], AF.Exp, scale=konst[:, 1:2]),
                inc=("RS3", 1))
            p.on("vector", lambda e, b2=b2: e.tensor_scalar(
                ep1[:, b2 * DM:(b2 + 1) * DM], ep1[:, b2 * DM:(b2 + 1) * DM],
                mcol[:, b2 * 8 + 3:b2 * 8 + 4], None, ALU.mult),
                wait=[("RS3", p.cnt("RS3"))], inc=("EP3", 1))
            zs2_t = p.on("sync", lambda e, tq=tq, b2=b2: e.dma_start(
                out=z2t4[tq], in_=ep1[:, b2 * DM:(b2 + 1) * DM]),
                wait=[("EP3", p.cnt("EP3"))], inc=("ZS2%d" % b2, 16))
            zs2_slot[b2] = zs2_t

    mmug_slot = [0, 0]
    zs2_slot = [0, 0]
    wfd_q = {}
    yqb_all = {}
    evf_all = {}
    emit_b1(0)
    wfd_q[0] = p.cnt("WFD"); yqb_all[0] = p.cnt("YQB"); evf_all[0] = p.cnt("EVF")
    for q in range(NQ):
        emit_experts(q, yqb_all[q], evf_all[q])
        cmb_q = [p.cnt("CMB%d" % s) for s in range(4)]
        if q + 1 < NQ:
            emit_b1(q + 1)
            wfd_q[q + 1] = p.cnt("WFD"); yqb_all[q + 1] = p.cnt("YQB"); evf_all[q + 1] = p.cnt("EVF")
        emit_b3(q, cmb_q)
    p.wait("sync", [("ZS20", p.cnt("ZS20")), ("ZS21", p.cnt("ZS21"))])


def emit_phase_c0(nc, p, al, d, P0, P1):
    wb = al("wbc0", [128, 8 * DI], BF16)
    zb = al("zbc0", [128, 8 * 4], BF16)
    z2tl = al("z2tl", [3, DM], F32)
    idf = al("idc0", [128, 128], F32)
    ob = al("obc0", [128, NJC * 4], F32)
    ps = PV(P0, 0, 1024)
    pt = [PV(P1, 0, 512), PV(P1, 512, 512)]

    W_in4 = d["W_in"].rearrange("(k p) c -> p k c", p=128)
    p.on("gpsimd", lambda e: e.dma_start(
        out=wb[:, :].rearrange("p (k c) -> p k c", k=8),
        in_=W_in4[:, :, 0:DI]), inc=("LW", 16))
    p.on("sync", lambda e: e.dma_start(out=z2tl[:, :], in_=d["z2"][T_CORE - 3:T_CORE, :]),
         inc=("LZ", 16))
    p.on("sync", lambda e: e.dma_start(out=idf[:, :], in_=d["identd"]), inc=("LZ", 16))
    p.on("vector", lambda e: e.memset(zb[:, :], 0.0), inc=("MS", 1))
    for k in range(8):
        sl = k % 2
        w = [("LZ", 32)] if k <= 1 else [("ZBC", p.cnt("ZBC") - 1)]
        p.on("tensor", lambda e, k=k, sl=sl: e.transpose(
            pt[sl][:, 0:3], z2tl[0:3, k * 128:(k + 1) * 128], idf[0:3, 0:3]),
            wait=w, inc=("TRZ", 1))
        p.on("scalar", lambda e, k=k, sl=sl: e.activation(
            zb[:, k * 4 + 1:k * 4 + 4], pt[sl][:, 0:3], AF.Copy),
            wait=[("TRZ", p.cnt("TRZ")), ("MS", 1)], inc=("ZBC", 1))
    for j in range(NJC):
        sl = (j % 2) * 512
        for k in range(8):
            w = None
            if j == 0 and k == 0:
                w = [("LW", 16), ("ZBC", 8)]
            elif k == 0 and j >= 2:
                w = [("EVC", p.cnt("EVC") - 1)]
            p.on("tensor", lambda e, j=j, k=k, sl=sl: e.matmul(
                ps[:, sl:sl + 4], wb[:, k * DI + j * 128:k * DI + (j + 1) * 128],
                zb[:, k * 4:(k + 1) * 4], start=(k == 0), stop=(k == 7)),
                wait=w, inc=(("MMC", 1) if k == 7 else None))
        p.on("vector", lambda e, j=j, sl=sl: e.tensor_copy(
            ob[:, j * 4:(j + 1) * 4], ps[:, sl:sl + 4]),
            wait=[("MMC", p.cnt("MMC"))], inc=("EVC", 1))
    p.on("sync", lambda e: e.dma_start(out=d["tail_self"], in_=ob[:, :]),
         wait=[("EVC", p.cnt("EVC"))], inc=("TS", 16))
    p.on("gpsimd", lambda e: e.collective_compute(
        "AllGather", ALU.bypass, replica_groups=PAIRS,
        ins=[d["tail_self"].opt()], outs=[d["tail_gat"].opt()]),
        wait=[("TS", 16)], inc=("CCT", 1))
    p.wait("gpsimd", [("CCT", 1)])


def emit_phase_c1(nc, p, al, d, P0, P1):
    z2 = d["z2"]
    W_x = d["W_x"]
    cwc_d = d["cwc"]
    cbc_d = d["cbc"]
    dpc_d = d["dpc"]
    alr_d = d["alr"]
    alc_d = d["alc"]
    Wout_d = d["Wout"]
    cumin_d = d["cumin"]
    sufst_d = d["sufst"]
    masktl_d = d["masktl"]
    ones128_d = d["ones128"]
    preln = d["preln"]
    szsp = d["szsp"]
    hend = d["hend"]
    cst_o = d["cst"]
    sgst_o = d["sgst"]

    wis = al("wis", [128, 2 * 8 * 128], BF16)
    wob = al("wobc", [128, NJC * DM], BF16)
    wxb = al("wxb", [128, NJC * 48], BF16)
    cwc = al("cwcs", [128, NJC * 4], F32)
    cbc = al("cbcs", [128, NJC], F32)
    dpb = al("dpbs", [128, NJC * 128], BF16)
    dpc_s = al("dpcs", [128, NJC], F32)
    onesb = al("onesb", [1, 128], BF16)
    rab = al("rabc", [128, DST], F32)
    cumin = al("cumins", [128, 128], F32)
    sufst = al("sufsts", [128, 128], F32)
    masktl = al("masktls", [128, 128], F32)
    ones128 = al("ones128s", [128, 128], F32)
    identf = al("identfc", [128, 128], F32)
    arow = al("arow", [1, DST], F32)
    acol = al("acolc", [16, 1], F32)
    abc = al("abc", [128, DST], F32)
    onesr = al("onesrc", [1, 128], F32)
    negs = al("negs", [1, 128], F32)
    konst = al("konstc", [128, 4], F32)
    cselc = al("cselc", [128, 1], F32)
    z2g = al("z2g", [128, 4 * DM], F32)
    z2tb = al("z2tb", [128, 8 * 512], BF16)
    xg = al("xg", [128, NJC * 520], BF16)
    zTg = al("zTg", [128, NJC * 512], BF16)
    xcT = al("xcT", [128, NJC * 128], F32)
    xcTb = al("xcTb", [128, NJC * 128], BF16)
    xcb = al("xcb", [128, NJC * 128], BF16)
    dcol = al("dcol", [128, 8], F32)
    bbm = al("bbm", [128, DST], F32)
    bbm2 = al("bbm2", [128, DST], BF16)
    srow = al("srowc", [1, 128], F32)
    srow_loc = al("srowlc", [1, 128], F32)
    slc = al("slcc", [128, 1], F32)
    sgc = al("sgc", [128, 1], F32)
    spv = al("spv", [128, 1], F32)
    dmt = al("dmtc", [128, 128], F32)
    escr = al("escr", [128, 128], BF16)
    g1s = al("g1s", [128, 128], BF16)
    grb = al("grbc", [16, 4], F32)
    accT = al("accTc", [128, 128], F32)
    acbf = al("acbf", [128, 128], BF16)
    ctsb = al("ctsb", [16, 128], F32)
    ctbb = al("ctbb", [16, 128], BF16)
    ctrow = al("ctrowc", [1, DST * 128], BF16)
    cgb = al("cgbc", [16, 128], BF16)
    ebuf = al("ebufc", [16, 128], F32)
    hst = al("hstc", [16, DI], F32)
    hbf = al("hbfc", [16, DI], BF16)
    ysc = al("yscc", [128, NJC * 128], BF16)
    preb = al("prebc", [128, DM], F32)
    cstb = al("cstbc", [128, NCH * DST], F32)
    sgst = al("sgstc", [128, NCH], F32)
    pbig = PV(P0, 0, 2048)
    pmid = PV(P1, 0, 1024)
    psml0 = PV(P1, 1024, 512)
    psml1 = PV(P1, 1536, 512)

    z2t4 = z2.rearrange("(g f p) m -> g p f m", f=4, p=128)
    pre4 = preln.rearrange("(ch p) m -> ch p m", p=128)
    szsp4 = szsp.rearrange("(j p) t -> p j t", p=128)

    W_in4 = d["W_in"].rearrange("(k p) c -> p k c", p=128)
    p.on("gpsimd", lambda e: e.dma_start(
        out=wob[:, :].rearrange("p (j m) -> p j m", j=NJC),
        in_=Wout_d.rearrange("(j p) m -> p j m", p=128)), inc=("LWC", 16))
    p.on("gpsimd", lambda e: e.dma_start(
        out=wxb[:, :].rearrange("p (j c) -> p j c", j=NJC),
        in_=W_x.rearrange("(j p) c -> p j c", p=128)), inc=("LWC", 16))
    for dst, srcd in ((cwc, cwc_d), (cbc, cbc_d), (cumin, cumin_d),
                      (sufst, sufst_d), (masktl, masktl_d), (ones128, ones128_d),
                      (identf, d["identd"]), (arow, alr_d), (acol, alc_d),
                      (dpc_s, dpc_d), (cselc, d["csel"])):
        p.on("sync", lambda e, dst=dst, srcd=srcd: e.dma_start(out=dst[:, :], in_=srcd),
             inc=("LDC", 16))
    ldc_tot = p.cnt("LDC")
    p.on("gpsimd", lambda e: e.dma_start(
        out=xg[:, :].rearrange("p (j c) -> p j c", j=NJC)[:, :, 0:4],
        in_=d["tail_gat"][0:128, :].rearrange("p (j c) -> p j c", j=NJC)), inc=("LWC", 16))
    p.on("vector", lambda e: e.tensor_scalar(
        xg[:, :].rearrange("p (j c) -> p j c", j=NJC)[:, :, 0:4],
        xg[:, :].rearrange("p (j c) -> p j c", j=NJC)[:, :, 0:4],
        cselc[:, 0:1], None, ALU.mult),
        wait=[("LWC", 48), ("LDC", ldc_tot)], inc=("TSC", 1))
    p.on("vector", lambda e: e.memset(onesr[:, :], 1.0))
    p.on("vector", lambda e: e.memset(konst[:, 0:1], -1.0))
    p.on("vector", lambda e: e.memset(spv[:, :], 0.0))
    p.on("vector", lambda e: e.memset(hst[:, :], 0.0), inc=("MST", 1))
    p.on("vector", lambda e: e.tensor_copy(hbf[:, :], hst[:, :]), inc=("HBF", 1))
    p.on("scalar", lambda e: e.activation(arow[:, :], arow[:, :], AF.Exp),
         wait=[("LDC", ldc_tot)], inc=("ARP", 1))
    p.on("vector", lambda e: e.tensor_scalar(arow[:, :], arow[:, :], -1.0, None, ALU.mult),
         wait=[("ARP", 1), ("MST", 1)], inc=("ARN", 1))
    p.on("tensor", lambda e: e.matmul(
        psml0[:, 0:DST], onesr[0:1, :], arow[0:1, :], start=True, stop=True),
        wait=[("ARN", 1)], inc=("MMA", 1))
    p.on("vector", lambda e: e.tensor_copy(abc[:, :], psml0[:, 0:DST]),
         wait=[("MMA", 1)], inc=("ABC", 1))
    p.on("scalar", lambda e: e.activation(acol[:, :], acol[:, :], AF.Exp),
         wait=[("LDC", ldc_tot)], inc=("ACX", 1))
    p.on("vector", lambda e: e.tensor_scalar(acol[:, :], acol[:, :], -1.0, None, ALU.mult),
         wait=[("ACX", 1)], inc=("ACL", 1))
    for j in range(NJC):
        p.on("vector", lambda e, j=j: e.tensor_scalar(
            dpb[:, j * 128:(j + 1) * 128], ones128[:, :], dpc_s[:, j:j + 1], None, ALU.mult),
            wait=([("LDC", ldc_tot)] if j == 0 else None),
            inc=(("DPB", 1) if j == NJC - 1 else None))
    p.on("vector", lambda e: e.tensor_copy(onesb[:, :], onesr[:, :]), inc=("ONB", 1))

    lwc_tot = 48
    spl_slot = [0, 0]
    wis_slot = [0, 0]

    for g in range(NGR):
        lz_w = []
        if g >= 1:
            lz_w = [("TRP", p.cnt("TRP")), ("PRE", p.cnt("PRE"))]
        lz_tgt = p.on("sync", lambda e, g=g: e.dma_start(
            out=z2g[:, :].rearrange("p (f m) -> p f m", f=4), in_=z2t4[g]),
            wait=(lz_w or None), inc=("LZ2", 16))
        for f4 in range(4):
            for k in range(8):
                i8 = f4 * 8 + k
                sl = i8 % 2
                psx = [psml0, psml1][sl]
                wtr = None
                if i8 == 0:
                    wtr = [("LZ2", lz_tgt)]
                    if g == 0:
                        wtr.append(("ABC", 1))
                    if g >= 1:
                        wtr += [("EVZ", p.cnt("EVZ")), ("CTS", p.cnt("CTS")),
                                ("DMB", p.cnt("DMB")), ("GS2", p.cnt("GS2")),
                                ("EBF", p.cnt("EBF")), ("SRW", p.cnt("SRW"))]
                elif i8 >= 2:
                    wtr = [("EVZ", p.cnt("EVZ") - 1)]
                p.on("tensor", lambda e, f4=f4, k=k, psx=psx: e.transpose(
                    psx[:, 128:256],
                    z2g[:, f4 * DM + k * 128:f4 * DM + (k + 1) * 128], identf[:, :]),
                    wait=wtr, inc=("TRP", 1))
                p.on("scalar", lambda e, f4=f4, k=k, psx=psx: e.activation(
                    z2tb[:, k * 512 + f4 * 128:k * 512 + (f4 + 1) * 128],
                    psx[:, 128:256], AF.Copy),
                    wait=[("TRP", p.cnt("TRP"))], inc=("EVZ", 1))
        evz_g = p.cnt("EVZ")
        if g >= 1:
            p.on("vector", lambda e: e.tensor_copy(
                xg[:, :].rearrange("p (j c) -> p j c", j=NJC)[:, :, 1:4],
                xg[:, :].rearrange("p (j c) -> p j c", j=NJC)[:, :, 513:516]),
                wait=[("CNV", p.cnt("CNV"))], inc=("XGC", 1))
        for o in range(32):
            sl = o % 2
            wlw = [("MMX", wis_slot[sl])] if wis_slot[sl] > 0 else None
            lwi_tgt = p.on("gpsimd", lambda e, o=o, sl=sl: e.dma_start(
                out=wis[:, sl * 1024:(sl + 1) * 1024].rearrange("p (k c) -> p k c", k=8),
                in_=W_in4[:, :, o * 128:(o + 1) * 128]),
                wait=wlw, inc=("LWI%d" % sl, 16))
            wmm = [("LWI%d" % sl, lwi_tgt)]
            if o == 0:
                wmm += [("EVZ", evz_g), ("LWC", lwc_tot), ("TSC", 1)]
            elif o >= 2:
                wmm += [("EVX", p.cnt("EVX") - 1)]
            for k in range(8):
                p.on("tensor", lambda e, o=o, k=k, sl=sl: e.matmul(
                    pmid[:, sl * 512:(sl + 1) * 512],
                    wis[:, sl * 1024 + k * 128:sl * 1024 + (k + 1) * 128],
                    z2tb[:, k * 512:(k + 1) * 512],
                    start=(k == 0), stop=(k == 7)),
                    wait=(wmm if k == 0 else None),
                    inc=(("MMX", 1) if k == 7 else None))
            wis_slot[sl] = p.cnt("MMX")
            wev = [("MMX", p.cnt("MMX"))]
            if o < 16:
                if g >= 1 and o == 0:
                    wev.append(("XGC", p.cnt("XGC")))
                    wev.append(("CNV", p.cnt("CNV")))
                p.on("scalar", lambda e, o=o, sl=sl: e.activation(
                    xg[:, o * 520 + 4:o * 520 + 4 + 512], pmid[:, sl * 512:(sl + 1) * 512], AF.Copy),
                    wait=wev, inc=("EVX", 1))
            else:
                if o == 16:
                    wev.append(("YP3", p.cnt("YP3")))
                    wev.append(("SPL", p.cnt("SPL")))
                p.on("scalar", lambda e, o=o, sl=sl: e.activation(
                    zTg[:, (o - 16) * 512:(o - 15) * 512], pmid[:, sl * 512:(sl + 1) * 512], AF.Copy),
                    wait=wev, inc=("EVX", 1))
        evx_g = p.cnt("EVX")
        p.on("scalar", lambda e: e.activation(zTg[:, :], zTg[:, :], AF.Silu),
             wait=[("EVX", evx_g)], inc=("SZD", 1))
        szd_g = p.cnt("SZD")
        p.on("sync", lambda e, g=g: e.dma_start(
            out=szsp4[:, :, g * 512:(g + 1) * 512],
            in_=zTg[:, :].rearrange("p (j t) -> p j t", j=NJC)),
            wait=[("SZD", szd_g)], inc=("SPL", 16))

        for cc in range(4):
            ch = g * 4 + cc
            wcv = [("EVX", evx_g), ("LDC", ldc_tot), ("TSC", 1)]
            if p.cnt("XDB") > 0:
                wcv.append(("XDB", p.cnt("XDB")))
                wcv.append(("TRX", p.cnt("TRX")))
                wcv.append(("YP1", p.cnt("YP1")))
            for j in range(NJC):
                base = j * 520 + cc * 128
                p.on("scalar", lambda e, j=j, base=base: e.activation(
                    xcT[:, j * 128:(j + 1) * 128], xg[:, base + 1:base + 129],
                    AF.Identity, bias=cbc[:, j:j + 1], scale=cwc[:, j * 4 + 0:j * 4 + 1]),
                    wait=(wcv if j == 0 else None), inc=("CV0", 1))
                for k in range(1, 4):
                    p.on("vector", lambda e, j=j, k=k, base=base: e.scalar_tensor_tensor(
                        xcT[:, j * 128:(j + 1) * 128], xg[:, base + 1 + k:base + 129 + k],
                        cwc[:, j * 4 + k:j * 4 + k + 1], xcT[:, j * 128:(j + 1) * 128],
                        ALU.mult, ALU.add),
                        wait=[("CV0", p.cnt("CV0"))] if k == 1 else [("CNV", p.cnt("CNV"))],
                        inc=("CNV", 1))
            p.on("scalar", lambda e: e.activation(xcT[:, :], xcT[:, :], AF.Silu),
                 wait=[("CNV", p.cnt("CNV"))], inc=("SCV", 1))
            p.on("vector", lambda e: e.tensor_copy(xcTb[:, :], xcT[:, :]),
                 wait=[("SCV", p.cnt("SCV")), ("DPB", 1)], inc=("XCB", 1))
            wxd = [("XCB", p.cnt("XCB"))]
            if p.cnt("CSD") > 0:
                wxd += [("DLT", p.cnt("DLT")), ("BBD", p.cnt("BBD")), ("CSD", p.cnt("CSD"))]
            for j in range(NJC):
                p.on("tensor", lambda e, j=j: e.matmul(
                    psml0[:, 0:48], xcTb[:, j * 128:(j + 1) * 128],
                    wxb[:, j * 48:(j + 1) * 48],
                    start=(j == 0), stop=(j == NJC - 1)),
                    wait=(wxd if j == 0 else None),
                    inc=(("XDB", 1) if j == NJC - 1 else None))
            xdb_c = p.cnt("XDB")
            p.on("scalar", lambda e: e.activation(
                dcol[:, 0:1], psml0[:, 0:1], AF.Abs),
                wait=[("XDB", xdb_c)], inc=("DAB", 1))
            p.on("scalar", lambda e: e.activation(
                dcol[:, 1:2], dcol[:, 0:1], AF.Exp, scale=konst[:, 0:1]),
                wait=[("DAB", p.cnt("DAB"))], inc=("DXP", 1))
            p.on("scalar", lambda e: e.activation(
                dcol[:, 2:3], dcol[:, 1:2], AF.Ln, bias=1.0), inc=("DLN", 1))
            p.on("scalar", lambda e: e.activation(
                dcol[:, 3:4], psml0[:, 0:1], AF.Relu), inc=("DRL", 1))
            p.on("vector", lambda e: e.tensor_add(
                dcol[:, 4:5], dcol[:, 3:4], dcol[:, 2:3]),
                wait=[("DRL", p.cnt("DRL"))], inc=("DLT", 1))
            dlt_c = p.cnt("DLT")
            p.on("vector", lambda e: e.tensor_scalar(
                bbm[:, :], psml0[:, 1:17], dcol[:, 4:5], None, ALU.mult), inc=("BBD", 1))
            p.on("vector", lambda e, ch=ch: e.tensor_copy(
                cstb[:, ch * DST:(ch + 1) * DST], psml0[:, 17:33]),
                inc=("CSD", 1))
            wcs = [("DLT", dlt_c), ("BBD", p.cnt("BBD")), ("CSD", p.cnt("CSD"))]
            if p.cnt("SGD") > 0:
                wcs += [("SGD", p.cnt("SGD")), ("RRD", p.cnt("RRD")), ("GDC", p.cnt("GDC"))]
            p.on("tensor", lambda e: e.matmul(
                psml0[:, 64:65], cumin[:, :], dcol[:, 4:5], start=True, stop=True),
                wait=wcs, inc=("MMS", 1))
            p.on("tensor", lambda e: e.matmul(
                psml0[:, 65:66], sufst[:, :], dcol[:, 4:5], start=True, stop=True),
                inc=("MMS", 1))
            p.on("tensor", lambda e: e.matmul(
                psml0[:, 66:67], ones128[:, :], dcol[:, 4:5], start=True, stop=True),
                inc=("MMS", 1))
            mms_c = p.cnt("MMS")
            p.on("vector", lambda e: e.tensor_copy(slc[:, :], psml0[:, 64:65]),
                 wait=[("MMS", mms_c)], inc=("SGW", 1))
            p.on("vector", lambda e: e.tensor_tensor(
                sgc[:, :], psml0[:, 64:65], spv[:, :], ALU.add),
                inc=("SGW", 1))
            p.on("vector", lambda e, ch=ch: e.tensor_copy(sgst[:, ch:ch + 1], sgc[:, :]))
            p.on("vector", lambda e: e.tensor_tensor(
                spv[:, :], spv[:, :], psml0[:, 66:67], ALU.add), inc=("SGD", 1))
            p.on("vector", lambda e: e.tensor_scalar(
                rab[:, :], abc[:, :], psml0[:, 65:66], None, ALU.mult),
                wait=[("MMS", mms_c), ("BB2", p.cnt("BB2"))], inc=("RRD", 1))
            p.on("scalar", lambda e: e.activation(
                rab[:, :], rab[:, :], AF.Exp),
                wait=[("RRD", p.cnt("RRD"))], inc=("ERB", 1))
            p.on("vector", lambda e: e.tensor_tensor(
                bbm2[:, :], rab[:, :], bbm[:, :], ALU.mult),
                wait=[("ERB", p.cnt("ERB"))], inc=("BB2", 1))
            wtx = [("MMS", mms_c), ("CSD", p.cnt("CSD")), ("SGW", p.cnt("SGW"))]
            if p.cnt("PRE") > 0:
                wtx.append(("PRE", p.cnt("PRE")))
            if p.cnt("SRW") > 0:
                wtx += [("SRW", p.cnt("SRW")), ("CTS", p.cnt("CTS"))]
            p.on("tensor", lambda e: e.transpose(
                psml1[0:1, 0:128], sgc[:, :], identf[:, :]),
                wait=wtx, inc=("TRX", 1))
            p.on("tensor", lambda e: e.transpose(
                psml1[0:1, 128:256], slc[:, :], identf[:, :]),
                inc=("TRX", 1))
            p.on("tensor", lambda e, ch=ch: e.transpose(
                psml1[0:16, 256:384], cstb[:, ch * DST:(ch + 1) * DST], identf[:, :]),
                inc=("TRX", 1))
            for j in range(NJC):
                p.on("tensor", lambda e, j=j: e.transpose(
                    pbig[:, j * 128:(j + 1) * 128], xcT[:, j * 128:(j + 1) * 128],
                    identf[:, :]),
                    inc=(("TRX", 1) if j == NJC - 1 else None))
            trx_c = p.cnt("TRX")
            p.on("vector", lambda e: e.tensor_copy(srow[:, :], psml1[0:1, 0:128]),
                 wait=[("TRX", trx_c)], inc=("SRW", 1))
            p.on("vector", lambda e: e.tensor_copy(srow_loc[:, :], psml1[0:1, 128:256]),
                 inc=("SRW", 1))
            p.on("vector", lambda e: e.tensor_copy(ctsb[:, :], psml1[0:16, 256:384]),
                 inc=("CTS", 1))
            cts_c = p.cnt("CTS")
            p.on("scalar", lambda e: e.activation(xcb[:, :], pbig[:, :], AF.Copy),
                 wait=[("TRX", trx_c)], inc=("XCC", 1))
            p.on("vector", lambda e: e.tensor_copy(ctbb[:, :], ctsb[:, :]), inc=("CTB", 1))
            p.on("sync", lambda e: e.dma_start(out=ctrow[0:1, :], in_=ctbb[:, :]),
                 wait=[("CTB", p.cnt("CTB")), ("PNM", p.cnt("PNM"))], inc=("CTR", 16))
            ctr_c = p.cnt("CTR")
            p.on("vector", lambda e: e.tensor_scalar(
                negs[:, :], srow[:, :], -1.0, None, ALU.mult),
                wait=[("SRW", p.cnt("SRW")), ("DMB", p.cnt("DMB"))], inc=("LRT", 1))
            p.on("tensor", lambda e: e.matmul(
                psml1[:, 256:384], onesr[0:1, :], srow[0:1, :], start=True, stop=False),
                wait=[("LRT", p.cnt("LRT")), ("CTS", cts_c)])
            p.on("tensor", lambda e: e.matmul(
                psml1[:, 256:384], negs[0:1, :], ones128[0:1, :], start=False, stop=True),
                inc=("MMT", 1))
            p.on("vector", lambda e: e.tensor_tensor(
                dmt[:, :], psml1[:, 256:384], masktl[:, :], ALU.add),
                wait=[("MMT", p.cnt("MMT"))], inc=("DMB", 1))
            dmb_c = p.cnt("DMB")
            for n in range(DST):
                wex = [("GS1", p.cnt("GS1"))]
                if n == 0:
                    wex = [("DMB", dmb_c), ("ABC", 1), ("SCB", p.cnt("SCB"))]
                p.on("scalar", lambda e, n=n: e.activation(
                    escr[:, :], dmt[:, :], AF.Exp, scale=abc[:, n:n + 1]),
                    wait=wex, inc=("EXN", 1))
                wg1 = [("EXN", p.cnt("EXN"))]
                if n == 0:
                    wg1.append(("BBD", p.cnt("BBD")))
                p.on("vector", lambda e, n=n: e.tensor_scalar(
                    g1s[:, :], escr[:, :], bbm[:, n:n + 1], None, ALU.mult),
                    wait=wg1, inc=("GS1", 1))
                psl = psml0 if n % 2 == 0 else psml1
                p.on("tensor", lambda e, n=n, psl=psl: e.matmul(
                    psl[:, 384:512], onesb[0:1, :], ctrow[0:1, n * 128:(n + 1) * 128],
                    start=True, stop=True),
                    wait=([("CTR", ctr_c), ("ONB", 1), ("DMB", dmb_c)] if n <= 1
                          else [("GS2", p.cnt("GS2") - 1)]),
                    inc=("PNM", 1))
                p.on("vector", lambda e, n=n, psl=psl: e.tensor_tensor(
                    g1s[:, :], g1s[:, :], psl[:, 384:512], ALU.mult),
                    wait=[("PNM", p.cnt("PNM"))], inc=("GS2", 1))
                if n == 0:
                    p.on("vector", lambda e: e.tensor_copy(accT[:, :], g1s[:, :]))
                else:
                    p.on("vector", lambda e: e.tensor_tensor(
                        accT[:, :], accT[:, :], g1s[:, :], ALU.add))
            p.on("vector", lambda e: e.tensor_copy(acbf[:, :], accT[:, :]), inc=("SCB", 1))
            scb_c = p.cnt("SCB")
            p.on("tensor", lambda e: e.matmul(
                psml0[0:16, 256:384], arow[0:1, :], srow_loc[0:1, :], start=True, stop=True),
                wait=[("SRW", p.cnt("SRW")), ("CGB", p.cnt("CGB")),
                      ("SCB", scb_c), ("RRD", p.cnt("RRD"))], inc=("PAM", 1))
            p.on("scalar", lambda e: e.activation(
                ebuf[:, :], psml0[0:16, 256:384], AF.Exp),
                wait=[("PAM", p.cnt("PAM"))], inc=("EBF", 1))
            p.on("vector", lambda e: e.tensor_tensor(
                cgb[:, :], ebuf[:, :], ctsb[:, :], ALU.mult),
                wait=[("EBF", p.cnt("EBF"))], inc=("CGB", 1))
            cgb_c = p.cnt("CGB")
            wym = [("SCB", scb_c), ("CGB", cgb_c), ("XCC", p.cnt("XCC")),
                   ("HBF", p.cnt("HBF")), ("YPE", p.cnt("YPE"))]
            for dj in range(NJC):
                p.on("tensor", lambda e, dj=dj: e.matmul(
                    pbig[:, dj * 128:(dj + 1) * 128], xcb[:, dj * 128:(dj + 1) * 128],
                    acbf[:, :], start=True, stop=False),
                    wait=(wym if dj == 0 else None))
                p.on("tensor", lambda e, dj=dj: e.matmul(
                    pbig[:, dj * 128:(dj + 1) * 128], hbf[:, dj * 128:(dj + 1) * 128],
                    cgb[:, :], start=False, stop=True),
                    inc=(("YMM", 1) if dj == NJC - 1 else None))
            ymm_c = p.cnt("YMM")
            p.on("scalar", lambda e: e.activation(ysc[:, :], pbig[:, :], AF.Copy),
                 wait=[("YMM", ymm_c), ("WOM", p.cnt("WOM"))], inc=("YPE", 1))
            p.on("vector", lambda e: e.tensor_tensor(
                xcTb[:, :], xcTb[:, :], dpb[:, :], ALU.mult),
                wait=[("YPE", p.cnt("YPE")), ("XDB", xdb_c)], inc=("YP1", 1))
            p.on("vector", lambda e: e.tensor_tensor(
                ysc[:, :], ysc[:, :], xcTb[:, :], ALU.add), inc=("YP2", 1))
            p.on("vector", lambda e, cc=cc: e.tensor_tensor(
                ysc[:, :], ysc[:, :],
                zTg[:, :].rearrange("p (j t) -> p j t", j=NJC)[:, :, cc * 128:(cc + 1) * 128],
                ALU.mult),
                wait=[("SZD", szd_g)], inc=("YP3", 1))
            yp3_c = p.cnt("YP3")
            wum = [("BB2", p.cnt("BB2")), ("YPE", p.cnt("YPE")), ("XCC", p.cnt("XCC"))]
            for f in range(4):
                p.on("tensor", lambda e, f=f: e.matmul(
                    pbig[0:16, f * 512:(f + 1) * 512], bbm2[:, :],
                    xcb[:, f * 512:(f + 1) * 512], start=True, stop=True),
                    wait=(wum if f == 0 else None),
                    inc=(("UMM", 1) if f == 3 else None))
            umm_c = p.cnt("UMM")
            p.on("vector", lambda e: e.tensor_scalar(
                grb[:, 0:1], acol[:, :], psml0[0:16, 66:67], None, ALU.mult),
                wait=[("ACL", 1)], inc=("GRD", 1))
            p.on("scalar", lambda e: e.activation(
                grb[:, 1:2], grb[:, 0:1], AF.Exp),
                wait=[("GRD", p.cnt("GRD"))], inc=("GDC", 1))
            p.on("vector", lambda e: e.tensor_scalar(
                hst[:, :], hst[:, :], grb[:, 1:2], None, ALU.mult),
                wait=[("GDC", p.cnt("GDC")), ("UMM", umm_c)])
            p.on("vector", lambda e: e.tensor_tensor(
                hst[:, :], hst[:, :], pbig[0:16, :], ALU.add), inc=("HUP", 1))
            p.on("vector", lambda e: e.tensor_copy(hbf[:, :], hst[:, :]),
                 wait=[("YMM", ymm_c)], inc=("HBF", 1))
            wwm = [("YP3", yp3_c), ("UMM", umm_c), ("HUP", p.cnt("HUP"))]
            if p.cnt("PRE") > 0:
                wwm.append(("PRE", p.cnt("PRE")))
            for dj in range(NJC):
                for f in range(2):
                    p.on("tensor", lambda e, dj=dj, f=f: e.matmul(
                        pbig[:, f * 512:(f + 1) * 512], ysc[:, dj * 128:(dj + 1) * 128],
                        wob[:, dj * 1024 + f * 512:dj * 1024 + (f + 1) * 512],
                        start=(dj == 0), stop=(dj == NJC - 1)),
                        wait=(wwm if (dj == 0 and f == 0) else None),
                        inc=(("WOM", 1) if (dj == NJC - 1 and f == 1) else None))
            wom_c = p.cnt("WOM")
            wpr = [("WOM", wom_c)]
            if spl_slot[0] > 0:
                wpr.append(("PSP", spl_slot[0]))
            p.on("vector", lambda e, cc=cc: e.tensor_tensor(
                preb[:, :], z2g[:, cc * DM:(cc + 1) * DM],
                pbig[:, 0:1024], ALU.add),
                wait=wpr, inc=("PRE", 1))
            psp_t = p.on("sync", lambda e, ch=ch: e.dma_start(
                out=pre4[ch], in_=preb[:, :]),
                wait=[("PRE", p.cnt("PRE"))], inc=("PSP", 16))
            spl_slot[0] = psp_t
    p.on("sync", lambda e: e.dma_start(out=hend, in_=hst[:, :]),
         wait=[("HUP", p.cnt("HUP"))], inc=("HSP", 16))
    p.on("sync", lambda e: e.dma_start(out=cst_o, in_=cstb[:, :]),
         wait=[("CSD", p.cnt("CSD"))], inc=("HSP", 16))
    p.on("sync", lambda e: e.dma_start(out=sgst_o, in_=sgst[:, :]),
         wait=[("SGD", p.cnt("SGD"))], inc=("HSP", 16))
    p.on("gpsimd", lambda e: e.collective_compute(
        "AllGather", ALU.bypass, replica_groups=PAIRS,
        ins=[d["hend"].opt()], outs=[d["hend_gat"].opt()]),
        wait=[("HSP", 16)], inc=("CCH", 1))
    p.wait("gpsimd", [("CCH", 1)])
    p.wait("sync", [("PSP", p.cnt("PSP")), ("SPL", p.cnt("SPL")), ("HSP", p.cnt("HSP"))])


def emit_phase_c2(nc, p, al, d, P0, P1):
    preln = d["preln"]
    szsp = d["szsp"]
    cst_d = d["cst"]
    sgst_d = d["sgst"]
    alr_d = d["alr"]
    Wout_d = d["Wout"]
    z3 = d["z3"]

    wob = al("wob2", [128, NJC * DM], BF16)
    identf = al("identf2", [128, 128], F32)
    arow = al("arow2", [1, DST], F32)
    hselb = al("hselb2", [16, DI], BF16)
    hself = al("hself2", [16, DI], F32)
    cstb = al("cstb2", [128, NCH * DST], F32)
    sgstb = al("sgst2", [128, NCH], F32)
    sgrow = al("sgrow2", [1, 128], F32)
    ctsb = al("ctsb2", [16, 128], F32)
    ebuf = al("ebuf2", [16, 128], F32)
    cgb = al("cgb2", [16, 128], BF16)
    szc = al("szc2", [128, NJC * 128], BF16)
    dyb = al("dyb2", [128, NJC * 128], BF16)
    prb = al("prb2", [128, 2 * DM], F32)
    ep1 = al("ep1c2", [128, 2 * DM], F32)
    mcol = al("mcol2", [128, 2 * 8], F32)
    konst = al("konst2", [128, 2], F32)
    cselc = al("cselc2", [128, 1], F32)
    pbig = PV(P0, 0, 2048)
    pmid = PV(P1, 0, 1024)
    psml = PV(P1, 1024, 512)

    pre4 = preln.rearrange("(ch p) m -> ch p m", p=128)
    z34 = z3.rearrange("(ch p) m -> ch p m", p=128)
    szsp4 = szsp.rearrange("(j p) t -> p j t", p=128)

    p.on("gpsimd", lambda e: e.dma_start(
        out=wob[:, :].rearrange("p (j m) -> p j m", j=NJC),
        in_=Wout_d.rearrange("(j p) m -> p j m", p=128)), inc=("LW", 16))
    p.on("gpsimd", lambda e: e.dma_start(out=hself[:, :], in_=d["hend_gat"][0:DST, :]),
         inc=("LW", 16))
    for dst, srcd in ((cstb, cst_d), (sgstb, sgst_d), (arow, alr_d),
                      (identf, d["identd"]), (cselc, d["csel"])):
        p.on("sync", lambda e, dst=dst, srcd=srcd: e.dma_start(out=dst[:, :], in_=srcd),
             inc=("LD", 16))
    ld_tot = p.cnt("LD")
    p.on("vector", lambda e: e.memset(konst[:, 0:1], LN_EPS))
    p.on("vector", lambda e: e.memset(konst[:, 1:2], -0.5), inc=("MST", 1))
    p.on("vector", lambda e: e.tensor_scalar(
        hselb[:, :], hself[:, :], cselc[0:16, 0:1], None, ALU.mult),
        wait=[("LW", 32), ("LD", ld_tot)], inc=("HSB", 1))
    p.on("scalar", lambda e: e.activation(arow[:, :], arow[:, :], AF.Exp),
         wait=[("LD", ld_tot)], inc=("ARP", 1))
    p.on("vector", lambda e: e.tensor_scalar(arow[:, :], arow[:, :], -1.0, None, ALU.mult),
         wait=[("ARP", 1)], inc=("ARN", 1))

    z3s_slot = [0, 0]
    prb_slot = [0, 0]
    for ch in range(NCH):
        b2 = ch % 2
        wlz = [("PRA", prb_slot[b2])] if prb_slot[b2] > 0 else None
        lpr_tgt = p.on("sync", lambda e, ch=ch, b2=b2: e.dma_start(
            out=prb[:, b2 * DM:(b2 + 1) * DM], in_=pre4[ch]),
            wait=wlz, inc=("LP%d" % b2, 16))
        wsz = [("DY3", p.cnt("DY3"))] if p.cnt("DY3") > 0 else None
        lsz_tgt = p.on("gpsimd", lambda e, ch=ch: e.dma_start(
            out=szc[:, :].rearrange("p (j t) -> p j t", j=NJC),
            in_=szsp4[:, :, ch * 128:(ch + 1) * 128]),
            wait=wsz, inc=("LSZ", 16))
        wtr = [("LD", ld_tot), ("ARN", 1)]
        if p.cnt("SRW") > 0:
            wtr += [("SRW", p.cnt("SRW")), ("CTS", p.cnt("CTS"))]
        p.on("tensor", lambda e, ch=ch: e.transpose(
            psml[0:1, 0:128], sgstb[:, ch:ch + 1], identf[:, :]),
            wait=wtr, inc=("TRX", 1))
        p.on("tensor", lambda e, ch=ch: e.transpose(
            psml[0:16, 128:256], cstb[:, ch * DST:(ch + 1) * DST], identf[:, :]),
            inc=("TRX", 1))
        trx_c = p.cnt("TRX")
        p.on("vector", lambda e: e.tensor_copy(sgrow[:, :], psml[0:1, 0:128]),
             wait=[("TRX", trx_c), ("PAM", p.cnt("PAM"))], inc=("SRW", 1))
        p.on("vector", lambda e: e.tensor_copy(ctsb[:, :], psml[0:16, 128:256]),
             inc=("CTS", 1))
        p.on("tensor", lambda e: e.matmul(
            psml[0:16, 256:384], arow[0:1, :], sgrow[0:1, :], start=True, stop=True),
            wait=[("SRW", p.cnt("SRW")), ("CTS", p.cnt("CTS")),
                  ("EBF", p.cnt("EBF"))], inc=("PAM", 1))
        p.on("scalar", lambda e: e.activation(
            ebuf[:, :], psml[0:16, 256:384], AF.Exp),
            wait=[("PAM", p.cnt("PAM"))], inc=("EBF", 1))
        p.on("vector", lambda e: e.tensor_tensor(
            cgb[:, :], ebuf[:, :], ctsb[:, :], ALU.mult),
            wait=[("EBF", p.cnt("EBF"))], inc=("CGB", 1))
        cgb_c = p.cnt("CGB")
        wym = [("CGB", cgb_c), ("HSB", 1)]
        if p.cnt("DYE") > 0:
            wym.append(("DYE", p.cnt("DYE")))
        for dj in range(NJC):
            p.on("tensor", lambda e, dj=dj: e.matmul(
                pbig[:, dj * 128:(dj + 1) * 128], hselb[:, dj * 128:(dj + 1) * 128],
                cgb[:, :], start=True, stop=True),
                wait=(wym if dj == 0 else None),
                inc=(("DYM", 1) if dj == NJC - 1 else None))
        dym_c = p.cnt("DYM")
        p.on("scalar", lambda e: e.activation(dyb[:, :], pbig[:, :], AF.Copy),
             wait=[("DYM", dym_c), ("WOM", p.cnt("WOM"))], inc=("DYE", 1))
        p.on("vector", lambda e: e.tensor_tensor(
            dyb[:, :], dyb[:, :], szc[:, :], ALU.mult),
            wait=[("DYE", p.cnt("DYE")), ("LSZ", lsz_tgt)], inc=("DY3", 1))
        dy3_c = p.cnt("DY3")
        wwm = [("DY3", dy3_c)]
        if p.cnt("PRA") > 0:
            wwm.append(("PRA", p.cnt("PRA")))
        for dj in range(NJC):
            for f in range(2):
                p.on("tensor", lambda e, dj=dj, f=f: e.matmul(
                    pmid[:, f * 512:(f + 1) * 512], dyb[:, dj * 128:(dj + 1) * 128],
                    wob[:, dj * 1024 + f * 512:dj * 1024 + (f + 1) * 512],
                    start=(dj == 0), stop=(dj == NJC - 1)),
                    wait=(wwm if (dj == 0 and f == 0) else None),
                    inc=(("WOM", 1) if (dj == NJC - 1 and f == 1) else None))
        wom_c = p.cnt("WOM")
        p.on("vector", lambda e, b2=b2: e.tensor_tensor(
            prb[:, b2 * DM:(b2 + 1) * DM], prb[:, b2 * DM:(b2 + 1) * DM],
            pmid[:, :], ALU.add),
            wait=[("WOM", wom_c), ("LP%d" % b2, lpr_tgt)], inc=("PRA", 1))
        p.on("vector", lambda e, b2=b2: e.reduce_sum(
            mcol[:, b2 * 8:b2 * 8 + 1], prb[:, b2 * DM:(b2 + 1) * DM], AX.X))
        p.on("vector", lambda e, b2=b2: e.tensor_scalar(
            mcol[:, b2 * 8 + 1:b2 * 8 + 2], mcol[:, b2 * 8:b2 * 8 + 1], 1.0 / DM, None, ALU.mult))
        wsub = [("ZS3%d" % b2, z3s_slot[b2])] if z3s_slot[b2] > 0 else None
        p.on("vector", lambda e, b2=b2: e.tensor_scalar(
            ep1[:, b2 * DM:(b2 + 1) * DM], prb[:, b2 * DM:(b2 + 1) * DM],
            mcol[:, b2 * 8 + 1:b2 * 8 + 2], None, ALU.subtract),
            wait=wsub, inc=("EZC", 1))
        p.on("scalar", lambda e, b2=b2: e.activation(
            prb[:, b2 * DM:(b2 + 1) * DM], ep1[:, b2 * DM:(b2 + 1) * DM], AF.Square),
            wait=[("EZC", p.cnt("EZC")), ("MST", 1)], inc=("SQ2", 1))
        p.on("vector", lambda e, b2=b2: e.reduce_sum(
            mcol[:, b2 * 8 + 2:b2 * 8 + 3], prb[:, b2 * DM:(b2 + 1) * DM], AX.X),
            wait=[("SQ2", p.cnt("SQ2"))])
        p.on("vector", lambda e, b2=b2: e.tensor_scalar(
            mcol[:, b2 * 8 + 2:b2 * 8 + 3], mcol[:, b2 * 8 + 2:b2 * 8 + 3], 1.0 / DM, None, ALU.mult),
            inc=("EV2", 1))
        p.on("scalar", lambda e, b2=b2: e.activation(
            mcol[:, b2 * 8 + 3:b2 * 8 + 4], mcol[:, b2 * 8 + 2:b2 * 8 + 3], AF.Ln, bias=konst[:, 0:1]),
            wait=[("EV2", p.cnt("EV2"))])
        p.on("scalar", lambda e, b2=b2: e.activation(
            mcol[:, b2 * 8 + 3:b2 * 8 + 4], mcol[:, b2 * 8 + 3:b2 * 8 + 4], AF.Exp, scale=konst[:, 1:2]),
            inc=("RS2", 1))
        p.on("vector", lambda e, b2=b2: e.tensor_scalar(
            ep1[:, b2 * DM:(b2 + 1) * DM], ep1[:, b2 * DM:(b2 + 1) * DM],
            mcol[:, b2 * 8 + 3:b2 * 8 + 4], None, ALU.mult),
            wait=[("RS2", p.cnt("RS2"))], inc=("EPF", 1))
        zs3_t = p.on("sync", lambda e, ch=ch, b2=b2: e.dma_start(
            out=z34[ch], in_=ep1[:, b2 * DM:(b2 + 1) * DM]),
            wait=[("EPF", p.cnt("EPF"))], inc=("ZS3%d" % b2, 16))
        z3s_slot[b2] = zs3_t
    p.wait("sync", [("ZS30", p.cnt("ZS30")), ("ZS31", p.cnt("ZS31"))])


def emit_phase_d(nc, p, al, d, P0, P1):
    z3 = d["z3"]
    w1T = d["w1T"]
    w2T = d["w2T"]
    h1T = d["h1T"]
    h2T = d["h2T"]
    w3T = d["w3T"]
    s1c = d["s1c"]
    s2c = d["s2c"]
    s1hc = d["s1hc"]
    s2hc = d["s2hc"]
    s3r = d["s3r"]
    z4 = d["z4"]

    identf = al("identfd", [128, 128], F32)
    onesr = al("onesrd", [1, 128], F32)
    konst = al("konstd", [128, 2], F32)
    s3b = al("s3b", [128, DM], F32)
    s3s = al("s3s", [1, DM], F32)
    s1s = al("s1s", [128, NJD], F32)
    s2s = al("s2s", [128, NJD], F32)
    s1hs = al("s1hs", [128, NJD], F32)
    s2hs = al("s2hs", [128, NJD], F32)
    z3b = al("z3b", [128, 2 * DM], F32)
    z3T = al("z3T", [128, 8 * 1024], BF16)
    w12j = al("w12j", [128, 2 * 2 * 8 * 128], BF16)
    hjb = al("hjb", [128, 2 * 32 * 128], BF16)
    h1b = al("h1b", [128, NJD * 1024], BF16)
    h2b = al("h2b", [128, NJD * 1024], BF16)
    silb = al("silbd", [128, 1024], BF16)
    z4b = z3b
    ep0 = al("ep0d", [128, 2 * DM], F32)
    ep1 = al("ep1d", [128, 2 * DM], F32)
    mcol = al("mcold", [128, 2 * 8], F32)
    p1 = PV(P0, 0, 1024)
    p2 = PV(P0, 1024, 1024)
    p3 = PV(P1, 0, 1024)
    p4 = PV(P1, 1024, 1024)
    pab = [(p1, p2), (p3, p4)]
    pd_ = [p1, p2]

    z3t4 = z3.rearrange("(tq p) m -> tq p m", p=128)
    z4t4 = z4.rearrange("(tq p) m -> tq p m", p=128)
    w1T4 = w1T.rearrange("(k p) n -> p k n", p=128)
    w2T4 = w2T.rearrange("(k p) n -> p k n", p=128)
    h1T4 = h1T.rearrange("(k p) n -> p k n", p=128)
    h2T4 = h2T.rearrange("(k p) n -> p k n", p=128)
    w3T4 = w3T.rearrange("(k p) m -> p k m", p=128)

    p.on("sync", lambda e: e.dma_start(out=identf[:, :], in_=d["identd"]), inc=("LG", 16))
    p.on("sync", lambda e: e.dma_start(out=s3s[:, :], in_=s3r), inc=("LG", 16))
    p.on("sync", lambda e: e.dma_start(out=s1s[:, :], in_=s1c), inc=("LG", 16))
    p.on("sync", lambda e: e.dma_start(out=s2s[:, :], in_=s2c), inc=("LG", 16))
    p.on("sync", lambda e: e.dma_start(out=s1hs[:, :], in_=s1hc), inc=("LG", 16))
    p.on("sync", lambda e: e.dma_start(out=s2hs[:, :], in_=s2hc), inc=("LG", 16))
    p.on("vector", lambda e: e.memset(konst[:, 0:1], LN_EPS))
    p.on("vector", lambda e: e.memset(konst[:, 1:2], -0.5))
    p.on("vector", lambda e: e.memset(onesr[:, :], 1.0), inc=("MST", 1))
    for f in range(2):
        p.on("tensor", lambda e, f=f: e.matmul(
            p2[:, f * 512:(f + 1) * 512], onesr[0:1, :], s3s[0:1, f * 512:(f + 1) * 512],
            start=True, stop=True), wait=[("MST", 1), ("LG", 96)], inc=("MMD", 1))
    p.on("vector", lambda e: e.tensor_copy(s3b[:, :], p2[:, :]), wait=[("MMD", 2)], inc=("S3B", 1))

    NQ = 4
    TQ = 8
    w12_slot = [0, 0]
    hj_slot = [None, None]
    z4s_slot = [0, 0]
    vstd_slot = [None, None]
    ge2_slot = [0, 0]
    ge3_slot = [0, 0]
    mm4_slot = [0, 0]
    prevq = {"MM1": 0, "MM2": 0, "MM3": 0, "MM4": 0, "GE2": 0, "EP4": 0,
             "SIL1": 0, "VST1": 0}

    for q in range(NQ):
        for T in range(TQ):
            tq = q * TQ + T
            b2 = tq % 2
            wld = [("YQ0", p.cnt("YQ0") - 1)] if p.cnt("YQ0") >= 2 else []
            if q >= 1 and T <= 1:
                wld = (wld or []) + [("EP4", prevq["EP4"])]
            lz_tgt = p.on("sync", lambda e, tq=tq, b2=b2: e.dma_start(
                out=z3b[:, b2 * DM:(b2 + 1) * DM], in_=z3t4[tq]),
                wait=(wld or None), inc=("LZ3%d" % b2, 16))
            wtr = [("LZ3%d" % b2, lz_tgt)]
            if T == 0:
                wtr.append(("MM4", prevq["MM4"]))
                wtr.append(("EP4", prevq["EP4"]))
            if p.cnt("YQ0") > 0:
                wtr.append(("YQ0", p.cnt("YQ0")))
            for k in range(8):
                p.on("tensor", lambda e, k=k, b2=b2: e.transpose(
                    p1[:, k * 128:(k + 1) * 128], z3b[:, b2 * DM + k * 128:b2 * DM + (k + 1) * 128],
                    identf[:, :]),
                    wait=(wtr if k == 0 else None),
                    inc=(("MT0", 1) if k == 7 else None))
            wev = [("MT0", p.cnt("MT0"))]
            if q >= 1 and T == 0:
                wev.append(("MM1", prevq["MM1"]))
            for k in range(8):
                p.on("scalar", lambda e, k=k, T=T: e.activation(
                    z3T[:, k * 1024 + T * 128:k * 1024 + (T + 1) * 128],
                    p1[:, k * 128:(k + 1) * 128], AF.Copy),
                    wait=(wev if k == 0 else None),
                    inc=(("YQ0", 1) if k == 7 else None))
        yq0_q = p.cnt("YQ0")
        mt0_q = p.cnt("MT0")

        for j in range(NJD):
            sl = j % 2
            wlw = [("MM1", w12_slot[sl])] if w12_slot[sl] > 0 else None
            lw_tgt = p.on("gpsimd", lambda e, j=j, sl=sl: e.dma_start(
                out=w12j[:, sl * 2048:sl * 2048 + 1024].rearrange("p (k n) -> p k n", k=8),
                in_=w1T4[:, :, j * 128:(j + 1) * 128]),
                wait=wlw, inc=("LW%d" % sl, 16))
            lw_tgt = p.on("gpsimd", lambda e, j=j, sl=sl: e.dma_start(
                out=w12j[:, sl * 2048 + 1024:sl * 2048 + 2048].rearrange("p (k n) -> p k n", k=8),
                in_=w2T4[:, :, j * 128:(j + 1) * 128]),
                inc=("LW%d" % sl, 16))
            pa, pb = pab[j % 2]
            wmm = [("LW%d" % sl, lw_tgt), ("YQ0", yq0_q)]
            if vstd_slot[j % 2] is not None:
                wmm.append(("VST1", vstd_slot[j % 2][0]))
                wmm.append(("SIL1", vstd_slot[j % 2][1]))
            if q == 0 and j <= 1:
                wmm.append(("S3B", 1))
            first = True
            for f in range(2):
                for k in range(8):
                    p.on("tensor", lambda e, j=j, sl=sl, f=f, k=k, pa=pa: e.matmul(
                        pa[:, f * 512:(f + 1) * 512],
                        w12j[:, sl * 2048 + k * 128:sl * 2048 + (k + 1) * 128],
                        z3T[:, k * 1024 + f * 512:k * 1024 + (f + 1) * 512],
                        start=(k == 0), stop=(k == 7)),
                        wait=(wmm if first else None))
                    first = False
                for k in range(8):
                    p.on("tensor", lambda e, j=j, sl=sl, f=f, k=k, pb=pb: e.matmul(
                        pb[:, f * 512:(f + 1) * 512],
                        w12j[:, sl * 2048 + 1024 + k * 128:sl * 2048 + 1024 + (k + 1) * 128],
                        z3T[:, k * 1024 + f * 512:k * 1024 + (f + 1) * 512],
                        start=(k == 0), stop=(k == 7)),
                        inc=(("MM1", 1) if (f == 1 and k == 7) else None))
            mm1_j = p.cnt("MM1")
            p.on("scalar", lambda e, j=j, pa=pa: e.activation(
                silb[:, :], pa[:, :], AF.Silu, scale=s1s[:, j:j + 1]),
                wait=[("MM1", mm1_j), ("VST1", p.cnt("VST1"))], inc=("SIL1", 1))
            vst_t = p.on("vector", lambda e, j=j, pb=pb: e.scalar_tensor_tensor(
                h1b[:, j * 1024:(j + 1) * 1024], pb[:, :], s2s[:, j:j + 1], silb[:, :],
                ALU.mult, ALU.mult),
                wait=[("SIL1", p.cnt("SIL1")), ("MM4", prevq["MM4"])], inc=("VST1", 1))
            vstd_slot[j % 2] = (vst_t, p.cnt("SIL1"))
            w12_slot[sl] = p.cnt("MM1")
        vst1_q = p.cnt("VST1")
        sil1_q = p.cnt("SIL1")

        for j2 in range(NJD):
            sl = j2 % 2
            wlh = [hj_slot[sl]] if hj_slot[sl] is not None else None
            lh_tgt = p.on("gpsimd", lambda e, j2=j2, sl=sl: e.dma_start(
                out=hjb[:, sl * 4096:(sl + 1) * 4096].rearrange("p (k n) -> p k n", k=32),
                in_=h1T4[:, :, j2 * 128:(j2 + 1) * 128]),
                wait=wlh, inc=("LH%d" % sl, 16))
            pcx = [p1, p2][j2 % 2]
            wmm = [("LH%d" % sl, lh_tgt), ("VST1", vst1_q), ("SIL1", sil1_q)]
            if ge2_slot[j2 % 2] > 0:
                wmm.append(("GE2", ge2_slot[j2 % 2]))
            first = True
            for f in range(2):
                for k in range(32):
                    p.on("tensor", lambda e, j2=j2, sl=sl, f=f, k=k, pcx=pcx: e.matmul(
                        pcx[:, f * 512:(f + 1) * 512],
                        hjb[:, sl * 4096 + k * 128:sl * 4096 + (k + 1) * 128],
                        h1b[:, k * 1024 + f * 512:k * 1024 + (f + 1) * 512],
                        start=(k == 0), stop=(k == 31)),
                        wait=(wmm if first else None),
                        inc=(("MM2", 1) if (f == 1 and k == 31) else None))
                    first = False
            wge = [("MM2", p.cnt("MM2"))]
            if q >= 1 and j2 == 0:
                wge.append(("MM3", prevq["MM3"]))
                wge.append(("MM4", prevq["MM4"]))
            ge_t = p.on("scalar", lambda e, j2=j2, pcx=pcx: e.activation(
                h2b[:, j2 * 1024:(j2 + 1) * 1024], pcx[:, :], AF.Gelu, scale=s1hs[:, j2:j2 + 1]),
                wait=wge, inc=("GE2", 1))
            ge2_slot[j2 % 2] = ge_t
            hj_slot[sl] = ("MM2", p.cnt("MM2"))
        ge2_q = p.cnt("GE2")
        mm2_q = p.cnt("MM2")

        for j3 in range(NJD):
            sl = j3 % 2
            wlh = [hj_slot[sl]] if hj_slot[sl] is not None else None
            lh_tgt = p.on("gpsimd", lambda e, j3=j3, sl=sl: e.dma_start(
                out=hjb[:, sl * 4096:(sl + 1) * 4096].rearrange("p (k n) -> p k n", k=32),
                in_=h2T4[:, :, j3 * 128:(j3 + 1) * 128]),
                wait=wlh, inc=("LH%d" % sl, 16))
            pcx = [p3, p4][j3 % 2]
            wmm = [("LH%d" % sl, lh_tgt), ("GE2", ge2_q)]
            if ge3_slot[j3 % 2] > 0:
                wmm.append(("GE3", ge3_slot[j3 % 2]))
            first = True
            for f in range(2):
                for k in range(32):
                    p.on("tensor", lambda e, j3=j3, sl=sl, f=f, k=k, pcx=pcx: e.matmul(
                        pcx[:, f * 512:(f + 1) * 512],
                        hjb[:, sl * 4096 + k * 128:sl * 4096 + (k + 1) * 128],
                        h2b[:, k * 1024 + f * 512:k * 1024 + (f + 1) * 512],
                        start=(k == 0), stop=(k == 31)),
                        wait=(wmm if first else None),
                        inc=(("MM3", 1) if (f == 1 and k == 31) else None))
                    first = False
            wge = [("MM3", p.cnt("MM3")), ("MM2", mm2_q)]
            ge_t = p.on("scalar", lambda e, j3=j3, pcx=pcx: e.activation(
                h1b[:, j3 * 1024:(j3 + 1) * 1024], pcx[:, :], AF.Gelu, scale=s2hs[:, j3:j3 + 1]),
                wait=wge, inc=("GE3", 1))
            ge3_slot[j3 % 2] = ge_t
            hj_slot[sl] = ("MM3", p.cnt("MM3"))
        ge3_q = p.cnt("GE3")

        wl3 = [("MM3", p.cnt("MM3"))]
        lw3_tgt = p.on("gpsimd", lambda e: e.dma_start(
            out=h2b[:, :].rearrange("p (k m) -> p k m", k=NJD),
            in_=w3T4),
            wait=wl3, inc=("LW3D", 16))
        for T in range(TQ):
            tq = q * TQ + T
            b2 = tq % 2
            wzr = [("MT0", mt0_q)]
            if p.cnt("EP4") >= 2:
                wzr.append(("EP4", p.cnt("EP4") - 1))
            lz4_tgt = p.on("sync", lambda e, tq=tq, b2=b2: e.dma_start(
                out=z4b[:, b2 * DM:(b2 + 1) * DM], in_=z3t4[tq]),
                wait=wzr, inc=("LZ4%d" % b2, 16))
            pdx = pd_[T % 2]
            wmm = [("GE3", ge3_q), ("LW3D", lw3_tgt)]
            if mm4_slot[T % 2] > 0:
                wmm.append(("EV4", mm4_slot[T % 2]))
            first = True
            for f in range(2):
                for k in range(32):
                    p.on("tensor", lambda e, T=T, f=f, k=k, pdx=pdx: e.matmul(
                        pdx[:, f * 512:(f + 1) * 512],
                        h1b[:, k * 1024 + T * 128:k * 1024 + (T + 1) * 128],
                        h2b[:, k * 1024 + f * 512:k * 1024 + (f + 1) * 512],
                        start=(k == 0), stop=(k == 31)),
                        wait=(wmm if first else None),
                        inc=(("MM4", 1) if (f == 1 and k == 31) else None))
                    first = False
            mm4_now = p.cnt("MM4")
            ev_t = p.on("vector", lambda e, b2=b2, pdx=pdx: e.tensor_mul(
                ep0[:, b2 * DM:(b2 + 1) * DM], pdx[:, :], s3b[:, :]),
                wait=[("MM4", mm4_now), ("LZ4%d" % b2, lz4_tgt)], inc=("EV4", 1))
            mm4_slot[T % 2] = ev_t
            p.on("vector", lambda e, b2=b2: e.tensor_add(
                ep0[:, b2 * DM:(b2 + 1) * DM], ep0[:, b2 * DM:(b2 + 1) * DM],
                z4b[:, b2 * DM:(b2 + 1) * DM]))
            p.on("vector", lambda e, b2=b2: e.reduce_sum(
                mcol[:, b2 * 8:b2 * 8 + 1], ep0[:, b2 * DM:(b2 + 1) * DM], AX.X))
            p.on("vector", lambda e, b2=b2: e.tensor_scalar(
                mcol[:, b2 * 8 + 1:b2 * 8 + 2], mcol[:, b2 * 8:b2 * 8 + 1], 1.0 / DM, None, ALU.mult))
            wsub = [("ZS4%d" % b2, z4s_slot[b2])] if z4s_slot[b2] > 0 else None
            p.on("vector", lambda e, b2=b2: e.tensor_scalar(
                ep1[:, b2 * DM:(b2 + 1) * DM], ep0[:, b2 * DM:(b2 + 1) * DM],
                mcol[:, b2 * 8 + 1:b2 * 8 + 2], None, ALU.subtract),
                wait=wsub, inc=("EZ4", 1))
            p.on("scalar", lambda e, b2=b2: e.activation(
                ep0[:, b2 * DM:(b2 + 1) * DM], ep1[:, b2 * DM:(b2 + 1) * DM], AF.Square),
                wait=[("EZ4", p.cnt("EZ4"))], inc=("SQ4", 1))
            p.on("vector", lambda e, b2=b2: e.reduce_sum(
                mcol[:, b2 * 8 + 2:b2 * 8 + 3], ep0[:, b2 * DM:(b2 + 1) * DM], AX.X),
                wait=[("SQ4", p.cnt("SQ4"))])
            p.on("vector", lambda e, b2=b2: e.tensor_scalar(
                mcol[:, b2 * 8 + 2:b2 * 8 + 3], mcol[:, b2 * 8 + 2:b2 * 8 + 3], 1.0 / DM, None, ALU.mult),
                inc=("EVV4", 1))
            p.on("scalar", lambda e, b2=b2: e.activation(
                mcol[:, b2 * 8 + 3:b2 * 8 + 4], mcol[:, b2 * 8 + 2:b2 * 8 + 3], AF.Ln, bias=konst[:, 0:1]),
                wait=[("EVV4", p.cnt("EVV4"))])
            p.on("scalar", lambda e, b2=b2: e.activation(
                mcol[:, b2 * 8 + 3:b2 * 8 + 4], mcol[:, b2 * 8 + 3:b2 * 8 + 4], AF.Exp, scale=konst[:, 1:2]),
                inc=("RS4", 1))
            p.on("vector", lambda e, b2=b2: e.tensor_scalar(
                ep1[:, b2 * DM:(b2 + 1) * DM], ep1[:, b2 * DM:(b2 + 1) * DM],
                mcol[:, b2 * 8 + 3:b2 * 8 + 4], None, ALU.mult),
                wait=[("RS4", p.cnt("RS4"))], inc=("EP4", 1))
            zs4_t = p.on("sync", lambda e, tq=tq, b2=b2: e.dma_start(
                out=z4t4[tq], in_=ep1[:, b2 * DM:(b2 + 1) * DM]),
                wait=[("EP4", p.cnt("EP4"))], inc=("ZS4%d" % b2, 16))
            z4s_slot[b2] = zs4_t
        prevq = {"MM1": p.cnt("MM1"), "MM2": p.cnt("MM2"), "MM3": p.cnt("MM3"),
                 "MM4": p.cnt("MM4"), "GE2": p.cnt("GE2"), "EP4": p.cnt("EP4"),
                 "SIL1": p.cnt("SIL1"), "VST1": p.cnt("VST1")}
    p.wait("sync", [("ZS40", p.cnt("ZS40")), ("ZS41", p.cnt("ZS41"))])


def build_fused():
    from contextlib import ExitStack
    with ExitStack() as st:
        nc = bass.Bass("TRN2", target_bir_lowering=False)
        d = {}

        def ein(name, shape, dtype=F32):
            d[name] = nc.dram_tensor(name, shape, dtype, kind="ExternalInput").ap()

        def itn(name, shape, dtype=F32):
            d[name] = nc.dram_tensor(name, shape, dtype).ap()

        ein("u", [T_CORE, DM]); ein("uT", [DM, T_CORE]); ein("uTprev", [DM, T_CORE])
        ein("BwT", [DM, NSTATE]); ein("CwT", [NSTATE, DM]); ein("ll2", [128, 2])
        ein("Drow", [1, DM])
        ein("gwT", [DM, NE]); ein("rmsw", [1, DM])
        ein("W1T", [NE, DM, DE]); ein("W2T", [NE, DM, DE]); ein("W3T", [NE, DE, DM])
        ein("identd", [128, 128])
        ein("W_in", [DM, 2 * DI]); ein("W_x", [DI, 48])
        ein("cwc", [128, NJC * 4]); ein("cbc", [128, NJC]); ein("dpc", [128, NJC])
        ein("alr", [1, DST]); ein("alc", [DST, 1]); ein("Wout", [DI, DM])
        ein("cumin", [128, 128]); ein("sufst", [128, 128]); ein("masktl", [128, 128])
        ein("ones128", [128, 128]); ein("csel", [128, 1])
        ein("w1T", [DM, DH]); ein("w2T", [DM, DH]); ein("h1T", [DH, DH])
        ein("h2T", [DH, DH]); ein("w3T", [DH, DM])
        ein("s1c", [128, NJD]); ein("s2c", [128, NJD]); ein("s1hc", [128, NJD])
        ein("s2hc", [128, NJD]); ein("s3r", [1, DM])
        itn("z1", [T_CORE, DM]); itn("z2", [T_CORE, DM]); itn("z3", [T_CORE, DM])
        itn("comb", [T_CORE, DM]); itn("preln", [T_CORE, DM])
        itn("szsp", [DI, T_CORE], BF16)
        itn("cst", [128, NCH * DST]); itn("sgst", [128, NCH])
        itn("tail_self", [128, NJC * 4]); itn("tail_gat", [256, NJC * 4])
        itn("hend", [DST, DI]); itn("hend_gat", [2 * DST, DI])
        d["z4"] = nc.dram_tensor("z4", [T_CORE, DM], F32, kind="ExternalOutput").ap()

        P0 = st.enter_context(nc.psum_tensor("PP0", [128, 2048], F32))
        P1 = st.enter_context(nc.psum_tensor("PP1", [128, 2048], F32))

        pool = Pool(nc)
        p = Prog(nc, pool)
        arenas = []
        for name, fn in (("a", emit_phase_a), ("b", emit_phase_b),
                         ("c0", emit_phase_c0), ("c1", emit_phase_c1),
                         ("c2", emit_phase_c2), ("dd", emit_phase_d)):
            ar = Arena(nc, name + "_")
            fn(nc, p, ar.al, d, P0, P1)
            arenas.append(ar)
            if name != "dd":
                p.phase_end()

        mx = max(ar.addr for ar in arenas)
        backing = st.enter_context(nc.sbuf_tensor("ARENA", [128, mx], mybir.dt.uint8))
        base_addr = nc.lookup_mloc(backing).addr
        for ar in arenas:
            ar.place(base_addr)

        with nc.Block() as block:
            p.emit(block)
    return nc


def host_inputs(inputs):
    U = np.asarray(inputs["L0_attn_u"], np.float32)
    Bw = np.asarray(inputs["L0_attn_B_w"], np.float32)
    Cw = np.asarray(inputs["L0_attn_C_w"], np.float32)
    ll = np.asarray(inputs["L0_attn_log_lambda"], np.float32)
    Dp = np.asarray(inputs["L0_attn_D"], np.float32)
    W1 = np.asarray(inputs["L0_ffn_W1"], np.float32)
    W2 = np.asarray(inputs["L0_ffn_W2"], np.float32)
    W3 = np.asarray(inputs["L0_ffn_W3"], np.float32)
    W_in = np.asarray(inputs["L1_attn_W_in"], np.float32)
    W_x = np.asarray(inputs["L1_attn_W_x"], np.float32)
    W_x48 = np.zeros((DI, 48), np.float32)
    W_x48[:, 0:33] = W_x
    conv_w = np.asarray(inputs["L1_attn_conv_w"], np.float32)
    conv_b = np.asarray(inputs["L1_attn_conv_b"], np.float32)
    D_param = np.asarray(inputs["L1_attn_D_param"], np.float32)
    A_log = np.asarray(inputs["L1_attn_A_log"], np.float32)
    W_out = np.asarray(inputs["L1_attn_W_out"], np.float32)
    w1 = np.asarray(inputs["L1_ffn_w1_w"], np.float32)
    w2 = np.asarray(inputs["L1_ffn_w2_w"], np.float32)
    h1 = np.asarray(inputs["L1_ffn_h1_w"], np.float32)
    h2 = np.asarray(inputs["L1_ffn_h2_w"], np.float32)
    w3 = np.asarray(inputs["L1_ffn_w3_w"], np.float32)
    t_ = np.ones((128, 128), np.float32)
    common = {
        "BwT": np.ascontiguousarray(Bw.T),
        "CwT": np.ascontiguousarray(Cw.T),
        "ll2": np.ascontiguousarray(ll.reshape(2, 128).T),
        "Drow": np.ascontiguousarray(Dp[None, :]),
        "gwT": np.ascontiguousarray(np.asarray(inputs["L0_ffn_gate_w"], np.float32).T),
        "rmsw": np.ascontiguousarray(np.asarray(inputs["L0_ffn_rms_w"], np.float32)[None, :]),
        "W1T": np.ascontiguousarray(W1.transpose(0, 2, 1)),
        "W2T": np.ascontiguousarray(W2.transpose(0, 2, 1)),
        "W3T": np.ascontiguousarray(W3.transpose(0, 2, 1)),
        "identd": np.eye(128, dtype=np.float32),
        "W_in": W_in, "W_x": W_x48,
        "cwc": np.ascontiguousarray(conv_w.reshape(NJC, 128, 4).transpose(1, 0, 2).reshape(128, NJC * 4)),
        "cbc": np.ascontiguousarray(conv_b.reshape(NJC, 128).T),
        "dpc": np.ascontiguousarray(D_param.reshape(NJC, 128).T),
        "alr": np.ascontiguousarray(A_log[None, :]),
        "alc": np.ascontiguousarray(A_log[:, None]),
        "Wout": W_out,
        "cumin": np.ascontiguousarray(np.triu(t_)),
        "sufst": np.ascontiguousarray(np.tril(t_, -1)),
        "masktl": np.ascontiguousarray(np.tril(t_, -1) * 1e30),
        "ones128": t_,
        "w1T": np.ascontiguousarray(w1.T), "w2T": np.ascontiguousarray(w2.T),
        "h1T": np.ascontiguousarray(h1.T), "h2T": np.ascontiguousarray(h2.T),
        "w3T": np.ascontiguousarray(w3.T),
        "s1c": np.ascontiguousarray(np.asarray(inputs["L1_ffn_w1_s"], np.float32).reshape(NJD, 128).T),
        "s2c": np.ascontiguousarray(np.asarray(inputs["L1_ffn_w2_s"], np.float32).reshape(NJD, 128).T),
        "s1hc": np.ascontiguousarray(np.asarray(inputs["L1_ffn_h1_s"], np.float32).reshape(NJD, 128).T),
        "s2hc": np.ascontiguousarray(np.asarray(inputs["L1_ffn_h2_s"], np.float32).reshape(NJD, 128).T),
        "s3r": np.ascontiguousarray(np.asarray(inputs["L1_ffn_w3_s"], np.float32)[None, :]),
    }
    maps = []
    for c in range(NCORES):
        b, h = c // 2, c % 2
        m = dict(common)
        u_c = np.ascontiguousarray(U[b, h * T_CORE:(h + 1) * T_CORE])
        m["u"] = u_c
        m["uT"] = np.ascontiguousarray(u_c.T)
        if h == 1:
            m["uTprev"] = np.ascontiguousarray(U[b, 0:T_CORE].T)
        else:
            m["uTprev"] = np.zeros((DM, T_CORE), np.float32)
        m["csel"] = np.full((128, 1), float(h), np.float32)
        maps.append(m)
    return maps


def kernel(**inputs):
    nc = build_fused()
    maps = host_inputs(inputs)
    res = run_bass_kernel_spmd(nc, maps, core_ids=list(range(NCORES)))
    z4 = np.empty((4, 8192, DM), np.float32)
    for c in range(NCORES):
        b, h = c // 2, c % 2
        z4[b, h * T_CORE:(h + 1) * T_CORE] = res.results[c]["z4"]
    return z4


# version 04
# speedup vs baseline: 1.1375x; 1.0626x over previous
import numpy as np
import concourse.bass as bass
import concourse.mybir as mybir
from concourse.bass_utils import run_bass_kernel_spmd

F32 = mybir.dt.float32
BF16 = mybir.dt.bfloat16
AF = mybir.ActivationFunctionType
ALU = mybir.AluOpType
AX = mybir.AxisListType

NCORES = 8
T_CORE = 4096
DM = 1024
NSTATE = 256
LN_EPS = 1e-5
RMS_EPS = 1e-6
NE = 8
DE = 2816
NJ = DE // 128
DI = 2048
NJC = DI // 128
DST = 16
NCH = 32
NGR = 8
DH = 4096
NJD = DH // 128
PAIRS = [[0, 1], [2, 3], [4, 5], [6, 7]]

ENGINES = ("sync", "gpsimd", "tensor", "vector", "scalar")


class Pool:

    def __init__(self, nc):
        self.nc = nc
        self.handles = []
        self.totals = []
        self.free = []
        self.map = {}
        self.counts = {}

    def get(self, name):
        if name not in self.map:
            if self.free:
                idx = self.free.pop(0)
            else:
                idx = len(self.handles)
                self.handles.append(self.nc.alloc_semaphore("ps%d" % idx))
                self.totals.append(0)
            self.map[name] = (idx, self.totals[idx])
            self.counts[name] = 0
        idx, base = self.map[name]
        return self.handles[idx], base

    def cnt(self, name):
        return self.counts.get(name, 0)

    def rebase(self):
        for name, (idx, base) in self.map.items():
            self.totals[idx] = base + self.counts[name]
        self.free = sorted(set(range(len(self.handles))))
        self.map = {}
        self.counts = {}


class Arena:

    def __init__(self, nc, prefix):
        self.nc = nc
        self.prefix = prefix
        self.addr = 0
        self.tensors = []

    def al(self, name, shape, dtype):
        per_part = 1
        for s in shape[1:]:
            per_part *= s
        per_part *= mybir.dt.size(dtype)
        pre = self.addr
        self.addr = (self.addr + per_part + 31) // 32 * 32
        h = self.nc.virtual_sbuf_tensor(self.prefix + name, shape, dtype)
        self.tensors.append((h, pre))
        return h

    def place(self, backing_addr):
        for h, off in self.tensors:
            self.nc._tensor(h.name, h.shape, h.dtype, type="SB")
            mloc = self.nc.lookup_mloc(h)
            mloc.allocated = not self.nc.target_bir_lowering
            mloc.addr = backing_addr + off


class PV:

    def __init__(self, h, c0, w):
        self.h, self.c0, self.w = h, c0, w

    def __getitem__(self, idx):
        ps, cs = idx
        a = self.c0 + (cs.start or 0)
        b = self.c0 + (cs.stop if cs.stop is not None else self.w)
        return self.h[ps, a:b]


class Prog:

    AUTO_CHAIN = ("vector", "scalar")

    def __init__(self, nc, pool):
        self.nc = nc
        self.pool = pool
        self.ops = {e: [] for e in ENGINES}
        self._last_edge = {e: None for e in ENGINES}
        self._nbar = 0

    def cnt(self, name):
        return self.pool.cnt(name)

    def on(self, eng, fn, wait=None, inc=None):
        if eng in self.AUTO_CHAIN:
            if inc is None:
                inc = ("_T" + eng, 1)
            prev = self._last_edge[eng]
            if prev is not None:
                wait = list(wait) if wait else []
                wait.append(prev)
        sem_h = None
        inc_amt = 0
        if inc is not None:
            name, amt = inc
            sem_h, _ = self.pool.get(name)
            self.pool.counts[name] += amt
            inc_amt = amt
            if eng in self.AUTO_CHAIN:
                self._last_edge[eng] = (name, self.pool.counts[name])
        waits = []
        if wait:
            for wname, wval in wait:
                if wval > 0:
                    h, b = self.pool.get(wname)
                    waits.append((h, b + wval))

        def run(e, fn=fn, waits=waits, sem_h=sem_h, inc_amt=inc_amt):
            for s, v in waits:
                e.wait_ge(s, v)
            ins = fn(e)
            if sem_h is not None:
                ins.then_inc(sem_h, inc_amt)
        self.ops[eng].append(run)
        if inc is not None:
            return self.pool.counts[inc[0]]
        return None

    def wait(self, eng, wait):
        waits = []
        for wname, wval in wait:
            if wval > 0:
                h, b = self.pool.get(wname)
                waits.append((h, b + wval))

        def run(e, waits=waits):
            for s, v in waits:
                e.wait_ge(s, v)
        self.ops[eng].append(run)

    def phase_end(self):
        self._nbar += 1
        name = "PBAR%d" % self._nbar
        h, base = self.pool.get(name)
        self.pool.counts[name] += len(ENGINES)
        target = base + len(ENGINES)
        for eng in ENGINES:
            def mark(e, h=h):
                e.nop().then_inc(h, 1)
            self.ops[eng].append(mark)
        self.pool.rebase()
        self._last_edge = {e: None for e in ENGINES}
        for eng in ENGINES:
            def barw(e, h=h, t=target):
                e.wait_ge(h, t)
            self.ops[eng].append(barw)

    def emit(self, block):
        for eng in ENGINES:
            ops = self.ops[eng]

            def run_all(e, ops=ops):
                for fn in ops:
                    fn(e)
            getattr(block, eng)(run_all)


def emit_phase_a(nc, p, al, d, P0, P1):
    u = d["u"]
    uT = d["uT"]
    uTprev = d["uTprev"]
    BwT = d["BwT"]
    CwT = d["CwT"]
    ll2 = d["ll2"]
    Drow = d["Drow"]
    z1 = d["z1"]

    bwt = al("bwt", [128, 8 * NSTATE], BF16)
    cwt = al("cwt", [128, 2 * DM], BF16)
    ll = al("ll", [128, 2], F32)
    lamc = al("lamc", [128, 2], F32)
    lamb = al("lamb", [128, 2 * 512], F32)
    drow_s = al("drow_s", [1, DM], F32)
    onesr = al("onesr", [1, 128], F32)
    d1b = al("d1b", [128, DM], F32)
    zcol = al("zcol", [128, 1], F32)
    utb = al("utb", [128, 2 * 8 * 512], BF16)
    uf = al("uf", [128, 2 * 4 * DM], F32)
    hf = al("hf", [128, 4 * 512], F32)
    hbf = al("hbf", [128, 2 * 2 * 512], BF16)
    ep0 = al("ep0", [128, 2 * DM], F32)
    ep1 = al("ep1", [128, 2 * DM], F32)
    sqb = al("sqb", [128, 2 * DM], F32)
    z1t = al("z1t", [128, 2 * DM], F32)
    mcol = al("mcol", [128, 8], F32)
    konst = al("konst", [128, 4], F32)
    psb0 = PV(P0, 0, 512)
    psb1 = PV(P0, 512, 512)
    psy0 = PV(P0, 1024, 1024)
    psy1 = PV(P1, 0, 1024)
    psd = PV(P1, 1024, 1024)

    psb = [psb0, psb1]
    psy = [psy0, psy1]

    uT3 = uT.rearrange("(c p) l -> p c l", p=128)
    uTp3 = uTprev.rearrange("(c p) l -> p c l", p=128)
    u4 = u.rearrange("(ch g p) m -> ch p g m", g=4, p=128)
    z14 = z1.rearrange("(ch g p) m -> ch g p m", g=4, p=128)

    p.on("gpsimd", lambda e: e.dma_start(
        out=bwt[:, :].rearrange("p (c n) -> p c n", c=8),
        in_=BwT.rearrange("(c p) n -> p c n", p=128)), inc=("LDWG", 16))
    p.on("gpsimd", lambda e: e.dma_start(
        out=cwt[:, :].rearrange("p (c m) -> p c m", c=2),
        in_=CwT.rearrange("(c p) m -> p c m", p=128)), inc=("LDWG", 16))
    p.on("sync", lambda e: e.dma_start(out=ll[:, :], in_=ll2), inc=("LDWS", 16))
    p.on("sync", lambda e: e.dma_start(out=drow_s[:, :], in_=Drow), inc=("LDWS", 16))
    p.on("scalar", lambda e: e.activation(lamc[:, :], ll[:, :], AF.Sigmoid),
         wait=[("LDWS", 32)], inc=("SIG", 1))
    p.on("vector", lambda e: e.memset(lamb[:, :], 1.0))
    p.on("vector", lambda e: e.memset(zcol[:, :], 0.0))
    p.on("vector", lambda e: e.memset(konst[:, 0:1], LN_EPS))
    p.on("vector", lambda e: e.memset(konst[:, 1:2], -0.5))
    p.on("vector", lambda e: e.memset(konst[:, 2:3], RMS_EPS))
    p.on("vector", lambda e: e.memset(onesr[:, :], 1.0), inc=("MST", 1))
    for nh in range(2):
        p.on("vector", lambda e, nh=nh: e.tensor_scalar(
            lamb[:, nh * 512:(nh + 1) * 512], lamb[:, nh * 512:(nh + 1) * 512],
            lamc[:, nh:nh + 1], None, ALU.mult),
            wait=[("SIG", 1)], inc=("PREP", 1))
    for f in range(2):
        p.on("tensor", lambda e, f=f: e.matmul(
            psd[:, f * 512:(f + 1) * 512], onesr[0:1, :], drow_s[0:1, f * 512:(f + 1) * 512],
            start=True, stop=True), wait=[("MST", 1), ("LDWS", 32)], inc=("MMD", 1))
    p.on("vector", lambda e: e.tensor_scalar(d1b[:, :], psd[:, :], 1.0, None, ALU.add),
         wait=[("MMD", 2)], inc=("PREP", 1))

    def emit_bu_mms(chunk, bufsel, sem_ld, ld_target, extra_wait):
        for nh in range(2):
            for c in range(8):
                last = (c == 7)
                w = [(sem_ld, ld_target), ("LDWG", 32)] if c == 0 else None
                if c == 0 and extra_wait is not None:
                    w = [(sem_ld, ld_target), ("LDWG", 32), extra_wait]
                p.on("tensor", lambda e, nh=nh, c=c, bufsel=bufsel, last=last: e.matmul(
                    psb[nh][:, :],
                    bwt[:, c * NSTATE + nh * 128: c * NSTATE + (nh + 1) * 128],
                    utb[:, bufsel * 4096 + c * 512:(bufsel) * 4096 + (c + 1) * 512],
                    start=(c == 0), stop=last),
                    wait=w, inc=(("MMB", 1) if last else None))

    for ch in range(8):
        bufsel = ch % 2
        wld = None
        if ch >= 2:
            wld = [("MMB", 2 * (ch - 1))]
        ldt_tgt = p.on("gpsimd", lambda e, ch=ch, bufsel=bufsel: e.dma_start(
            out=utb[:, bufsel * 4096:(bufsel + 1) * 4096].rearrange("p (c l) -> p c l", c=8),
            in_=uTp3[:, :, ch * 512:(ch + 1) * 512]),
            wait=wld, inc=("LDT%d" % bufsel, 16))
        extra = ("SCN", 2 * ch) if ch >= 1 else None
        emit_bu_mms(ch, bufsel, "LDT%d" % bufsel, ldt_tgt, extra)
        for nh in range(2):
            init_ap = zcol[:, 0:1] if ch == 0 else hf[:, ((ch - 1) % 2) * 1024 + nh * 512 + 511: ((ch - 1) % 2) * 1024 + nh * 512 + 512]
            p.on("vector", lambda e, ch=ch, nh=nh, init_ap=init_ap: e.tensor_tensor_scan(
                hf[:, (ch % 2) * 1024 + nh * 512:(ch % 2) * 1024 + nh * 512 + 512],
                lamb[:, nh * 512:(nh + 1) * 512],
                psb[nh][:, :],
                init_ap, ALU.mult, ALU.add),
                wait=[("MMB", 2 * ch + nh + 1), ("PREP", 2)],
                inc=("SCN", 1))
    h0_ap = [hf[:, 1024 + nh * 512 + 511:1024 + nh * 512 + 512] for nh in range(2)]

    MMB0 = p.cnt("MMB")
    SCN0 = p.cnt("SCN")
    for ch in range(8):
        bufsel = ch % 2
        wld = [("MMB", MMB0 + 2 * (ch - 1))] if ch >= 2 else [("MMB", MMB0)]
        if ch == 0:
            wld = [("SCN", 16)]
        ldt_tgt = p.on("gpsimd", lambda e, ch=ch, bufsel=bufsel: e.dma_start(
            out=utb[:, bufsel * 4096:(bufsel + 1) * 4096].rearrange("p (c l) -> p c l", c=8),
            in_=uT3[:, :, ch * 512:(ch + 1) * 512]),
            wait=wld, inc=("LDT%d" % bufsel, 16))
        wlu = [("EPI", 4 * (ch - 1))] if ch >= 2 else None
        ldu_tgt = p.on("sync", lambda e, ch=ch, bufsel=bufsel: e.dma_start(
            out=uf[:, bufsel * 4096:(bufsel + 1) * 4096].rearrange("p (g m) -> p g m", g=4),
            in_=u4[ch]),
            wait=wlu, inc=("LDU%d" % bufsel, 16))
        extra = ("SCN", SCN0 + 2 * ch) if ch >= 1 else None
        emit_bu_mms(ch, bufsel, "LDT%d" % bufsel, ldt_tgt, extra)
        for nh in range(2):
            init_ap = h0_ap[nh] if ch == 0 else hf[:, ((ch - 1) % 2) * 1024 + nh * 512 + 511: ((ch - 1) % 2) * 1024 + nh * 512 + 512]
            p.on("vector", lambda e, ch=ch, nh=nh, init_ap=init_ap: e.tensor_tensor_scan(
                hf[:, (ch % 2) * 1024 + nh * 512:(ch % 2) * 1024 + nh * 512 + 512],
                lamb[:, nh * 512:(nh + 1) * 512],
                psb[nh][:, :],
                init_ap, ALU.mult, ALU.add),
                wait=[("MMB", MMB0 + 2 * ch + nh + 1)],
                inc=("SCN", 1))
            wcast = [("SCN", SCN0 + 2 * ch + nh + 1)]
            if ch >= 2:
                wcast.append(("MMY", 4 * (ch - 1)))
            p.on("vector", lambda e, ch=ch, nh=nh: e.tensor_copy(
                hbf[:, (ch % 2) * 1024 + nh * 512:(ch % 2) * 1024 + nh * 512 + 512],
                hf[:, (ch % 2) * 1024 + nh * 512:(ch % 2) * 1024 + nh * 512 + 512]),
                wait=wcast, inc=("HBF", 1))
        for g in range(4):
            gi = 4 * ch + g
            ps = psy[g % 2]
            k = 0
            for nh in range(2):
                for f in range(2):
                    k += 1
                    w = []
                    if k == 1:
                        w = [("HBF", 2 * (ch + 1)), ("LDWG", 32)]
                        if gi >= 2:
                            w.append(("PSR", gi - 1))
                    p.on("tensor", lambda e, ch=ch, g=g, nh=nh, f=f, ps=ps: e.matmul(
                        ps[:, f * 512:(f + 1) * 512],
                        hbf[:, (ch % 2) * 1024 + nh * 512 + g * 128:(ch % 2) * 1024 + nh * 512 + (g + 1) * 128],
                        cwt[:, nh * DM + f * 512:nh * DM + (f + 1) * 512],
                        start=(nh == 0), stop=(nh == 1)),
                        wait=(w if w else None), inc=(("MMY", 1) if k == 4 else None))
            b2 = gi % 2
            uap = uf[:, (ch % 2) * 4096 + g * DM:(ch % 2) * 4096 + (g + 1) * DM]
            p.on("vector", lambda e, uap=uap, b2=b2: e.tensor_mul(
                ep0[:, b2 * DM:(b2 + 1) * DM], uap, d1b[:, :]),
                wait=[("LDU%d" % (ch % 2), ldu_tgt)], inc=None)
            p.on("vector", lambda e, b2=b2, ps=ps: e.tensor_add(
                ep0[:, b2 * DM:(b2 + 1) * DM], ep0[:, b2 * DM:(b2 + 1) * DM], ps[:, :]),
                wait=[("MMY", gi + 1)], inc=("PSR", 1))
            p.on("vector", lambda e, b2=b2: e.reduce_sum(
                mcol[:, b2 * 4:b2 * 4 + 1], ep0[:, b2 * DM:(b2 + 1) * DM], AX.X), inc=None)
            p.on("vector", lambda e, b2=b2: e.tensor_scalar(
                mcol[:, b2 * 4 + 1:b2 * 4 + 2], mcol[:, b2 * 4:b2 * 4 + 1], 1.0 / DM, None, ALU.mult), inc=None)
            p.on("vector", lambda e, b2=b2: e.tensor_scalar(
                ep1[:, b2 * DM:(b2 + 1) * DM], ep0[:, b2 * DM:(b2 + 1) * DM],
                mcol[:, b2 * 4 + 1:b2 * 4 + 2], None, ALU.subtract),
                inc=("EZC", 1))
            p.on("scalar", lambda e, b2=b2: e.activation(
                sqb[:, b2 * DM:(b2 + 1) * DM], ep1[:, b2 * DM:(b2 + 1) * DM], AF.Square),
                wait=[("EZC", gi + 1)], inc=("SQD", 1))
            p.on("vector", lambda e, b2=b2: e.reduce_sum(
                mcol[:, b2 * 4 + 2:b2 * 4 + 3], sqb[:, b2 * DM:(b2 + 1) * DM], AX.X),
                wait=[("SQD", gi + 1)], inc=None)
            p.on("vector", lambda e, b2=b2: e.tensor_scalar(
                mcol[:, b2 * 4 + 2:b2 * 4 + 3], mcol[:, b2 * 4 + 2:b2 * 4 + 3], 1.0 / DM, None, ALU.mult),
                inc=("EVV", 1))
            p.on("scalar", lambda e, b2=b2: e.activation(
                mcol[:, b2 * 4 + 3:b2 * 4 + 4], mcol[:, b2 * 4 + 2:b2 * 4 + 3], AF.Ln, bias=konst[:, 0:1]),
                wait=[("EVV", gi + 1)], inc=None)
            p.on("scalar", lambda e, b2=b2: e.activation(
                mcol[:, b2 * 4 + 3:b2 * 4 + 4], mcol[:, b2 * 4 + 3:b2 * 4 + 4], AF.Exp, scale=konst[:, 1:2]),
                inc=("RSD", 1))
            wz = [("RSD", gi + 1)]
            if gi >= 2:
                wz.append(("ZS%d" % b2, p.cnt("ZS%d" % b2)))
            p.on("vector", lambda e, b2=b2: e.tensor_scalar(
                z1t[:, b2 * DM:(b2 + 1) * DM], ep1[:, b2 * DM:(b2 + 1) * DM],
                mcol[:, b2 * 4 + 3:b2 * 4 + 4], None, ALU.mult),
                wait=wz, inc=("EPI", 1))
            p.on("sync", lambda e, ch=ch, g=g, b2=b2: e.dma_start(
                out=z14[ch, g], in_=z1t[:, b2 * DM:(b2 + 1) * DM]),
                wait=[("EPI", gi + 1)], inc=("ZS%d" % b2, 16))
    p.wait("sync", [("ZS0", p.cnt("ZS0")), ("ZS1", p.cnt("ZS1"))])


def emit_phase_b(nc, p, al, d, P0, P1):
    z1 = d["z1"]
    gwT = d["gwT"]
    rmsw = d["rmsw"]
    W1P = d["W1P"]
    W2P = d["W2P"]
    W3P = d["W3P"]
    ident_ap = d["identd"]
    z2 = d["z2"]
    comb = d["comb"]

    rmsb = al("rmsb", [128, DM], F32)
    rmsr = al("rmsr", [1, DM], F32)
    onesr = al("onesr", [1, 128], F32)
    konst = al("konst", [128, 4], F32)
    identf = al("identf_s", [128, 128], F32)
    gw = al("gw", [128, 8 * NE], F32)
    z1b = al("z1b", [128, 2 * DM], F32)
    sqb = al("sqb", [128, DM], F32)
    rcol = al("rcol", [128, 2 * 4], F32)
    yf = al("yf", [128, DM], F32)
    yT32 = al("yT32", [128, DM], F32)
    ytb = al("ytb", [128, 8 * 1024], BF16)
    lgb = al("lgb", [128, 8 * 8], F32)
    gcol = al("gcol", [128, 8], F32)
    wf = al("wf", [128, 8 * NE], F32)
    w1j = al("w1j", [128, 2 * 8 * 128], BF16)
    w2j = al("w2j", [128, 2 * 8 * 128], BF16)
    w3b = al("w3b", [128, NJ * DM], BF16)
    hb = al("hb", [128, NJ * 1024], BF16)
    silb = al("silb", [128, 2 * 512], F32)
    outs = al("outs", [128, 4 * DM], F32)
    combb = al("combb", [128, 2 * DM], F32)
    z2t = al("z2t", [128, 2 * DM], F32)
    ep0 = al("ep0b", [128, 2 * DM], F32)
    ep1 = al("ep1b", [128, 2 * DM], F32)
    mcol = al("mcolb", [128, 2 * 8], F32)
    pu0 = PV(P0, 0, 512)
    pu1 = PV(P0, 512, 512)
    pg0 = PV(P0, 1024, 512)
    pg1 = PV(P0, 1536, 512)
    po0 = PV(P1, 0, 1024)
    po1 = PV(P1, 1024, 1024)
    pu = [pu0, pu1]
    pg = [pg0, pg1]
    po = [po0, po1]
    ps4 = [pu0, pu1, pg0, pg1]

    z1t4 = z1.rearrange("(tq p) m -> tq p m", p=128)
    z2t4 = z2.rearrange("(tq p) m -> tq p m", p=128)
    comb4 = comb.rearrange("(tq p) m -> tq p m", p=128)

    p.on("sync", lambda e: e.dma_start(out=gw[:, :].rearrange("p (k e) -> p k e", k=8),
                                       in_=gwT.rearrange("(k p) e -> p k e", p=128)), inc=("LG", 16))
    p.on("sync", lambda e: e.dma_start(out=rmsr[:, :], in_=rmsw), inc=("LG", 16))
    p.on("sync", lambda e: e.dma_start(out=identf[:, :], in_=ident_ap), inc=("LG", 16))
    p.on("vector", lambda e: e.memset(konst[:, 0:1], RMS_EPS))
    p.on("vector", lambda e: e.memset(konst[:, 1:2], -0.5))
    p.on("vector", lambda e: e.memset(konst[:, 2:3], LN_EPS))
    p.on("vector", lambda e: e.memset(onesr[:, :], 1.0), inc=("MST", 1))
    for f in range(2):
        p.on("tensor", lambda e, f=f: e.matmul(
            po0[:, f * 512:(f + 1) * 512], onesr[0:1, :], rmsr[0:1, f * 512:(f + 1) * 512],
            start=True, stop=True), wait=[("MST", 1), ("LG", 48)], inc=("MMD", 1))
    p.on("vector", lambda e: e.tensor_copy(rmsb[:, :], po0[:, :]), wait=[("MMD", 2)], inc=("RMB", 1))

    NQ = 4
    TQ = 8
    vst_slot = [0, 0]
    sil_slot = [0, 0]
    outs_slot = [0, 0, 0, 0]

    def evac_src(k):
        return ps4[k // 2][:, (k % 2) * 128:(k % 2) * 128 + 128]

    def emit_b1(q):
        for T in range(TQ):
            tq = q * TQ + T
            b2 = tq % 2
            wld = [("YR", p.cnt("YR") - 1)] if tq >= 2 else None
            lz_tgt = p.on("sync", lambda e, tq=tq, b2=b2: e.dma_start(
                out=z1b[:, b2 * DM:(b2 + 1) * DM], in_=z1t4[tq]),
                wait=wld, inc=("LZ%d" % b2, 16))
            wsq = [("LZ%d" % b2, lz_tgt), ("EVV", p.cnt("EVV"))]
            p.on("scalar", lambda e, b2=b2: e.activation(
                sqb[:, :], z1b[:, b2 * DM:(b2 + 1) * DM], AF.Square),
                wait=wsq, inc=("SQD", 1))
            p.on("vector", lambda e, b2=b2: e.reduce_sum(
                rcol[:, b2 * 4:b2 * 4 + 1], sqb[:, :], AX.X),
                wait=[("SQD", p.cnt("SQD"))])
            p.on("vector", lambda e, b2=b2: e.tensor_scalar(
                rcol[:, b2 * 4:b2 * 4 + 1], rcol[:, b2 * 4:b2 * 4 + 1], 1.0 / DM, None, ALU.mult),
                inc=("EVV", 1))
            p.on("scalar", lambda e, b2=b2: e.activation(
                rcol[:, b2 * 4 + 1:b2 * 4 + 2], rcol[:, b2 * 4:b2 * 4 + 1], AF.Ln, bias=konst[:, 0:1]),
                wait=[("EVV", p.cnt("EVV"))])
            p.on("scalar", lambda e, b2=b2: e.activation(
                rcol[:, b2 * 4 + 1:b2 * 4 + 2], rcol[:, b2 * 4 + 1:b2 * 4 + 2], AF.Exp, scale=konst[:, 1:2]),
                inc=("RSD", 1))
            wy = [("RSD", p.cnt("RSD")), ("RMB", 1)]
            if tq >= 1:
                wy.append(("MTT", p.cnt("MTT")))
            p.on("vector", lambda e, b2=b2: e.tensor_scalar(
                yf[:, :], z1b[:, b2 * DM:(b2 + 1) * DM], rcol[:, b2 * 4 + 1:b2 * 4 + 2], None, ALU.mult),
                wait=wy)
            yr_tgt = p.on("vector", lambda e: e.tensor_mul(yf[:, :], yf[:, :], rmsb[:, :]),
                          inc=("YR", 1))
            wtr = [("YR", yr_tgt)]
            if tq >= 1:
                wtr.append(("EVF", p.cnt("EVF")))
                wtr.append(("YQB", p.cnt("YQB")))
            if q >= 1 and T == 0:
                wtr.append(("VST", p.cnt("VST")))
            for k in range(8):
                p.on("tensor", lambda e, k=k: e.transpose(
                    evac_src(k), yf[:, k * 128:(k + 1) * 128], identf[:, :]),
                    wait=(wtr if k == 0 else None),
                    inc=(("MTT", 1) if k == 7 else None))
            mtt_tgt = p.cnt("MTT")
            for k in range(8):
                p.on("vector", lambda e, k=k: e.tensor_copy(
                    yT32[:, k * 128:(k + 1) * 128], evac_src(k)),
                    wait=([("MTT", mtt_tgt)] if k == 0 else None),
                    inc=(("EVF", 1) if k == 7 else None))
            wyq = [("MTT", mtt_tgt), ("EVF", p.cnt("EVF"))]
            if q >= 1 and T == 0:
                wyq.append(("MMUG", p.cnt("MMUG")))
            for k in range(8):
                p.on("scalar", lambda e, k=k, T=T: e.activation(
                    ytb[:, k * 1024 + T * 128:k * 1024 + (T + 1) * 128], evac_src(k), AF.Copy),
                    wait=(wyq if k == 0 else None),
                    inc=(("YQB", 1) if k == 7 else None))
            evf_tgt = p.cnt("EVF")
            wlg = [("EVF", evf_tgt), ("LGC", p.cnt("LGC"))]
            if q >= 1 and T == 0:
                wlg.append(("VEV", p.cnt("VEV")))
            for k in range(8):
                p.on("tensor", lambda e, k=k: e.matmul(
                    po1[:, 0:8], yT32[:, k * 128:(k + 1) * 128], gw[:, k * NE:(k + 1) * NE],
                    start=(k == 0), stop=(k == 7)),
                    wait=(wlg if k == 0 else None),
                    inc=(("MML", 1) if k == 7 else None))
            p.on("vector", lambda e: e.tensor_copy(lgb[:, 0:8], po1[:, 0:8]),
                 wait=[("MML", p.cnt("MML"))], inc=("LGC", 1))
            p.on("vector", lambda e: e.reduce_max(gcol[:, 0:1], lgb[:, 0:8], AX.X))
            p.on("vector", lambda e: e.tensor_scalar(
                lgb[:, 8:16], lgb[:, 0:8], gcol[:, 0:1], None, ALU.is_ge))
            p.on("vector", lambda e: e.scalar_tensor_tensor(
                lgb[:, 16:24], lgb[:, 8:16], -1e30, lgb[:, 0:8], ALU.mult, ALU.add))
            p.on("vector", lambda e: e.reduce_max(gcol[:, 1:2], lgb[:, 16:24], AX.X))
            p.on("vector", lambda e: e.tensor_scalar(
                lgb[:, 24:32], lgb[:, 0:8], gcol[:, 1:2], None, ALU.is_ge))
            p.on("vector", lambda e: e.tensor_scalar(
                gcol[:, 2:3], gcol[:, 0:1], -1.0, None, ALU.mult), inc=("GX1", 1))
            p.on("scalar", lambda e: e.activation(
                lgb[:, 32:40], lgb[:, 0:8], AF.Exp, bias=gcol[:, 2:3]),
                wait=[("GX1", p.cnt("GX1"))], inc=("GX2", 1))
            p.on("vector", lambda e: e.tensor_mul(
                lgb[:, 40:48], lgb[:, 32:40], lgb[:, 24:32]),
                wait=[("GX2", p.cnt("GX2"))])
            p.on("vector", lambda e: e.reduce_sum(gcol[:, 3:4], lgb[:, 40:48], AX.X))
            p.on("vector", lambda e: e.reciprocal(gcol[:, 4:5], gcol[:, 3:4]))
            p.on("vector", lambda e, T=T: e.tensor_scalar(
                wf[:, T * 8:(T + 1) * 8], lgb[:, 40:48], gcol[:, 4:5], None, ALU.mult),
                inc=("WFD", 1))

    def emit_experts(q, yqb_q, evf_q):
        for ex in range(NE):
            for j in range(NJ):
                sl = j % 2
                wlw = [("MMUG", mmug_slot[sl])] if mmug_slot[sl] > 0 else None
                lw1_tgt = p.on("gpsimd", lambda e, ex=ex, j=j, sl=sl: e.dma_start(
                    out=w1j[:, sl * 1024:(sl + 1) * 1024],
                    in_=W1P[ex, j]),
                    wait=wlw, inc=("LW1%d" % sl, 16))
                lw2_tgt = p.on("gpsimd", lambda e, ex=ex, j=j, sl=sl: e.dma_start(
                    out=w2j[:, sl * 1024:(sl + 1) * 1024],
                    in_=W2P[ex, j]),
                    inc=("LW2%d" % sl, 16))
                wmm = [("LW1%d" % sl, lw1_tgt), ("LW2%d" % sl, lw2_tgt),
                       ("YQB", yqb_q), ("EVF", evf_q)]
                if vst_slot[j % 2] > 0:
                    wmm.append(("VST", vst_slot[j % 2]))
                for f in range(2):
                    for k in range(8):
                        p.on("tensor", lambda e, j=j, sl=sl, f=f, k=k: e.matmul(
                            pu[j % 2][:, :], w1j[:, sl * 1024 + k * 128:sl * 1024 + (k + 1) * 128],
                            ytb[:, k * 1024 + f * 512:k * 1024 + (f + 1) * 512],
                            start=(k == 0), stop=(k == 7)),
                            wait=(wmm if (f == 0 and k == 0) else None))
                    for k in range(8):
                        p.on("tensor", lambda e, j=j, sl=sl, f=f, k=k: e.matmul(
                            pg[j % 2][:, :], w2j[:, sl * 1024 + k * 128:sl * 1024 + (k + 1) * 128],
                            ytb[:, k * 1024 + f * 512:k * 1024 + (f + 1) * 512],
                            start=(k == 0), stop=(k == 7)),
                            inc=(("MMUG", 1) if (f == 1 and k == 7) else None))
                mmug_slot[sl] = p.cnt("MMUG")
                mmug_j = p.cnt("MMUG")
                for f in range(2):
                    wsil = [("MMUG", mmug_j)]
                    if sil_slot[f] > 0:
                        wsil.append(("VST", sil_slot[f]))
                    p.on("scalar", lambda e, j=j, f=f: e.activation(
                        silb[:, f * 512:(f + 1) * 512], pu[j % 2][:, :], AF.Silu),
                        wait=wsil, inc=("SIL", 1))
                    vst_t = p.on("vector", lambda e, j=j, f=f: e.tensor_mul(
                        hb[:, j * 1024 + f * 512:j * 1024 + (f + 1) * 512],
                        silb[:, f * 512:(f + 1) * 512], pg[j % 2][:, :]),
                        wait=[("SIL", p.cnt("SIL"))], inc=("VST", 1))
                    sil_slot[f] = vst_t
                vst_slot[j % 2] = p.cnt("VST")
            vst_eq = p.cnt("VST")
            wl3 = [("MMO", p.cnt("MMO"))] if p.cnt("MMO") > 0 else None
            lw3_tgt = p.on("gpsimd", lambda e, ex=ex: e.dma_start(
                out=w3b[:, :], in_=W3P[ex]),
                wait=wl3, inc=("LW3", 16))
            for ob in range(4):
                wob = [("VST", vst_eq), ("LW3", lw3_tgt)]
                if p.cnt("VEV") > 0:
                    wob.append(("VEV", p.cnt("VEV")))
                first = True
                for j in range(NJ):
                    for ts in range(2):
                        for f in range(2):
                            p.on("tensor", lambda e, j=j, ob=ob, ts=ts, f=f: e.matmul(
                                po[ts][:, f * 512:(f + 1) * 512],
                                hb[:, j * 1024 + ob * 256 + ts * 128:j * 1024 + ob * 256 + (ts + 1) * 128],
                                w3b[:, j * 1024 + f * 512:j * 1024 + (f + 1) * 512],
                                start=(j == 0), stop=(j == NJ - 1)),
                                wait=(wob if first else None),
                                inc=(("MMO", 1) if (j == NJ - 1 and ts == 1 and f == 1) else None))
                            first = False
                mmo_ob = p.cnt("MMO")
                for ts in range(2):
                    osl = (ob * 2 + ts) % 4
                    T = ob * 2 + ts
                    wev = [("MMO", mmo_ob), ("WFD", wfd_q[q])]
                    if outs_slot[osl] > 0:
                        wev.append(("CMB%d" % osl, outs_slot[osl]))
                    p.on("vector", lambda e, ts=ts, osl=osl, T=T, ex=ex: e.tensor_scalar(
                        outs[:, osl * DM:(osl + 1) * DM], po[ts][:, :],
                        wf[:, T * 8 + ex:T * 8 + ex + 1], None, ALU.mult),
                        wait=wev, inc=("VEV", 1))
                    vev_now = p.cnt("VEV")
                    cmb_t = p.on("gpsimd", lambda e, q=q, ob=ob, ts=ts, osl=osl, ex=ex: e.dma_start(
                        out=comb4[q * 8 + ob * 2 + ts],
                        in_=outs[:, osl * DM:(osl + 1) * DM],
                        accum_op=(ALU.bypass if ex == 0 else ALU.add)),
                        wait=[("VEV", vev_now)], inc=("CMB%d" % osl, 16))
                    outs_slot[osl] = cmb_t

    def emit_b3(q, cmb_q):
        for T in range(TQ):
            tq = q * TQ + T
            b2 = tq % 2
            i3 = p.cnt("EP3")
            war3 = [("EP3", i3 - 1)] if i3 >= 2 else None
            wlc = [("CMB%d" % s, cmb_q[s]) for s in range(4)] + (war3 or [])
            lc_tgt = p.on("sync", lambda e, tq=tq, b2=b2: e.dma_start(
                out=combb[:, b2 * DM:(b2 + 1) * DM], in_=comb4[tq]),
                wait=wlc, inc=("LC%d" % b2, 16))
            lzr_tgt = p.on("sync", lambda e, tq=tq, b2=b2: e.dma_start(
                out=z2t[:, b2 * DM:(b2 + 1) * DM], in_=z1t4[tq]),
                wait=war3, inc=("LZR%d" % b2, 16))
            wpre = [("LC%d" % b2, lc_tgt), ("LZR%d" % b2, lzr_tgt)]
            if zs2_slot[b2] > 0:
                wpre.append(("ZS2%d" % b2, zs2_slot[b2]))
            p.on("vector", lambda e, b2=b2: e.tensor_add(
                ep0[:, b2 * DM:(b2 + 1) * DM], z2t[:, b2 * DM:(b2 + 1) * DM],
                combb[:, b2 * DM:(b2 + 1) * DM]),
                wait=wpre)
            p.on("vector", lambda e, b2=b2: e.reduce_sum(
                mcol[:, b2 * 8:b2 * 8 + 1], ep0[:, b2 * DM:(b2 + 1) * DM], AX.X))
            p.on("vector", lambda e, b2=b2: e.tensor_scalar(
                mcol[:, b2 * 8 + 1:b2 * 8 + 2], mcol[:, b2 * 8:b2 * 8 + 1], 1.0 / DM, None, ALU.mult))
            p.on("vector", lambda e, b2=b2: e.tensor_scalar(
                ep1[:, b2 * DM:(b2 + 1) * DM], ep0[:, b2 * DM:(b2 + 1) * DM],
                mcol[:, b2 * 8 + 1:b2 * 8 + 2], None, ALU.subtract),
                inc=("EZC", 1))
            p.on("scalar", lambda e, b2=b2: e.activation(
                ep0[:, b2 * DM:(b2 + 1) * DM], ep1[:, b2 * DM:(b2 + 1) * DM], AF.Square),
                wait=[("EZC", p.cnt("EZC"))], inc=("SQ3", 1))
            p.on("vector", lambda e, b2=b2: e.reduce_sum(
                mcol[:, b2 * 8 + 2:b2 * 8 + 3], ep0[:, b2 * DM:(b2 + 1) * DM], AX.X),
                wait=[("SQ3", p.cnt("SQ3"))])
            p.on("vector", lambda e, b2=b2: e.tensor_scalar(
                mcol[:, b2 * 8 + 2:b2 * 8 + 3], mcol[:, b2 * 8 + 2:b2 * 8 + 3], 1.0 / DM, None, ALU.mult),
                inc=("EV3", 1))
            p.on("scalar", lambda e, b2=b2: e.activation(
                mcol[:, b2 * 8 + 3:b2 * 8 + 4], mcol[:, b2 * 8 + 2:b2 * 8 + 3], AF.Ln, bias=konst[:, 2:3]),
                wait=[("EV3", p.cnt("EV3"))])
            p.on("scalar", lambda e, b2=b2: e.activation(
                mcol[:, b2 * 8 + 3:b2 * 8 + 4], mcol[:, b2 * 8 + 3:b2 * 8 + 4], AF.Exp, scale=konst[:, 1:2]),
                inc=("RS3", 1))
            p.on("vector", lambda e, b2=b2: e.tensor_scalar(
                ep1[:, b2 * DM:(b2 + 1) * DM], ep1[:, b2 * DM:(b2 + 1) * DM],
                mcol[:, b2 * 8 + 3:b2 * 8 + 4], None, ALU.mult),
                wait=[("RS3", p.cnt("RS3"))], inc=("EP3", 1))
            zs2_t = p.on("sync", lambda e, tq=tq, b2=b2: e.dma_start(
                out=z2t4[tq], in_=ep1[:, b2 * DM:(b2 + 1) * DM]),
                wait=[("EP3", p.cnt("EP3"))], inc=("ZS2%d" % b2, 16))
            zs2_slot[b2] = zs2_t

    mmug_slot = [0, 0]
    zs2_slot = [0, 0]
    wfd_q = {}
    yqb_all = {}
    evf_all = {}
    emit_b1(0)
    wfd_q[0] = p.cnt("WFD"); yqb_all[0] = p.cnt("YQB"); evf_all[0] = p.cnt("EVF")
    for q in range(NQ):
        emit_experts(q, yqb_all[q], evf_all[q])
        cmb_q = [p.cnt("CMB%d" % s) for s in range(4)]
        if q + 1 < NQ:
            emit_b1(q + 1)
            wfd_q[q + 1] = p.cnt("WFD"); yqb_all[q + 1] = p.cnt("YQB"); evf_all[q + 1] = p.cnt("EVF")
        emit_b3(q, cmb_q)
    p.wait("sync", [("ZS20", p.cnt("ZS20")), ("ZS21", p.cnt("ZS21"))])


def emit_phase_c0(nc, p, al, d, P0, P1):
    wb = al("wbc0", [128, 8 * DI], BF16)
    zb = al("zbc0", [128, 8 * 4], BF16)
    z2tl = al("z2tl", [3, DM], F32)
    idf = al("idc0", [128, 128], F32)
    ob = al("obc0", [128, NJC * 4], F32)
    ps = PV(P0, 0, 1024)
    pt = [PV(P1, 0, 512), PV(P1, 512, 512)]

    W_in4 = d["W_in"].rearrange("(k p) c -> p k c", p=128)
    p.on("gpsimd", lambda e: e.dma_start(
        out=wb[:, :].rearrange("p (k c) -> p k c", k=8),
        in_=W_in4[:, :, 0:DI]), inc=("LW", 16))
    p.on("sync", lambda e: e.dma_start(out=z2tl[:, :], in_=d["z2"][T_CORE - 3:T_CORE, :]),
         inc=("LZ", 16))
    p.on("sync", lambda e: e.dma_start(out=idf[:, :], in_=d["identd"]), inc=("LZ", 16))
    p.on("vector", lambda e: e.memset(zb[:, :], 0.0), inc=("MS", 1))
    for k in range(8):
        sl = k % 2
        w = [("LZ", 32)] if k <= 1 else [("ZBC", p.cnt("ZBC") - 1)]
        p.on("tensor", lambda e, k=k, sl=sl: e.transpose(
            pt[sl][:, 0:3], z2tl[0:3, k * 128:(k + 1) * 128], idf[0:3, 0:3]),
            wait=w, inc=("TRZ", 1))
        p.on("scalar", lambda e, k=k, sl=sl: e.activation(
            zb[:, k * 4 + 1:k * 4 + 4], pt[sl][:, 0:3], AF.Copy),
            wait=[("TRZ", p.cnt("TRZ")), ("MS", 1)], inc=("ZBC", 1))
    for j in range(NJC):
        sl = (j % 2) * 512
        for k in range(8):
            w = None
            if j == 0 and k == 0:
                w = [("LW", 16), ("ZBC", 8)]
            elif k == 0 and j >= 2:
                w = [("EVC", p.cnt("EVC") - 1)]
            p.on("tensor", lambda e, j=j, k=k, sl=sl: e.matmul(
                ps[:, sl:sl + 4], wb[:, k * DI + j * 128:k * DI + (j + 1) * 128],
                zb[:, k * 4:(k + 1) * 4], start=(k == 0), stop=(k == 7)),
                wait=w, inc=(("MMC", 1) if k == 7 else None))
        p.on("vector", lambda e, j=j, sl=sl: e.tensor_copy(
            ob[:, j * 4:(j + 1) * 4], ps[:, sl:sl + 4]),
            wait=[("MMC", p.cnt("MMC"))], inc=("EVC", 1))
    p.on("sync", lambda e: e.dma_start(out=d["tail_self"], in_=ob[:, :]),
         wait=[("EVC", p.cnt("EVC"))], inc=("TS", 16))
    p.on("gpsimd", lambda e: e.collective_compute(
        "AllGather", ALU.bypass, replica_groups=PAIRS,
        ins=[d["tail_self"].opt()], outs=[d["tail_gat"].opt()]),
        wait=[("TS", 16)], inc=("CCT", 1))
    p.wait("gpsimd", [("CCT", 1)])


def emit_phase_c1(nc, p, al, d, P0, P1):
    z2 = d["z2"]
    cwc_d = d["cwc"]
    cbc_d = d["cbc"]
    dpc_d = d["dpc"]
    alr_d = d["alr"]
    alc_d = d["alc"]
    cumin_d = d["cumin"]
    sufst_d = d["sufst"]
    masktl_d = d["masktl"]
    ones128_d = d["ones128"]
    preln = d["preln"]
    szsp = d["szsp"]
    hend = d["hend"]
    cst_o = d["cst"]
    sgst_o = d["sgst"]

    wis = al("wis", [128, 2 * 8 * 128], BF16)
    wob = al("wobc", [128, NJC * DM], BF16)
    wxb = al("wxb", [128, NJC * 48], BF16)
    cwc = al("cwcs", [128, NJC * 4], F32)
    cbc = al("cbcs", [128, NJC], F32)
    dpb = al("dpbs", [128, NJC * 128], BF16)
    dpc_s = al("dpcs", [128, NJC], F32)
    onesb = al("onesb", [1, 128], BF16)
    rab = al("rabc", [128, DST], F32)
    cumin = al("cumins", [128, 128], F32)
    sufst = al("sufsts", [128, 128], F32)
    masktl = al("masktls", [128, 128], F32)
    ones128 = al("ones128s", [128, 128], F32)
    identf = al("identfc", [128, 128], F32)
    arow = al("arow", [1, DST], F32)
    acol = al("acolc", [16, 1], F32)
    abc = al("abc", [128, DST], F32)
    onesr = al("onesrc", [1, 128], F32)
    negs = al("negs", [1, 128], F32)
    konst = al("konstc", [128, 4], F32)
    cselc = al("cselc", [128, 1], F32)
    z2g = al("z2g", [128, 4 * DM], F32)
    z2tb = al("z2tb", [128, 8 * 512], BF16)
    xg = al("xg", [128, NJC * 520], BF16)
    zTg = al("zTg", [128, NJC * 512], BF16)
    xcT = al("xcT", [128, NJC * 128], F32)
    xcTb = al("xcTb", [128, NJC * 128], BF16)
    xcb = al("xcb", [128, NJC * 128], BF16)
    dcol = al("dcol", [128, 8], F32)
    bbm = al("bbm", [128, DST], F32)
    bbm2 = al("bbm2", [128, DST], BF16)
    srow = al("srowc", [1, 128], F32)
    srow_loc = al("srowlc", [1, 128], F32)
    slc = al("slcc", [128, 1], F32)
    sgc = al("sgc", [128, 1], F32)
    spv = al("spv", [128, 1], F32)
    dmt = al("dmtc", [128, 128], F32)
    escr = al("escr", [128, 128], BF16)
    g1s = al("g1s", [128, 128], BF16)
    grb = al("grbc", [16, 4], F32)
    accT = al("accTc", [128, 128], F32)
    acbf = al("acbf", [128, 128], BF16)
    ctsb = al("ctsb", [16, 128], F32)
    ctbb = al("ctbb", [16, 128], BF16)
    ctrow = al("ctrowc", [1, DST * 128], BF16)
    cgb = al("cgbc", [16, 128], BF16)
    ebuf = al("ebufc", [16, 128], F32)
    hst = al("hstc", [16, DI], F32)
    hbf = al("hbfc", [16, DI], BF16)
    ysc = al("yscc", [128, NJC * 128], BF16)
    preb = al("prebc", [128, DM], F32)
    cstb = al("cstbc", [128, NCH * DST], F32)
    sgst = al("sgstc", [128, NCH], F32)
    pbig = PV(P0, 0, 2048)
    pmid = PV(P1, 0, 1024)
    psml0 = PV(P1, 1024, 512)
    psml1 = PV(P1, 1536, 512)

    z2t4 = z2.rearrange("(g f p) m -> g p f m", f=4, p=128)
    pre4 = preln.rearrange("(ch p) m -> ch p m", p=128)
    szsp4 = szsp.rearrange("(j p) t -> p j t", p=128)

    W_in4 = d["W_in"].rearrange("(k p) c -> p k c", p=128)
    p.on("gpsimd", lambda e: e.dma_start(
        out=wob[:, :], in_=d["WoutP"]), inc=("LWC", 16))
    p.on("gpsimd", lambda e: e.dma_start(
        out=wxb[:, :], in_=d["WxP"]), inc=("LWC", 16))
    for dst, srcd in ((cwc, cwc_d), (cbc, cbc_d), (cumin, cumin_d),
                      (sufst, sufst_d), (masktl, masktl_d), (ones128, ones128_d),
                      (identf, d["identd"]), (arow, alr_d), (acol, alc_d),
                      (dpc_s, dpc_d), (cselc, d["csel"])):
        p.on("sync", lambda e, dst=dst, srcd=srcd: e.dma_start(out=dst[:, :], in_=srcd),
             inc=("LDC", 16))
    ldc_tot = p.cnt("LDC")
    p.on("gpsimd", lambda e: e.dma_start(
        out=xg[:, :].rearrange("p (j c) -> p j c", j=NJC)[:, :, 0:4],
        in_=d["tail_gat"][0:128, :].rearrange("p (j c) -> p j c", j=NJC)), inc=("LWC", 16))
    p.on("vector", lambda e: e.tensor_scalar(
        xg[:, :].rearrange("p (j c) -> p j c", j=NJC)[:, :, 0:4],
        xg[:, :].rearrange("p (j c) -> p j c", j=NJC)[:, :, 0:4],
        cselc[:, 0:1], None, ALU.mult),
        wait=[("LWC", 48), ("LDC", ldc_tot)], inc=("TSC", 1))
    p.on("vector", lambda e: e.memset(onesr[:, :], 1.0))
    p.on("vector", lambda e: e.memset(konst[:, 0:1], -1.0))
    p.on("vector", lambda e: e.memset(spv[:, :], 0.0))
    p.on("vector", lambda e: e.memset(hst[:, :], 0.0), inc=("MST", 1))
    p.on("vector", lambda e: e.tensor_copy(hbf[:, :], hst[:, :]), inc=("HBF", 1))
    p.on("scalar", lambda e: e.activation(arow[:, :], arow[:, :], AF.Exp),
         wait=[("LDC", ldc_tot)], inc=("ARP", 1))
    p.on("vector", lambda e: e.tensor_scalar(arow[:, :], arow[:, :], -1.0, None, ALU.mult),
         wait=[("ARP", 1), ("MST", 1)], inc=("ARN", 1))
    p.on("tensor", lambda e: e.matmul(
        psml0[:, 0:DST], onesr[0:1, :], arow[0:1, :], start=True, stop=True),
        wait=[("ARN", 1)], inc=("MMA", 1))
    p.on("vector", lambda e: e.tensor_copy(abc[:, :], psml0[:, 0:DST]),
         wait=[("MMA", 1)], inc=("ABC", 1))
    p.on("scalar", lambda e: e.activation(acol[:, :], acol[:, :], AF.Exp),
         wait=[("LDC", ldc_tot)], inc=("ACX", 1))
    p.on("vector", lambda e: e.tensor_scalar(acol[:, :], acol[:, :], -1.0, None, ALU.mult),
         wait=[("ACX", 1)], inc=("ACL", 1))
    for j in range(NJC):
        p.on("vector", lambda e, j=j: e.tensor_scalar(
            dpb[:, j * 128:(j + 1) * 128], ones128[:, :], dpc_s[:, j:j + 1], None, ALU.mult),
            wait=([("LDC", ldc_tot)] if j == 0 else None),
            inc=(("DPB", 1) if j == NJC - 1 else None))
    p.on("vector", lambda e: e.tensor_copy(onesb[:, :], onesr[:, :]), inc=("ONB", 1))

    lwc_tot = 48
    spl_slot = [0, 0]
    wis_slot = [0, 0]

    for g in range(NGR):
        lz_w = []
        if g >= 1:
            lz_w = [("TRP", p.cnt("TRP")), ("PRE", p.cnt("PRE"))]
        lz_tgt = p.on("sync", lambda e, g=g: e.dma_start(
            out=z2g[:, :].rearrange("p (f m) -> p f m", f=4), in_=z2t4[g]),
            wait=(lz_w or None), inc=("LZ2", 16))
        for f4 in range(4):
            for k in range(8):
                i8 = f4 * 8 + k
                sl = i8 % 2
                psx = [psml0, psml1][sl]
                wtr = None
                if i8 == 0:
                    wtr = [("LZ2", lz_tgt)]
                    if g == 0:
                        wtr.append(("ABC", 1))
                    if g >= 1:
                        wtr += [("EVZ", p.cnt("EVZ")), ("CTS", p.cnt("CTS")),
                                ("DMB", p.cnt("DMB")), ("GS2", p.cnt("GS2")),
                                ("EBF", p.cnt("EBF")), ("SRW", p.cnt("SRW"))]
                elif i8 >= 2:
                    wtr = [("EVZ", p.cnt("EVZ") - 1)]
                p.on("tensor", lambda e, f4=f4, k=k, psx=psx: e.transpose(
                    psx[:, 128:256],
                    z2g[:, f4 * DM + k * 128:f4 * DM + (k + 1) * 128], identf[:, :]),
                    wait=wtr, inc=("TRP", 1))
                p.on("scalar", lambda e, f4=f4, k=k, psx=psx: e.activation(
                    z2tb[:, k * 512 + f4 * 128:k * 512 + (f4 + 1) * 128],
                    psx[:, 128:256], AF.Copy),
                    wait=[("TRP", p.cnt("TRP"))], inc=("EVZ", 1))
        evz_g = p.cnt("EVZ")
        if g >= 1:
            p.on("vector", lambda e: e.tensor_copy(
                xg[:, :].rearrange("p (j c) -> p j c", j=NJC)[:, :, 1:4],
                xg[:, :].rearrange("p (j c) -> p j c", j=NJC)[:, :, 513:516]),
                wait=[("CNV", p.cnt("CNV"))], inc=("XGC", 1))
        for o in range(32):
            sl = o % 2
            wlw = [("MMX", wis_slot[sl])] if wis_slot[sl] > 0 else None
            lwi_tgt = p.on("gpsimd", lambda e, o=o, sl=sl: e.dma_start(
                out=wis[:, sl * 1024:(sl + 1) * 1024],
                in_=d["WinP"][o]),
                wait=wlw, inc=("LWI%d" % sl, 16))
            wmm = [("LWI%d" % sl, lwi_tgt)]
            if o == 0:
                wmm += [("EVZ", evz_g), ("LWC", lwc_tot), ("TSC", 1)]
            elif o >= 2:
                wmm += [("EVX", p.cnt("EVX") - 1)]
            for k in range(8):
                p.on("tensor", lambda e, o=o, k=k, sl=sl: e.matmul(
                    pmid[:, sl * 512:(sl + 1) * 512],
                    wis[:, sl * 1024 + k * 128:sl * 1024 + (k + 1) * 128],
                    z2tb[:, k * 512:(k + 1) * 512],
                    start=(k == 0), stop=(k == 7)),
                    wait=(wmm if k == 0 else None),
                    inc=(("MMX", 1) if k == 7 else None))
            wis_slot[sl] = p.cnt("MMX")
            wev = [("MMX", p.cnt("MMX"))]
            if o < 16:
                if g >= 1 and o == 0:
                    wev.append(("XGC", p.cnt("XGC")))
                    wev.append(("CNV", p.cnt("CNV")))
                p.on("scalar", lambda e, o=o, sl=sl: e.activation(
                    xg[:, o * 520 + 4:o * 520 + 4 + 512], pmid[:, sl * 512:(sl + 1) * 512], AF.Copy),
                    wait=wev, inc=("EVX", 1))
            else:
                if o == 16:
                    wev.append(("YP3", p.cnt("YP3")))
                    wev.append(("SPL", p.cnt("SPL")))
                p.on("scalar", lambda e, o=o, sl=sl: e.activation(
                    zTg[:, (o - 16) * 512:(o - 15) * 512], pmid[:, sl * 512:(sl + 1) * 512], AF.Copy),
                    wait=wev, inc=("EVX", 1))
        evx_g = p.cnt("EVX")
        p.on("scalar", lambda e: e.activation(zTg[:, :], zTg[:, :], AF.Silu),
             wait=[("EVX", evx_g)], inc=("SZD", 1))
        szd_g = p.cnt("SZD")
        p.on("sync", lambda e, g=g: e.dma_start(
            out=szsp4[:, :, g * 512:(g + 1) * 512],
            in_=zTg[:, :].rearrange("p (j t) -> p j t", j=NJC)),
            wait=[("SZD", szd_g)], inc=("SPL", 16))

        for cc in range(4):
            ch = g * 4 + cc
            wcv = [("EVX", evx_g), ("LDC", ldc_tot), ("TSC", 1)]
            if p.cnt("XDB") > 0:
                wcv.append(("XDB", p.cnt("XDB")))
                wcv.append(("TRX", p.cnt("TRX")))
                wcv.append(("YP1", p.cnt("YP1")))
            for j in range(NJC):
                base = j * 520 + cc * 128
                p.on("scalar", lambda e, j=j, base=base: e.activation(
                    xcT[:, j * 128:(j + 1) * 128], xg[:, base + 1:base + 129],
                    AF.Identity, bias=cbc[:, j:j + 1], scale=cwc[:, j * 4 + 0:j * 4 + 1]),
                    wait=(wcv if j == 0 else None), inc=("CV0", 1))
                for k in range(1, 4):
                    p.on("vector", lambda e, j=j, k=k, base=base: e.scalar_tensor_tensor(
                        xcT[:, j * 128:(j + 1) * 128], xg[:, base + 1 + k:base + 129 + k],
                        cwc[:, j * 4 + k:j * 4 + k + 1], xcT[:, j * 128:(j + 1) * 128],
                        ALU.mult, ALU.add),
                        wait=[("CV0", p.cnt("CV0"))] if k == 1 else [("CNV", p.cnt("CNV"))],
                        inc=("CNV", 1))
            p.on("scalar", lambda e: e.activation(xcT[:, :], xcT[:, :], AF.Silu),
                 wait=[("CNV", p.cnt("CNV"))], inc=("SCV", 1))
            p.on("vector", lambda e: e.tensor_copy(xcTb[:, :], xcT[:, :]),
                 wait=[("SCV", p.cnt("SCV")), ("DPB", 1)], inc=("XCB", 1))
            wxd = [("XCB", p.cnt("XCB"))]
            if p.cnt("CSD") > 0:
                wxd += [("DLT", p.cnt("DLT")), ("BBD", p.cnt("BBD")), ("CSD", p.cnt("CSD"))]
            for j in range(NJC):
                p.on("tensor", lambda e, j=j: e.matmul(
                    psml0[:, 0:48], xcTb[:, j * 128:(j + 1) * 128],
                    wxb[:, j * 48:(j + 1) * 48],
                    start=(j == 0), stop=(j == NJC - 1)),
                    wait=(wxd if j == 0 else None),
                    inc=(("XDB", 1) if j == NJC - 1 else None))
            xdb_c = p.cnt("XDB")
            p.on("scalar", lambda e: e.activation(
                dcol[:, 0:1], psml0[:, 0:1], AF.Abs),
                wait=[("XDB", xdb_c)], inc=("DAB", 1))
            p.on("scalar", lambda e: e.activation(
                dcol[:, 1:2], dcol[:, 0:1], AF.Exp, scale=konst[:, 0:1]),
                wait=[("DAB", p.cnt("DAB"))], inc=("DXP", 1))
            p.on("scalar", lambda e: e.activation(
                dcol[:, 2:3], dcol[:, 1:2], AF.Ln, bias=1.0), inc=("DLN", 1))
            p.on("scalar", lambda e: e.activation(
                dcol[:, 3:4], psml0[:, 0:1], AF.Relu), inc=("DRL", 1))
            p.on("vector", lambda e: e.tensor_add(
                dcol[:, 4:5], dcol[:, 3:4], dcol[:, 2:3]),
                wait=[("DRL", p.cnt("DRL"))], inc=("DLT", 1))
            dlt_c = p.cnt("DLT")
            p.on("vector", lambda e: e.tensor_scalar(
                bbm[:, :], psml0[:, 1:17], dcol[:, 4:5], None, ALU.mult), inc=("BBD", 1))
            p.on("vector", lambda e, ch=ch: e.tensor_copy(
                cstb[:, ch * DST:(ch + 1) * DST], psml0[:, 17:33]),
                inc=("CSD", 1))
            wcs = [("DLT", dlt_c), ("BBD", p.cnt("BBD")), ("CSD", p.cnt("CSD"))]
            if p.cnt("SGD") > 0:
                wcs += [("SGD", p.cnt("SGD")), ("RRD", p.cnt("RRD")), ("GDC", p.cnt("GDC"))]
            p.on("tensor", lambda e: e.matmul(
                psml0[:, 64:65], cumin[:, :], dcol[:, 4:5], start=True, stop=True),
                wait=wcs, inc=("MMS", 1))
            p.on("tensor", lambda e: e.matmul(
                psml0[:, 65:66], sufst[:, :], dcol[:, 4:5], start=True, stop=True),
                inc=("MMS", 1))
            p.on("tensor", lambda e: e.matmul(
                psml0[:, 66:67], ones128[:, :], dcol[:, 4:5], start=True, stop=True),
                inc=("MMS", 1))
            mms_c = p.cnt("MMS")
            p.on("vector", lambda e: e.tensor_copy(slc[:, :], psml0[:, 64:65]),
                 wait=[("MMS", mms_c)], inc=("SGW", 1))
            p.on("vector", lambda e: e.tensor_tensor(
                sgc[:, :], psml0[:, 64:65], spv[:, :], ALU.add),
                inc=("SGW", 1))
            p.on("vector", lambda e, ch=ch: e.tensor_copy(sgst[:, ch:ch + 1], sgc[:, :]))
            p.on("vector", lambda e: e.tensor_tensor(
                spv[:, :], spv[:, :], psml0[:, 66:67], ALU.add), inc=("SGD", 1))
            p.on("vector", lambda e: e.tensor_scalar(
                rab[:, :], abc[:, :], psml0[:, 65:66], None, ALU.mult),
                wait=[("MMS", mms_c), ("BB2", p.cnt("BB2"))], inc=("RRD", 1))
            p.on("scalar", lambda e: e.activation(
                rab[:, :], rab[:, :], AF.Exp),
                wait=[("RRD", p.cnt("RRD"))], inc=("ERB", 1))
            p.on("vector", lambda e: e.tensor_tensor(
                bbm2[:, :], rab[:, :], bbm[:, :], ALU.mult),
                wait=[("ERB", p.cnt("ERB"))], inc=("BB2", 1))
            wtx = [("MMS", mms_c), ("CSD", p.cnt("CSD")), ("SGW", p.cnt("SGW"))]
            if p.cnt("PRE") > 0:
                wtx.append(("PRE", p.cnt("PRE")))
            if p.cnt("SRW") > 0:
                wtx += [("SRW", p.cnt("SRW")), ("CTS", p.cnt("CTS"))]
            p.on("tensor", lambda e: e.transpose(
                psml1[0:1, 0:128], sgc[:, :], identf[:, :]),
                wait=wtx, inc=("TRX", 1))
            p.on("tensor", lambda e: e.transpose(
                psml1[0:1, 128:256], slc[:, :], identf[:, :]),
                inc=("TRX", 1))
            p.on("tensor", lambda e, ch=ch: e.transpose(
                psml1[0:16, 256:384], cstb[:, ch * DST:(ch + 1) * DST], identf[:, :]),
                inc=("TRX", 1))
            for j in range(NJC):
                p.on("tensor", lambda e, j=j: e.transpose(
                    pbig[:, j * 128:(j + 1) * 128], xcT[:, j * 128:(j + 1) * 128],
                    identf[:, :]),
                    inc=(("TRX", 1) if j == NJC - 1 else None))
            trx_c = p.cnt("TRX")
            p.on("vector", lambda e: e.tensor_copy(srow[:, :], psml1[0:1, 0:128]),
                 wait=[("TRX", trx_c)], inc=("SRW", 1))
            p.on("vector", lambda e: e.tensor_copy(srow_loc[:, :], psml1[0:1, 128:256]),
                 inc=("SRW", 1))
            p.on("vector", lambda e: e.tensor_copy(ctsb[:, :], psml1[0:16, 256:384]),
                 inc=("CTS", 1))
            cts_c = p.cnt("CTS")
            p.on("scalar", lambda e: e.activation(xcb[:, :], pbig[:, :], AF.Copy),
                 wait=[("TRX", trx_c)], inc=("XCC", 1))
            p.on("vector", lambda e: e.tensor_copy(ctbb[:, :], ctsb[:, :]), inc=("CTB", 1))
            p.on("sync", lambda e: e.dma_start(out=ctrow[0:1, :], in_=ctbb[:, :]),
                 wait=[("CTB", p.cnt("CTB")), ("PNM", p.cnt("PNM"))], inc=("CTR", 16))
            ctr_c = p.cnt("CTR")
            p.on("vector", lambda e: e.tensor_scalar(
                negs[:, :], srow[:, :], -1.0, None, ALU.mult),
                wait=[("SRW", p.cnt("SRW")), ("DMB", p.cnt("DMB"))], inc=("LRT", 1))
            p.on("tensor", lambda e: e.matmul(
                psml1[:, 256:384], onesr[0:1, :], srow[0:1, :], start=True, stop=False),
                wait=[("LRT", p.cnt("LRT")), ("CTS", cts_c)])
            p.on("tensor", lambda e: e.matmul(
                psml1[:, 256:384], negs[0:1, :], ones128[0:1, :], start=False, stop=True),
                inc=("MMT", 1))
            p.on("vector", lambda e: e.tensor_tensor(
                dmt[:, :], psml1[:, 256:384], masktl[:, :], ALU.add),
                wait=[("MMT", p.cnt("MMT"))], inc=("DMB", 1))
            dmb_c = p.cnt("DMB")
            for n in range(DST):
                wex = [("GS1", p.cnt("GS1"))]
                if n == 0:
                    wex = [("DMB", dmb_c), ("ABC", 1), ("SCB", p.cnt("SCB"))]
                p.on("scalar", lambda e, n=n: e.activation(
                    escr[:, :], dmt[:, :], AF.Exp, scale=abc[:, n:n + 1]),
                    wait=wex, inc=("EXN", 1))
                wg1 = [("EXN", p.cnt("EXN"))]
                if n == 0:
                    wg1.append(("BBD", p.cnt("BBD")))
                p.on("vector", lambda e, n=n: e.tensor_scalar(
                    g1s[:, :], escr[:, :], bbm[:, n:n + 1], None, ALU.mult),
                    wait=wg1, inc=("GS1", 1))
                psl = psml0 if n % 2 == 0 else psml1
                p.on("tensor", lambda e, n=n, psl=psl: e.matmul(
                    psl[:, 384:512], onesb[0:1, :], ctrow[0:1, n * 128:(n + 1) * 128],
                    start=True, stop=True),
                    wait=([("CTR", ctr_c), ("ONB", 1), ("DMB", dmb_c)] if n <= 1
                          else [("GS2", p.cnt("GS2") - 1)]),
                    inc=("PNM", 1))
                p.on("vector", lambda e, n=n, psl=psl: e.tensor_tensor(
                    g1s[:, :], g1s[:, :], psl[:, 384:512], ALU.mult),
                    wait=[("PNM", p.cnt("PNM"))], inc=("GS2", 1))
                if n == 0:
                    p.on("vector", lambda e: e.tensor_copy(accT[:, :], g1s[:, :]))
                else:
                    p.on("vector", lambda e: e.tensor_tensor(
                        accT[:, :], accT[:, :], g1s[:, :], ALU.add))
            p.on("vector", lambda e: e.tensor_copy(acbf[:, :], accT[:, :]), inc=("SCB", 1))
            scb_c = p.cnt("SCB")
            p.on("tensor", lambda e: e.matmul(
                psml0[0:16, 256:384], arow[0:1, :], srow_loc[0:1, :], start=True, stop=True),
                wait=[("SRW", p.cnt("SRW")), ("CGB", p.cnt("CGB")),
                      ("SCB", scb_c), ("RRD", p.cnt("RRD"))], inc=("PAM", 1))
            p.on("scalar", lambda e: e.activation(
                ebuf[:, :], psml0[0:16, 256:384], AF.Exp),
                wait=[("PAM", p.cnt("PAM"))], inc=("EBF", 1))
            p.on("vector", lambda e: e.tensor_tensor(
                cgb[:, :], ebuf[:, :], ctsb[:, :], ALU.mult),
                wait=[("EBF", p.cnt("EBF"))], inc=("CGB", 1))
            cgb_c = p.cnt("CGB")
            wym = [("SCB", scb_c), ("CGB", cgb_c), ("XCC", p.cnt("XCC")),
                   ("HBF", p.cnt("HBF")), ("YPE", p.cnt("YPE"))]
            for dj in range(NJC):
                p.on("tensor", lambda e, dj=dj: e.matmul(
                    pbig[:, dj * 128:(dj + 1) * 128], xcb[:, dj * 128:(dj + 1) * 128],
                    acbf[:, :], start=True, stop=False),
                    wait=(wym if dj == 0 else None))
                p.on("tensor", lambda e, dj=dj: e.matmul(
                    pbig[:, dj * 128:(dj + 1) * 128], hbf[:, dj * 128:(dj + 1) * 128],
                    cgb[:, :], start=False, stop=True),
                    inc=(("YMM", 1) if dj == NJC - 1 else None))
            ymm_c = p.cnt("YMM")
            p.on("scalar", lambda e: e.activation(ysc[:, :], pbig[:, :], AF.Copy),
                 wait=[("YMM", ymm_c), ("WOM", p.cnt("WOM"))], inc=("YPE", 1))
            p.on("vector", lambda e: e.tensor_tensor(
                xcTb[:, :], xcTb[:, :], dpb[:, :], ALU.mult),
                wait=[("YPE", p.cnt("YPE")), ("XDB", xdb_c)], inc=("YP1", 1))
            p.on("vector", lambda e: e.tensor_tensor(
                ysc[:, :], ysc[:, :], xcTb[:, :], ALU.add), inc=("YP2", 1))
            p.on("vector", lambda e, cc=cc: e.tensor_tensor(
                ysc[:, :], ysc[:, :],
                zTg[:, :].rearrange("p (j t) -> p j t", j=NJC)[:, :, cc * 128:(cc + 1) * 128],
                ALU.mult),
                wait=[("SZD", szd_g)], inc=("YP3", 1))
            yp3_c = p.cnt("YP3")
            wum = [("BB2", p.cnt("BB2")), ("YPE", p.cnt("YPE")), ("XCC", p.cnt("XCC"))]
            for f in range(4):
                p.on("tensor", lambda e, f=f: e.matmul(
                    pbig[0:16, f * 512:(f + 1) * 512], bbm2[:, :],
                    xcb[:, f * 512:(f + 1) * 512], start=True, stop=True),
                    wait=(wum if f == 0 else None),
                    inc=(("UMM", 1) if f == 3 else None))
            umm_c = p.cnt("UMM")
            p.on("vector", lambda e: e.tensor_scalar(
                grb[:, 0:1], acol[:, :], psml0[0:16, 66:67], None, ALU.mult),
                wait=[("ACL", 1)], inc=("GRD", 1))
            p.on("scalar", lambda e: e.activation(
                grb[:, 1:2], grb[:, 0:1], AF.Exp),
                wait=[("GRD", p.cnt("GRD"))], inc=("GDC", 1))
            p.on("vector", lambda e: e.tensor_scalar(
                hst[:, :], hst[:, :], grb[:, 1:2], None, ALU.mult),
                wait=[("GDC", p.cnt("GDC")), ("UMM", umm_c)])
            p.on("vector", lambda e: e.tensor_tensor(
                hst[:, :], hst[:, :], pbig[0:16, :], ALU.add), inc=("HUP", 1))
            p.on("vector", lambda e: e.tensor_copy(hbf[:, :], hst[:, :]),
                 wait=[("YMM", ymm_c)], inc=("HBF", 1))
            wwm = [("YP3", yp3_c), ("UMM", umm_c), ("HUP", p.cnt("HUP"))]
            if p.cnt("PRE") > 0:
                wwm.append(("PRE", p.cnt("PRE")))
            for dj in range(NJC):
                for f in range(2):
                    p.on("tensor", lambda e, dj=dj, f=f: e.matmul(
                        pbig[:, f * 512:(f + 1) * 512], ysc[:, dj * 128:(dj + 1) * 128],
                        wob[:, dj * 1024 + f * 512:dj * 1024 + (f + 1) * 512],
                        start=(dj == 0), stop=(dj == NJC - 1)),
                        wait=(wwm if (dj == 0 and f == 0) else None),
                        inc=(("WOM", 1) if (dj == NJC - 1 and f == 1) else None))
            wom_c = p.cnt("WOM")
            wpr = [("WOM", wom_c)]
            if spl_slot[0] > 0:
                wpr.append(("PSP", spl_slot[0]))
            p.on("vector", lambda e, cc=cc: e.tensor_tensor(
                preb[:, :], z2g[:, cc * DM:(cc + 1) * DM],
                pbig[:, 0:1024], ALU.add),
                wait=wpr, inc=("PRE", 1))
            psp_t = p.on("sync", lambda e, ch=ch: e.dma_start(
                out=pre4[ch], in_=preb[:, :]),
                wait=[("PRE", p.cnt("PRE"))], inc=("PSP", 16))
            spl_slot[0] = psp_t
    p.on("sync", lambda e: e.dma_start(out=hend, in_=hst[:, :]),
         wait=[("HUP", p.cnt("HUP"))], inc=("HSP", 16))
    p.on("sync", lambda e: e.dma_start(out=cst_o, in_=cstb[:, :]),
         wait=[("CSD", p.cnt("CSD"))], inc=("HSP", 16))
    p.on("sync", lambda e: e.dma_start(out=sgst_o, in_=sgst[:, :]),
         wait=[("SGD", p.cnt("SGD"))], inc=("HSP", 16))
    p.on("gpsimd", lambda e: e.collective_compute(
        "AllGather", ALU.bypass, replica_groups=PAIRS,
        ins=[d["hend"].opt()], outs=[d["hend_gat"].opt()]),
        wait=[("HSP", 16)], inc=("CCH", 1))
    p.wait("gpsimd", [("CCH", 1)])
    p.wait("sync", [("PSP", p.cnt("PSP")), ("SPL", p.cnt("SPL")), ("HSP", p.cnt("HSP"))])


def emit_phase_c2(nc, p, al, d, P0, P1):
    preln = d["preln"]
    szsp = d["szsp"]
    cst_d = d["cst"]
    sgst_d = d["sgst"]
    alr_d = d["alr"]
    z3 = d["z3"]

    wob = al("wob2", [128, NJC * DM], BF16)
    identf = al("identf2", [128, 128], F32)
    arow = al("arow2", [1, DST], F32)
    hselb = al("hselb2", [16, DI], BF16)
    hself = al("hself2", [16, DI], F32)
    cstb = al("cstb2", [128, NCH * DST], F32)
    sgstb = al("sgst2", [128, NCH], F32)
    sgrow = al("sgrow2", [1, 128], F32)
    ctsb = al("ctsb2", [16, 128], F32)
    ebuf = al("ebuf2", [16, 128], F32)
    cgb = al("cgb2", [16, 128], BF16)
    szc = al("szc2", [128, NJC * 128], BF16)
    dyb = al("dyb2", [128, NJC * 128], BF16)
    prb = al("prb2", [128, 2 * DM], F32)
    ep1 = al("ep1c2", [128, 2 * DM], F32)
    mcol = al("mcol2", [128, 2 * 8], F32)
    konst = al("konst2", [128, 2], F32)
    cselc = al("cselc2", [128, 1], F32)
    pbig = PV(P0, 0, 2048)
    pmid = PV(P1, 0, 1024)
    psml = PV(P1, 1024, 512)

    pre4 = preln.rearrange("(ch p) m -> ch p m", p=128)
    z34 = z3.rearrange("(ch p) m -> ch p m", p=128)
    szsp4 = szsp.rearrange("(j p) t -> p j t", p=128)

    p.on("gpsimd", lambda e: e.dma_start(
        out=wob[:, :], in_=d["WoutP"]), inc=("LW", 16))
    p.on("gpsimd", lambda e: e.dma_start(out=hself[:, :], in_=d["hend_gat"][0:DST, :]),
         inc=("LW", 16))
    for dst, srcd in ((cstb, cst_d), (sgstb, sgst_d), (arow, alr_d),
                      (identf, d["identd"]), (cselc, d["csel"])):
        p.on("sync", lambda e, dst=dst, srcd=srcd: e.dma_start(out=dst[:, :], in_=srcd),
             inc=("LD", 16))
    ld_tot = p.cnt("LD")
    p.on("vector", lambda e: e.memset(konst[:, 0:1], LN_EPS))
    p.on("vector", lambda e: e.memset(konst[:, 1:2], -0.5), inc=("MST", 1))
    p.on("vector", lambda e: e.tensor_scalar(
        hselb[:, :], hself[:, :], cselc[0:16, 0:1], None, ALU.mult),
        wait=[("LW", 32), ("LD", ld_tot)], inc=("HSB", 1))
    p.on("scalar", lambda e: e.activation(arow[:, :], arow[:, :], AF.Exp),
         wait=[("LD", ld_tot)], inc=("ARP", 1))
    p.on("vector", lambda e: e.tensor_scalar(arow[:, :], arow[:, :], -1.0, None, ALU.mult),
         wait=[("ARP", 1)], inc=("ARN", 1))

    z3s_slot = [0, 0]
    prb_slot = [0, 0]
    for ch in range(NCH):
        b2 = ch % 2
        wlz = [("PRA", prb_slot[b2])] if prb_slot[b2] > 0 else None
        lpr_tgt = p.on("sync", lambda e, ch=ch, b2=b2: e.dma_start(
            out=prb[:, b2 * DM:(b2 + 1) * DM], in_=pre4[ch]),
            wait=wlz, inc=("LP%d" % b2, 16))
        wsz = [("DY3", p.cnt("DY3"))] if p.cnt("DY3") > 0 else None
        lsz_tgt = p.on("gpsimd", lambda e, ch=ch: e.dma_start(
            out=szc[:, :].rearrange("p (j t) -> p j t", j=NJC),
            in_=szsp4[:, :, ch * 128:(ch + 1) * 128]),
            wait=wsz, inc=("LSZ", 16))
        wtr = [("LD", ld_tot), ("ARN", 1)]
        if p.cnt("SRW") > 0:
            wtr += [("SRW", p.cnt("SRW")), ("CTS", p.cnt("CTS"))]
        p.on("tensor", lambda e, ch=ch: e.transpose(
            psml[0:1, 0:128], sgstb[:, ch:ch + 1], identf[:, :]),
            wait=wtr, inc=("TRX", 1))
        p.on("tensor", lambda e, ch=ch: e.transpose(
            psml[0:16, 128:256], cstb[:, ch * DST:(ch + 1) * DST], identf[:, :]),
            inc=("TRX", 1))
        trx_c = p.cnt("TRX")
        p.on("vector", lambda e: e.tensor_copy(sgrow[:, :], psml[0:1, 0:128]),
             wait=[("TRX", trx_c), ("PAM", p.cnt("PAM"))], inc=("SRW", 1))
        p.on("vector", lambda e: e.tensor_copy(ctsb[:, :], psml[0:16, 128:256]),
             inc=("CTS", 1))
        p.on("tensor", lambda e: e.matmul(
            psml[0:16, 256:384], arow[0:1, :], sgrow[0:1, :], start=True, stop=True),
            wait=[("SRW", p.cnt("SRW")), ("CTS", p.cnt("CTS")),
                  ("EBF", p.cnt("EBF"))], inc=("PAM", 1))
        p.on("scalar", lambda e: e.activation(
            ebuf[:, :], psml[0:16, 256:384], AF.Exp),
            wait=[("PAM", p.cnt("PAM"))], inc=("EBF", 1))
        p.on("vector", lambda e: e.tensor_tensor(
            cgb[:, :], ebuf[:, :], ctsb[:, :], ALU.mult),
            wait=[("EBF", p.cnt("EBF"))], inc=("CGB", 1))
        cgb_c = p.cnt("CGB")
        wym = [("CGB", cgb_c), ("HSB", 1)]
        if p.cnt("DYE") > 0:
            wym.append(("DYE", p.cnt("DYE")))
        for dj in range(NJC):
            p.on("tensor", lambda e, dj=dj: e.matmul(
                pbig[:, dj * 128:(dj + 1) * 128], hselb[:, dj * 128:(dj + 1) * 128],
                cgb[:, :], start=True, stop=True),
                wait=(wym if dj == 0 else None),
                inc=(("DYM", 1) if dj == NJC - 1 else None))
        dym_c = p.cnt("DYM")
        p.on("scalar", lambda e: e.activation(dyb[:, :], pbig[:, :], AF.Copy),
             wait=[("DYM", dym_c), ("WOM", p.cnt("WOM"))], inc=("DYE", 1))
        p.on("vector", lambda e: e.tensor_tensor(
            dyb[:, :], dyb[:, :], szc[:, :], ALU.mult),
            wait=[("DYE", p.cnt("DYE")), ("LSZ", lsz_tgt)], inc=("DY3", 1))
        dy3_c = p.cnt("DY3")
        wwm = [("DY3", dy3_c)]
        if p.cnt("PRA") > 0:
            wwm.append(("PRA", p.cnt("PRA")))
        for dj in range(NJC):
            for f in range(2):
                p.on("tensor", lambda e, dj=dj, f=f: e.matmul(
                    pmid[:, f * 512:(f + 1) * 512], dyb[:, dj * 128:(dj + 1) * 128],
                    wob[:, dj * 1024 + f * 512:dj * 1024 + (f + 1) * 512],
                    start=(dj == 0), stop=(dj == NJC - 1)),
                    wait=(wwm if (dj == 0 and f == 0) else None),
                    inc=(("WOM", 1) if (dj == NJC - 1 and f == 1) else None))
        wom_c = p.cnt("WOM")
        p.on("vector", lambda e, b2=b2: e.tensor_tensor(
            prb[:, b2 * DM:(b2 + 1) * DM], prb[:, b2 * DM:(b2 + 1) * DM],
            pmid[:, :], ALU.add),
            wait=[("WOM", wom_c), ("LP%d" % b2, lpr_tgt)], inc=("PRA", 1))
        p.on("vector", lambda e, b2=b2: e.reduce_sum(
            mcol[:, b2 * 8:b2 * 8 + 1], prb[:, b2 * DM:(b2 + 1) * DM], AX.X))
        p.on("vector", lambda e, b2=b2: e.tensor_scalar(
            mcol[:, b2 * 8 + 1:b2 * 8 + 2], mcol[:, b2 * 8:b2 * 8 + 1], 1.0 / DM, None, ALU.mult))
        wsub = [("ZS3%d" % b2, z3s_slot[b2])] if z3s_slot[b2] > 0 else None
        p.on("vector", lambda e, b2=b2: e.tensor_scalar(
            ep1[:, b2 * DM:(b2 + 1) * DM], prb[:, b2 * DM:(b2 + 1) * DM],
            mcol[:, b2 * 8 + 1:b2 * 8 + 2], None, ALU.subtract),
            wait=wsub, inc=("EZC", 1))
        p.on("scalar", lambda e, b2=b2: e.activation(
            prb[:, b2 * DM:(b2 + 1) * DM], ep1[:, b2 * DM:(b2 + 1) * DM], AF.Square),
            wait=[("EZC", p.cnt("EZC")), ("MST", 1)], inc=("SQ2", 1))
        p.on("vector", lambda e, b2=b2: e.reduce_sum(
            mcol[:, b2 * 8 + 2:b2 * 8 + 3], prb[:, b2 * DM:(b2 + 1) * DM], AX.X),
            wait=[("SQ2", p.cnt("SQ2"))])
        p.on("vector", lambda e, b2=b2: e.tensor_scalar(
            mcol[:, b2 * 8 + 2:b2 * 8 + 3], mcol[:, b2 * 8 + 2:b2 * 8 + 3], 1.0 / DM, None, ALU.mult),
            inc=("EV2", 1))
        p.on("scalar", lambda e, b2=b2: e.activation(
            mcol[:, b2 * 8 + 3:b2 * 8 + 4], mcol[:, b2 * 8 + 2:b2 * 8 + 3], AF.Ln, bias=konst[:, 0:1]),
            wait=[("EV2", p.cnt("EV2"))])
        p.on("scalar", lambda e, b2=b2: e.activation(
            mcol[:, b2 * 8 + 3:b2 * 8 + 4], mcol[:, b2 * 8 + 3:b2 * 8 + 4], AF.Exp, scale=konst[:, 1:2]),
            inc=("RS2", 1))
        p.on("vector", lambda e, b2=b2: e.tensor_scalar(
            ep1[:, b2 * DM:(b2 + 1) * DM], ep1[:, b2 * DM:(b2 + 1) * DM],
            mcol[:, b2 * 8 + 3:b2 * 8 + 4], None, ALU.mult),
            wait=[("RS2", p.cnt("RS2"))], inc=("EPF", 1))
        zs3_t = p.on("sync", lambda e, ch=ch, b2=b2: e.dma_start(
            out=z34[ch], in_=ep1[:, b2 * DM:(b2 + 1) * DM]),
            wait=[("EPF", p.cnt("EPF"))], inc=("ZS3%d" % b2, 16))
        z3s_slot[b2] = zs3_t
    p.wait("sync", [("ZS30", p.cnt("ZS30")), ("ZS31", p.cnt("ZS31"))])


def emit_phase_d(nc, p, al, d, P0, P1):
    z3 = d["z3"]
    w1P = d["w1P"]
    w2P = d["w2P"]
    h1P = d["h1P"]
    h2P = d["h2P"]
    w3P = d["w3P"]
    s1c = d["s1c"]
    s2c = d["s2c"]
    s1hc = d["s1hc"]
    s2hc = d["s2hc"]
    s3r = d["s3r"]
    z4 = d["z4"]

    identf = al("identfd", [128, 128], F32)
    onesr = al("onesrd", [1, 128], F32)
    konst = al("konstd", [128, 2], F32)
    s3b = al("s3b", [128, DM], F32)
    s3s = al("s3s", [1, DM], F32)
    s1s = al("s1s", [128, NJD], F32)
    s2s = al("s2s", [128, NJD], F32)
    s1hs = al("s1hs", [128, NJD], F32)
    s2hs = al("s2hs", [128, NJD], F32)
    z3b = al("z3b", [128, 2 * DM], F32)
    z3T = al("z3T", [128, 8 * 1024], BF16)
    w12j = al("w12j", [128, 2 * 2 * 8 * 128], BF16)
    hjb = al("hjb", [128, 2 * 32 * 128], BF16)
    h1b = al("h1b", [128, NJD * 1024], BF16)
    h2b = al("h2b", [128, NJD * 1024], BF16)
    silb = al("silbd", [128, 1024], BF16)
    z4b = z3b
    ep0 = al("ep0d", [128, 2 * DM], F32)
    ep1 = al("ep1d", [128, 2 * DM], F32)
    mcol = al("mcold", [128, 2 * 8], F32)
    p1 = PV(P0, 0, 1024)
    p2 = PV(P0, 1024, 1024)
    p3 = PV(P1, 0, 1024)
    p4 = PV(P1, 1024, 1024)
    pab = [(p1, p2), (p3, p4)]
    pd_ = [p1, p2]

    z3t4 = z3.rearrange("(tq p) m -> tq p m", p=128)
    z4t4 = z4.rearrange("(tq p) m -> tq p m", p=128)

    p.on("sync", lambda e: e.dma_start(out=identf[:, :], in_=d["identd"]), inc=("LG", 16))
    p.on("sync", lambda e: e.dma_start(out=s3s[:, :], in_=s3r), inc=("LG", 16))
    p.on("sync", lambda e: e.dma_start(out=s1s[:, :], in_=s1c), inc=("LG", 16))
    p.on("sync", lambda e: e.dma_start(out=s2s[:, :], in_=s2c), inc=("LG", 16))
    p.on("sync", lambda e: e.dma_start(out=s1hs[:, :], in_=s1hc), inc=("LG", 16))
    p.on("sync", lambda e: e.dma_start(out=s2hs[:, :], in_=s2hc), inc=("LG", 16))
    p.on("vector", lambda e: e.memset(konst[:, 0:1], LN_EPS))
    p.on("vector", lambda e: e.memset(konst[:, 1:2], -0.5))
    p.on("vector", lambda e: e.memset(onesr[:, :], 1.0), inc=("MST", 1))
    for f in range(2):
        p.on("tensor", lambda e, f=f: e.matmul(
            p2[:, f * 512:(f + 1) * 512], onesr[0:1, :], s3s[0:1, f * 512:(f + 1) * 512],
            start=True, stop=True), wait=[("MST", 1), ("LG", 96)], inc=("MMD", 1))
    p.on("vector", lambda e: e.tensor_copy(s3b[:, :], p2[:, :]), wait=[("MMD", 2)], inc=("S3B", 1))

    NQ = 4
    TQ = 8
    w12_slot = [0, 0]
    hj_slot = [None, None]
    z4s_slot = [0, 0]
    vstd_slot = [None, None]
    ge2_slot = [0, 0]
    ge3_slot = [0, 0]
    mm4_slot = [0, 0]
    prevq = {"MM1": 0, "MM2": 0, "MM3": 0, "MM4": 0, "GE2": 0, "EP4": 0,
             "SIL1": 0, "VST1": 0}

    for q in range(NQ):
        for T in range(TQ):
            tq = q * TQ + T
            b2 = tq % 2
            wld = [("YQ0", p.cnt("YQ0") - 1)] if p.cnt("YQ0") >= 2 else []
            if q >= 1 and T <= 1:
                wld = (wld or []) + [("EP4", prevq["EP4"])]
            lz_tgt = p.on("sync", lambda e, tq=tq, b2=b2: e.dma_start(
                out=z3b[:, b2 * DM:(b2 + 1) * DM], in_=z3t4[tq]),
                wait=(wld or None), inc=("LZ3%d" % b2, 16))
            wtr = [("LZ3%d" % b2, lz_tgt)]
            if T == 0:
                wtr.append(("MM4", prevq["MM4"]))
                wtr.append(("EP4", prevq["EP4"]))
            if p.cnt("YQ0") > 0:
                wtr.append(("YQ0", p.cnt("YQ0")))
            for k in range(8):
                p.on("tensor", lambda e, k=k, b2=b2: e.transpose(
                    p1[:, k * 128:(k + 1) * 128], z3b[:, b2 * DM + k * 128:b2 * DM + (k + 1) * 128],
                    identf[:, :]),
                    wait=(wtr if k == 0 else None),
                    inc=(("MT0", 1) if k == 7 else None))
            wev = [("MT0", p.cnt("MT0"))]
            if q >= 1 and T == 0:
                wev.append(("MM1", prevq["MM1"]))
            for k in range(8):
                p.on("scalar", lambda e, k=k, T=T: e.activation(
                    z3T[:, k * 1024 + T * 128:k * 1024 + (T + 1) * 128],
                    p1[:, k * 128:(k + 1) * 128], AF.Copy),
                    wait=(wev if k == 0 else None),
                    inc=(("YQ0", 1) if k == 7 else None))
        yq0_q = p.cnt("YQ0")
        mt0_q = p.cnt("MT0")

        for j in range(NJD):
            sl = j % 2
            wlw = [("MM1", w12_slot[sl])] if w12_slot[sl] > 0 else None
            lw_tgt = p.on("gpsimd", lambda e, j=j, sl=sl: e.dma_start(
                out=w12j[:, sl * 2048:sl * 2048 + 1024],
                in_=w1P[j]),
                wait=wlw, inc=("LW%d" % sl, 16))
            lw_tgt = p.on("gpsimd", lambda e, j=j, sl=sl: e.dma_start(
                out=w12j[:, sl * 2048 + 1024:sl * 2048 + 2048],
                in_=w2P[j]),
                inc=("LW%d" % sl, 16))
            pa, pb = pab[j % 2]
            wmm = [("LW%d" % sl, lw_tgt), ("YQ0", yq0_q)]
            if vstd_slot[j % 2] is not None:
                wmm.append(("VST1", vstd_slot[j % 2][0]))
                wmm.append(("SIL1", vstd_slot[j % 2][1]))
            if q == 0 and j <= 1:
                wmm.append(("S3B", 1))
            first = True
            for f in range(2):
                for k in range(8):
                    p.on("tensor", lambda e, j=j, sl=sl, f=f, k=k, pa=pa: e.matmul(
                        pa[:, f * 512:(f + 1) * 512],
                        w12j[:, sl * 2048 + k * 128:sl * 2048 + (k + 1) * 128],
                        z3T[:, k * 1024 + f * 512:k * 1024 + (f + 1) * 512],
                        start=(k == 0), stop=(k == 7)),
                        wait=(wmm if first else None))
                    first = False
                for k in range(8):
                    p.on("tensor", lambda e, j=j, sl=sl, f=f, k=k, pb=pb: e.matmul(
                        pb[:, f * 512:(f + 1) * 512],
                        w12j[:, sl * 2048 + 1024 + k * 128:sl * 2048 + 1024 + (k + 1) * 128],
                        z3T[:, k * 1024 + f * 512:k * 1024 + (f + 1) * 512],
                        start=(k == 0), stop=(k == 7)),
                        inc=(("MM1", 1) if (f == 1 and k == 7) else None))
            mm1_j = p.cnt("MM1")
            p.on("scalar", lambda e, j=j, pa=pa: e.activation(
                silb[:, :], pa[:, :], AF.Silu, scale=s1s[:, j:j + 1]),
                wait=[("MM1", mm1_j), ("VST1", p.cnt("VST1"))], inc=("SIL1", 1))
            vst_t = p.on("vector", lambda e, j=j, pb=pb: e.scalar_tensor_tensor(
                h1b[:, j * 1024:(j + 1) * 1024], pb[:, :], s2s[:, j:j + 1], silb[:, :],
                ALU.mult, ALU.mult),
                wait=[("SIL1", p.cnt("SIL1")), ("MM4", prevq["MM4"])], inc=("VST1", 1))
            vstd_slot[j % 2] = (vst_t, p.cnt("SIL1"))
            w12_slot[sl] = p.cnt("MM1")
        vst1_q = p.cnt("VST1")
        sil1_q = p.cnt("SIL1")

        for j2 in range(NJD):
            sl = j2 % 2
            wlh = [hj_slot[sl]] if hj_slot[sl] is not None else None
            lh_tgt = p.on("gpsimd", lambda e, j2=j2, sl=sl: e.dma_start(
                out=hjb[:, sl * 4096:(sl + 1) * 4096],
                in_=h1P[j2]),
                wait=wlh, inc=("LH%d" % sl, 16))
            pcx = [p1, p2][j2 % 2]
            wmm = [("LH%d" % sl, lh_tgt), ("VST1", vst1_q), ("SIL1", sil1_q)]
            if ge2_slot[j2 % 2] > 0:
                wmm.append(("GE2", ge2_slot[j2 % 2]))
            first = True
            for f in range(2):
                for k in range(32):
                    p.on("tensor", lambda e, j2=j2, sl=sl, f=f, k=k, pcx=pcx: e.matmul(
                        pcx[:, f * 512:(f + 1) * 512],
                        hjb[:, sl * 4096 + k * 128:sl * 4096 + (k + 1) * 128],
                        h1b[:, k * 1024 + f * 512:k * 1024 + (f + 1) * 512],
                        start=(k == 0), stop=(k == 31)),
                        wait=(wmm if first else None),
                        inc=(("MM2", 1) if (f == 1 and k == 31) else None))
                    first = False
            wge = [("MM2", p.cnt("MM2"))]
            if q >= 1 and j2 == 0:
                wge.append(("MM3", prevq["MM3"]))
                wge.append(("MM4", prevq["MM4"]))
            ge_t = p.on("scalar", lambda e, j2=j2, pcx=pcx: e.activation(
                h2b[:, j2 * 1024:(j2 + 1) * 1024], pcx[:, :], AF.Gelu, scale=s1hs[:, j2:j2 + 1]),
                wait=wge, inc=("GE2", 1))
            ge2_slot[j2 % 2] = ge_t
            hj_slot[sl] = ("MM2", p.cnt("MM2"))
        ge2_q = p.cnt("GE2")
        mm2_q = p.cnt("MM2")

        for j3 in range(NJD):
            sl = j3 % 2
            wlh = [hj_slot[sl]] if hj_slot[sl] is not None else None
            lh_tgt = p.on("gpsimd", lambda e, j3=j3, sl=sl: e.dma_start(
                out=hjb[:, sl * 4096:(sl + 1) * 4096],
                in_=h2P[j3]),
                wait=wlh, inc=("LH%d" % sl, 16))
            pcx = [p3, p4][j3 % 2]
            wmm = [("LH%d" % sl, lh_tgt), ("GE2", ge2_q)]
            if ge3_slot[j3 % 2] > 0:
                wmm.append(("GE3", ge3_slot[j3 % 2]))
            first = True
            for f in range(2):
                for k in range(32):
                    p.on("tensor", lambda e, j3=j3, sl=sl, f=f, k=k, pcx=pcx: e.matmul(
                        pcx[:, f * 512:(f + 1) * 512],
                        hjb[:, sl * 4096 + k * 128:sl * 4096 + (k + 1) * 128],
                        h2b[:, k * 1024 + f * 512:k * 1024 + (f + 1) * 512],
                        start=(k == 0), stop=(k == 31)),
                        wait=(wmm if first else None),
                        inc=(("MM3", 1) if (f == 1 and k == 31) else None))
                    first = False
            wge = [("MM3", p.cnt("MM3")), ("MM2", mm2_q)]
            ge_t = p.on("scalar", lambda e, j3=j3, pcx=pcx: e.activation(
                h1b[:, j3 * 1024:(j3 + 1) * 1024], pcx[:, :], AF.Gelu, scale=s2hs[:, j3:j3 + 1]),
                wait=wge, inc=("GE3", 1))
            ge3_slot[j3 % 2] = ge_t
            hj_slot[sl] = ("MM3", p.cnt("MM3"))
        ge3_q = p.cnt("GE3")

        wl3 = [("MM3", p.cnt("MM3"))]
        lw3_tgt = p.on("gpsimd", lambda e: e.dma_start(
            out=h2b[:, :], in_=w3P),
            wait=wl3, inc=("LW3D", 16))
        for T in range(TQ):
            tq = q * TQ + T
            b2 = tq % 2
            wzr = [("MT0", mt0_q)]
            if p.cnt("EP4") >= 2:
                wzr.append(("EP4", p.cnt("EP4") - 1))
            lz4_tgt = p.on("sync", lambda e, tq=tq, b2=b2: e.dma_start(
                out=z4b[:, b2 * DM:(b2 + 1) * DM], in_=z3t4[tq]),
                wait=wzr, inc=("LZ4%d" % b2, 16))
            pdx = pd_[T % 2]
            wmm = [("GE3", ge3_q), ("LW3D", lw3_tgt)]
            if mm4_slot[T % 2] > 0:
                wmm.append(("EV4", mm4_slot[T % 2]))
            first = True
            for f in range(2):
                for k in range(32):
                    p.on("tensor", lambda e, T=T, f=f, k=k, pdx=pdx: e.matmul(
                        pdx[:, f * 512:(f + 1) * 512],
                        h1b[:, k * 1024 + T * 128:k * 1024 + (T + 1) * 128],
                        h2b[:, k * 1024 + f * 512:k * 1024 + (f + 1) * 512],
                        start=(k == 0), stop=(k == 31)),
                        wait=(wmm if first else None),
                        inc=(("MM4", 1) if (f == 1 and k == 31) else None))
                    first = False
            mm4_now = p.cnt("MM4")
            ev_t = p.on("vector", lambda e, b2=b2, pdx=pdx: e.tensor_mul(
                ep0[:, b2 * DM:(b2 + 1) * DM], pdx[:, :], s3b[:, :]),
                wait=[("MM4", mm4_now), ("LZ4%d" % b2, lz4_tgt)], inc=("EV4", 1))
            mm4_slot[T % 2] = ev_t
            p.on("vector", lambda e, b2=b2: e.tensor_add(
                ep0[:, b2 * DM:(b2 + 1) * DM], ep0[:, b2 * DM:(b2 + 1) * DM],
                z4b[:, b2 * DM:(b2 + 1) * DM]))
            p.on("vector", lambda e, b2=b2: e.reduce_sum(
                mcol[:, b2 * 8:b2 * 8 + 1], ep0[:, b2 * DM:(b2 + 1) * DM], AX.X))
            p.on("vector", lambda e, b2=b2: e.tensor_scalar(
                mcol[:, b2 * 8 + 1:b2 * 8 + 2], mcol[:, b2 * 8:b2 * 8 + 1], 1.0 / DM, None, ALU.mult))
            wsub = [("ZS4%d" % b2, z4s_slot[b2])] if z4s_slot[b2] > 0 else None
            p.on("vector", lambda e, b2=b2: e.tensor_scalar(
                ep1[:, b2 * DM:(b2 + 1) * DM], ep0[:, b2 * DM:(b2 + 1) * DM],
                mcol[:, b2 * 8 + 1:b2 * 8 + 2], None, ALU.subtract),
                wait=wsub, inc=("EZ4", 1))
            p.on("scalar", lambda e, b2=b2: e.activation(
                ep0[:, b2 * DM:(b2 + 1) * DM], ep1[:, b2 * DM:(b2 + 1) * DM], AF.Square),
                wait=[("EZ4", p.cnt("EZ4"))], inc=("SQ4", 1))
            p.on("vector", lambda e, b2=b2: e.reduce_sum(
                mcol[:, b2 * 8 + 2:b2 * 8 + 3], ep0[:, b2 * DM:(b2 + 1) * DM], AX.X),
                wait=[("SQ4", p.cnt("SQ4"))])
            p.on("vector", lambda e, b2=b2: e.tensor_scalar(
                mcol[:, b2 * 8 + 2:b2 * 8 + 3], mcol[:, b2 * 8 + 2:b2 * 8 + 3], 1.0 / DM, None, ALU.mult),
                inc=("EVV4", 1))
            p.on("scalar", lambda e, b2=b2: e.activation(
                mcol[:, b2 * 8 + 3:b2 * 8 + 4], mcol[:, b2 * 8 + 2:b2 * 8 + 3], AF.Ln, bias=konst[:, 0:1]),
                wait=[("EVV4", p.cnt("EVV4"))])
            p.on("scalar", lambda e, b2=b2: e.activation(
                mcol[:, b2 * 8 + 3:b2 * 8 + 4], mcol[:, b2 * 8 + 3:b2 * 8 + 4], AF.Exp, scale=konst[:, 1:2]),
                inc=("RS4", 1))
            p.on("vector", lambda e, b2=b2: e.tensor_scalar(
                ep1[:, b2 * DM:(b2 + 1) * DM], ep1[:, b2 * DM:(b2 + 1) * DM],
                mcol[:, b2 * 8 + 3:b2 * 8 + 4], None, ALU.mult),
                wait=[("RS4", p.cnt("RS4"))], inc=("EP4", 1))
            zs4_t = p.on("sync", lambda e, tq=tq, b2=b2: e.dma_start(
                out=z4t4[tq], in_=ep1[:, b2 * DM:(b2 + 1) * DM]),
                wait=[("EP4", p.cnt("EP4"))], inc=("ZS4%d" % b2, 16))
            z4s_slot[b2] = zs4_t
        prevq = {"MM1": p.cnt("MM1"), "MM2": p.cnt("MM2"), "MM3": p.cnt("MM3"),
                 "MM4": p.cnt("MM4"), "GE2": p.cnt("GE2"), "EP4": p.cnt("EP4"),
                 "SIL1": p.cnt("SIL1"), "VST1": p.cnt("VST1")}
    p.wait("sync", [("ZS40", p.cnt("ZS40")), ("ZS41", p.cnt("ZS41"))])


def build_fused():
    from contextlib import ExitStack
    with ExitStack() as st:
        nc = bass.Bass("TRN2", target_bir_lowering=False)
        d = {}

        def ein(name, shape, dtype=F32):
            d[name] = nc.dram_tensor(name, shape, dtype, kind="ExternalInput").ap()

        def itn(name, shape, dtype=F32):
            d[name] = nc.dram_tensor(name, shape, dtype).ap()

        ein("u", [T_CORE, DM]); ein("uT", [DM, T_CORE]); ein("uTprev", [DM, T_CORE])
        ein("BwT", [DM, NSTATE]); ein("CwT", [NSTATE, DM]); ein("ll2", [128, 2])
        ein("Drow", [1, DM])
        ein("gwT", [DM, NE]); ein("rmsw", [1, DM])
        ein("W1P", [NE, NJ, 128, 1024]); ein("W2P", [NE, NJ, 128, 1024])
        ein("W3P", [NE, 128, NJ * DM])
        ein("identd", [128, 128])
        ein("W_in", [DM, 2 * DI]); ein("W_x", [DI, 48])
        ein("cwc", [128, NJC * 4]); ein("cbc", [128, NJC]); ein("dpc", [128, NJC])
        ein("alr", [1, DST]); ein("alc", [DST, 1]); ein("WoutP", [128, NJC * DM])
        ein("WinP", [32, 128, 1024]); ein("WxP", [128, NJC * 48])
        ein("cumin", [128, 128]); ein("sufst", [128, 128]); ein("masktl", [128, 128])
        ein("ones128", [128, 128]); ein("csel", [128, 1])
        ein("w1P", [NJD, 128, 1024]); ein("w2P", [NJD, 128, 1024])
        ein("h1P", [NJD, 128, 4096]); ein("h2P", [NJD, 128, 4096])
        ein("w3P", [128, NJD * DM])
        ein("s1c", [128, NJD]); ein("s2c", [128, NJD]); ein("s1hc", [128, NJD])
        ein("s2hc", [128, NJD]); ein("s3r", [1, DM])
        itn("z1", [T_CORE, DM]); itn("z2", [T_CORE, DM]); itn("z3", [T_CORE, DM])
        itn("comb", [T_CORE, DM]); itn("preln", [T_CORE, DM])
        itn("szsp", [DI, T_CORE], BF16)
        itn("cst", [128, NCH * DST]); itn("sgst", [128, NCH])
        itn("tail_self", [128, NJC * 4]); itn("tail_gat", [256, NJC * 4])
        itn("hend", [DST, DI]); itn("hend_gat", [2 * DST, DI])
        d["z4"] = nc.dram_tensor("z4", [T_CORE, DM], F32, kind="ExternalOutput").ap()

        P0 = st.enter_context(nc.psum_tensor("PP0", [128, 2048], F32))
        P1 = st.enter_context(nc.psum_tensor("PP1", [128, 2048], F32))

        pool = Pool(nc)
        p = Prog(nc, pool)
        arenas = []
        for name, fn in (("a", emit_phase_a), ("b", emit_phase_b),
                         ("c0", emit_phase_c0), ("c1", emit_phase_c1),
                         ("c2", emit_phase_c2), ("dd", emit_phase_d)):
            ar = Arena(nc, name + "_")
            fn(nc, p, ar.al, d, P0, P1)
            arenas.append(ar)
            if name != "dd":
                p.phase_end()

        mx = max(ar.addr for ar in arenas)
        backing = st.enter_context(nc.sbuf_tensor("ARENA", [128, mx], mybir.dt.uint8))
        base_addr = nc.lookup_mloc(backing).addr
        for ar in arenas:
            ar.place(base_addr)

        with nc.Block() as block:
            p.emit(block)
    return nc


def host_inputs(inputs):
    U = np.asarray(inputs["L0_attn_u"], np.float32)
    Bw = np.asarray(inputs["L0_attn_B_w"], np.float32)
    Cw = np.asarray(inputs["L0_attn_C_w"], np.float32)
    ll = np.asarray(inputs["L0_attn_log_lambda"], np.float32)
    Dp = np.asarray(inputs["L0_attn_D"], np.float32)
    W1 = np.asarray(inputs["L0_ffn_W1"], np.float32)
    W2 = np.asarray(inputs["L0_ffn_W2"], np.float32)
    W3 = np.asarray(inputs["L0_ffn_W3"], np.float32)
    W_in = np.asarray(inputs["L1_attn_W_in"], np.float32)
    W_x = np.asarray(inputs["L1_attn_W_x"], np.float32)
    W_x48 = np.zeros((DI, 48), np.float32)
    W_x48[:, 0:33] = W_x
    conv_w = np.asarray(inputs["L1_attn_conv_w"], np.float32)
    conv_b = np.asarray(inputs["L1_attn_conv_b"], np.float32)
    D_param = np.asarray(inputs["L1_attn_D_param"], np.float32)
    A_log = np.asarray(inputs["L1_attn_A_log"], np.float32)
    W_out = np.asarray(inputs["L1_attn_W_out"], np.float32)
    w1 = np.asarray(inputs["L1_ffn_w1_w"], np.float32)
    w2 = np.asarray(inputs["L1_ffn_w2_w"], np.float32)
    h1 = np.asarray(inputs["L1_ffn_h1_w"], np.float32)
    h2 = np.asarray(inputs["L1_ffn_h2_w"], np.float32)
    w3 = np.asarray(inputs["L1_ffn_w3_w"], np.float32)
    t_ = np.ones((128, 128), np.float32)
    common = {
        "BwT": np.ascontiguousarray(Bw.T),
        "CwT": np.ascontiguousarray(Cw.T),
        "ll2": np.ascontiguousarray(ll.reshape(2, 128).T),
        "Drow": np.ascontiguousarray(Dp[None, :]),
        "gwT": np.ascontiguousarray(np.asarray(inputs["L0_ffn_gate_w"], np.float32).T),
        "rmsw": np.ascontiguousarray(np.asarray(inputs["L0_ffn_rms_w"], np.float32)[None, :]),
        "W1P": np.ascontiguousarray(W1.reshape(NE, NJ, 128, 8, 128).transpose(0, 1, 4, 3, 2).reshape(NE, NJ, 128, 1024)),
        "W2P": np.ascontiguousarray(W2.reshape(NE, NJ, 128, 8, 128).transpose(0, 1, 4, 3, 2).reshape(NE, NJ, 128, 1024)),
        "W3P": np.ascontiguousarray(W3.reshape(NE, DM, NJ, 128).transpose(0, 3, 2, 1).reshape(NE, 128, NJ * DM)),
        "identd": np.eye(128, dtype=np.float32),
        "W_in": W_in, "W_x": W_x48,
        "cwc": np.ascontiguousarray(conv_w.reshape(NJC, 128, 4).transpose(1, 0, 2).reshape(128, NJC * 4)),
        "cbc": np.ascontiguousarray(conv_b.reshape(NJC, 128).T),
        "dpc": np.ascontiguousarray(D_param.reshape(NJC, 128).T),
        "alr": np.ascontiguousarray(A_log[None, :]),
        "alc": np.ascontiguousarray(A_log[:, None]),
        "WoutP": np.ascontiguousarray(W_out.reshape(NJC, 128, DM).transpose(1, 0, 2).reshape(128, NJC * DM)),
        "WinP": np.ascontiguousarray(W_in.reshape(8, 128, 32, 128).transpose(2, 1, 0, 3).reshape(32, 128, 1024)),
        "WxP": np.ascontiguousarray(W_x48.reshape(NJC, 128, 48).transpose(1, 0, 2).reshape(128, NJC * 48)),
        "cumin": np.ascontiguousarray(np.triu(t_)),
        "sufst": np.ascontiguousarray(np.tril(t_, -1)),
        "masktl": np.ascontiguousarray(np.tril(t_, -1) * 1e30),
        "ones128": t_,
        "w1P": np.ascontiguousarray(w1.reshape(NJD, 128, 8, 128).transpose(0, 3, 2, 1).reshape(NJD, 128, 1024)),
        "w2P": np.ascontiguousarray(w2.reshape(NJD, 128, 8, 128).transpose(0, 3, 2, 1).reshape(NJD, 128, 1024)),
        "h1P": np.ascontiguousarray(h1.reshape(NJD, 128, 32, 128).transpose(0, 3, 2, 1).reshape(NJD, 128, 4096)),
        "h2P": np.ascontiguousarray(h2.reshape(NJD, 128, 32, 128).transpose(0, 3, 2, 1).reshape(NJD, 128, 4096)),
        "w3P": np.ascontiguousarray(w3.transpose(1, 0).reshape(NJD, 128, DM).transpose(1, 0, 2).reshape(128, NJD * DM)),
        "s1c": np.ascontiguousarray(np.asarray(inputs["L1_ffn_w1_s"], np.float32).reshape(NJD, 128).T),
        "s2c": np.ascontiguousarray(np.asarray(inputs["L1_ffn_w2_s"], np.float32).reshape(NJD, 128).T),
        "s1hc": np.ascontiguousarray(np.asarray(inputs["L1_ffn_h1_s"], np.float32).reshape(NJD, 128).T),
        "s2hc": np.ascontiguousarray(np.asarray(inputs["L1_ffn_h2_s"], np.float32).reshape(NJD, 128).T),
        "s3r": np.ascontiguousarray(np.asarray(inputs["L1_ffn_w3_s"], np.float32)[None, :]),
    }
    maps = []
    for c in range(NCORES):
        b, h = c // 2, c % 2
        m = dict(common)
        u_c = np.ascontiguousarray(U[b, h * T_CORE:(h + 1) * T_CORE])
        m["u"] = u_c
        m["uT"] = np.ascontiguousarray(u_c.T)
        if h == 1:
            m["uTprev"] = np.ascontiguousarray(U[b, 0:T_CORE].T)
        else:
            m["uTprev"] = np.zeros((DM, T_CORE), np.float32)
        m["csel"] = np.full((128, 1), float(h), np.float32)
        maps.append(m)
    return maps


def kernel(**inputs):
    nc = build_fused()
    maps = host_inputs(inputs)
    res = run_bass_kernel_spmd(nc, maps, core_ids=list(range(NCORES)))
    z4 = np.empty((4, 8192, DM), np.float32)
    for c in range(NCORES):
        b, h = c // 2, c % 2
        z4[b, h * T_CORE:(h + 1) * T_CORE] = res.results[c]["z4"]
    return z4


# version 05 (final)
# speedup vs baseline: 1.1689x; 1.0277x over previous
import numpy as np
import concourse.bass as bass
import concourse.mybir as mybir
from concourse.bass_utils import run_bass_kernel_spmd

F32 = mybir.dt.float32
BF16 = mybir.dt.bfloat16
AF = mybir.ActivationFunctionType
ALU = mybir.AluOpType
AX = mybir.AxisListType

NCORES = 8
T_CORE = 4096
DM = 1024
NSTATE = 256
LN_EPS = 1e-5
RMS_EPS = 1e-6
NE = 8
DE = 2816
NJ = DE // 128
DI = 2048
NJC = DI // 128
DST = 16
NCH = 32
NGR = 8
DH = 4096
NJD = DH // 128
PAIRS = [[0, 1], [2, 3], [4, 5], [6, 7]]

ENGINES = ("sync", "gpsimd", "tensor", "vector", "scalar")


class Pool:

    def __init__(self, nc):
        self.nc = nc
        self.handles = []
        self.totals = []
        self.free = []
        self.map = {}
        self.counts = {}

    def get(self, name):
        if name not in self.map:
            if self.free:
                idx = self.free.pop(0)
            else:
                idx = len(self.handles)
                self.handles.append(self.nc.alloc_semaphore("ps%d" % idx))
                self.totals.append(0)
            self.map[name] = (idx, self.totals[idx])
            self.counts[name] = 0
        idx, base = self.map[name]
        return self.handles[idx], base

    def cnt(self, name):
        return self.counts.get(name, 0)

    def rebase(self):
        for name, (idx, base) in self.map.items():
            self.totals[idx] = base + self.counts[name]
        self.free = sorted(set(range(len(self.handles))))
        self.map = {}
        self.counts = {}


class Arena:

    def __init__(self, nc, prefix):
        self.nc = nc
        self.prefix = prefix
        self.addr = 0
        self.tensors = []

    def al(self, name, shape, dtype):
        per_part = 1
        for s in shape[1:]:
            per_part *= s
        per_part *= mybir.dt.size(dtype)
        pre = self.addr
        self.addr = (self.addr + per_part + 31) // 32 * 32
        h = self.nc.virtual_sbuf_tensor(self.prefix + name, shape, dtype)
        self.tensors.append((h, pre))
        return h

    def place(self, backing_addr):
        for h, off in self.tensors:
            self.nc._tensor(h.name, h.shape, h.dtype, type="SB")
            mloc = self.nc.lookup_mloc(h)
            mloc.allocated = not self.nc.target_bir_lowering
            mloc.addr = backing_addr + off


class PV:

    def __init__(self, h, c0, w):
        self.h, self.c0, self.w = h, c0, w

    def __getitem__(self, idx):
        ps, cs = idx
        a = self.c0 + (cs.start or 0)
        b = self.c0 + (cs.stop if cs.stop is not None else self.w)
        return self.h[ps, a:b]


class Prog:

    AUTO_CHAIN = ("vector", "scalar")

    def __init__(self, nc, pool):
        self.nc = nc
        self.pool = pool
        self.ops = {e: [] for e in ENGINES}
        self._last_edge = {e: None for e in ENGINES}
        self._nbar = 0

    def cnt(self, name):
        return self.pool.cnt(name)

    def on(self, eng, fn, wait=None, inc=None):
        if eng in self.AUTO_CHAIN:
            if inc is None:
                inc = ("_T" + eng, 1)
            prev = self._last_edge[eng]
            if prev is not None:
                wait = list(wait) if wait else []
                wait.append(prev)
        sem_h = None
        inc_amt = 0
        if inc is not None:
            name, amt = inc
            sem_h, _ = self.pool.get(name)
            self.pool.counts[name] += amt
            inc_amt = amt
            if eng in self.AUTO_CHAIN:
                self._last_edge[eng] = (name, self.pool.counts[name])
        waits = []
        if wait:
            for wname, wval in wait:
                if wval > 0:
                    h, b = self.pool.get(wname)
                    waits.append((h, b + wval))

        def run(e, fn=fn, waits=waits, sem_h=sem_h, inc_amt=inc_amt):
            for s, v in waits:
                e.wait_ge(s, v)
            ins = fn(e)
            if sem_h is not None:
                ins.then_inc(sem_h, inc_amt)
        self.ops[eng].append(run)
        if inc is not None:
            return self.pool.counts[inc[0]]
        return None

    def wait(self, eng, wait):
        waits = []
        for wname, wval in wait:
            if wval > 0:
                h, b = self.pool.get(wname)
                waits.append((h, b + wval))

        def run(e, waits=waits):
            for s, v in waits:
                e.wait_ge(s, v)
        self.ops[eng].append(run)

    def phase_end(self):
        self._nbar += 1
        name = "PBAR%d" % self._nbar
        h, base = self.pool.get(name)
        self.pool.counts[name] += len(ENGINES)
        target = base + len(ENGINES)
        for eng in ENGINES:
            def mark(e, h=h):
                e.nop().then_inc(h, 1)
            self.ops[eng].append(mark)
        self.pool.rebase()
        self._last_edge = {e: None for e in ENGINES}
        for eng in ENGINES:
            def barw(e, h=h, t=target):
                e.wait_ge(h, t)
            self.ops[eng].append(barw)

    def emit(self, block):
        for eng in ENGINES:
            ops = self.ops[eng]

            def run_all(e, ops=ops):
                for fn in ops:
                    fn(e)
            getattr(block, eng)(run_all)


def emit_phase_a(nc, p, al, d, P0, P1):
    u = d["u"]
    uT = d["uT"]
    uTprev = d["uTprev"]
    BwT = d["BwT"]
    CwT = d["CwT"]
    ll2 = d["ll2"]
    Drow = d["Drow"]
    z1 = d["z1"]

    bwt = al("bwt", [128, 8 * NSTATE], BF16)
    cwt = al("cwt", [128, 2 * DM], BF16)
    ll = al("ll", [128, 2], F32)
    lamc = al("lamc", [128, 2], F32)
    lamb = al("lamb", [128, 2 * 512], F32)
    drow_s = al("drow_s", [1, DM], F32)
    onesr = al("onesr", [1, 128], F32)
    d1b = al("d1b", [128, DM], F32)
    zcol = al("zcol", [128, 1], F32)
    utb = al("utb", [128, 2 * 8 * 512], BF16)
    uf = al("uf", [128, 2 * 4 * DM], F32)
    hf = al("hf", [128, 4 * 512], F32)
    hbf = al("hbf", [128, 2 * 2 * 512], BF16)
    ep0 = al("ep0", [128, 2 * DM], F32)
    ep1 = al("ep1", [128, 2 * DM], F32)
    sqb = al("sqb", [128, 2 * DM], F32)
    z1t = al("z1t", [128, 2 * DM], F32)
    mcol = al("mcol", [128, 8], F32)
    konst = al("konst", [128, 4], F32)
    psb0 = PV(P0, 0, 512)
    psb1 = PV(P0, 512, 512)
    psy0 = PV(P0, 1024, 1024)
    psy1 = PV(P1, 0, 1024)
    psd = PV(P1, 1024, 1024)

    psb = [psb0, psb1]
    psy = [psy0, psy1]

    uT3 = uT.rearrange("(c p) l -> p c l", p=128)
    uTp3 = uTprev.rearrange("(c p) l -> p c l", p=128)
    u4 = u.rearrange("(ch g p) m -> ch p g m", g=4, p=128)
    z14 = z1.rearrange("(ch g p) m -> ch g p m", g=4, p=128)

    p.on("gpsimd", lambda e: e.dma_start(
        out=bwt[:, :].rearrange("p (c n) -> p c n", c=8),
        in_=BwT.rearrange("(c p) n -> p c n", p=128)), inc=("LDWG", 16))
    p.on("gpsimd", lambda e: e.dma_start(
        out=cwt[:, :].rearrange("p (c m) -> p c m", c=2),
        in_=CwT.rearrange("(c p) m -> p c m", p=128)), inc=("LDWG", 16))
    p.on("sync", lambda e: e.dma_start(out=ll[:, :], in_=ll2), inc=("LDWS", 16))
    p.on("sync", lambda e: e.dma_start(out=drow_s[:, :], in_=Drow), inc=("LDWS", 16))
    p.on("scalar", lambda e: e.activation(lamc[:, :], ll[:, :], AF.Sigmoid),
         wait=[("LDWS", 32)], inc=("SIG", 1))
    p.on("vector", lambda e: e.memset(lamb[:, :], 1.0))
    p.on("vector", lambda e: e.memset(zcol[:, :], 0.0))
    p.on("vector", lambda e: e.memset(konst[:, 0:1], LN_EPS))
    p.on("vector", lambda e: e.memset(konst[:, 1:2], -0.5))
    p.on("vector", lambda e: e.memset(konst[:, 2:3], RMS_EPS))
    p.on("vector", lambda e: e.memset(onesr[:, :], 1.0), inc=("MST", 1))
    for nh in range(2):
        p.on("vector", lambda e, nh=nh: e.tensor_scalar(
            lamb[:, nh * 512:(nh + 1) * 512], lamb[:, nh * 512:(nh + 1) * 512],
            lamc[:, nh:nh + 1], None, ALU.mult),
            wait=[("SIG", 1)], inc=("PREP", 1))
    for f in range(2):
        p.on("tensor", lambda e, f=f: e.matmul(
            psd[:, f * 512:(f + 1) * 512], onesr[0:1, :], drow_s[0:1, f * 512:(f + 1) * 512],
            start=True, stop=True), wait=[("MST", 1), ("LDWS", 32)], inc=("MMD", 1))
    p.on("vector", lambda e: e.tensor_scalar(d1b[:, :], psd[:, :], 1.0, None, ALU.add),
         wait=[("MMD", 2)], inc=("PREP", 1))

    def emit_bu_mms(chunk, bufsel, sem_ld, ld_target, extra_wait):
        for nh in range(2):
            for c in range(8):
                last = (c == 7)
                w = [(sem_ld, ld_target), ("LDWG", 32)] if c == 0 else None
                if c == 0 and extra_wait is not None:
                    w = [(sem_ld, ld_target), ("LDWG", 32), extra_wait]
                p.on("tensor", lambda e, nh=nh, c=c, bufsel=bufsel, last=last: e.matmul(
                    psb[nh][:, :],
                    bwt[:, c * NSTATE + nh * 128: c * NSTATE + (nh + 1) * 128],
                    utb[:, bufsel * 4096 + c * 512:(bufsel) * 4096 + (c + 1) * 512],
                    start=(c == 0), stop=last),
                    wait=w, inc=(("MMB", 1) if last else None))

    for ch in range(8):
        bufsel = ch % 2
        wld = None
        if ch >= 2:
            wld = [("MMB", 2 * (ch - 1))]
        ldt_tgt = p.on("gpsimd", lambda e, ch=ch, bufsel=bufsel: e.dma_start(
            out=utb[:, bufsel * 4096:(bufsel + 1) * 4096].rearrange("p (c l) -> p c l", c=8),
            in_=uTp3[:, :, ch * 512:(ch + 1) * 512]),
            wait=wld, inc=("LDT%d" % bufsel, 16))
        extra = ("SCN", 2 * ch) if ch >= 1 else None
        emit_bu_mms(ch, bufsel, "LDT%d" % bufsel, ldt_tgt, extra)
        for nh in range(2):
            init_ap = zcol[:, 0:1] if ch == 0 else hf[:, ((ch - 1) % 2) * 1024 + nh * 512 + 511: ((ch - 1) % 2) * 1024 + nh * 512 + 512]
            p.on("vector", lambda e, ch=ch, nh=nh, init_ap=init_ap: e.tensor_tensor_scan(
                hf[:, (ch % 2) * 1024 + nh * 512:(ch % 2) * 1024 + nh * 512 + 512],
                lamb[:, nh * 512:(nh + 1) * 512],
                psb[nh][:, :],
                init_ap, ALU.mult, ALU.add),
                wait=[("MMB", 2 * ch + nh + 1), ("PREP", 2)],
                inc=("SCN", 1))
    h0_ap = [hf[:, 1024 + nh * 512 + 511:1024 + nh * 512 + 512] for nh in range(2)]

    MMB0 = p.cnt("MMB")
    SCN0 = p.cnt("SCN")
    for ch in range(8):
        bufsel = ch % 2
        wld = [("MMB", MMB0 + 2 * (ch - 1))] if ch >= 2 else [("MMB", MMB0)]
        if ch == 0:
            wld = [("SCN", 16)]
        ldt_tgt = p.on("gpsimd", lambda e, ch=ch, bufsel=bufsel: e.dma_start(
            out=utb[:, bufsel * 4096:(bufsel + 1) * 4096].rearrange("p (c l) -> p c l", c=8),
            in_=uT3[:, :, ch * 512:(ch + 1) * 512]),
            wait=wld, inc=("LDT%d" % bufsel, 16))
        wlu = [("EPI", 4 * (ch - 1))] if ch >= 2 else None
        ldu_tgt = p.on("sync", lambda e, ch=ch, bufsel=bufsel: e.dma_start(
            out=uf[:, bufsel * 4096:(bufsel + 1) * 4096].rearrange("p (g m) -> p g m", g=4),
            in_=u4[ch]),
            wait=wlu, inc=("LDU%d" % bufsel, 16))
        extra = ("SCN", SCN0 + 2 * ch) if ch >= 1 else None
        emit_bu_mms(ch, bufsel, "LDT%d" % bufsel, ldt_tgt, extra)
        for nh in range(2):
            init_ap = h0_ap[nh] if ch == 0 else hf[:, ((ch - 1) % 2) * 1024 + nh * 512 + 511: ((ch - 1) % 2) * 1024 + nh * 512 + 512]
            p.on("vector", lambda e, ch=ch, nh=nh, init_ap=init_ap: e.tensor_tensor_scan(
                hf[:, (ch % 2) * 1024 + nh * 512:(ch % 2) * 1024 + nh * 512 + 512],
                lamb[:, nh * 512:(nh + 1) * 512],
                psb[nh][:, :],
                init_ap, ALU.mult, ALU.add),
                wait=[("MMB", MMB0 + 2 * ch + nh + 1)],
                inc=("SCN", 1))
            wcast = [("SCN", SCN0 + 2 * ch + nh + 1)]
            if ch >= 2:
                wcast.append(("MMY", 4 * (ch - 1)))
            p.on("vector", lambda e, ch=ch, nh=nh: e.tensor_copy(
                hbf[:, (ch % 2) * 1024 + nh * 512:(ch % 2) * 1024 + nh * 512 + 512],
                hf[:, (ch % 2) * 1024 + nh * 512:(ch % 2) * 1024 + nh * 512 + 512]),
                wait=wcast, inc=("HBF", 1))
        for g in range(4):
            gi = 4 * ch + g
            ps = psy[g % 2]
            k = 0
            for nh in range(2):
                for f in range(2):
                    k += 1
                    w = []
                    if k == 1:
                        w = [("HBF", 2 * (ch + 1)), ("LDWG", 32)]
                        if gi >= 2:
                            w.append(("PSR", gi - 1))
                    p.on("tensor", lambda e, ch=ch, g=g, nh=nh, f=f, ps=ps: e.matmul(
                        ps[:, f * 512:(f + 1) * 512],
                        hbf[:, (ch % 2) * 1024 + nh * 512 + g * 128:(ch % 2) * 1024 + nh * 512 + (g + 1) * 128],
                        cwt[:, nh * DM + f * 512:nh * DM + (f + 1) * 512],
                        start=(nh == 0), stop=(nh == 1)),
                        wait=(w if w else None), inc=(("MMY", 1) if k == 4 else None))
            b2 = gi % 2
            uap = uf[:, (ch % 2) * 4096 + g * DM:(ch % 2) * 4096 + (g + 1) * DM]
            p.on("vector", lambda e, uap=uap, b2=b2: e.tensor_mul(
                ep0[:, b2 * DM:(b2 + 1) * DM], uap, d1b[:, :]),
                wait=[("LDU%d" % (ch % 2), ldu_tgt)], inc=None)
            p.on("vector", lambda e, b2=b2, ps=ps: e.tensor_add(
                ep0[:, b2 * DM:(b2 + 1) * DM], ep0[:, b2 * DM:(b2 + 1) * DM], ps[:, :]),
                wait=[("MMY", gi + 1)], inc=("PSR", 1))
            p.on("vector", lambda e, b2=b2: e.reduce_sum(
                mcol[:, b2 * 4:b2 * 4 + 1], ep0[:, b2 * DM:(b2 + 1) * DM], AX.X), inc=None)
            p.on("vector", lambda e, b2=b2: e.tensor_scalar(
                mcol[:, b2 * 4 + 1:b2 * 4 + 2], mcol[:, b2 * 4:b2 * 4 + 1], 1.0 / DM, None, ALU.mult), inc=None)
            p.on("vector", lambda e, b2=b2: e.tensor_scalar(
                ep1[:, b2 * DM:(b2 + 1) * DM], ep0[:, b2 * DM:(b2 + 1) * DM],
                mcol[:, b2 * 4 + 1:b2 * 4 + 2], None, ALU.subtract),
                inc=("EZC", 1))
            p.on("scalar", lambda e, b2=b2: e.activation(
                sqb[:, b2 * DM:(b2 + 1) * DM], ep1[:, b2 * DM:(b2 + 1) * DM], AF.Square),
                wait=[("EZC", gi + 1)], inc=("SQD", 1))
            p.on("vector", lambda e, b2=b2: e.reduce_sum(
                mcol[:, b2 * 4 + 2:b2 * 4 + 3], sqb[:, b2 * DM:(b2 + 1) * DM], AX.X),
                wait=[("SQD", gi + 1)], inc=None)
            p.on("vector", lambda e, b2=b2: e.tensor_scalar(
                mcol[:, b2 * 4 + 2:b2 * 4 + 3], mcol[:, b2 * 4 + 2:b2 * 4 + 3], 1.0 / DM, None, ALU.mult),
                inc=("EVV", 1))
            p.on("scalar", lambda e, b2=b2: e.activation(
                mcol[:, b2 * 4 + 3:b2 * 4 + 4], mcol[:, b2 * 4 + 2:b2 * 4 + 3], AF.Ln, bias=konst[:, 0:1]),
                wait=[("EVV", gi + 1)], inc=None)
            p.on("scalar", lambda e, b2=b2: e.activation(
                mcol[:, b2 * 4 + 3:b2 * 4 + 4], mcol[:, b2 * 4 + 3:b2 * 4 + 4], AF.Exp, scale=konst[:, 1:2]),
                inc=("RSD", 1))
            wz = [("RSD", gi + 1)]
            if gi >= 2:
                wz.append(("ZS%d" % b2, p.cnt("ZS%d" % b2)))
            p.on("vector", lambda e, b2=b2: e.tensor_scalar(
                z1t[:, b2 * DM:(b2 + 1) * DM], ep1[:, b2 * DM:(b2 + 1) * DM],
                mcol[:, b2 * 4 + 3:b2 * 4 + 4], None, ALU.mult),
                wait=wz, inc=("EPI", 1))
            p.on("sync", lambda e, ch=ch, g=g, b2=b2: e.dma_start(
                out=z14[ch, g], in_=z1t[:, b2 * DM:(b2 + 1) * DM]),
                wait=[("EPI", gi + 1)], inc=("ZS%d" % b2, 16))
    p.wait("sync", [("ZS0", p.cnt("ZS0")), ("ZS1", p.cnt("ZS1"))])


def emit_phase_b(nc, p, al, d, P0, P1):
    z1 = d["z1"]
    gwT = d["gwT"]
    rmsw = d["rmsw"]
    W1P = d["W1P"]
    W2P = d["W2P"]
    W3P = d["W3P"]
    W1C = d["W1C"]
    W2C = d["W2C"]
    W3C = d["W3C"]
    ident_ap = d["identd"]
    z2 = d["z2"]
    comb = d["comb"]

    rmsb = al("rmsb", [128, DM], F32)
    rmsr = al("rmsr", [1, DM], F32)
    onesr = al("onesr", [1, 128], F32)
    konst = al("konst", [128, 4], F32)
    identf = al("identf_s", [128, 128], F32)
    gw = al("gw", [128, 8 * NE], F32)
    z1b = al("z1b", [128, 2 * DM], F32)
    sqb = al("sqb", [128, DM], F32)
    rcol = al("rcol", [128, 2 * 4], F32)
    yf = al("yf", [128, DM], F32)
    yT32 = al("yT32", [128, DM], F32)
    ytb = al("ytb", [128, 8 * 1024], BF16)
    lgb = al("lgb", [128, 8 * 8], F32)
    gcol = al("gcol", [128, 8], F32)
    wf = al("wf", [128, 8 * NE], F32)
    w1j = al("w1j", [128, 2 * 8 * 128], BF16)
    w2j = al("w2j", [128, 2 * 8 * 128], BF16)
    w3b = al("w3b", [128, NJ * DM], BF16)
    hb = al("hb", [128, NJ * 1024], BF16)
    silb = al("silb", [128, 2 * 512], F32)
    outs = al("outs", [128, 4 * DM], F32)
    combb = al("combb", [128, 2 * DM], F32)
    z2t = al("z2t", [128, 2 * DM], F32)
    ep0 = al("ep0b", [128, 2 * DM], F32)
    ep1 = al("ep1b", [128, 2 * DM], F32)
    mcol = al("mcolb", [128, 2 * 8], F32)
    pu0 = PV(P0, 0, 512)
    pu1 = PV(P0, 512, 512)
    pg0 = PV(P0, 1024, 512)
    pg1 = PV(P0, 1536, 512)
    po0 = PV(P1, 0, 1024)
    po1 = PV(P1, 1024, 1024)
    pu = [pu0, pu1]
    pg = [pg0, pg1]
    po = [po0, po1]
    ps4 = [pu0, pu1, pg0, pg1]

    z1t4 = z1.rearrange("(tq p) m -> tq p m", p=128)
    z2t4 = z2.rearrange("(tq p) m -> tq p m", p=128)
    comb4 = comb.rearrange("(tq p) m -> tq p m", p=128)

    p.on("sync", lambda e: e.dma_start(out=gw[:, :].rearrange("p (k e) -> p k e", k=8),
                                       in_=gwT.rearrange("(k p) e -> p k e", p=128)), inc=("LG", 16))
    p.on("sync", lambda e: e.dma_start(out=rmsr[:, :], in_=rmsw), inc=("LG", 16))
    p.on("sync", lambda e: e.dma_start(out=identf[:, :], in_=ident_ap), inc=("LG", 16))
    p.on("vector", lambda e: e.memset(konst[:, 0:1], RMS_EPS))
    p.on("vector", lambda e: e.memset(konst[:, 1:2], -0.5))
    p.on("vector", lambda e: e.memset(konst[:, 2:3], LN_EPS))
    p.on("vector", lambda e: e.memset(onesr[:, :], 1.0), inc=("MST", 1))
    for f in range(2):
        p.on("tensor", lambda e, f=f: e.matmul(
            po0[:, f * 512:(f + 1) * 512], onesr[0:1, :], rmsr[0:1, f * 512:(f + 1) * 512],
            start=True, stop=True), wait=[("MST", 1), ("LG", 48)], inc=("MMD", 1))
    p.on("vector", lambda e: e.tensor_copy(rmsb[:, :], po0[:, :]), wait=[("MMD", 2)], inc=("RMB", 1))

    NQ = 4
    TQ = 8
    vst_slot = [0, 0]
    sil_slot = [0, 0]
    outs_slot = [0, 0, 0, 0]

    def evac_src(k):
        return ps4[k // 2][:, (k % 2) * 128:(k % 2) * 128 + 128]

    def emit_b1(q):
        for T in range(TQ):
            tq = q * TQ + T
            b2 = tq % 2
            wld = [("YR", p.cnt("YR") - 1)] if tq >= 2 else None
            lz_tgt = p.on("sync", lambda e, tq=tq, b2=b2: e.dma_start(
                out=z1b[:, b2 * DM:(b2 + 1) * DM], in_=z1t4[tq]),
                wait=wld, inc=("LZ%d" % b2, 16))
            wsq = [("LZ%d" % b2, lz_tgt), ("EVV", p.cnt("EVV"))]
            p.on("scalar", lambda e, b2=b2: e.activation(
                sqb[:, :], z1b[:, b2 * DM:(b2 + 1) * DM], AF.Square),
                wait=wsq, inc=("SQD", 1))
            p.on("vector", lambda e, b2=b2: e.reduce_sum(
                rcol[:, b2 * 4:b2 * 4 + 1], sqb[:, :], AX.X),
                wait=[("SQD", p.cnt("SQD"))])
            p.on("vector", lambda e, b2=b2: e.tensor_scalar(
                rcol[:, b2 * 4:b2 * 4 + 1], rcol[:, b2 * 4:b2 * 4 + 1], 1.0 / DM, None, ALU.mult),
                inc=("EVV", 1))
            p.on("scalar", lambda e, b2=b2: e.activation(
                rcol[:, b2 * 4 + 1:b2 * 4 + 2], rcol[:, b2 * 4:b2 * 4 + 1], AF.Ln, bias=konst[:, 0:1]),
                wait=[("EVV", p.cnt("EVV"))])
            p.on("scalar", lambda e, b2=b2: e.activation(
                rcol[:, b2 * 4 + 1:b2 * 4 + 2], rcol[:, b2 * 4 + 1:b2 * 4 + 2], AF.Exp, scale=konst[:, 1:2]),
                inc=("RSD", 1))
            wy = [("RSD", p.cnt("RSD")), ("RMB", 1)]
            if tq >= 1:
                wy.append(("MTT", p.cnt("MTT")))
            p.on("vector", lambda e, b2=b2: e.tensor_scalar(
                yf[:, :], z1b[:, b2 * DM:(b2 + 1) * DM], rcol[:, b2 * 4 + 1:b2 * 4 + 2], None, ALU.mult),
                wait=wy)
            yr_tgt = p.on("vector", lambda e: e.tensor_mul(yf[:, :], yf[:, :], rmsb[:, :]),
                          inc=("YR", 1))
            wtr = [("YR", yr_tgt)]
            if tq >= 1:
                wtr.append(("EVF", p.cnt("EVF")))
                wtr.append(("YQB", p.cnt("YQB")))
            if q >= 1 and T == 0:
                wtr.append(("VST", p.cnt("VST")))
            for k in range(8):
                p.on("tensor", lambda e, k=k: e.transpose(
                    evac_src(k), yf[:, k * 128:(k + 1) * 128], identf[:, :]),
                    wait=(wtr if k == 0 else None),
                    inc=(("MTT", 1) if k == 7 else None))
            mtt_tgt = p.cnt("MTT")
            for k in range(8):
                p.on("vector", lambda e, k=k: e.tensor_copy(
                    yT32[:, k * 128:(k + 1) * 128], evac_src(k)),
                    wait=([("MTT", mtt_tgt)] if k == 0 else None),
                    inc=(("EVF", 1) if k == 7 else None))
            wyq = [("MTT", mtt_tgt), ("EVF", p.cnt("EVF"))]
            if q >= 1 and T == 0:
                wyq.append(("MMUG", p.cnt("MMUG")))
            for k in range(8):
                p.on("scalar", lambda e, k=k, T=T: e.activation(
                    ytb[:, k * 1024 + T * 128:k * 1024 + (T + 1) * 128], evac_src(k), AF.Copy),
                    wait=(wyq if k == 0 else None),
                    inc=(("YQB", 1) if k == 7 else None))
            evf_tgt = p.cnt("EVF")
            wlg = [("EVF", evf_tgt), ("LGC", p.cnt("LGC"))]
            if q >= 1 and T == 0:
                wlg.append(("VEV", p.cnt("VEV")))
            for k in range(8):
                p.on("tensor", lambda e, k=k: e.matmul(
                    po1[:, 0:8], yT32[:, k * 128:(k + 1) * 128], gw[:, k * NE:(k + 1) * NE],
                    start=(k == 0), stop=(k == 7)),
                    wait=(wlg if k == 0 else None),
                    inc=(("MML", 1) if k == 7 else None))
            p.on("vector", lambda e: e.tensor_copy(lgb[:, 0:8], po1[:, 0:8]),
                 wait=[("MML", p.cnt("MML"))], inc=("LGC", 1))
            p.on("vector", lambda e: e.reduce_max(gcol[:, 0:1], lgb[:, 0:8], AX.X))
            p.on("vector", lambda e: e.tensor_scalar(
                lgb[:, 8:16], lgb[:, 0:8], gcol[:, 0:1], None, ALU.is_ge))
            p.on("vector", lambda e: e.scalar_tensor_tensor(
                lgb[:, 16:24], lgb[:, 8:16], -1e30, lgb[:, 0:8], ALU.mult, ALU.add))
            p.on("vector", lambda e: e.reduce_max(gcol[:, 1:2], lgb[:, 16:24], AX.X))
            p.on("vector", lambda e: e.tensor_scalar(
                lgb[:, 24:32], lgb[:, 0:8], gcol[:, 1:2], None, ALU.is_ge))
            p.on("vector", lambda e: e.tensor_scalar(
                gcol[:, 2:3], gcol[:, 0:1], -1.0, None, ALU.mult), inc=("GX1", 1))
            p.on("scalar", lambda e: e.activation(
                lgb[:, 32:40], lgb[:, 0:8], AF.Exp, bias=gcol[:, 2:3]),
                wait=[("GX1", p.cnt("GX1"))], inc=("GX2", 1))
            p.on("vector", lambda e: e.tensor_mul(
                lgb[:, 40:48], lgb[:, 32:40], lgb[:, 24:32]),
                wait=[("GX2", p.cnt("GX2"))])
            p.on("vector", lambda e: e.reduce_sum(gcol[:, 3:4], lgb[:, 40:48], AX.X))
            p.on("vector", lambda e: e.reciprocal(gcol[:, 4:5], gcol[:, 3:4]))
            p.on("vector", lambda e, T=T: e.tensor_scalar(
                wf[:, T * 8:(T + 1) * 8], lgb[:, 40:48], gcol[:, 4:5], None, ALU.mult),
                inc=("WFD", 1))

    def emit_experts(q, yqb_q, evf_q):
        for ex in range(NE):
            for j in range(NJ):
                sl = j % 2
                wlw = [("MMUG", mmug_slot[sl])] if mmug_slot[sl] > 0 else None
                if wb_slot[sl] > 0:
                    wlw = (wlw or []) + [("WBC%d" % sl, wb_slot[sl])]
                src1 = W1P[ex, j] if q == 0 else W1C[ex, j]
                src2 = W2P[ex, j] if q == 0 else W2C[ex, j]
                lw1_tgt = p.on("gpsimd", lambda e, src1=src1, sl=sl: e.dma_start(
                    out=w1j[:, sl * 1024:(sl + 1) * 1024],
                    in_=src1),
                    wait=wlw, inc=("LW1%d" % sl, 16))
                lw2_tgt = p.on("gpsimd", lambda e, src2=src2, sl=sl: e.dma_start(
                    out=w2j[:, sl * 1024:(sl + 1) * 1024],
                    in_=src2),
                    inc=("LW2%d" % sl, 16))
                if q == 0:
                    p.on("sync", lambda e, ex=ex, j=j, sl=sl: e.dma_start(
                        out=W1C[ex, j], in_=w1j[:, sl * 1024:(sl + 1) * 1024]),
                        wait=[("LW1%d" % sl, lw1_tgt)], inc=("WBC%d" % sl, 16))
                    p.on("sync", lambda e, ex=ex, j=j, sl=sl: e.dma_start(
                        out=W2C[ex, j], in_=w2j[:, sl * 1024:(sl + 1) * 1024]),
                        wait=[("LW2%d" % sl, lw2_tgt)], inc=("WBC%d" % sl, 16))
                    wb_slot[sl] = p.cnt("WBC%d" % sl)
                wmm = [("LW1%d" % sl, lw1_tgt), ("LW2%d" % sl, lw2_tgt),
                       ("YQB", yqb_q), ("EVF", evf_q)]
                if vst_slot[j % 2] > 0:
                    wmm.append(("VST", vst_slot[j % 2]))
                for f in range(2):
                    for k in range(8):
                        p.on("tensor", lambda e, j=j, sl=sl, f=f, k=k: e.matmul(
                            pu[j % 2][:, :], w1j[:, sl * 1024 + k * 128:sl * 1024 + (k + 1) * 128],
                            ytb[:, k * 1024 + f * 512:k * 1024 + (f + 1) * 512],
                            start=(k == 0), stop=(k == 7)),
                            wait=(wmm if (f == 0 and k == 0) else None))
                    for k in range(8):
                        p.on("tensor", lambda e, j=j, sl=sl, f=f, k=k: e.matmul(
                            pg[j % 2][:, :], w2j[:, sl * 1024 + k * 128:sl * 1024 + (k + 1) * 128],
                            ytb[:, k * 1024 + f * 512:k * 1024 + (f + 1) * 512],
                            start=(k == 0), stop=(k == 7)),
                            inc=(("MMUG", 1) if (f == 1 and k == 7) else None))
                mmug_slot[sl] = p.cnt("MMUG")
                mmug_j = p.cnt("MMUG")
                for f in range(2):
                    wsil = [("MMUG", mmug_j)]
                    if sil_slot[f] > 0:
                        wsil.append(("VST", sil_slot[f]))
                    p.on("scalar", lambda e, j=j, f=f: e.activation(
                        silb[:, f * 512:(f + 1) * 512], pu[j % 2][:, :], AF.Silu),
                        wait=wsil, inc=("SIL", 1))
                    vst_t = p.on("vector", lambda e, j=j, f=f: e.tensor_mul(
                        hb[:, j * 1024 + f * 512:j * 1024 + (f + 1) * 512],
                        silb[:, f * 512:(f + 1) * 512], pg[j % 2][:, :]),
                        wait=[("SIL", p.cnt("SIL"))], inc=("VST", 1))
                    sil_slot[f] = vst_t
                vst_slot[j % 2] = p.cnt("VST")
            vst_eq = p.cnt("VST")
            wl3 = [("MMO", p.cnt("MMO"))] if p.cnt("MMO") > 0 else None
            if wb3_slot[0] > 0:
                wl3 = (wl3 or []) + [("WB3", wb3_slot[0])]
            src3 = W3P[ex] if q == 0 else W3C[ex]
            lw3_tgt = p.on("gpsimd", lambda e, src3=src3: e.dma_start(
                out=w3b[:, :], in_=src3),
                wait=wl3, inc=("LW3", 16))
            if q == 0:
                p.on("sync", lambda e, ex=ex: e.dma_start(
                    out=W3C[ex], in_=w3b[:, :]),
                    wait=[("LW3", lw3_tgt)], inc=("WB3", 16))
                wb3_slot[0] = p.cnt("WB3")
            for ob in range(4):
                wob = [("VST", vst_eq), ("LW3", lw3_tgt)]
                if p.cnt("VEV") > 0:
                    wob.append(("VEV", p.cnt("VEV")))
                first = True
                for j in range(NJ):
                    for ts in range(2):
                        for f in range(2):
                            p.on("tensor", lambda e, j=j, ob=ob, ts=ts, f=f: e.matmul(
                                po[ts][:, f * 512:(f + 1) * 512],
                                hb[:, j * 1024 + ob * 256 + ts * 128:j * 1024 + ob * 256 + (ts + 1) * 128],
                                w3b[:, j * 1024 + f * 512:j * 1024 + (f + 1) * 512],
                                start=(j == 0), stop=(j == NJ - 1)),
                                wait=(wob if first else None),
                                inc=(("MMO", 1) if (j == NJ - 1 and ts == 1 and f == 1) else None))
                            first = False
                mmo_ob = p.cnt("MMO")
                for ts in range(2):
                    osl = (ob * 2 + ts) % 4
                    T = ob * 2 + ts
                    wev = [("MMO", mmo_ob), ("WFD", wfd_q[q])]
                    if outs_slot[osl] > 0:
                        wev.append(("CMB%d" % osl, outs_slot[osl]))
                    p.on("vector", lambda e, ts=ts, osl=osl, T=T, ex=ex: e.tensor_scalar(
                        outs[:, osl * DM:(osl + 1) * DM], po[ts][:, :],
                        wf[:, T * 8 + ex:T * 8 + ex + 1], None, ALU.mult),
                        wait=wev, inc=("VEV", 1))
                    vev_now = p.cnt("VEV")
                    cmb_t = p.on("gpsimd", lambda e, q=q, ob=ob, ts=ts, osl=osl, ex=ex: e.dma_start(
                        out=comb4[q * 8 + ob * 2 + ts],
                        in_=outs[:, osl * DM:(osl + 1) * DM],
                        accum_op=(ALU.bypass if ex == 0 else ALU.add)),
                        wait=[("VEV", vev_now)], inc=("CMB%d" % osl, 16))
                    outs_slot[osl] = cmb_t

    def emit_b3(q, cmb_q):
        for T in range(TQ):
            tq = q * TQ + T
            b2 = tq % 2
            i3 = p.cnt("EP3")
            war3 = [("EP3", i3 - 1)] if i3 >= 2 else None
            wlc = [("CMB%d" % s, cmb_q[s]) for s in range(4)] + (war3 or [])
            lc_tgt = p.on("sync", lambda e, tq=tq, b2=b2: e.dma_start(
                out=combb[:, b2 * DM:(b2 + 1) * DM], in_=comb4[tq]),
                wait=wlc, inc=("LC%d" % b2, 16))
            lzr_tgt = p.on("sync", lambda e, tq=tq, b2=b2: e.dma_start(
                out=z2t[:, b2 * DM:(b2 + 1) * DM], in_=z1t4[tq]),
                wait=war3, inc=("LZR%d" % b2, 16))
            wpre = [("LC%d" % b2, lc_tgt), ("LZR%d" % b2, lzr_tgt)]
            if zs2_slot[b2] > 0:
                wpre.append(("ZS2%d" % b2, zs2_slot[b2]))
            p.on("vector", lambda e, b2=b2: e.tensor_add(
                ep0[:, b2 * DM:(b2 + 1) * DM], z2t[:, b2 * DM:(b2 + 1) * DM],
                combb[:, b2 * DM:(b2 + 1) * DM]),
                wait=wpre)
            p.on("vector", lambda e, b2=b2: e.reduce_sum(
                mcol[:, b2 * 8:b2 * 8 + 1], ep0[:, b2 * DM:(b2 + 1) * DM], AX.X))
            p.on("vector", lambda e, b2=b2: e.tensor_scalar(
                mcol[:, b2 * 8 + 1:b2 * 8 + 2], mcol[:, b2 * 8:b2 * 8 + 1], 1.0 / DM, None, ALU.mult))
            p.on("vector", lambda e, b2=b2: e.tensor_scalar(
                ep1[:, b2 * DM:(b2 + 1) * DM], ep0[:, b2 * DM:(b2 + 1) * DM],
                mcol[:, b2 * 8 + 1:b2 * 8 + 2], None, ALU.subtract),
                inc=("EZC", 1))
            p.on("scalar", lambda e, b2=b2: e.activation(
                ep0[:, b2 * DM:(b2 + 1) * DM], ep1[:, b2 * DM:(b2 + 1) * DM], AF.Square),
                wait=[("EZC", p.cnt("EZC"))], inc=("SQ3", 1))
            p.on("vector", lambda e, b2=b2: e.reduce_sum(
                mcol[:, b2 * 8 + 2:b2 * 8 + 3], ep0[:, b2 * DM:(b2 + 1) * DM], AX.X),
                wait=[("SQ3", p.cnt("SQ3"))])
            p.on("vector", lambda e, b2=b2: e.tensor_scalar(
                mcol[:, b2 * 8 + 2:b2 * 8 + 3], mcol[:, b2 * 8 + 2:b2 * 8 + 3], 1.0 / DM, None, ALU.mult),
                inc=("EV3", 1))
            p.on("scalar", lambda e, b2=b2: e.activation(
                mcol[:, b2 * 8 + 3:b2 * 8 + 4], mcol[:, b2 * 8 + 2:b2 * 8 + 3], AF.Ln, bias=konst[:, 2:3]),
                wait=[("EV3", p.cnt("EV3"))])
            p.on("scalar", lambda e, b2=b2: e.activation(
                mcol[:, b2 * 8 + 3:b2 * 8 + 4], mcol[:, b2 * 8 + 3:b2 * 8 + 4], AF.Exp, scale=konst[:, 1:2]),
                inc=("RS3", 1))
            p.on("vector", lambda e, b2=b2: e.tensor_scalar(
                ep1[:, b2 * DM:(b2 + 1) * DM], ep1[:, b2 * DM:(b2 + 1) * DM],
                mcol[:, b2 * 8 + 3:b2 * 8 + 4], None, ALU.mult),
                wait=[("RS3", p.cnt("RS3"))], inc=("EP3", 1))
            zs2_t = p.on("sync", lambda e, tq=tq, b2=b2: e.dma_start(
                out=z2t4[tq], in_=ep1[:, b2 * DM:(b2 + 1) * DM]),
                wait=[("EP3", p.cnt("EP3"))], inc=("ZS2%d" % b2, 16))
            zs2_slot[b2] = zs2_t

    mmug_slot = [0, 0]
    wb_slot = [0, 0]
    wb3_slot = [0]
    zs2_slot = [0, 0]
    wfd_q = {}
    yqb_all = {}
    evf_all = {}
    emit_b1(0)
    wfd_q[0] = p.cnt("WFD"); yqb_all[0] = p.cnt("YQB"); evf_all[0] = p.cnt("EVF")
    for q in range(NQ):
        emit_experts(q, yqb_all[q], evf_all[q])
        cmb_q = [p.cnt("CMB%d" % s) for s in range(4)]
        if q + 1 < NQ:
            emit_b1(q + 1)
            wfd_q[q + 1] = p.cnt("WFD"); yqb_all[q + 1] = p.cnt("YQB"); evf_all[q + 1] = p.cnt("EVF")
        emit_b3(q, cmb_q)
    p.wait("sync", [("ZS20", p.cnt("ZS20")), ("ZS21", p.cnt("ZS21"))])


def emit_phase_c0(nc, p, al, d, P0, P1):
    wb = al("wbc0", [128, 8 * DI], BF16)
    zb = al("zbc0", [128, 8 * 4], BF16)
    z2tl = al("z2tl", [3, DM], F32)
    idf = al("idc0", [128, 128], F32)
    ob = al("obc0", [128, NJC * 4], F32)
    ps = PV(P0, 0, 1024)
    pt = [PV(P1, 0, 512), PV(P1, 512, 512)]

    W_in4 = d["W_in"].rearrange("(k p) c -> p k c", p=128)
    p.on("gpsimd", lambda e: e.dma_start(
        out=wb[:, :].rearrange("p (k c) -> p k c", k=8),
        in_=W_in4[:, :, 0:DI]), inc=("LW", 16))
    p.on("sync", lambda e: e.dma_start(out=z2tl[:, :], in_=d["z2"][T_CORE - 3:T_CORE, :]),
         inc=("LZ", 16))
    p.on("sync", lambda e: e.dma_start(out=idf[:, :], in_=d["identd"]), inc=("LZ", 16))
    p.on("vector", lambda e: e.memset(zb[:, :], 0.0), inc=("MS", 1))
    for k in range(8):
        sl = k % 2
        w = [("LZ", 32)] if k <= 1 else [("ZBC", p.cnt("ZBC") - 1)]
        p.on("tensor", lambda e, k=k, sl=sl: e.transpose(
            pt[sl][:, 0:3], z2tl[0:3, k * 128:(k + 1) * 128], idf[0:3, 0:3]),
            wait=w, inc=("TRZ", 1))
        p.on("scalar", lambda e, k=k, sl=sl: e.activation(
            zb[:, k * 4 + 1:k * 4 + 4], pt[sl][:, 0:3], AF.Copy),
            wait=[("TRZ", p.cnt("TRZ")), ("MS", 1)], inc=("ZBC", 1))
    for j in range(NJC):
        sl = (j % 2) * 512
        for k in range(8):
            w = None
            if j == 0 and k == 0:
                w = [("LW", 16), ("ZBC", 8)]
            elif k == 0 and j >= 2:
                w = [("EVC", p.cnt("EVC") - 1)]
            p.on("tensor", lambda e, j=j, k=k, sl=sl: e.matmul(
                ps[:, sl:sl + 4], wb[:, k * DI + j * 128:k * DI + (j + 1) * 128],
                zb[:, k * 4:(k + 1) * 4], start=(k == 0), stop=(k == 7)),
                wait=w, inc=(("MMC", 1) if k == 7 else None))
        p.on("vector", lambda e, j=j, sl=sl: e.tensor_copy(
            ob[:, j * 4:(j + 1) * 4], ps[:, sl:sl + 4]),
            wait=[("MMC", p.cnt("MMC"))], inc=("EVC", 1))
    p.on("sync", lambda e: e.dma_start(out=d["tail_self"], in_=ob[:, :]),
         wait=[("EVC", p.cnt("EVC"))], inc=("TS", 16))
    p.on("gpsimd", lambda e: e.collective_compute(
        "AllGather", ALU.bypass, replica_groups=PAIRS,
        ins=[d["tail_self"].opt()], outs=[d["tail_gat"].opt()]),
        wait=[("TS", 16)], inc=("CCT", 1))
    p.wait("gpsimd", [("CCT", 1)])


def emit_phase_c1(nc, p, al, d, P0, P1):
    z2 = d["z2"]
    cwc_d = d["cwc"]
    cbc_d = d["cbc"]
    dpc_d = d["dpc"]
    alr_d = d["alr"]
    alc_d = d["alc"]
    cumin_d = d["cumin"]
    sufst_d = d["sufst"]
    masktl_d = d["masktl"]
    ones128_d = d["ones128"]
    preln = d["preln"]
    szsp = d["szsp"]
    hend = d["hend"]
    cst_o = d["cst"]
    sgst_o = d["sgst"]

    wis = al("wis", [128, 2 * 8 * 128], BF16)
    wob = al("wobc", [128, NJC * DM], BF16)
    wxb = al("wxb", [128, NJC * 48], BF16)
    cwc = al("cwcs", [128, NJC * 4], F32)
    cbc = al("cbcs", [128, NJC], F32)
    dpb = al("dpbs", [128, NJC * 128], BF16)
    dpc_s = al("dpcs", [128, NJC], F32)
    onesb = al("onesb", [1, 128], BF16)
    rab = al("rabc", [128, DST], F32)
    cumin = al("cumins", [128, 128], F32)
    sufst = al("sufsts", [128, 128], F32)
    masktl = al("masktls", [128, 128], F32)
    ones128 = al("ones128s", [128, 128], F32)
    identf = al("identfc", [128, 128], F32)
    arow = al("arow", [1, DST], F32)
    acol = al("acolc", [16, 1], F32)
    abc = al("abc", [128, DST], F32)
    onesr = al("onesrc", [1, 128], F32)
    negs = al("negs", [1, 128], F32)
    konst = al("konstc", [128, 4], F32)
    cselc = al("cselc", [128, 1], F32)
    z2g = al("z2g", [128, 4 * DM], F32)
    z2tb = al("z2tb", [128, 8 * 512], BF16)
    xg = al("xg", [128, NJC * 520], BF16)
    zTg = al("zTg", [128, NJC * 512], BF16)
    xcT = al("xcT", [128, NJC * 128], F32)
    xcTb = al("xcTb", [128, NJC * 128], BF16)
    xcb = al("xcb", [128, NJC * 128], BF16)
    dcol = al("dcol", [128, 8], F32)
    bbm = al("bbm", [128, DST], F32)
    bbm2 = al("bbm2", [128, DST], BF16)
    srow = al("srowc", [1, 128], F32)
    srow_loc = al("srowlc", [1, 128], F32)
    slc = al("slcc", [128, 1], F32)
    sgc = al("sgc", [128, 1], F32)
    spv = al("spv", [128, 1], F32)
    dmt = al("dmtc", [128, 128], F32)
    escr = al("escr", [128, 128], BF16)
    g1s = al("g1s", [128, 128], BF16)
    grb = al("grbc", [16, 4], F32)
    accT = al("accTc", [128, 128], F32)
    acbf = al("acbf", [128, 128], BF16)
    ctsb = al("ctsb", [16, 128], F32)
    ctbb = al("ctbb", [16, 128], BF16)
    ctrow = al("ctrowc", [1, DST * 128], BF16)
    cgb = al("cgbc", [16, 128], BF16)
    ebuf = al("ebufc", [16, 128], F32)
    hst = al("hstc", [16, DI], F32)
    hbf = al("hbfc", [16, DI], BF16)
    ysc = al("yscc", [128, NJC * 128], BF16)
    preb = al("prebc", [128, DM], F32)
    cstb = al("cstbc", [128, NCH * DST], F32)
    sgst = al("sgstc", [128, NCH], F32)
    pbig = PV(P0, 0, 2048)
    pmid = PV(P1, 0, 1024)
    psml0 = PV(P1, 1024, 512)
    psml1 = PV(P1, 1536, 512)

    z2t4 = z2.rearrange("(g f p) m -> g p f m", f=4, p=128)
    pre4 = preln.rearrange("(ch p) m -> ch p m", p=128)
    szsp4 = szsp.rearrange("(j p) t -> p j t", p=128)

    W_in4 = d["W_in"].rearrange("(k p) c -> p k c", p=128)
    p.on("gpsimd", lambda e: e.dma_start(
        out=wob[:, :], in_=d["WoutP"]), inc=("LWC", 16))
    p.on("gpsimd", lambda e: e.dma_start(
        out=wxb[:, :], in_=d["WxP"]), inc=("LWC", 16))
    for dst, srcd in ((cwc, cwc_d), (cbc, cbc_d), (cumin, cumin_d),
                      (sufst, sufst_d), (masktl, masktl_d), (ones128, ones128_d),
                      (identf, d["identd"]), (arow, alr_d), (acol, alc_d),
                      (dpc_s, dpc_d), (cselc, d["csel"])):
        p.on("sync", lambda e, dst=dst, srcd=srcd: e.dma_start(out=dst[:, :], in_=srcd),
             inc=("LDC", 16))
    ldc_tot = p.cnt("LDC")
    p.on("gpsimd", lambda e: e.dma_start(
        out=xg[:, :].rearrange("p (j c) -> p j c", j=NJC)[:, :, 0:4],
        in_=d["tail_gat"][0:128, :].rearrange("p (j c) -> p j c", j=NJC)), inc=("LWC", 16))
    p.on("vector", lambda e: e.tensor_scalar(
        xg[:, :].rearrange("p (j c) -> p j c", j=NJC)[:, :, 0:4],
        xg[:, :].rearrange("p (j c) -> p j c", j=NJC)[:, :, 0:4],
        cselc[:, 0:1], None, ALU.mult),
        wait=[("LWC", 48), ("LDC", ldc_tot)], inc=("TSC", 1))
    p.on("vector", lambda e: e.memset(onesr[:, :], 1.0))
    p.on("vector", lambda e: e.memset(konst[:, 0:1], -1.0))
    p.on("vector", lambda e: e.memset(spv[:, :], 0.0))
    p.on("vector", lambda e: e.memset(hst[:, :], 0.0), inc=("MST", 1))
    p.on("vector", lambda e: e.tensor_copy(hbf[:, :], hst[:, :]), inc=("HBF", 1))
    p.on("scalar", lambda e: e.activation(arow[:, :], arow[:, :], AF.Exp),
         wait=[("LDC", ldc_tot)], inc=("ARP", 1))
    p.on("vector", lambda e: e.tensor_scalar(arow[:, :], arow[:, :], -1.0, None, ALU.mult),
         wait=[("ARP", 1), ("MST", 1)], inc=("ARN", 1))
    p.on("tensor", lambda e: e.matmul(
        psml0[:, 0:DST], onesr[0:1, :], arow[0:1, :], start=True, stop=True),
        wait=[("ARN", 1)], inc=("MMA", 1))
    p.on("vector", lambda e: e.tensor_copy(abc[:, :], psml0[:, 0:DST]),
         wait=[("MMA", 1)], inc=("ABC", 1))
    p.on("scalar", lambda e: e.activation(acol[:, :], acol[:, :], AF.Exp),
         wait=[("LDC", ldc_tot)], inc=("ACX", 1))
    p.on("vector", lambda e: e.tensor_scalar(acol[:, :], acol[:, :], -1.0, None, ALU.mult),
         wait=[("ACX", 1)], inc=("ACL", 1))
    for j in range(NJC):
        p.on("vector", lambda e, j=j: e.tensor_scalar(
            dpb[:, j * 128:(j + 1) * 128], ones128[:, :], dpc_s[:, j:j + 1], None, ALU.mult),
            wait=([("LDC", ldc_tot)] if j == 0 else None),
            inc=(("DPB", 1) if j == NJC - 1 else None))
    p.on("vector", lambda e: e.tensor_copy(onesb[:, :], onesr[:, :]), inc=("ONB", 1))

    lwc_tot = 48
    spl_slot = [0, 0]
    wis_slot = [0, 0]

    for g in range(NGR):
        lz_w = []
        if g >= 1:
            lz_w = [("TRP", p.cnt("TRP")), ("PRE", p.cnt("PRE"))]
        lz_tgt = p.on("sync", lambda e, g=g: e.dma_start(
            out=z2g[:, :].rearrange("p (f m) -> p f m", f=4), in_=z2t4[g]),
            wait=(lz_w or None), inc=("LZ2", 16))
        for f4 in range(4):
            for k in range(8):
                i8 = f4 * 8 + k
                sl = i8 % 2
                psx = [psml0, psml1][sl]
                wtr = None
                if i8 == 0:
                    wtr = [("LZ2", lz_tgt)]
                    if g == 0:
                        wtr.append(("ABC", 1))
                    if g >= 1:
                        wtr += [("EVZ", p.cnt("EVZ")), ("CTS", p.cnt("CTS")),
                                ("DMB", p.cnt("DMB")), ("GS2", p.cnt("GS2")),
                                ("EBF", p.cnt("EBF")), ("SRW", p.cnt("SRW"))]
                elif i8 >= 2:
                    wtr = [("EVZ", p.cnt("EVZ") - 1)]
                p.on("tensor", lambda e, f4=f4, k=k, psx=psx: e.transpose(
                    psx[:, 128:256],
                    z2g[:, f4 * DM + k * 128:f4 * DM + (k + 1) * 128], identf[:, :]),
                    wait=wtr, inc=("TRP", 1))
                p.on("scalar", lambda e, f4=f4, k=k, psx=psx: e.activation(
                    z2tb[:, k * 512 + f4 * 128:k * 512 + (f4 + 1) * 128],
                    psx[:, 128:256], AF.Copy),
                    wait=[("TRP", p.cnt("TRP"))], inc=("EVZ", 1))
        evz_g = p.cnt("EVZ")
        if g >= 1:
            p.on("vector", lambda e: e.tensor_copy(
                xg[:, :].rearrange("p (j c) -> p j c", j=NJC)[:, :, 1:4],
                xg[:, :].rearrange("p (j c) -> p j c", j=NJC)[:, :, 513:516]),
                wait=[("CNV", p.cnt("CNV"))], inc=("XGC", 1))
        for o in range(32):
            sl = o % 2
            wlw = [("MMX", wis_slot[sl])] if wis_slot[sl] > 0 else None
            lwi_tgt = p.on("gpsimd", lambda e, o=o, sl=sl: e.dma_start(
                out=wis[:, sl * 1024:(sl + 1) * 1024],
                in_=d["WinP"][o]),
                wait=wlw, inc=("LWI%d" % sl, 16))
            wmm = [("LWI%d" % sl, lwi_tgt)]
            if o == 0:
                wmm += [("EVZ", evz_g), ("LWC", lwc_tot), ("TSC", 1)]
            elif o >= 2:
                wmm += [("EVX", p.cnt("EVX") - 1)]
            for k in range(8):
                p.on("tensor", lambda e, o=o, k=k, sl=sl: e.matmul(
                    pmid[:, sl * 512:(sl + 1) * 512],
                    wis[:, sl * 1024 + k * 128:sl * 1024 + (k + 1) * 128],
                    z2tb[:, k * 512:(k + 1) * 512],
                    start=(k == 0), stop=(k == 7)),
                    wait=(wmm if k == 0 else None),
                    inc=(("MMX", 1) if k == 7 else None))
            wis_slot[sl] = p.cnt("MMX")
            wev = [("MMX", p.cnt("MMX"))]
            if o < 16:
                if g >= 1 and o == 0:
                    wev.append(("XGC", p.cnt("XGC")))
                    wev.append(("CNV", p.cnt("CNV")))
                p.on("scalar", lambda e, o=o, sl=sl: e.activation(
                    xg[:, o * 520 + 4:o * 520 + 4 + 512], pmid[:, sl * 512:(sl + 1) * 512], AF.Copy),
                    wait=wev, inc=("EVX", 1))
            else:
                if o == 16:
                    wev.append(("YP3", p.cnt("YP3")))
                    wev.append(("SPL", p.cnt("SPL")))
                p.on("scalar", lambda e, o=o, sl=sl: e.activation(
                    zTg[:, (o - 16) * 512:(o - 15) * 512], pmid[:, sl * 512:(sl + 1) * 512], AF.Copy),
                    wait=wev, inc=("EVX", 1))
        evx_g = p.cnt("EVX")
        p.on("scalar", lambda e: e.activation(zTg[:, :], zTg[:, :], AF.Silu),
             wait=[("EVX", evx_g)], inc=("SZD", 1))
        szd_g = p.cnt("SZD")
        p.on("sync", lambda e, g=g: e.dma_start(
            out=szsp4[:, :, g * 512:(g + 1) * 512],
            in_=zTg[:, :].rearrange("p (j t) -> p j t", j=NJC)),
            wait=[("SZD", szd_g)], inc=("SPL", 16))

        for cc in range(4):
            ch = g * 4 + cc
            wcv = [("EVX", evx_g), ("LDC", ldc_tot), ("TSC", 1)]
            if p.cnt("XDB") > 0:
                wcv.append(("XDB", p.cnt("XDB")))
                wcv.append(("TRX", p.cnt("TRX")))
                wcv.append(("YP1", p.cnt("YP1")))
            for j in range(NJC):
                base = j * 520 + cc * 128
                p.on("scalar", lambda e, j=j, base=base: e.activation(
                    xcT[:, j * 128:(j + 1) * 128], xg[:, base + 1:base + 129],
                    AF.Identity, bias=cbc[:, j:j + 1], scale=cwc[:, j * 4 + 0:j * 4 + 1]),
                    wait=(wcv if j == 0 else None), inc=("CV0", 1))
                for k in range(1, 4):
                    p.on("vector", lambda e, j=j, k=k, base=base: e.scalar_tensor_tensor(
                        xcT[:, j * 128:(j + 1) * 128], xg[:, base + 1 + k:base + 129 + k],
                        cwc[:, j * 4 + k:j * 4 + k + 1], xcT[:, j * 128:(j + 1) * 128],
                        ALU.mult, ALU.add),
                        wait=[("CV0", p.cnt("CV0"))] if k == 1 else [("CNV", p.cnt("CNV"))],
                        inc=("CNV", 1))
            p.on("scalar", lambda e: e.activation(xcT[:, :], xcT[:, :], AF.Silu),
                 wait=[("CNV", p.cnt("CNV"))], inc=("SCV", 1))
            p.on("vector", lambda e: e.tensor_copy(xcTb[:, :], xcT[:, :]),
                 wait=[("SCV", p.cnt("SCV")), ("DPB", 1)], inc=("XCB", 1))
            wxd = [("XCB", p.cnt("XCB"))]
            if p.cnt("CSD") > 0:
                wxd += [("DLT", p.cnt("DLT")), ("BBD", p.cnt("BBD")), ("CSD", p.cnt("CSD"))]
            for j in range(NJC):
                p.on("tensor", lambda e, j=j: e.matmul(
                    psml0[:, 0:48], xcTb[:, j * 128:(j + 1) * 128],
                    wxb[:, j * 48:(j + 1) * 48],
                    start=(j == 0), stop=(j == NJC - 1)),
                    wait=(wxd if j == 0 else None),
                    inc=(("XDB", 1) if j == NJC - 1 else None))
            xdb_c = p.cnt("XDB")
            p.on("scalar", lambda e: e.activation(
                dcol[:, 0:1], psml0[:, 0:1], AF.Abs),
                wait=[("XDB", xdb_c)], inc=("DAB", 1))
            p.on("scalar", lambda e: e.activation(
                dcol[:, 1:2], dcol[:, 0:1], AF.Exp, scale=konst[:, 0:1]),
                wait=[("DAB", p.cnt("DAB"))], inc=("DXP", 1))
            p.on("scalar", lambda e: e.activation(
                dcol[:, 2:3], dcol[:, 1:2], AF.Ln, bias=1.0), inc=("DLN", 1))
            p.on("scalar", lambda e: e.activation(
                dcol[:, 3:4], psml0[:, 0:1], AF.Relu), inc=("DRL", 1))
            p.on("vector", lambda e: e.tensor_add(
                dcol[:, 4:5], dcol[:, 3:4], dcol[:, 2:3]),
                wait=[("DRL", p.cnt("DRL"))], inc=("DLT", 1))
            dlt_c = p.cnt("DLT")
            p.on("vector", lambda e: e.tensor_scalar(
                bbm[:, :], psml0[:, 1:17], dcol[:, 4:5], None, ALU.mult), inc=("BBD", 1))
            p.on("vector", lambda e, ch=ch: e.tensor_copy(
                cstb[:, ch * DST:(ch + 1) * DST], psml0[:, 17:33]),
                inc=("CSD", 1))
            wcs = [("DLT", dlt_c), ("BBD", p.cnt("BBD")), ("CSD", p.cnt("CSD"))]
            if p.cnt("SGD") > 0:
                wcs += [("SGD", p.cnt("SGD")), ("RRD", p.cnt("RRD")), ("GDC", p.cnt("GDC"))]
            p.on("tensor", lambda e: e.matmul(
                psml0[:, 64:65], cumin[:, :], dcol[:, 4:5], start=True, stop=True),
                wait=wcs, inc=("MMS", 1))
            p.on("tensor", lambda e: e.matmul(
                psml0[:, 65:66], sufst[:, :], dcol[:, 4:5], start=True, stop=True),
                inc=("MMS", 1))
            p.on("tensor", lambda e: e.matmul(
                psml0[:, 66:67], ones128[:, :], dcol[:, 4:5], start=True, stop=True),
                inc=("MMS", 1))
            mms_c = p.cnt("MMS")
            p.on("vector", lambda e: e.tensor_copy(slc[:, :], psml0[:, 64:65]),
                 wait=[("MMS", mms_c)], inc=("SGW", 1))
            p.on("vector", lambda e: e.tensor_tensor(
                sgc[:, :], psml0[:, 64:65], spv[:, :], ALU.add),
                inc=("SGW", 1))
            p.on("vector", lambda e, ch=ch: e.tensor_copy(sgst[:, ch:ch + 1], sgc[:, :]))
            p.on("vector", lambda e: e.tensor_tensor(
                spv[:, :], spv[:, :], psml0[:, 66:67], ALU.add), inc=("SGD", 1))
            p.on("vector", lambda e: e.tensor_scalar(
                rab[:, :], abc[:, :], psml0[:, 65:66], None, ALU.mult),
                wait=[("MMS", mms_c), ("BB2", p.cnt("BB2"))], inc=("RRD", 1))
            p.on("scalar", lambda e: e.activation(
                rab[:, :], rab[:, :], AF.Exp),
                wait=[("RRD", p.cnt("RRD"))], inc=("ERB", 1))
            p.on("vector", lambda e: e.tensor_tensor(
                bbm2[:, :], rab[:, :], bbm[:, :], ALU.mult),
                wait=[("ERB", p.cnt("ERB"))], inc=("BB2", 1))
            wtx = [("MMS", mms_c), ("CSD", p.cnt("CSD")), ("SGW", p.cnt("SGW"))]
            if p.cnt("PRE") > 0:
                wtx.append(("PRE", p.cnt("PRE")))
            if p.cnt("SRW") > 0:
                wtx += [("SRW", p.cnt("SRW")), ("CTS", p.cnt("CTS"))]
            p.on("tensor", lambda e: e.transpose(
                psml1[0:1, 0:128], sgc[:, :], identf[:, :]),
                wait=wtx, inc=("TRX", 1))
            p.on("tensor", lambda e: e.transpose(
                psml1[0:1, 128:256], slc[:, :], identf[:, :]),
                inc=("TRX", 1))
            p.on("tensor", lambda e, ch=ch: e.transpose(
                psml1[0:16, 256:384], cstb[:, ch * DST:(ch + 1) * DST], identf[:, :]),
                inc=("TRX", 1))
            for j in range(NJC):
                p.on("tensor", lambda e, j=j: e.transpose(
                    pbig[:, j * 128:(j + 1) * 128], xcT[:, j * 128:(j + 1) * 128],
                    identf[:, :]),
                    inc=(("TRX", 1) if j == NJC - 1 else None))
            trx_c = p.cnt("TRX")
            p.on("vector", lambda e: e.tensor_copy(srow[:, :], psml1[0:1, 0:128]),
                 wait=[("TRX", trx_c)], inc=("SRW", 1))
            p.on("vector", lambda e: e.tensor_copy(srow_loc[:, :], psml1[0:1, 128:256]),
                 inc=("SRW", 1))
            p.on("vector", lambda e: e.tensor_copy(ctsb[:, :], psml1[0:16, 256:384]),
                 inc=("CTS", 1))
            cts_c = p.cnt("CTS")
            p.on("scalar", lambda e: e.activation(xcb[:, :], pbig[:, :], AF.Copy),
                 wait=[("TRX", trx_c)], inc=("XCC", 1))
            p.on("vector", lambda e: e.tensor_copy(ctbb[:, :], ctsb[:, :]), inc=("CTB", 1))
            p.on("sync", lambda e: e.dma_start(out=ctrow[0:1, :], in_=ctbb[:, :]),
                 wait=[("CTB", p.cnt("CTB")), ("PNM", p.cnt("PNM"))], inc=("CTR", 16))
            ctr_c = p.cnt("CTR")
            p.on("vector", lambda e: e.tensor_scalar(
                negs[:, :], srow[:, :], -1.0, None, ALU.mult),
                wait=[("SRW", p.cnt("SRW")), ("DMB", p.cnt("DMB"))], inc=("LRT", 1))
            p.on("tensor", lambda e: e.matmul(
                psml1[:, 256:384], onesr[0:1, :], srow[0:1, :], start=True, stop=False),
                wait=[("LRT", p.cnt("LRT")), ("CTS", cts_c)])
            p.on("tensor", lambda e: e.matmul(
                psml1[:, 256:384], negs[0:1, :], ones128[0:1, :], start=False, stop=True),
                inc=("MMT", 1))
            p.on("vector", lambda e: e.tensor_tensor(
                dmt[:, :], psml1[:, 256:384], masktl[:, :], ALU.add),
                wait=[("MMT", p.cnt("MMT"))], inc=("DMB", 1))
            dmb_c = p.cnt("DMB")
            for n in range(DST):
                wex = [("GS1", p.cnt("GS1"))]
                if n == 0:
                    wex = [("DMB", dmb_c), ("ABC", 1), ("SCB", p.cnt("SCB"))]
                p.on("scalar", lambda e, n=n: e.activation(
                    escr[:, :], dmt[:, :], AF.Exp, scale=abc[:, n:n + 1]),
                    wait=wex, inc=("EXN", 1))
                wg1 = [("EXN", p.cnt("EXN"))]
                if n == 0:
                    wg1.append(("BBD", p.cnt("BBD")))
                p.on("vector", lambda e, n=n: e.tensor_scalar(
                    g1s[:, :], escr[:, :], bbm[:, n:n + 1], None, ALU.mult),
                    wait=wg1, inc=("GS1", 1))
                psl = psml0 if n % 2 == 0 else psml1
                p.on("tensor", lambda e, n=n, psl=psl: e.matmul(
                    psl[:, 384:512], onesb[0:1, :], ctrow[0:1, n * 128:(n + 1) * 128],
                    start=True, stop=True),
                    wait=([("CTR", ctr_c), ("ONB", 1), ("DMB", dmb_c)] if n <= 1
                          else [("GS2", p.cnt("GS2") - 1)]),
                    inc=("PNM", 1))
                p.on("vector", lambda e, n=n, psl=psl: e.tensor_tensor(
                    g1s[:, :], g1s[:, :], psl[:, 384:512], ALU.mult),
                    wait=[("PNM", p.cnt("PNM"))], inc=("GS2", 1))
                if n == 0:
                    p.on("vector", lambda e: e.tensor_copy(accT[:, :], g1s[:, :]))
                else:
                    p.on("vector", lambda e: e.tensor_tensor(
                        accT[:, :], accT[:, :], g1s[:, :], ALU.add))
            p.on("vector", lambda e: e.tensor_copy(acbf[:, :], accT[:, :]), inc=("SCB", 1))
            scb_c = p.cnt("SCB")
            p.on("tensor", lambda e: e.matmul(
                psml0[0:16, 256:384], arow[0:1, :], srow_loc[0:1, :], start=True, stop=True),
                wait=[("SRW", p.cnt("SRW")), ("CGB", p.cnt("CGB")),
                      ("SCB", scb_c), ("RRD", p.cnt("RRD"))], inc=("PAM", 1))
            p.on("scalar", lambda e: e.activation(
                ebuf[:, :], psml0[0:16, 256:384], AF.Exp),
                wait=[("PAM", p.cnt("PAM"))], inc=("EBF", 1))
            p.on("vector", lambda e: e.tensor_tensor(
                cgb[:, :], ebuf[:, :], ctsb[:, :], ALU.mult),
                wait=[("EBF", p.cnt("EBF"))], inc=("CGB", 1))
            cgb_c = p.cnt("CGB")
            wym = [("SCB", scb_c), ("CGB", cgb_c), ("XCC", p.cnt("XCC")),
                   ("HBF", p.cnt("HBF")), ("YPE", p.cnt("YPE"))]
            for dj in range(NJC):
                p.on("tensor", lambda e, dj=dj: e.matmul(
                    pbig[:, dj * 128:(dj + 1) * 128], xcb[:, dj * 128:(dj + 1) * 128],
                    acbf[:, :], start=True, stop=False),
                    wait=(wym if dj == 0 else None))
                p.on("tensor", lambda e, dj=dj: e.matmul(
                    pbig[:, dj * 128:(dj + 1) * 128], hbf[:, dj * 128:(dj + 1) * 128],
                    cgb[:, :], start=False, stop=True),
                    inc=(("YMM", 1) if dj == NJC - 1 else None))
            ymm_c = p.cnt("YMM")
            p.on("scalar", lambda e: e.activation(ysc[:, :], pbig[:, :], AF.Copy),
                 wait=[("YMM", ymm_c), ("WOM", p.cnt("WOM"))], inc=("YPE", 1))
            p.on("vector", lambda e: e.tensor_tensor(
                xcTb[:, :], xcTb[:, :], dpb[:, :], ALU.mult),
                wait=[("YPE", p.cnt("YPE")), ("XDB", xdb_c)], inc=("YP1", 1))
            p.on("vector", lambda e: e.tensor_tensor(
                ysc[:, :], ysc[:, :], xcTb[:, :], ALU.add), inc=("YP2", 1))
            p.on("vector", lambda e, cc=cc: e.tensor_tensor(
                ysc[:, :], ysc[:, :],
                zTg[:, :].rearrange("p (j t) -> p j t", j=NJC)[:, :, cc * 128:(cc + 1) * 128],
                ALU.mult),
                wait=[("SZD", szd_g)], inc=("YP3", 1))
            yp3_c = p.cnt("YP3")
            wum = [("BB2", p.cnt("BB2")), ("YPE", p.cnt("YPE")), ("XCC", p.cnt("XCC"))]
            for f in range(4):
                p.on("tensor", lambda e, f=f: e.matmul(
                    pbig[0:16, f * 512:(f + 1) * 512], bbm2[:, :],
                    xcb[:, f * 512:(f + 1) * 512], start=True, stop=True),
                    wait=(wum if f == 0 else None),
                    inc=(("UMM", 1) if f == 3 else None))
            umm_c = p.cnt("UMM")
            p.on("vector", lambda e: e.tensor_scalar(
                grb[:, 0:1], acol[:, :], psml0[0:16, 66:67], None, ALU.mult),
                wait=[("ACL", 1)], inc=("GRD", 1))
            p.on("scalar", lambda e: e.activation(
                grb[:, 1:2], grb[:, 0:1], AF.Exp),
                wait=[("GRD", p.cnt("GRD"))], inc=("GDC", 1))
            p.on("vector", lambda e: e.tensor_scalar(
                hst[:, :], hst[:, :], grb[:, 1:2], None, ALU.mult),
                wait=[("GDC", p.cnt("GDC")), ("UMM", umm_c)])
            p.on("vector", lambda e: e.tensor_tensor(
                hst[:, :], hst[:, :], pbig[0:16, :], ALU.add), inc=("HUP", 1))
            p.on("vector", lambda e: e.tensor_copy(hbf[:, :], hst[:, :]),
                 wait=[("YMM", ymm_c)], inc=("HBF", 1))
            wwm = [("YP3", yp3_c), ("UMM", umm_c), ("HUP", p.cnt("HUP"))]
            if p.cnt("PRE") > 0:
                wwm.append(("PRE", p.cnt("PRE")))
            for dj in range(NJC):
                for f in range(2):
                    p.on("tensor", lambda e, dj=dj, f=f: e.matmul(
                        pbig[:, f * 512:(f + 1) * 512], ysc[:, dj * 128:(dj + 1) * 128],
                        wob[:, dj * 1024 + f * 512:dj * 1024 + (f + 1) * 512],
                        start=(dj == 0), stop=(dj == NJC - 1)),
                        wait=(wwm if (dj == 0 and f == 0) else None),
                        inc=(("WOM", 1) if (dj == NJC - 1 and f == 1) else None))
            wom_c = p.cnt("WOM")
            wpr = [("WOM", wom_c)]
            if spl_slot[0] > 0:
                wpr.append(("PSP", spl_slot[0]))
            p.on("vector", lambda e, cc=cc: e.tensor_tensor(
                preb[:, :], z2g[:, cc * DM:(cc + 1) * DM],
                pbig[:, 0:1024], ALU.add),
                wait=wpr, inc=("PRE", 1))
            psp_t = p.on("sync", lambda e, ch=ch: e.dma_start(
                out=pre4[ch], in_=preb[:, :]),
                wait=[("PRE", p.cnt("PRE"))], inc=("PSP", 16))
            spl_slot[0] = psp_t
    p.on("sync", lambda e: e.dma_start(out=hend, in_=hst[:, :]),
         wait=[("HUP", p.cnt("HUP"))], inc=("HSP", 16))
    p.on("sync", lambda e: e.dma_start(out=cst_o, in_=cstb[:, :]),
         wait=[("CSD", p.cnt("CSD"))], inc=("HSP", 16))
    p.on("sync", lambda e: e.dma_start(out=sgst_o, in_=sgst[:, :]),
         wait=[("SGD", p.cnt("SGD"))], inc=("HSP", 16))
    p.on("gpsimd", lambda e: e.collective_compute(
        "AllGather", ALU.bypass, replica_groups=PAIRS,
        ins=[d["hend"].opt()], outs=[d["hend_gat"].opt()]),
        wait=[("HSP", 16)], inc=("CCH", 1))
    p.wait("gpsimd", [("CCH", 1)])
    p.wait("sync", [("PSP", p.cnt("PSP")), ("SPL", p.cnt("SPL")), ("HSP", p.cnt("HSP"))])


def emit_phase_c2(nc, p, al, d, P0, P1):
    preln = d["preln"]
    szsp = d["szsp"]
    cst_d = d["cst"]
    sgst_d = d["sgst"]
    alr_d = d["alr"]
    z3 = d["z3"]

    wob = al("wob2", [128, NJC * DM], BF16)
    identf = al("identf2", [128, 128], F32)
    arow = al("arow2", [1, DST], F32)
    hselb = al("hselb2", [16, DI], BF16)
    hself = al("hself2", [16, DI], F32)
    cstb = al("cstb2", [128, NCH * DST], F32)
    sgstb = al("sgst2", [128, NCH], F32)
    sgrow = al("sgrow2", [1, 128], F32)
    ctsb = al("ctsb2", [16, 128], F32)
    ebuf = al("ebuf2", [16, 128], F32)
    cgb = al("cgb2", [16, 128], BF16)
    szc = al("szc2", [128, NJC * 128], BF16)
    dyb = al("dyb2", [128, NJC * 128], BF16)
    prb = al("prb2", [128, 2 * DM], F32)
    ep1 = al("ep1c2", [128, 2 * DM], F32)
    mcol = al("mcol2", [128, 2 * 8], F32)
    konst = al("konst2", [128, 2], F32)
    cselc = al("cselc2", [128, 1], F32)
    pbig = PV(P0, 0, 2048)
    pmid = PV(P1, 0, 1024)
    psml = PV(P1, 1024, 512)

    pre4 = preln.rearrange("(ch p) m -> ch p m", p=128)
    z34 = z3.rearrange("(ch p) m -> ch p m", p=128)
    szsp4 = szsp.rearrange("(j p) t -> p j t", p=128)

    p.on("gpsimd", lambda e: e.dma_start(
        out=wob[:, :], in_=d["WoutP"]), inc=("LW", 16))
    p.on("gpsimd", lambda e: e.dma_start(out=hself[:, :], in_=d["hend_gat"][0:DST, :]),
         inc=("LW", 16))
    for dst, srcd in ((cstb, cst_d), (sgstb, sgst_d), (arow, alr_d),
                      (identf, d["identd"]), (cselc, d["csel"])):
        p.on("sync", lambda e, dst=dst, srcd=srcd: e.dma_start(out=dst[:, :], in_=srcd),
             inc=("LD", 16))
    ld_tot = p.cnt("LD")
    p.on("vector", lambda e: e.memset(konst[:, 0:1], LN_EPS))
    p.on("vector", lambda e: e.memset(konst[:, 1:2], -0.5), inc=("MST", 1))
    p.on("vector", lambda e: e.tensor_scalar(
        hselb[:, :], hself[:, :], cselc[0:16, 0:1], None, ALU.mult),
        wait=[("LW", 32), ("LD", ld_tot)], inc=("HSB", 1))
    p.on("scalar", lambda e: e.activation(arow[:, :], arow[:, :], AF.Exp),
         wait=[("LD", ld_tot)], inc=("ARP", 1))
    p.on("vector", lambda e: e.tensor_scalar(arow[:, :], arow[:, :], -1.0, None, ALU.mult),
         wait=[("ARP", 1)], inc=("ARN", 1))

    z3s_slot = [0, 0]
    prb_slot = [0, 0]
    for ch in range(NCH):
        b2 = ch % 2
        wlz = [("PRA", prb_slot[b2])] if prb_slot[b2] > 0 else None
        lpr_tgt = p.on("sync", lambda e, ch=ch, b2=b2: e.dma_start(
            out=prb[:, b2 * DM:(b2 + 1) * DM], in_=pre4[ch]),
            wait=wlz, inc=("LP%d" % b2, 16))
        wsz = [("DY3", p.cnt("DY3"))] if p.cnt("DY3") > 0 else None
        lsz_tgt = p.on("gpsimd", lambda e, ch=ch: e.dma_start(
            out=szc[:, :].rearrange("p (j t) -> p j t", j=NJC),
            in_=szsp4[:, :, ch * 128:(ch + 1) * 128]),
            wait=wsz, inc=("LSZ", 16))
        wtr = [("LD", ld_tot), ("ARN", 1)]
        if p.cnt("SRW") > 0:
            wtr += [("SRW", p.cnt("SRW")), ("CTS", p.cnt("CTS"))]
        p.on("tensor", lambda e, ch=ch: e.transpose(
            psml[0:1, 0:128], sgstb[:, ch:ch + 1], identf[:, :]),
            wait=wtr, inc=("TRX", 1))
        p.on("tensor", lambda e, ch=ch: e.transpose(
            psml[0:16, 128:256], cstb[:, ch * DST:(ch + 1) * DST], identf[:, :]),
            inc=("TRX", 1))
        trx_c = p.cnt("TRX")
        p.on("vector", lambda e: e.tensor_copy(sgrow[:, :], psml[0:1, 0:128]),
             wait=[("TRX", trx_c), ("PAM", p.cnt("PAM"))], inc=("SRW", 1))
        p.on("vector", lambda e: e.tensor_copy(ctsb[:, :], psml[0:16, 128:256]),
             inc=("CTS", 1))
        p.on("tensor", lambda e: e.matmul(
            psml[0:16, 256:384], arow[0:1, :], sgrow[0:1, :], start=True, stop=True),
            wait=[("SRW", p.cnt("SRW")), ("CTS", p.cnt("CTS")),
                  ("EBF", p.cnt("EBF"))], inc=("PAM", 1))
        p.on("scalar", lambda e: e.activation(
            ebuf[:, :], psml[0:16, 256:384], AF.Exp),
            wait=[("PAM", p.cnt("PAM"))], inc=("EBF", 1))
        p.on("vector", lambda e: e.tensor_tensor(
            cgb[:, :], ebuf[:, :], ctsb[:, :], ALU.mult),
            wait=[("EBF", p.cnt("EBF"))], inc=("CGB", 1))
        cgb_c = p.cnt("CGB")
        wym = [("CGB", cgb_c), ("HSB", 1)]
        if p.cnt("DYE") > 0:
            wym.append(("DYE", p.cnt("DYE")))
        for dj in range(NJC):
            p.on("tensor", lambda e, dj=dj: e.matmul(
                pbig[:, dj * 128:(dj + 1) * 128], hselb[:, dj * 128:(dj + 1) * 128],
                cgb[:, :], start=True, stop=True),
                wait=(wym if dj == 0 else None),
                inc=(("DYM", 1) if dj == NJC - 1 else None))
        dym_c = p.cnt("DYM")
        p.on("scalar", lambda e: e.activation(dyb[:, :], pbig[:, :], AF.Copy),
             wait=[("DYM", dym_c), ("WOM", p.cnt("WOM"))], inc=("DYE", 1))
        p.on("vector", lambda e: e.tensor_tensor(
            dyb[:, :], dyb[:, :], szc[:, :], ALU.mult),
            wait=[("DYE", p.cnt("DYE")), ("LSZ", lsz_tgt)], inc=("DY3", 1))
        dy3_c = p.cnt("DY3")
        wwm = [("DY3", dy3_c)]
        if p.cnt("PRA") > 0:
            wwm.append(("PRA", p.cnt("PRA")))
        for dj in range(NJC):
            for f in range(2):
                p.on("tensor", lambda e, dj=dj, f=f: e.matmul(
                    pmid[:, f * 512:(f + 1) * 512], dyb[:, dj * 128:(dj + 1) * 128],
                    wob[:, dj * 1024 + f * 512:dj * 1024 + (f + 1) * 512],
                    start=(dj == 0), stop=(dj == NJC - 1)),
                    wait=(wwm if (dj == 0 and f == 0) else None),
                    inc=(("WOM", 1) if (dj == NJC - 1 and f == 1) else None))
        wom_c = p.cnt("WOM")
        p.on("vector", lambda e, b2=b2: e.tensor_tensor(
            prb[:, b2 * DM:(b2 + 1) * DM], prb[:, b2 * DM:(b2 + 1) * DM],
            pmid[:, :], ALU.add),
            wait=[("WOM", wom_c), ("LP%d" % b2, lpr_tgt)], inc=("PRA", 1))
        p.on("vector", lambda e, b2=b2: e.reduce_sum(
            mcol[:, b2 * 8:b2 * 8 + 1], prb[:, b2 * DM:(b2 + 1) * DM], AX.X))
        p.on("vector", lambda e, b2=b2: e.tensor_scalar(
            mcol[:, b2 * 8 + 1:b2 * 8 + 2], mcol[:, b2 * 8:b2 * 8 + 1], 1.0 / DM, None, ALU.mult))
        wsub = [("ZS3%d" % b2, z3s_slot[b2])] if z3s_slot[b2] > 0 else None
        p.on("vector", lambda e, b2=b2: e.tensor_scalar(
            ep1[:, b2 * DM:(b2 + 1) * DM], prb[:, b2 * DM:(b2 + 1) * DM],
            mcol[:, b2 * 8 + 1:b2 * 8 + 2], None, ALU.subtract),
            wait=wsub, inc=("EZC", 1))
        p.on("scalar", lambda e, b2=b2: e.activation(
            prb[:, b2 * DM:(b2 + 1) * DM], ep1[:, b2 * DM:(b2 + 1) * DM], AF.Square),
            wait=[("EZC", p.cnt("EZC")), ("MST", 1)], inc=("SQ2", 1))
        p.on("vector", lambda e, b2=b2: e.reduce_sum(
            mcol[:, b2 * 8 + 2:b2 * 8 + 3], prb[:, b2 * DM:(b2 + 1) * DM], AX.X),
            wait=[("SQ2", p.cnt("SQ2"))])
        p.on("vector", lambda e, b2=b2: e.tensor_scalar(
            mcol[:, b2 * 8 + 2:b2 * 8 + 3], mcol[:, b2 * 8 + 2:b2 * 8 + 3], 1.0 / DM, None, ALU.mult),
            inc=("EV2", 1))
        p.on("scalar", lambda e, b2=b2: e.activation(
            mcol[:, b2 * 8 + 3:b2 * 8 + 4], mcol[:, b2 * 8 + 2:b2 * 8 + 3], AF.Ln, bias=konst[:, 0:1]),
            wait=[("EV2", p.cnt("EV2"))])
        p.on("scalar", lambda e, b2=b2: e.activation(
            mcol[:, b2 * 8 + 3:b2 * 8 + 4], mcol[:, b2 * 8 + 3:b2 * 8 + 4], AF.Exp, scale=konst[:, 1:2]),
            inc=("RS2", 1))
        p.on("vector", lambda e, b2=b2: e.tensor_scalar(
            ep1[:, b2 * DM:(b2 + 1) * DM], ep1[:, b2 * DM:(b2 + 1) * DM],
            mcol[:, b2 * 8 + 3:b2 * 8 + 4], None, ALU.mult),
            wait=[("RS2", p.cnt("RS2"))], inc=("EPF", 1))
        zs3_t = p.on("sync", lambda e, ch=ch, b2=b2: e.dma_start(
            out=z34[ch], in_=ep1[:, b2 * DM:(b2 + 1) * DM]),
            wait=[("EPF", p.cnt("EPF"))], inc=("ZS3%d" % b2, 16))
        z3s_slot[b2] = zs3_t
    p.wait("sync", [("ZS30", p.cnt("ZS30")), ("ZS31", p.cnt("ZS31"))])


def emit_phase_d(nc, p, al, d, P0, P1):
    z3 = d["z3"]
    w1P = d["w1P"]
    w2P = d["w2P"]
    h1P = d["h1P"]
    h2P = d["h2P"]
    w3P = d["w3P"]
    s1c = d["s1c"]
    s2c = d["s2c"]
    s1hc = d["s1hc"]
    s2hc = d["s2hc"]
    s3r = d["s3r"]
    z4 = d["z4"]

    identf = al("identfd", [128, 128], F32)
    onesr = al("onesrd", [1, 128], F32)
    konst = al("konstd", [128, 2], F32)
    s3b = al("s3b", [128, DM], F32)
    s3s = al("s3s", [1, DM], F32)
    s1s = al("s1s", [128, NJD], F32)
    s2s = al("s2s", [128, NJD], F32)
    s1hs = al("s1hs", [128, NJD], F32)
    s2hs = al("s2hs", [128, NJD], F32)
    z3b = al("z3b", [128, 2 * DM], F32)
    z3T = al("z3T", [128, 8 * 1024], BF16)
    w12j = al("w12j", [128, 2 * 2 * 8 * 128], BF16)
    hjb = al("hjb", [128, 2 * 32 * 128], BF16)
    h1b = al("h1b", [128, NJD * 1024], BF16)
    h2b = al("h2b", [128, NJD * 1024], BF16)
    silb = al("silbd", [128, 1024], BF16)
    z4b = z3b
    ep0 = al("ep0d", [128, 2 * DM], F32)
    ep1 = al("ep1d", [128, 2 * DM], F32)
    mcol = al("mcold", [128, 2 * 8], F32)
    p1 = PV(P0, 0, 1024)
    p2 = PV(P0, 1024, 1024)
    p3 = PV(P1, 0, 1024)
    p4 = PV(P1, 1024, 1024)
    pab = [(p1, p2), (p3, p4)]
    pd_ = [p1, p2]

    z3t4 = z3.rearrange("(tq p) m -> tq p m", p=128)
    z4t4 = z4.rearrange("(tq p) m -> tq p m", p=128)

    p.on("sync", lambda e: e.dma_start(out=identf[:, :], in_=d["identd"]), inc=("LG", 16))
    p.on("sync", lambda e: e.dma_start(out=s3s[:, :], in_=s3r), inc=("LG", 16))
    p.on("sync", lambda e: e.dma_start(out=s1s[:, :], in_=s1c), inc=("LG", 16))
    p.on("sync", lambda e: e.dma_start(out=s2s[:, :], in_=s2c), inc=("LG", 16))
    p.on("sync", lambda e: e.dma_start(out=s1hs[:, :], in_=s1hc), inc=("LG", 16))
    p.on("sync", lambda e: e.dma_start(out=s2hs[:, :], in_=s2hc), inc=("LG", 16))
    p.on("vector", lambda e: e.memset(konst[:, 0:1], LN_EPS))
    p.on("vector", lambda e: e.memset(konst[:, 1:2], -0.5))
    p.on("vector", lambda e: e.memset(onesr[:, :], 1.0), inc=("MST", 1))
    for f in range(2):
        p.on("tensor", lambda e, f=f: e.matmul(
            p2[:, f * 512:(f + 1) * 512], onesr[0:1, :], s3s[0:1, f * 512:(f + 1) * 512],
            start=True, stop=True), wait=[("MST", 1), ("LG", 96)], inc=("MMD", 1))
    p.on("vector", lambda e: e.tensor_copy(s3b[:, :], p2[:, :]), wait=[("MMD", 2)], inc=("S3B", 1))

    NQ = 4
    TQ = 8
    w12_slot = [0, 0]
    hj_slot = [None, None]
    z4s_slot = [0, 0]
    vstd_slot = [None, None]
    ge2_slot = [0, 0]
    ge3_slot = [0, 0]
    mm4_slot = [0, 0]
    prevq = {"MM1": 0, "MM2": 0, "MM3": 0, "MM4": 0, "GE2": 0, "EP4": 0,
             "SIL1": 0, "VST1": 0}

    for q in range(NQ):
        for T in range(TQ):
            tq = q * TQ + T
            b2 = tq % 2
            wld = [("YQ0", p.cnt("YQ0") - 1)] if p.cnt("YQ0") >= 2 else []
            if q >= 1 and T <= 1:
                wld = (wld or []) + [("EP4", prevq["EP4"])]
            lz_tgt = p.on("sync", lambda e, tq=tq, b2=b2: e.dma_start(
                out=z3b[:, b2 * DM:(b2 + 1) * DM], in_=z3t4[tq]),
                wait=(wld or None), inc=("LZ3%d" % b2, 16))
            wtr = [("LZ3%d" % b2, lz_tgt)]
            if T == 0:
                wtr.append(("MM4", prevq["MM4"]))
                wtr.append(("EP4", prevq["EP4"]))
            if p.cnt("YQ0") > 0:
                wtr.append(("YQ0", p.cnt("YQ0")))
            for k in range(8):
                p.on("tensor", lambda e, k=k, b2=b2: e.transpose(
                    p1[:, k * 128:(k + 1) * 128], z3b[:, b2 * DM + k * 128:b2 * DM + (k + 1) * 128],
                    identf[:, :]),
                    wait=(wtr if k == 0 else None),
                    inc=(("MT0", 1) if k == 7 else None))
            wev = [("MT0", p.cnt("MT0"))]
            if q >= 1 and T == 0:
                wev.append(("MM1", prevq["MM1"]))
            for k in range(8):
                p.on("scalar", lambda e, k=k, T=T: e.activation(
                    z3T[:, k * 1024 + T * 128:k * 1024 + (T + 1) * 128],
                    p1[:, k * 128:(k + 1) * 128], AF.Copy),
                    wait=(wev if k == 0 else None),
                    inc=(("YQ0", 1) if k == 7 else None))
        yq0_q = p.cnt("YQ0")
        mt0_q = p.cnt("MT0")

        for j in range(NJD):
            sl = j % 2
            wlw = [("MM1", w12_slot[sl])] if w12_slot[sl] > 0 else None
            lw_tgt = p.on("gpsimd", lambda e, j=j, sl=sl: e.dma_start(
                out=w12j[:, sl * 2048:sl * 2048 + 1024],
                in_=w1P[j]),
                wait=wlw, inc=("LW%d" % sl, 16))
            lw_tgt = p.on("gpsimd", lambda e, j=j, sl=sl: e.dma_start(
                out=w12j[:, sl * 2048 + 1024:sl * 2048 + 2048],
                in_=w2P[j]),
                inc=("LW%d" % sl, 16))
            pa, pb = pab[j % 2]
            wmm = [("LW%d" % sl, lw_tgt), ("YQ0", yq0_q)]
            if vstd_slot[j % 2] is not None:
                wmm.append(("VST1", vstd_slot[j % 2][0]))
                wmm.append(("SIL1", vstd_slot[j % 2][1]))
            if q == 0 and j <= 1:
                wmm.append(("S3B", 1))
            first = True
            for f in range(2):
                for k in range(8):
                    p.on("tensor", lambda e, j=j, sl=sl, f=f, k=k, pa=pa: e.matmul(
                        pa[:, f * 512:(f + 1) * 512],
                        w12j[:, sl * 2048 + k * 128:sl * 2048 + (k + 1) * 128],
                        z3T[:, k * 1024 + f * 512:k * 1024 + (f + 1) * 512],
                        start=(k == 0), stop=(k == 7)),
                        wait=(wmm if first else None))
                    first = False
                for k in range(8):
                    p.on("tensor", lambda e, j=j, sl=sl, f=f, k=k, pb=pb: e.matmul(
                        pb[:, f * 512:(f + 1) * 512],
                        w12j[:, sl * 2048 + 1024 + k * 128:sl * 2048 + 1024 + (k + 1) * 128],
                        z3T[:, k * 1024 + f * 512:k * 1024 + (f + 1) * 512],
                        start=(k == 0), stop=(k == 7)),
                        inc=(("MM1", 1) if (f == 1 and k == 7) else None))
            mm1_j = p.cnt("MM1")
            p.on("scalar", lambda e, j=j, pa=pa: e.activation(
                silb[:, :], pa[:, :], AF.Silu, scale=s1s[:, j:j + 1]),
                wait=[("MM1", mm1_j), ("VST1", p.cnt("VST1"))], inc=("SIL1", 1))
            vst_t = p.on("vector", lambda e, j=j, pb=pb: e.scalar_tensor_tensor(
                h1b[:, j * 1024:(j + 1) * 1024], pb[:, :], s2s[:, j:j + 1], silb[:, :],
                ALU.mult, ALU.mult),
                wait=[("SIL1", p.cnt("SIL1")), ("MM4", prevq["MM4"])], inc=("VST1", 1))
            vstd_slot[j % 2] = (vst_t, p.cnt("SIL1"))
            w12_slot[sl] = p.cnt("MM1")
        vst1_q = p.cnt("VST1")
        sil1_q = p.cnt("SIL1")

        for j2 in range(NJD):
            sl = j2 % 2
            wlh = [hj_slot[sl]] if hj_slot[sl] is not None else None
            lh_tgt = p.on("gpsimd", lambda e, j2=j2, sl=sl: e.dma_start(
                out=hjb[:, sl * 4096:(sl + 1) * 4096],
                in_=h1P[j2]),
                wait=wlh, inc=("LH%d" % sl, 16))
            pcx = [p1, p2][j2 % 2]
            wmm = [("LH%d" % sl, lh_tgt), ("VST1", vst1_q), ("SIL1", sil1_q)]
            if ge2_slot[j2 % 2] > 0:
                wmm.append(("GE2", ge2_slot[j2 % 2]))
            first = True
            for f in range(2):
                for k in range(32):
                    p.on("tensor", lambda e, j2=j2, sl=sl, f=f, k=k, pcx=pcx: e.matmul(
                        pcx[:, f * 512:(f + 1) * 512],
                        hjb[:, sl * 4096 + k * 128:sl * 4096 + (k + 1) * 128],
                        h1b[:, k * 1024 + f * 512:k * 1024 + (f + 1) * 512],
                        start=(k == 0), stop=(k == 31)),
                        wait=(wmm if first else None),
                        inc=(("MM2", 1) if (f == 1 and k == 31) else None))
                    first = False
            wge = [("MM2", p.cnt("MM2"))]
            if q >= 1 and j2 == 0:
                wge.append(("MM3", prevq["MM3"]))
                wge.append(("MM4", prevq["MM4"]))
            ge_t = p.on("scalar", lambda e, j2=j2, pcx=pcx: e.activation(
                h2b[:, j2 * 1024:(j2 + 1) * 1024], pcx[:, :], AF.Gelu, scale=s1hs[:, j2:j2 + 1]),
                wait=wge, inc=("GE2", 1))
            ge2_slot[j2 % 2] = ge_t
            hj_slot[sl] = ("MM2", p.cnt("MM2"))
        ge2_q = p.cnt("GE2")
        mm2_q = p.cnt("MM2")

        for j3 in range(NJD):
            sl = j3 % 2
            wlh = [hj_slot[sl]] if hj_slot[sl] is not None else None
            lh_tgt = p.on("gpsimd", lambda e, j3=j3, sl=sl: e.dma_start(
                out=hjb[:, sl * 4096:(sl + 1) * 4096],
                in_=h2P[j3]),
                wait=wlh, inc=("LH%d" % sl, 16))
            pcx = [p3, p4][j3 % 2]
            wmm = [("LH%d" % sl, lh_tgt), ("GE2", ge2_q)]
            if ge3_slot[j3 % 2] > 0:
                wmm.append(("GE3", ge3_slot[j3 % 2]))
            first = True
            for f in range(2):
                for k in range(32):
                    p.on("tensor", lambda e, j3=j3, sl=sl, f=f, k=k, pcx=pcx: e.matmul(
                        pcx[:, f * 512:(f + 1) * 512],
                        hjb[:, sl * 4096 + k * 128:sl * 4096 + (k + 1) * 128],
                        h2b[:, k * 1024 + f * 512:k * 1024 + (f + 1) * 512],
                        start=(k == 0), stop=(k == 31)),
                        wait=(wmm if first else None),
                        inc=(("MM3", 1) if (f == 1 and k == 31) else None))
                    first = False
            wge = [("MM3", p.cnt("MM3")), ("MM2", mm2_q)]
            ge_t = p.on("scalar", lambda e, j3=j3, pcx=pcx: e.activation(
                h1b[:, j3 * 1024:(j3 + 1) * 1024], pcx[:, :], AF.Gelu, scale=s2hs[:, j3:j3 + 1]),
                wait=wge, inc=("GE3", 1))
            ge3_slot[j3 % 2] = ge_t
            hj_slot[sl] = ("MM3", p.cnt("MM3"))
        ge3_q = p.cnt("GE3")

        wl3 = [("MM3", p.cnt("MM3"))]
        lw3_tgt = p.on("gpsimd", lambda e: e.dma_start(
            out=h2b[:, :], in_=w3P),
            wait=wl3, inc=("LW3D", 16))
        for T in range(TQ):
            tq = q * TQ + T
            b2 = tq % 2
            wzr = [("MT0", mt0_q)]
            if p.cnt("EP4") >= 2:
                wzr.append(("EP4", p.cnt("EP4") - 1))
            lz4_tgt = p.on("sync", lambda e, tq=tq, b2=b2: e.dma_start(
                out=z4b[:, b2 * DM:(b2 + 1) * DM], in_=z3t4[tq]),
                wait=wzr, inc=("LZ4%d" % b2, 16))
            pdx = pd_[T % 2]
            wmm = [("GE3", ge3_q), ("LW3D", lw3_tgt)]
            if mm4_slot[T % 2] > 0:
                wmm.append(("EV4", mm4_slot[T % 2]))
            first = True
            for f in range(2):
                for k in range(32):
                    p.on("tensor", lambda e, T=T, f=f, k=k, pdx=pdx: e.matmul(
                        pdx[:, f * 512:(f + 1) * 512],
                        h1b[:, k * 1024 + T * 128:k * 1024 + (T + 1) * 128],
                        h2b[:, k * 1024 + f * 512:k * 1024 + (f + 1) * 512],
                        start=(k == 0), stop=(k == 31)),
                        wait=(wmm if first else None),
                        inc=(("MM4", 1) if (f == 1 and k == 31) else None))
                    first = False
            mm4_now = p.cnt("MM4")
            ev_t = p.on("vector", lambda e, b2=b2, pdx=pdx: e.tensor_mul(
                ep0[:, b2 * DM:(b2 + 1) * DM], pdx[:, :], s3b[:, :]),
                wait=[("MM4", mm4_now), ("LZ4%d" % b2, lz4_tgt)], inc=("EV4", 1))
            mm4_slot[T % 2] = ev_t
            p.on("vector", lambda e, b2=b2: e.tensor_add(
                ep0[:, b2 * DM:(b2 + 1) * DM], ep0[:, b2 * DM:(b2 + 1) * DM],
                z4b[:, b2 * DM:(b2 + 1) * DM]))
            p.on("vector", lambda e, b2=b2: e.reduce_sum(
                mcol[:, b2 * 8:b2 * 8 + 1], ep0[:, b2 * DM:(b2 + 1) * DM], AX.X))
            p.on("vector", lambda e, b2=b2: e.tensor_scalar(
                mcol[:, b2 * 8 + 1:b2 * 8 + 2], mcol[:, b2 * 8:b2 * 8 + 1], 1.0 / DM, None, ALU.mult))
            wsub = [("ZS4%d" % b2, z4s_slot[b2])] if z4s_slot[b2] > 0 else None
            p.on("vector", lambda e, b2=b2: e.tensor_scalar(
                ep1[:, b2 * DM:(b2 + 1) * DM], ep0[:, b2 * DM:(b2 + 1) * DM],
                mcol[:, b2 * 8 + 1:b2 * 8 + 2], None, ALU.subtract),
                wait=wsub, inc=("EZ4", 1))
            p.on("scalar", lambda e, b2=b2: e.activation(
                ep0[:, b2 * DM:(b2 + 1) * DM], ep1[:, b2 * DM:(b2 + 1) * DM], AF.Square),
                wait=[("EZ4", p.cnt("EZ4"))], inc=("SQ4", 1))
            p.on("vector", lambda e, b2=b2: e.reduce_sum(
                mcol[:, b2 * 8 + 2:b2 * 8 + 3], ep0[:, b2 * DM:(b2 + 1) * DM], AX.X),
                wait=[("SQ4", p.cnt("SQ4"))])
            p.on("vector", lambda e, b2=b2: e.tensor_scalar(
                mcol[:, b2 * 8 + 2:b2 * 8 + 3], mcol[:, b2 * 8 + 2:b2 * 8 + 3], 1.0 / DM, None, ALU.mult),
                inc=("EVV4", 1))
            p.on("scalar", lambda e, b2=b2: e.activation(
                mcol[:, b2 * 8 + 3:b2 * 8 + 4], mcol[:, b2 * 8 + 2:b2 * 8 + 3], AF.Ln, bias=konst[:, 0:1]),
                wait=[("EVV4", p.cnt("EVV4"))])
            p.on("scalar", lambda e, b2=b2: e.activation(
                mcol[:, b2 * 8 + 3:b2 * 8 + 4], mcol[:, b2 * 8 + 3:b2 * 8 + 4], AF.Exp, scale=konst[:, 1:2]),
                inc=("RS4", 1))
            p.on("vector", lambda e, b2=b2: e.tensor_scalar(
                ep1[:, b2 * DM:(b2 + 1) * DM], ep1[:, b2 * DM:(b2 + 1) * DM],
                mcol[:, b2 * 8 + 3:b2 * 8 + 4], None, ALU.mult),
                wait=[("RS4", p.cnt("RS4"))], inc=("EP4", 1))
            zs4_t = p.on("sync", lambda e, tq=tq, b2=b2: e.dma_start(
                out=z4t4[tq], in_=ep1[:, b2 * DM:(b2 + 1) * DM]),
                wait=[("EP4", p.cnt("EP4"))], inc=("ZS4%d" % b2, 16))
            z4s_slot[b2] = zs4_t
        prevq = {"MM1": p.cnt("MM1"), "MM2": p.cnt("MM2"), "MM3": p.cnt("MM3"),
                 "MM4": p.cnt("MM4"), "GE2": p.cnt("GE2"), "EP4": p.cnt("EP4"),
                 "SIL1": p.cnt("SIL1"), "VST1": p.cnt("VST1")}
    p.wait("sync", [("ZS40", p.cnt("ZS40")), ("ZS41", p.cnt("ZS41"))])


def build_fused():
    from contextlib import ExitStack
    with ExitStack() as st:
        nc = bass.Bass("TRN2", target_bir_lowering=False)
        d = {}

        def ein(name, shape, dtype=F32):
            d[name] = nc.dram_tensor(name, shape, dtype, kind="ExternalInput").ap()

        def itn(name, shape, dtype=F32):
            d[name] = nc.dram_tensor(name, shape, dtype).ap()

        ein("u", [T_CORE, DM]); ein("uT", [DM, T_CORE]); ein("uTprev", [DM, T_CORE])
        ein("BwT", [DM, NSTATE]); ein("CwT", [NSTATE, DM]); ein("ll2", [128, 2])
        ein("Drow", [1, DM])
        ein("gwT", [DM, NE]); ein("rmsw", [1, DM])
        ein("W1P", [NE, NJ, 128, 1024]); ein("W2P", [NE, NJ, 128, 1024])
        ein("W3P", [NE, 128, NJ * DM])
        ein("identd", [128, 128])
        ein("W_in", [DM, 2 * DI]); ein("W_x", [DI, 48])
        ein("cwc", [128, NJC * 4]); ein("cbc", [128, NJC]); ein("dpc", [128, NJC])
        ein("alr", [1, DST]); ein("alc", [DST, 1]); ein("WoutP", [128, NJC * DM])
        ein("WinP", [32, 128, 1024]); ein("WxP", [128, NJC * 48])
        ein("cumin", [128, 128]); ein("sufst", [128, 128]); ein("masktl", [128, 128])
        ein("ones128", [128, 128]); ein("csel", [128, 1])
        ein("w1P", [NJD, 128, 1024]); ein("w2P", [NJD, 128, 1024])
        ein("h1P", [NJD, 128, 4096]); ein("h2P", [NJD, 128, 4096])
        ein("w3P", [128, NJD * DM])
        ein("s1c", [128, NJD]); ein("s2c", [128, NJD]); ein("s1hc", [128, NJD])
        ein("s2hc", [128, NJD]); ein("s3r", [1, DM])
        itn("z1", [T_CORE, DM]); itn("z2", [T_CORE, DM]); itn("z3", [T_CORE, DM])
        itn("W1C", [NE, NJ, 128, 1024], BF16); itn("W2C", [NE, NJ, 128, 1024], BF16)
        itn("W3C", [NE, 128, NJ * DM], BF16)
        itn("comb", [T_CORE, DM]); itn("preln", [T_CORE, DM])
        itn("szsp", [DI, T_CORE], BF16)
        itn("cst", [128, NCH * DST]); itn("sgst", [128, NCH])
        itn("tail_self", [128, NJC * 4]); itn("tail_gat", [256, NJC * 4])
        itn("hend", [DST, DI]); itn("hend_gat", [2 * DST, DI])
        d["z4"] = nc.dram_tensor("z4", [T_CORE, DM], F32, kind="ExternalOutput").ap()

        P0 = st.enter_context(nc.psum_tensor("PP0", [128, 2048], F32))
        P1 = st.enter_context(nc.psum_tensor("PP1", [128, 2048], F32))

        pool = Pool(nc)
        p = Prog(nc, pool)
        arenas = []
        for name, fn in (("a", emit_phase_a), ("b", emit_phase_b),
                         ("c0", emit_phase_c0), ("c1", emit_phase_c1),
                         ("c2", emit_phase_c2), ("dd", emit_phase_d)):
            ar = Arena(nc, name + "_")
            fn(nc, p, ar.al, d, P0, P1)
            arenas.append(ar)
            if name != "dd":
                p.phase_end()

        mx = max(ar.addr for ar in arenas)
        backing = st.enter_context(nc.sbuf_tensor("ARENA", [128, mx], mybir.dt.uint8))
        base_addr = nc.lookup_mloc(backing).addr
        for ar in arenas:
            ar.place(base_addr)

        with nc.Block() as block:
            p.emit(block)
    return nc


def host_inputs(inputs):
    U = np.asarray(inputs["L0_attn_u"], np.float32)
    Bw = np.asarray(inputs["L0_attn_B_w"], np.float32)
    Cw = np.asarray(inputs["L0_attn_C_w"], np.float32)
    ll = np.asarray(inputs["L0_attn_log_lambda"], np.float32)
    Dp = np.asarray(inputs["L0_attn_D"], np.float32)
    W1 = np.asarray(inputs["L0_ffn_W1"], np.float32)
    W2 = np.asarray(inputs["L0_ffn_W2"], np.float32)
    W3 = np.asarray(inputs["L0_ffn_W3"], np.float32)
    W_in = np.asarray(inputs["L1_attn_W_in"], np.float32)
    W_x = np.asarray(inputs["L1_attn_W_x"], np.float32)
    W_x48 = np.zeros((DI, 48), np.float32)
    W_x48[:, 0:33] = W_x
    conv_w = np.asarray(inputs["L1_attn_conv_w"], np.float32)
    conv_b = np.asarray(inputs["L1_attn_conv_b"], np.float32)
    D_param = np.asarray(inputs["L1_attn_D_param"], np.float32)
    A_log = np.asarray(inputs["L1_attn_A_log"], np.float32)
    W_out = np.asarray(inputs["L1_attn_W_out"], np.float32)
    w1 = np.asarray(inputs["L1_ffn_w1_w"], np.float32)
    w2 = np.asarray(inputs["L1_ffn_w2_w"], np.float32)
    h1 = np.asarray(inputs["L1_ffn_h1_w"], np.float32)
    h2 = np.asarray(inputs["L1_ffn_h2_w"], np.float32)
    w3 = np.asarray(inputs["L1_ffn_w3_w"], np.float32)
    t_ = np.ones((128, 128), np.float32)
    common = {
        "BwT": np.ascontiguousarray(Bw.T),
        "CwT": np.ascontiguousarray(Cw.T),
        "ll2": np.ascontiguousarray(ll.reshape(2, 128).T),
        "Drow": np.ascontiguousarray(Dp[None, :]),
        "gwT": np.ascontiguousarray(np.asarray(inputs["L0_ffn_gate_w"], np.float32).T),
        "rmsw": np.ascontiguousarray(np.asarray(inputs["L0_ffn_rms_w"], np.float32)[None, :]),
        "W1P": np.ascontiguousarray(W1.reshape(NE, NJ, 128, 8, 128).transpose(0, 1, 4, 3, 2).reshape(NE, NJ, 128, 1024)),
        "W2P": np.ascontiguousarray(W2.reshape(NE, NJ, 128, 8, 128).transpose(0, 1, 4, 3, 2).reshape(NE, NJ, 128, 1024)),
        "W3P": np.ascontiguousarray(W3.reshape(NE, DM, NJ, 128).transpose(0, 3, 2, 1).reshape(NE, 128, NJ * DM)),
        "identd": np.eye(128, dtype=np.float32),
        "W_in": W_in, "W_x": W_x48,
        "cwc": np.ascontiguousarray(conv_w.reshape(NJC, 128, 4).transpose(1, 0, 2).reshape(128, NJC * 4)),
        "cbc": np.ascontiguousarray(conv_b.reshape(NJC, 128).T),
        "dpc": np.ascontiguousarray(D_param.reshape(NJC, 128).T),
        "alr": np.ascontiguousarray(A_log[None, :]),
        "alc": np.ascontiguousarray(A_log[:, None]),
        "WoutP": np.ascontiguousarray(W_out.reshape(NJC, 128, DM).transpose(1, 0, 2).reshape(128, NJC * DM)),
        "WinP": np.ascontiguousarray(W_in.reshape(8, 128, 32, 128).transpose(2, 1, 0, 3).reshape(32, 128, 1024)),
        "WxP": np.ascontiguousarray(W_x48.reshape(NJC, 128, 48).transpose(1, 0, 2).reshape(128, NJC * 48)),
        "cumin": np.ascontiguousarray(np.triu(t_)),
        "sufst": np.ascontiguousarray(np.tril(t_, -1)),
        "masktl": np.ascontiguousarray(np.tril(t_, -1) * 1e30),
        "ones128": t_,
        "w1P": np.ascontiguousarray(w1.reshape(NJD, 128, 8, 128).transpose(0, 3, 2, 1).reshape(NJD, 128, 1024)),
        "w2P": np.ascontiguousarray(w2.reshape(NJD, 128, 8, 128).transpose(0, 3, 2, 1).reshape(NJD, 128, 1024)),
        "h1P": np.ascontiguousarray(h1.reshape(NJD, 128, 32, 128).transpose(0, 3, 2, 1).reshape(NJD, 128, 4096)),
        "h2P": np.ascontiguousarray(h2.reshape(NJD, 128, 32, 128).transpose(0, 3, 2, 1).reshape(NJD, 128, 4096)),
        "w3P": np.ascontiguousarray(w3.transpose(1, 0).reshape(NJD, 128, DM).transpose(1, 0, 2).reshape(128, NJD * DM)),
        "s1c": np.ascontiguousarray(np.asarray(inputs["L1_ffn_w1_s"], np.float32).reshape(NJD, 128).T),
        "s2c": np.ascontiguousarray(np.asarray(inputs["L1_ffn_w2_s"], np.float32).reshape(NJD, 128).T),
        "s1hc": np.ascontiguousarray(np.asarray(inputs["L1_ffn_h1_s"], np.float32).reshape(NJD, 128).T),
        "s2hc": np.ascontiguousarray(np.asarray(inputs["L1_ffn_h2_s"], np.float32).reshape(NJD, 128).T),
        "s3r": np.ascontiguousarray(np.asarray(inputs["L1_ffn_w3_s"], np.float32)[None, :]),
    }
    maps = []
    for c in range(NCORES):
        b, h = c // 2, c % 2
        m = dict(common)
        u_c = np.ascontiguousarray(U[b, h * T_CORE:(h + 1) * T_CORE])
        m["u"] = u_c
        m["uT"] = np.ascontiguousarray(u_c.T)
        if h == 1:
            m["uTprev"] = np.ascontiguousarray(U[b, 0:T_CORE].T)
        else:
            m["uTprev"] = np.zeros((DM, T_CORE), np.float32)
        m["csel"] = np.full((128, 1), float(h), np.float32)
        maps.append(m)
    return maps


def kernel(**inputs):
    nc = build_fused()
    maps = host_inputs(inputs)
    res = run_bass_kernel_spmd(nc, maps, core_ids=list(range(NCORES)))
    z4 = np.empty((4, 8192, DM), np.float32)
    for c in range(NCORES):
        b, h = c // 2, c % 2
        z4[b, h * T_CORE:(h + 1) * T_CORE] = res.results[c]["z4"]
    return z4
